# version 02
# baseline (speedup 1.0000x reference)
import contextlib
import os as _os
import numpy as np
import ml_dtypes
import concourse.bass as bass
import concourse.mybir as mybir
from concourse.bass_utils import run_bass_kernel_spmd
from concourse.alu_op_type import AluOpType as OP

F32 = mybir.dt.float32
BF16 = mybir.dt.bfloat16
AF = mybir.ActivationFunctionType
AX = mybir.AxisListType.X

T = 2048
D = 1024
DCH = 8
TT = 512
CA = 64
CB = 128
FF = 4096
NE = 8
DEXP = 2048
AFLOOR = 0.09
RFLOOR = 0.30

STAGE = 99
DUMPS = ()


def _f32(a):
    return np.ascontiguousarray(np.asarray(a, dtype=np.float32))


def host_constants():
    c = {}
    c["id128"] = np.ascontiguousarray(np.eye(128).astype(ml_dtypes.bfloat16))
    c["ones_col"] = np.ones((128, 1), dtype=ml_dtypes.bfloat16)
    b2 = np.zeros((128, 2), np.float32)
    b2[0:64, 0] = 1.0
    b2[64:128, 1] = 1.0
    c["b2"] = np.ascontiguousarray(b2.astype(ml_dtypes.bfloat16))
    jj, tt_ = np.meshgrid(np.arange(64), np.arange(64), indexing="ij")
    m = np.zeros((64, 128), np.float32)
    m[:, 0:64] = -(jj <= tt_).astype(np.float32)
    m[:, 64:128] = -(jj < tt_).astype(np.float32)
    c["maskA"] = np.ascontiguousarray(np.tile(m, (2, 1)).astype(ml_dtypes.bfloat16))
    jj, tt_ = np.meshgrid(np.arange(CB), np.arange(CB), indexing="ij")
    c["maskB"] = np.ascontiguousarray((jj <= tt_).astype(ml_dtypes.bfloat16))
    rt = (jj <= tt_).astype(np.float32) - (jj <= 63).astype(np.float32)
    c["rbT"] = np.ascontiguousarray(rt.astype(ml_dtypes.bfloat16))
    mh = np.zeros((128, 1), np.float32)
    mh[64:128] = 1.0
    c["mcol_hi"] = np.ascontiguousarray(mh.astype(ml_dtypes.bfloat16))
    ml_ = np.zeros((128, 1), np.float32)
    ml_[0:64] = 1.0
    c["mcol_lo"] = np.ascontiguousarray(ml_.astype(ml_dtypes.bfloat16))
    return c


def host_prep(inputs):
    ins = {k: np.asarray(v) for k, v in inputs.items()}
    common = dict(host_constants())

    def col8(v):
        return _f32(np.asarray(v, np.float32).reshape(8, 128).T)

    A = "L0_attn_a_"
    common["nw_col"] = col8(ins[A + "norm_in_w"])
    common["cb_col"] = col8(ins[A + "conv_b"])
    cw = np.asarray(ins[A + "conv_w"], np.float32)[:, 0, :].reshape(8, 128, 4)
    common["cw_col"] = _f32(cw.transpose(1, 0, 2).reshape(128, 32))
    common["ab_col"] = col8(ins[A + "alpha_b"])
    common["hnw_col"] = col8(np.asarray(ins[A + "head_norm_w"], np.float32).reshape(-1))
    common["fnw_col"] = col8(ins[A + "ff_norm_w"])
    common["u2b_col"] = col8(ins[A + "u2_b"])
    common["u1b_col"] = _f32(np.asarray(ins[A + "u1_b"], np.float32).reshape(64, 1))
    common["bb_col"] = _f32(np.asarray(ins[A + "beta_b"], np.float32).reshape(16, 1))
    for nm in ["q_w", "k_w", "v_w", "alpha_w", "out_w", "u1_w", "u2_w",
               "ff_w1", "ff_w3", "ff_w2", "beta_w"]:
        common["a_" + nm] = _f32(ins[A + nm])
    B = "L0_attn_b_"
    for nm in ["Wq", "Wk", "Wv", "Wo"]:
        common["b_" + nm] = _f32(ins[B + nm])

    def colb(v):
        w = np.asarray(v, np.float32).reshape(2, 4, 128)
        return _f32(w.transpose(2, 0, 1).reshape(128, 8))

    common["wg_col"] = colb(ins[B + "W_gamma"])
    common["bg_col"] = colb(ins[B + "b_gamma"])
    common["wl_col"] = colb(ins[B + "W_lambda"])
    common["bl_col"] = colb(ins[B + "b_lambda"])

    def bcast_tm(v):
        w = np.asarray(v, np.float32).reshape(1, 1024)
        return _f32(np.repeat(w, 128, axis=0))

    common["wg_bc"] = bcast_tm(ins[B + "W_gamma"])
    common["bg_bc"] = bcast_tm(ins[B + "b_gamma"])
    common["wl_bc"] = bcast_tm(ins[B + "W_lambda"])
    common["bl_bc"] = bcast_tm(ins[B + "b_lambda"])
    Fp = "L0_ffn_"
    for nm in ["Wr", "Wg", "Wu", "Wd", "Sg", "Su", "Sd"]:
        common["f_" + nm] = _f32(ins[Fp + nm])
    x = np.asarray(ins[A + "x"], np.float32)
    return [dict(common, x=_f32(x[b].T)) for b in range(8)]


class Sem:
    def __init__(self, handle):
        self.h = handle
        self.n = 0

    def inc(self, instr, k=1):
        instr.then_inc(self.h, k)
        self.n += k
        return instr


class Builder:

    def __init__(self, nc, stage, dumps):
        self.nc = nc
        self.stage = stage
        self.dumps = dumps
        self.marks = {}

    def mk(self, name, sem):
        self.marks[name] = (sem, sem.n)

    def w(self, eng, name):
        if name not in self.marks:
            if ".-" in name or name.endswith(".-1") or ".-1." in name:
                return
            raise KeyError(f"wait on missing mark: {name}")
        sem, n = self.marks[name]
        if sem in getattr(self, "dma_sems", ()):
            n = sem.n
        if n > 0:
            eng.wait_ge(sem.h, n)

    def bar(self, eng):
        for s in self.sems:
            if s.n > 0:
                eng.wait_ge(s.h, s.n)

    def fence(self, eng, sem):
        if sem.n > 0:
            eng.wait_ge(sem.h, sem.n)


def build(stage=99, dumps=()):
    nc = bass.Bass("TRN2", target_bir_lowering=False)
    bld = Builder(nc, stage, dumps)
    dt = nc.dram_tensor

    X = dt("x", [D, T], F32, kind="ExternalInput").ap()
    W = {}
    for nm, shp in [("a_q_w", [D, D]), ("a_k_w", [D, D]), ("a_v_w", [D, D]),
                    ("a_alpha_w", [D, D]), ("a_out_w", [D, D]),
                    ("a_beta_w", [D, 16]), ("a_u1_w", [D, 64]), ("a_u2_w", [64, D]),
                    ("a_ff_w1", [D, FF]), ("a_ff_w3", [D, FF]), ("a_ff_w2", [FF, D]),
                    ("b_Wq", [D, D]), ("b_Wk", [D, D]), ("b_Wv", [D, D]), ("b_Wo", [D, D]),
                    ("f_Wr", [D, NE]), ("f_Wg", [NE, D, DEXP]), ("f_Wu", [NE, D, DEXP]),
                    ("f_Wd", [NE, DEXP, D]), ("f_Sg", [D, DEXP]), ("f_Su", [D, DEXP]),
                    ("f_Sd", [DEXP, D])]:
        W[nm] = dt(nm, shp, F32, kind="ExternalInput").ap()
    C = {}
    for nm, shp, d_ in [("id128", [128, 128], BF16), ("ones_col", [128, 1], BF16),
                        ("b2", [128, 2], BF16), ("maskA", [128, 128], BF16),
                        ("maskB", [CB, CB], BF16), ("rbT", [CB, CB], BF16),
                        ("mcol_hi", [128, 1], BF16), ("mcol_lo", [128, 1], BF16),
                        ("nw_col", [128, 8], F32), ("cb_col", [128, 8], F32),
                        ("cw_col", [128, 32], F32), ("ab_col", [128, 8], F32),
                        ("hnw_col", [128, 8], F32), ("fnw_col", [128, 8], F32),
                        ("u2b_col", [128, 8], F32), ("u1b_col", [64, 1], F32),
                        ("bb_col", [16, 1], F32), ("wg_col", [128, 8], F32),
                        ("bg_col", [128, 8], F32), ("wl_col", [128, 8], F32),
                        ("bl_col", [128, 8], F32), ("wg_bc", [128, 1024], F32),
                        ("bg_bc", [128, 1024], F32), ("wl_bc", [128, 1024], F32),
                        ("bl_bc", [128, 1024], F32)]:
        C[nm] = dt(nm, shp, d_, kind="ExternalInput").ap()
    OUT = dt("out", [D, T], F32, kind="ExternalOutput").ap()

    def scr(nm, shp, d_):
        kind = "ExternalOutput" if nm in dumps else "Internal"
        return dt(nm, shp, d_, kind=kind).ap()

    qn_d = scr("qn_d", [D, T], BF16)
    kn_d = scr("kn_d", [D, T], BF16)
    vtm_d = scr("vtm_d", [T, D], BF16)
    L_d = scr("L_d", [D, T + 64], F32)
    btm_d = scr("btm_d", [T, 16], F32)
    gate_d = scr("gate_d", [D, T], BF16)
    row_d = scr("row_d", [64, T], F32)
    y16_d = scr("y16_d", [D, T], BF16)
    wbT_d = scr("wbT_d", [4, D, D], BF16)
    gates_d = scr("gates_d", [NE, T], F32)
    wbar_d = scr("wbar_d", [2, T, 512], BF16)
    vtmB_d = scr("vtmB_d", [2, T, 512], BF16)
    grow_d = scr("grow_d", [64, 1024], F32)
    ep_d = scr("ep_d", [2, T, 512], BF16)
    ytm_d = scr("ytm_d", [2, T, 512], BF16)
    vfm_d = scr("vfm_d", [D, T], BF16)
    dbg_d = scr("dbg_d", [128, 8192], F32)

    Xv = X.rearrange("(c p) t -> p c t", p=128)
    OUTv = OUT.rearrange("(c p) t -> p c t", p=128)
    qn_v = qn_d.rearrange("(c p) t -> p c t", p=128)
    kn_v = kn_d.rearrange("(c p) t -> p c t", p=128)
    L_v = L_d.rearrange("(c p) t -> p c t", p=128)
    gate_v = gate_d.rearrange("(c p) t -> p c t", p=128)
    y16_v = y16_d.rearrange("(c p) t -> p c t", p=128)

    es = contextlib.ExitStack()
    with es:
        sbuf = lambda nm, shp, d_: es.enter_context(nc.sbuf_tensor(nm, shp, d_))
        XR = sbuf("XR", [128, DCH * T], F32)
        A16 = sbuf("A16", [128, DCH * T], BF16)
        B16 = sbuf("B16", [128, DCH * T], BF16)
        C16 = sbuf("C16", [128, 8192], BF16)
        WBs = sbuf("WBs", [128, 12288], BF16)
        FS = sbuf("FS", [128, 4096], F32)
        WBC = sbuf("WBC", [128, 1536], F32)
        SS = sbuf("SS", [128, 4096], F32)
        KON = sbuf("KON", [128, 136], BF16)
        MKA = sbuf("MKA", [128, 128], BF16)
        MKB = sbuf("MKB", [128, 128], BF16)
        RT16 = sbuf("RT16", [128, 128], BF16)
        CCc = sbuf("CCc", [128, 128], F32)
        psum = lambda nm, shp, d_: es.enter_context(nc.psum_tensor(nm, shp, d_))
        PB = [psum(f"pb{i}", [128, 512], F32) for i in range(7)]
        PT = psum("pt16", [128, 1024], BF16)

        SP = Sem(es.enter_context(nc.semaphore("sp")))
        GP = Sem(es.enter_context(nc.semaphore("gp")))
        AC = Sem(es.enter_context(nc.semaphore("ac")))
        DV = Sem(es.enter_context(nc.semaphore("dv")))
        PE = Sem(es.enter_context(nc.semaphore("pe")))
        SPL = Sem(es.enter_context(nc.semaphore("spl")))
        BCS = Sem(es.enter_context(nc.semaphore("bcs")))
        GL = Sem(es.enter_context(nc.semaphore("gl")))
        WS = [Sem(es.enter_context(nc.semaphore("ws0"))),
              Sem(es.enter_context(nc.semaphore("ws1"))),
              Sem(es.enter_context(nc.semaphore("ws2")))]
        WT = [Sem(es.enter_context(nc.semaphore("wt0"))),
              Sem(es.enter_context(nc.semaphore("wt1"))),
              Sem(es.enter_context(nc.semaphore("wt2")))]
        bld.sems = [SP, GP, AC, DV, PE, SPL, BCS, GL] + WS + WT
        bld.dma_sems = (SP, GP, SPL, BCS, GL, WS[0], WS[1], WS[2], WT[0], WT[1], WT[2])
        block = es.enter_context(nc.Block())
        mk, w_, bar = bld.mk, bld.w, bld.bar

        def fm(pool, c, t0=0, t1=None):
            t1 = T if t1 is None else t1
            return pool[:, c * T + t0: c * T + t1]

        id16 = KON[:, 0:128]
        ones16 = KON[:, 128:129]
        b2c = KON[:, 129:131]
        mhi = KON[:, 131:132]
        mlo = KON[:, 132:133]
        ccbase = {"nw": 0, "cb": 8, "ab": 16, "hnw": 24, "fnw": 32, "u2b": 40,
                  "wg": 48, "bg": 56, "wl": 64, "bl": 72, "cw": 80, "u1b": 112, "bb": 113}

        def ccol(name, i=0):
            b = ccbase[name] + i
            return CCc[:, b:b + 1]

        @block.sync
        def _(q):
            SP.inc(q.dma_start(out=XR[:].rearrange("p (c t) -> p c t", c=DCH), in_=Xv), 16)
            SP.inc(q.dma_start(out=KON[:, 0:128], in_=C["id128"]), 16)
            SP.inc(q.dma_start(out=KON[:, 128:129], in_=C["ones_col"]), 16)
            SP.inc(q.dma_start(out=KON[:, 129:131], in_=C["b2"]), 16)
            SP.inc(q.dma_start(out=KON[:, 131:132], in_=C["mcol_hi"]), 16)
            SP.inc(q.dma_start(out=KON[:, 132:133], in_=C["mcol_lo"]), 16)
            SP.inc(q.dma_start(out=MKA[:], in_=C["maskA"]), 16)
            SP.inc(q.dma_start(out=MKB[:], in_=C["maskB"]), 16)
            SP.inc(q.dma_start(out=RT16[:], in_=C["rbT"]), 16)
            for i, nm in enumerate(["nw_col", "cb_col", "ab_col", "hnw_col",
                                    "fnw_col", "u2b_col", "wg_col", "bg_col",
                                    "wl_col", "bl_col"]):
                SP.inc(q.dma_start(out=CCc[:, 8 * i:8 * i + 8], in_=C[nm]), 16)
            SP.inc(q.dma_start(out=CCc[:, 80:112], in_=C["cw_col"]), 16)
            SP.inc(q.dma_start(out=CCc[0:64, 112:113], in_=C["u1b_col"]), 16)
            SP.inc(q.dma_start(out=CCc[0:16, 113:114], in_=C["bb_col"]), 16)

        @block.vector
        def _(v):
            v.memset(FS[:, 0:64], 0.0)
            v.memset(CCc[:, 114:115], 1e-5)
            v.memset(CCc[:, 115:116], 1e-6)
            DV.inc(v.memset(CCc[:, 116:117], 1e-5 * 1e-6))

        @block.gpsimd
        def _(g):
            bar(g)
            for c in range(DCH):
                GP.inc(g.dma_start(out=L_v[:, c, 0:64], in_=FS[:, 0:64]), 16)
            for i, nm in enumerate(["b_Wq", "b_Wk", "b_Wv", "b_Wo"]):
                GP.inc(g.dma_start(out=wbT_d[i], in_=W[nm]), 16)
            GP.inc(g.dma_start(
                out=WBs[:, 11264:11776].rearrange("p (k n) -> p k n", k=8),
                in_=W["a_u1_w"].rearrange("(k p) n -> p k n", p=128)), 16)
            GP.inc(g.dma_start(out=WBs[0:64, 10240:11264], in_=W["a_u2_w"]), 16)
            GP.inc(g.dma_start(
                out=WBs[:, 11776:11904].rearrange("p (k n) -> p k n", k=8),
                in_=W["a_beta_w"].rearrange("(k p) n -> p k n", p=128)), 16)
        u1w16 = WBs[:, 11264:11776]
        u2w16 = WBs[0:64, 10240:11264]
        bw16 = WBs[:, 11776:11904]
        ws = [WBs[:, 0:4096], WBs[:, 4096:8192], WBs[:, 8192:12288]]

        def load_whalf(g, wap, half, slot):
            WS[slot].inc(g.dma_start(
                out=ws[slot].rearrange("p (k n) -> p k n", k=8),
                in_=wap.rearrange("(k p) n -> p k n", p=128)[:, :, half * 512:(half + 1) * 512]), 16)

        def wblk(slot, k, mo4):
            return ws[slot][:, k * 512 + mo4 * 128: k * 512 + mo4 * 128 + 128]

        def stats_rows(tag, src_pool, row_x, row_s, want_mean=True):
            def sqs(c):
                return C16[:, c * 512:(c + 1) * 512]

            for tt in range(4):
                @block.scalar
                def _(s, tt=tt):
                    if tt == 0:
                        bar(s)
                    for c in range(DCH):
                        w_(s, f"{tag}.mm.{tt-1}.{c}")
                        AC.inc(s.activation(sqs(c), fm(src_pool, c, tt * TT, (tt + 1) * TT), AF.Square))
                        mk(f"{tag}.sq.{tt}.{c}", AC)

                @block.tensor
                def _(t, tt=tt):
                    w_(t, f"{tag}.rows.{tt-1}")
                    for c in range(DCH):
                        w_(t, f"{tag}.sq.{tt}.{c}")
                        if want_mean:
                            t.matmul(PB[4][0:1, :], ones16, fm(src_pool, c, tt * TT, (tt + 1) * TT),
                                     start=(c == 0), stop=(c == DCH - 1))
                        PE.inc(t.matmul(PB[5][0:1, :], ones16, sqs(c),
                                        start=(c == 0), stop=(c == DCH - 1)))
                        mk(f"{tag}.mm.{tt}.{c}", PE)
                    mk(f"{tag}.mmtt.{tt}", PE)

                @block.scalar
                def _(s, tt=tt):
                    w_(s, f"{tag}.mmtt.{tt}")
                    if want_mean:
                        s.activation(FS[0:1, tt * 512:(tt + 1) * 512], PB[4][0:1, :], AF.Copy)
                    AC.inc(s.activation(FS[0:1, 2048 + tt * 512: 2048 + (tt + 1) * 512], PB[5][0:1, :], AF.Copy))
                    mk(f"{tag}.rows.{tt}", AC)

            @block.sync
            def _(q):
                w_(q, f"{tag}.rows.3")
                if want_mean:
                    SP.inc(q.dma_start(out=row_d[row_x:row_x + 1, :], in_=FS[0:1, 0:2048]), 16)
                SP.inc(q.dma_start(out=row_d[row_s:row_s + 1, :], in_=FS[0:1, 2048:4096]), 16)
                mk(f"{tag}.rsp", SP)

        def row_math(tag, emit_ops, src_rows, n_src, dst_rows, n_dst):
            @block.sync
            def _(q):
                bar(q)
                for i, r in enumerate(src_rows):
                    SP.inc(q.dma_start(out=FS[:, 2048 + 16 * i: 2048 + 16 * (i + 1)],
                                       in_=bass.AP(row_d.tensor, r * 2048, [[16, 128], [1, 16]])), 16)
                mk(f"{tag}.rl", SP)

            emit_ops()

            @block.sync
            def _(q):
                bar(q)
                for i, r in enumerate(dst_rows):
                    SP.inc(q.dma_start(out=bass.AP(row_d.tensor, r * 2048, [[16, 128], [1, 16]]),
                                       in_=FS[:, 2560 + 16 * i: 2560 + 16 * (i + 1)]), 16)
                mk(f"{tag}.msp", SP)

        def rep512(q, row, tt, dst):
            BCS.inc(q.dma_start(out=dst, in_=bass.AP(
                row_d.tensor, row * 2048 + tt * 512, [[0, 128], [1, 512]])), 16)

        def rep_head64(q, rowbase, c, tt, dst):
            BCS.inc(q.dma_start(out=dst, in_=bass.AP(
                row_d.tensor, (rowbase + 2 * c) * 2048 + tt * 512,
                [[2048, 2], [0, 64], [1, 512]])), 16)

        if stage >= 1:
            @block.vector
            def _(v):
                bar(v)
                for c in range(DCH):
                    DV.inc(v.tensor_copy(fm(B16, c), fm(XR, c)))

            stats_rows("ln1", B16, 0, 1, want_mean=True)

            def ln1_math():
                fn = bld.fence

                @block.vector
                def _(v):
                    bar(v)
                    DV.inc(v.tensor_scalar(FS[:, 2560:2576], FS[:, 2048:2064], 1.0 / D, None, OP.mult))
                    DV.inc(v.tensor_scalar(FS[:, 2080:2096], FS[:, 2064:2080], 1.0 / D, None, OP.mult))
                    fn(v, DV)
                    DV.inc(v.tensor_tensor(FS[:, 2096:2112], FS[:, 2560:2576], FS[:, 2560:2576], OP.mult))
                    fn(v, DV)
                    DV.inc(v.tensor_tensor(FS[:, 2112:2128], FS[:, 2080:2096], FS[:, 2096:2112], OP.subtract))
                    mk("ln1.var", DV)

                @block.scalar
                def _(s):
                    w_(s, "ln1.var")
                    AC.inc(s.activation(FS[:, 2128:2144], FS[:, 2112:2128], AF.Ln, bias=CCc[:, 114:115]))
                    fn(s, AC)
                    AC.inc(s.activation(FS[:, 2576:2592], FS[:, 2128:2144], AF.Exp, scale=-0.5))
                    AC.inc(s.activation(FS[:, 2144:2160], FS[:, 2112:2128], AF.Ln, scale=1.0 + 1e-6, bias=CCc[:, 116:117]))
                    fn(s, AC)
                    AC.inc(s.activation(FS[:, 2592:2608], FS[:, 2144:2160], AF.Exp, scale=-0.5))
                    mk("ln1.c", AC)
            row_math("ln1m", ln1_math, [0, 1], 2, [0, 2, 3], 3)

            mb, cpb, cfb = WBC[:, 0:512], WBC[:, 512:1024], WBC[:, 1024:1536]
            xst = C16[:, 1024:1536]
            s1t = C16[0:64, 1536:2048]
            gsl = [C16[:, 2048:2560], C16[:, 2560:3072]]
            t1s = FS[:, 2048:2560]

            for tt in range(4):
                @block.sync
                def _(q, tt=tt):
                    bar(q)
                    rep512(q, 0, tt, mb)
                    rep512(q, 2, tt, cpb)
                    rep512(q, 3, tt, cfb)
                    mk("p1.bc", BCS)

                for c in range(DCH):
                    @block.vector
                    def _(v, tt=tt, c=c):
                        if c == 0:
                            w_(v, "p1.bc")
                        w_(v, f"p1.u1.{tt}.{c-1}" if c > 0 else f"p1.u1.{tt-1}.{DCH-1}")
                        fn(v, DV) if False else bld.fence(v, DV)
                        DV.inc(v.tensor_tensor(t1s, XR[:, c * T + tt * TT: c * T + (tt + 1) * TT], mb, OP.subtract))
                        bld.fence(v, DV)
                        v.scalar_tensor_tensor(fm(A16, c, tt * TT, (tt + 1) * TT),
                                               t1s, ccol("nw", c), cfb, OP.mult, OP.mult)
                        DV.inc(v.tensor_tensor(xst, t1s, cpb, OP.mult))
                        mk(f"p1.xs.{tt}.{c}", DV)

                    @block.tensor
                    def _(t, tt=tt, c=c):
                        w_(t, f"p1.xs.{tt}.{c}")
                        if c == 0:
                            w_(t, f"p1.s1.{tt-1}")
                        PE.inc(t.matmul(PB[6][0:64, :], u1w16[:, c * 64:(c + 1) * 64], xst,
                                        start=(c == 0), stop=(c == DCH - 1)))
                        mk(f"p1.u1.{tt}.{c}", PE)

                @block.scalar
                def _(s, tt=tt):
                    w_(s, f"p1.u1.{tt}.{DCH-1}")
                    w_(s, f"p1.gmm.{tt-1}.{DCH-1}")
                    AC.inc(s.activation(s1t, PB[6][0:64, :], AF.Silu, bias=CCc[0:64, 112:113]))
                    mk(f"p1.s1.{tt}", AC)

                for mo in range(DCH):
                    @block.tensor
                    def _(t, tt=tt, mo=mo):
                        if mo == 0:
                            w_(t, f"p1.s1.{tt}")
                        w_(t, f"p1.gact.{tt}.{mo-2}" if mo >= 2 else f"p1.gact.{tt-1}.{6+mo}")
                        PE.inc(t.matmul(PB[4 + (mo % 2)][:, :], u2w16[:, mo * 128:(mo + 1) * 128],
                                        s1t, start=True, stop=True))
                        mk(f"p1.gmm.{tt}.{mo}", PE)

                    @block.scalar
                    def _(s, tt=tt, mo=mo):
                        w_(s, f"p1.gmm.{tt}.{mo}")
                        w_(s, f"p1.gsp.{tt}.{mo-2}" if mo >= 2 else f"p1.gsp.{tt-1}.{6+mo}")
                        AC.inc(s.activation(gsl[mo % 2], PB[4 + (mo % 2)][:, :], AF.Sigmoid, bias=ccol("u2b", mo)))
                        mk(f"p1.gact.{tt}.{mo}", AC)

                    @block.sync
                    def _(q, tt=tt, mo=mo):
                        w_(q, f"p1.gact.{tt}.{mo}")
                        SPL.inc(q.dma_start(out=gate_v[:, mo, tt * TT:(tt + 1) * TT], in_=gsl[mo % 2]), 16)
                        mk(f"p1.gsp.{tt}.{mo}", SPL)

            hcv = FS[:, 0:2048]
            for c in range(DCH):
                @block.vector
                def _(v, c=c):
                    if c == 0:
                        bar(v)
                    w_(v, f"p1.hs.{c-1}")
                    DV.inc(v.tensor_scalar(hcv, fm(A16, c), ccol("cw", 4 * c + 3), None, OP.mult))
                    bld.fence(v, DV)
                    DV.inc(v.scalar_tensor_tensor(hcv[:, 1:2048], fm(A16, c, 0, 2047), ccol("cw", 4 * c + 2), hcv[:, 1:2048], OP.mult, OP.add))
                    bld.fence(v, DV)
                    DV.inc(v.scalar_tensor_tensor(hcv[:, 2:2048], fm(A16, c, 0, 2046), ccol("cw", 4 * c + 1), hcv[:, 2:2048], OP.mult, OP.add))
                    bld.fence(v, DV)
                    DV.inc(v.scalar_tensor_tensor(hcv[:, 3:2048], fm(A16, c, 0, 2045), ccol("cw", 4 * c + 0), hcv[:, 3:2048], OP.mult, OP.add))
                    mk(f"p1.hc.{c}", DV)

                @block.scalar
                def _(s, c=c):
                    w_(s, f"p1.hc.{c}")
                    AC.inc(s.activation(fm(B16, c), hcv, AF.Silu, bias=ccol("cb", c)))
                    mk(f"p1.hs.{c}", AC)

        def proj_fm(tag, wap, tile_cb, n_mo=8):
            @block.gpsimd
            def _(g):
                bar(g)
                load_whalf(g, wap, 0, 0)
                mk(f"{tag}.w0", GP)
                load_whalf(g, wap, 1, 1)
                mk(f"{tag}.w1", GP)

            for mo in range(n_mo):
                half = mo // 4
                for tt in range(4):
                    @block.tensor
                    def _(t, mo=mo, tt=tt, half=half):
                        if mo == 0 and tt == 0:
                            bar(t)
                        if tt == 0:
                            bld.fence(t, WS[half])
                        w_(t, f"{tag}.ev.{mo}.{tt-1}" if tt > 0 else f"{tag}.ev.{mo-2}.3")
                        for k in range(DCH):
                            mmi = t.matmul(PB[mo % 2][:], wblk(half, k, mo % 4),
                                           fm(B16, k, tt * TT, (tt + 1) * TT),
                                           start=(k == 0), stop=(k == DCH - 1))
                        PE.inc(mmi)
                        mk(f"{tag}.mm.{mo}.{tt}", PE)
                    tile_cb(mo, tt)

        def qk_sink(tag):
            def cb(mo, tt):
                sqs = C16[:, tt * 512:(tt + 1) * 512]

                @block.vector
                def _(v, mo=mo, tt=tt, sqs=sqs):
                    w_(v, f"{tag}.mm.{mo}.{tt}")
                    DV.inc(v.tensor_copy(fm(A16, mo, tt * TT, (tt + 1) * TT), PB[mo % 2][:]))
                    mk(f"{tag}.ev.{mo}.{tt}", DV)

                @block.scalar
                def _(s, mo=mo, tt=tt, sqs=sqs):
                    w_(s, f"{tag}.ev.{mo}.{tt}")
                    w_(s, f"{tag}.nmm.{mo-1}.{tt}")
                    AC.inc(s.activation(sqs, fm(A16, mo, tt * TT, (tt + 1) * TT), AF.Square))
                    mk(f"{tag}.sq.{mo}.{tt}", AC)

                @block.tensor
                def _(t, mo=mo, tt=tt, sqs=sqs):
                    w_(t, f"{tag}.sq.{mo}.{tt}")
                    w_(t, f"{tag}.rev.{mo}.{tt-1}" if tt > 0 else f"{tag}.rev.{mo-1}.3")
                    PE.inc(t.matmul(PB[6][0:2, :], b2c, sqs, start=True, stop=True))
                    mk(f"{tag}.nmm.{mo}.{tt}", PE)

                @block.scalar
                def _(s, mo=mo, tt=tt):
                    w_(s, f"{tag}.nmm.{mo}.{tt}")
                    if tt == 0:
                        w_(s, f"{tag}.rsp.{mo-1}")
                    AC.inc(s.activation(FS[0:2, 2048 + tt * 512: 2048 + (tt + 1) * 512],
                                        PB[6][0:2, :], AF.Copy))
                    mk(f"{tag}.rev.{mo}.{tt}", AC)

                if tt == 3:
                    @block.sync
                    def _(q, mo=mo):
                        w_(q, f"{tag}.rev.{mo}.3")
                        SPL.inc(q.dma_start(out=row_d[4 + 2 * mo: 6 + 2 * mo, :], in_=FS[0:2, 2048:4096]), 16)
                        mk(f"{tag}.rsp.{mo}", SPL)
            return cb

        def qk_normmath(tag):
            def ops():
                @block.scalar
                def _(s):
                    bar(s)
                    AC.inc(s.activation(FS[:, 2816:3072], FS[:, 2048:2304], AF.Sqrt))
                    fn0 = bld.fence
                    fn0(s, AC)
                    AC.inc(s.activation(FS[:, 3072:3328], FS[:, 2816:3072], AF.Ln, bias=CCc[:, 115:116]))
                    fn0(s, AC)
                    AC.inc(s.activation(FS[:, 2560:2816], FS[:, 3072:3328], AF.Exp, scale=-1.0))
                    mk(f"{tag}.rmath", AC)

            @block.sync
            def _(q):
                bar(q)
                SP.inc(q.dma_start(out=FS[:, 2048:2304],
                                   in_=bass.AP(row_d.tensor, 4 * 2048, [[256, 128], [1, 256]])), 16)
            ops()

            @block.sync
            def _(q):
                bar(q)
                SP.inc(q.dma_start(out=bass.AP(row_d.tensor, 20 * 2048, [[256, 128], [1, 256]]),
                                   in_=FS[:, 2560:2816]), 16)
                mk(f"{tag}.rqsp", SP)

        def qk_apply(tag, dest_v):
            rqb = WBC[:, 0:512]
            qnt = [C16[:, 1024:1536], C16[:, 1536:2048]]

            for c in range(DCH):
                for tt in range(4):
                    i = c * 4 + tt

                    @block.sync
                    def _(q, c=c, tt=tt):
                        if c == 0 and tt == 0:
                            bar(q)
                        w_(q, f"{tag}.ap.{c}.{tt-1}" if tt > 0 else f"{tag}.ap.{c-1}.3")
                        rep_head64(q, 20, c, tt, rqb)
                        mk(f"{tag}.bc.{c}.{tt}", BCS)

                    @block.vector
                    def _(v, c=c, tt=tt, i=i):
                        w_(v, f"{tag}.bc.{c}.{tt}")
                        w_(v, f"{tag}.sp.{c}.{tt-2}" if tt >= 2 else f"{tag}.sp.{c-1}.{2+tt}")
                        DV.inc(v.tensor_tensor(qnt[i % 2], fm(A16, c, tt * TT, (tt + 1) * TT), rqb, OP.mult))
                        mk(f"{tag}.ap.{c}.{tt}", DV)

                    @block.sync
                    def _(q, c=c, tt=tt, i=i):
                        w_(q, f"{tag}.ap.{c}.{tt}")
                        SPL.inc(q.dma_start(out=dest_v[:, c, tt * TT:(tt + 1) * TT], in_=qnt[i % 2]), 16)
                        mk(f"{tag}.sp.{c}.{tt}", SPL)

        if stage >= 2:
            proj_fm("q", W["a_q_w"], qk_sink("q"))
            qk_normmath("q")
            qk_apply("q", qn_v)
            proj_fm("k", W["a_k_w"], qk_sink("k"))
            qk_normmath("k")
            qk_apply("k", kn_v)

            lsl_a = [FS[:, 2560:3072], FS[:, 3072:3584]]
            onct = WBC[:, 0:512]

            @block.vector
            def _(v):
                bar(v)
                DV.inc(v.memset(onct, 1.0))
                mk("al.ones", DV)

            def a_cb(mo, tt):
                asl = FS[:, 2048:2560]

                @block.scalar
                def _(s, mo=mo, tt=tt, asl=asl):
                    if mo == 0 and tt == 0:
                        w_(s, "al.ones")
                    w_(s, f"al.mm.{mo}.{tt}")
                    w_(s, f"al.sc.{mo}.{tt-1}" if tt > 0 else f"al.sc.{mo-1}.3")
                    AC.inc(s.activation(asl, PB[mo % 2][:], AF.Sigmoid, bias=ccol("ab", mo)))
                    mk(f"al.sg.{mo}.{tt}", AC)

                @block.vector
                def _(v, mo=mo, tt=tt, asl=asl):
                    w_(v, f"al.sg.{mo}.{tt}")
                    DV.inc(v.tensor_scalar(asl, asl, AFLOOR, None, OP.max))
                    mk(f"al.cl.{mo}.{tt}", DV)
                    mk(f"al.ev.{mo}.{tt}", DV)

                @block.scalar
                def _(s, mo=mo, tt=tt, asl=asl):
                    w_(s, f"al.cl.{mo}.{tt}")
                    AC.inc(s.activation(asl, asl, AF.Ln))
                    mk(f"al.ln.{mo}.{tt}", AC)

                @block.vector
                def _(v, mo=mo, tt=tt, asl=asl):
                    w_(v, f"al.ln.{mo}.{tt}")
                    w_(v, f"al.lsp.{mo}.{tt-2}" if tt >= 2 else f"al.lsp.{mo-1}.{2+tt}")
                    init = 0.0 if tt == 0 else lsl_a[(tt - 1) % 2][:, 511:512]
                    DV.inc(v.tensor_tensor_scan(lsl_a[tt % 2], onct, asl, init, OP.mult, OP.add))
                    mk(f"al.sc.{mo}.{tt}", DV)

                @block.sync
                def _(q, mo=mo, tt=tt):
                    w_(q, f"al.sc.{mo}.{tt}")
                    SPL.inc(q.dma_start(out=L_v[:, mo, 64 + tt * TT: 64 + (tt + 1) * TT], in_=lsl_a[tt % 2]), 16)
                    mk(f"al.lsp.{mo}.{tt}", SPL)
            proj_fm("al", W["a_alpha_w"], a_cb)

            @block.gpsimd
            def _(g):
                bar(g)
                load_whalf(g, W["a_v_w"], 0, 0)
                mk("v.w0", GP)
                load_whalf(g, W["a_v_w"], 1, 1)
                mk("v.w1", GP)

            vsl = [C16[:, 0:512], C16[:, 512:1024]]
            for i_ in range(32):
                nh, ww = i_ // 16, i_ % 16

                @block.tensor
                def _(t, i_=i_, nh=nh, ww=ww):
                    if i_ == 0:
                        bar(t)
                    if ww == 0:
                        bld.fence(t, WS[nh])
                    w_(t, f"v.ev.{i_-2}")
                    for k in range(DCH):
                        mmi = t.matmul(PB[i_ % 2][:], fm(B16, k, ww * 128, (ww + 1) * 128),
                                       ws[nh][:, k * 512:(k + 1) * 512],
                                       start=(k == 0), stop=(k == DCH - 1))
                    PE.inc(mmi)
                    mk(f"v.mm.{i_}", PE)

                @block.vector
                def _(v, i_=i_):
                    w_(v, f"v.mm.{i_}")
                    w_(v, f"v.sp.{i_-2}")
                    DV.inc(v.tensor_copy(vsl[i_ % 2], PB[i_ % 2][:]))
                    mk(f"v.ev.{i_}", DV)

                @block.sync
                def _(q, i_=i_, nh=nh, ww=ww):
                    w_(q, f"v.ev.{i_}")
                    SPL.inc(q.dma_start(out=vtm_d[ww * 128:(ww + 1) * 128, nh * 512:(nh + 1) * 512],
                                       in_=vsl[i_ % 2]), 16)
                    mk(f"v.sp.{i_}", SPL)

            bsl = C16[0:16, 3584:4096]
            bts = [FS[:, 3584:3600], FS[:, 3600:3616]]
            for tt in range(4):
                @block.tensor
                def _(t, tt=tt):
                    if tt == 0:
                        bar(t)
                    w_(t, f"b.sg.{tt-1}")
                    for k in range(DCH):
                        mmi = t.matmul(PB[6][0:16, :], bw16[:, k * 16:(k + 1) * 16],
                                       fm(B16, k, tt * TT, (tt + 1) * TT), start=(k == 0), stop=(k == DCH - 1))
                    PE.inc(mmi)
                    mk(f"b.mm.{tt}", PE)

                @block.scalar
                def _(s, tt=tt):
                    w_(s, f"b.mm.{tt}")
                    w_(s, f"b.tr.{tt-1}.3")
                    AC.inc(s.activation(bsl, PB[6][0:16, :], AF.Sigmoid, bias=CCc[0:16, 113:114]))
                    mk(f"b.sg.{tt}", AC)

                for blk in range(4):
                    i_ = tt * 4 + blk

                    @block.tensor
                    def _(t, tt=tt, blk=blk):
                        if blk == 0:
                            w_(t, f"b.sg.{tt}")
                        w_(t, f"b.ev.{tt}.{blk-1}" if blk > 0 else f"b.ev.{tt-1}.3")
                        PE.inc(t.transpose(PT[0:128, 0:16], bsl[:, blk * 128:(blk + 1) * 128], id16[0:16, 0:16]))
                        mk(f"b.tr.{tt}.{blk}", PE)

                    @block.vector
                    def _(v, tt=tt, blk=blk, i_=i_):
                        w_(v, f"b.tr.{tt}.{blk}")
                        w_(v, f"b.sp.{i_-2}")
                        DV.inc(v.tensor_copy(bts[i_ % 2], PT[:, 0:16]))
                        mk(f"b.ev.{tt}.{blk}", DV)

                    @block.sync
                    def _(q, tt=tt, blk=blk, i_=i_):
                        w_(q, f"b.ev.{tt}.{blk}")
                        SPL.inc(q.dma_start(out=btm_d[tt * 512 + blk * 128: tt * 512 + (blk + 1) * 128, :],
                                           in_=bts[i_ % 2]), 16)
                        mk(f"b.sp.{i_}", SPL)

        if stage >= 3:
            emit_scan_a(bld, block, dict(locals()))

        if stage >= 4:
            emit_tail_a(bld, block, dict(locals()))

        if stage >= 5:
            emit_ffn(bld, block, dict(locals()), round_=0)

        WGB = SS[:, 0:1024]
        BGB = SS[:, 1024:2048]
        WLB = SS[:, 2048:3072]
        BLB = SS[:, 3072:4096]

        if stage >= 6:
            emit_attn_b(bld, block, dict(locals()))

        if stage >= 7:
            emit_ffn(bld, block, dict(locals()), round_=1)

        @block.sync
        def _(q):
            bar(q)
            for c in range(DCH):
                SP.inc(q.dma_start(out=OUTv[:, c, :], in_=XR[:, c * T:(c + 1) * T]), 16)
            q.wait_ge(SP.h, SP.n)
    return nc


def emit_scan_a(bld, block, env):
    g = lambda k: env[k]
    mk, w_, bar = bld.mk, bld.w, bld.bar
    SP, GP, AC, DV, PE = g("SP"), g("GP"), g("AC"), g("DV"), g("PE")
    SPL, BCS, GL, WS = g("SPL"), g("BCS"), g("GL"), g("WS")
    A16, B16, C16, WBs, FS, SS = g("A16"), g("B16"), g("C16"), g("WBs"), g("FS"), g("SS")
    PB, PT = g("PB"), g("PT")
    MKA, id16 = g("MKA"), g("id16")
    qn_v, kn_v, L_v = g("qn_v"), g("kn_v"), g("L_v")
    vtm_d, btm_d = g("vtm_d"), g("btm_d")
    fm = g("fm")
    OPx = OP
    import os as _os2

    NG = T // 256
    vtmg = WBs[:, 0:4096]
    nb_t = WBs[:, 4224:4352]
    Zp = [WBs[:, 4352 + i * 64: 4352 + (i + 1) * 64] for i in range(6)]
    ntr = WBs[:, 4736:4800]
    Wt_ = WBs[:, 5248:5760]
    yc = [WBs[:, 5760:6272], WBs[:, 6272:6784]]
    Uc = WBs[:, 6784:7296]
    khf = WBs[:, 9344:9856]
    khT = WBs[:, 9856:10880]
    zps = WBs[:, 10880:11392]
    Lg = lambda ch, a, b: FS[:, ch * 257 + a: ch * 257 + b]
    cmg = FS[:, 2056:2088]
    ceg = FS[:, 2088:2120]
    nmg = FS[:, 2120:2152]
    STT = FS[:, 2184:2696]
    btg = FS[:, 2696:2760]

    def apx(sl, *dims):
        return bass.AP(sl.tensor, sl.offset, [sl.ap[0]] + list(dims))

    @block.vector
    def _(v):
        bar(v)
        DV.inc(v.memset(STT, 0.0))
        mk("sa.st.-1.7", DV)

    for gi in range(NG):
        t0 = gi * 256

        @block.sync
        def _(q, gi=gi, t0=t0):
            if gi == 0:
                bar(q)
            else:
                w_(q, f"sa.grpdone.{gi-1}")
            GL.inc(q.dma_start(out=C16[:, 0:2048].rearrange("p (c t) -> p c t", c=8),
                               in_=qn_v[:, :, t0:t0 + 256]), 16)
            GL.inc(q.dma_start(out=C16[:, 2048:4096].rearrange("p (c t) -> p c t", c=8),
                               in_=kn_v[:, :, t0:t0 + 256]), 16)
            GL.inc(q.dma_start(out=FS[:, 0:2056].rearrange("p (c t) -> p c t", c=8),
                               in_=L_v[:, :, 64 + t0 - 1: 64 + t0 + 256]), 16)
            for half in range(2):
                o = half * 64
                GL.inc(q.dma_start(out=vtmg[o:o + 64, :].rearrange("p (c n) -> p c n", c=4),
                                   in_=bass.AP(vtm_d.tensor, t0 * 1024,
                                               [[1024, 64], [65536, 4], [1, 1024]])), 16)
                GL.inc(q.dma_start(out=btg[o:o + 64, :].rearrange("p (c n) -> p c n", c=4),
                                   in_=bass.AP(btm_d.tensor, t0 * 16,
                                               [[16, 64], [1024, 4], [1, 16]])), 16)
            mk(f"sa.ld.{gi}", GL)

        @block.vector
        def _(v, gi=gi):
            w_(v, f"sa.ld.{gi}")
            Lv_ = FS[:, 0:2056]
            midAP = apx(Lv_[:, 32:33], [257, 8], [64, 4])
            startAP = apx(Lv_[:, 0:1], [257, 8], [64, 4])
            endAP = apx(Lv_[:, 64:65], [257, 8], [64, 4])
            nmv = nmg.rearrange("p (a b) -> p a b", a=8)
            cmv = cmg.rearrange("p (a b) -> p a b", a=8)
            cev = ceg.rearrange("p (a b) -> p a b", a=8)
            v.tensor_scalar(nmv, midAP, -1.0, None, OPx.mult)
            v.tensor_tensor(cmv, midAP, startAP, OPx.subtract)
            DV.inc(v.tensor_tensor(cev, endAP, startAP, OPx.subtract))
            mk(f"sa.nm.{gi}", DV)

        @block.scalar
        def _(s, gi=gi):
            w_(s, f"sa.nm.{gi}")
            s.activation(cmg, cmg, AF.Exp)
            s.activation(ceg, ceg, AF.Exp)
            for ch in range(8):
                for c_ in range(4):
                    src = Lg(ch, 1 + c_ * 64, 1 + c_ * 64 + 64)
                    nmcol = nmg[:, ch * 4 + c_: ch * 4 + c_ + 1]
                    midcol = Lg(ch, 1 + c_ * 64 + 31, 1 + c_ * 64 + 32)
                    s.activation(A16[:, 4096 + ch * 256 + c_ * 64: 4096 + ch * 256 + c_ * 64 + 64],
                                 src, AF.Exp, bias=nmcol)
                    aci = s.activation(A16[:, 6144 + ch * 256 + c_ * 64: 6144 + ch * 256 + c_ * 64 + 64],
                                       src, AF.Exp, bias=midcol, scale=-1.0)
            AC.inc(aci)
            mk(f"sa.exp.{gi}", AC)

        @block.vector
        def _(v, gi=gi):
            w_(v, f"sa.exp.{gi}")
            for ch in range(8):
                v.tensor_tensor(A16[:, ch * 512: ch * 512 + 256],
                                C16[:, ch * 256: ch * 256 + 256],
                                A16[:, 4096 + ch * 256: 4096 + ch * 256 + 256], OPx.mult)
                v.tensor_tensor(A16[:, ch * 512 + 256: ch * 512 + 512],
                                C16[:, 2048 + ch * 256: 2048 + ch * 256 + 256],
                                A16[:, 4096 + ch * 256: 4096 + ch * 256 + 256], OPx.mult)
                wbi = v.tensor_tensor(C16[:, 4096 + ch * 256: 4096 + ch * 256 + 256],
                                      C16[:, 2048 + ch * 256: 2048 + ch * 256 + 256],
                                      A16[:, 6144 + ch * 256: 6144 + ch * 256 + 256], OPx.mult)
            DV.inc(wbi)
            mk(f"sa.wb.{gi}", DV)

        SA_LIM = int(_os2.environ.get("SA_LIM", "9999"))
        for c_ in range(4):
            ci = gi * 4 + c_
            if ci >= SA_LIM:
                mk(f"sa.st.{ci}.7", DV)
                continue

            @block.vector
            def _(v, gi=gi, c_=c_, ci=ci):
                w_(v, f"sa.wb.{gi}")
                w_(v, f"sa.st.{ci-1}.7")
                for ch2 in range(8):
                    cmAP = apx(cmg[:, ch2 * 4 + c_: ch2 * 4 + c_ + 1], [0, 64])
                    DV.inc(v.tensor_tensor(zps[:, ch2 * 64:(ch2 + 1) * 64],
                                           STT[:, ch2 * 64:(ch2 + 1) * 64], cmAP, OPx.mult))
                mk(f"sa.z0p.{ci}", DV)

            @block.tensor
            def _(t, c_=c_, ci=ci):
                w_(t, f"sa.z0p.{ci}")
                w_(t, f"sa.wt.{ci-1}")
                for h in range(16):
                    ch, o = h // 2, (h % 2) * 64
                    mmi = t.matmul(PB[2][o:o + 64, ch * 64: ch * 64 + 64],
                                   A16[o:o + 64, ch * 512 + 256 + c_ * 64: ch * 512 + 256 + c_ * 64 + 64],
                                   zps[o:o + 64, ch * 64:ch * 64 + 64],
                                   start=True, stop=True)
                PE.inc(mmi)
                mk(f"sa.rk.{ci}", PE)

            @block.vector
            def _(v, c_=c_, ci=ci):
                w_(v, f"sa.rk.{ci}")
                ev = bass.AP(vtmg.tensor, vtmg.offset + c_ * 1024,
                             [vtmg[0:64, 0:1].ap[0], [128, 8], [1, 64]])
                od = bass.AP(vtmg.tensor, vtmg[64:128, 0:1].offset + c_ * 1024 + 64,
                             [vtmg[64:128, 0:1].ap[0], [128, 8], [1, 64]])
                v.tensor_tensor(Wt_[0:64, :].rearrange("p (a b) -> p a b", a=8),
                                PB[2][0:64, :].rearrange("p (a b) -> p a b", a=8), ev, OPx.subtract)
                DV.inc(v.tensor_tensor(Wt_[64:128, :].rearrange("p (a b) -> p a b", a=8),
                                       PB[2][64:128, :].rearrange("p (a b) -> p a b", a=8), od, OPx.subtract))
                mk(f"sa.wt.{ci}", DV)

            for ch in range(8):
                @block.tensor
                def _(t, c_=c_, ci=ci, ch=ch):
                    w_(t, f"sa.wt.{ci}")
                    w_(t, f"sa.z1.{ci}.{ch-1}")
                    for half in range(2):
                        o = half * 64
                        rhs = A16[o:o + 64, ch * 512: ch * 512 + 512].rearrange(
                            "p (qk t) -> p qk t", qk=2)[:, :, c_ * 64:(c_ + 1) * 64]
                        mmi = t.matmul(PB[0][o:o + 64, 0:128],
                                       C16[o:o + 64, 4096 + ch * 256 + c_ * 64: 4096 + ch * 256 + c_ * 64 + 64],
                                       rhs, start=True, stop=True)
                    PE.inc(mmi)
                    mk(f"sa.pa.{ci}.{ch}", PE)

                @block.vector
                def _(v, c_=c_, ci=ci, ch=ch):
                    w_(v, f"sa.pa.{ci}.{ch}")
                    DV.inc(v.tensor_tensor(nb_t[:], PB[0][:, 0:128], MKA[:], OPx.mult))
                    bld.fence(v, DV)
                    for half in range(2):
                        o = half * 64
                        bcol = btg[o:o + 64, c_ * 16 + 2 * ch + half: c_ * 16 + 2 * ch + half + 1]
                        zli = v.tensor_scalar(Zp[0][o:o + 64, :], nb_t[o:o + 64, 64:128],
                                              bcol, None, OPx.mult)
                    DV.inc(zli)
                    mk(f"sa.z1.{ci}.{ch}", DV)

                for lev in range(1, 6):
                    @block.tensor
                    def _(t, ci=ci, ch=ch, lev=lev):
                        w_(t, f"sa.z1.{ci}.{ch}" if lev == 1 else f"sa.sqe.{ci}.{ch}.{lev-1}")
                        t.transpose(PT[0:64, 0:64], Zp[lev - 1][0:64, :], id16[0:64, 0:64])
                        PE.inc(t.transpose(PT[64:128, 0:64], Zp[lev - 1][64:128, :], id16[64:128, 64:128]))
                        mk(f"sa.tr.{ci}.{ch}.{lev}", PE)

                    @block.vector
                    def _(v, ci=ci, ch=ch, lev=lev):
                        w_(v, f"sa.tr.{ci}.{ch}.{lev}")
                        DV.inc(v.tensor_copy(ntr[:], PT[:, 0:64]))
                        mk(f"sa.tre.{ci}.{ch}.{lev}", DV)

                    @block.tensor
                    def _(t, ci=ci, ch=ch, lev=lev):
                        w_(t, f"sa.tre.{ci}.{ch}.{lev}")
                        for half in range(2):
                            o = half * 64
                            mmi = t.matmul(PB[1][o:o + 64, 0:64], ntr[o:o + 64, :],
                                           Zp[lev - 1][o:o + 64, :], start=True, stop=True)
                        PE.inc(mmi)
                        mk(f"sa.sq.{ci}.{ch}.{lev}", PE)

                    @block.vector
                    def _(v, ci=ci, ch=ch, lev=lev):
                        w_(v, f"sa.sq.{ci}.{ch}.{lev}")
                        DV.inc(v.tensor_copy(Zp[lev][:], PB[1][:, 0:64]))
                        mk(f"sa.sqe.{ci}.{ch}.{lev}", DV)

                for li, zi in enumerate([5, 4, 3, 2, 1, 0]):
                    @block.tensor
                    def _(t, ci=ci, ch=ch, li=li, zi=zi):
                        w_(t, f"sa.sqe.{ci}.{ch}.5" if li == 0 else f"sa.app.{ci}.{ch}.{li-1}")
                        src = Wt_ if li == 0 else yc[(li - 1) % 2]
                        for half in range(2):
                            o = half * 64
                            mmi = t.matmul(PB[4][o:o + 64, 0:64], Zp[zi][o:o + 64, :],
                                           src[o:o + 64, ch * 64: ch * 64 + 64],
                                           start=True, stop=True)
                        PE.inc(mmi)
                        mk(f"sa.appm.{ci}.{ch}.{li}", PE)

                    @block.vector
                    def _(v, ci=ci, ch=ch, li=li):
                        w_(v, f"sa.appm.{ci}.{ch}.{li}")
                        if li > 0:
                            w_(v, f"sa.app.{ci}.{ch}.{li-1}")
                        src = Wt_ if li == 0 else yc[(li - 1) % 2]
                        DV.inc(v.tensor_tensor(yc[li % 2][:, ch * 64: ch * 64 + 64],
                                               src[:, ch * 64: ch * 64 + 64],
                                               PB[4][:, 0:64], OPx.add))
                        mk(f"sa.app.{ci}.{ch}.{li}", DV)

                @block.vector
                def _(v, ci=ci, ch=ch, c_=c_):
                    w_(v, f"sa.app.{ci}.{ch}.5")
                    for half in range(2):
                        o = half * 64
                        bcol = btg[o:o + 64, c_ * 16 + 2 * ch + half: c_ * 16 + 2 * ch + half + 1]
                        uci = v.tensor_scalar(Uc[o:o + 64, ch * 64: ch * 64 + 64],
                                              yc[1][o:o + 64, ch * 64: ch * 64 + 64],
                                              bcol, None, OPx.mult)
                    DV.inc(uci)
                    mk(f"sa.uc.{ci}.{ch}", DV)

                @block.tensor
                def _(t, ci=ci, ch=ch, c_=c_):
                    w_(t, f"sa.uc.{ci}.{ch}")
                    w_(t, f"sa.ye.{ci}.{ch-1}")
                    for half in range(2):
                        o = half * 64
                        t.matmul(PB[5][o:o + 64, 0:64],
                                 zps[o:o + 64, ch * 64: ch * 64 + 64],
                                 A16[o:o + 64, ch * 512 + c_ * 64: ch * 512 + c_ * 64 + 64],
                                 start=True, stop=False)
                        mmi = t.matmul(PB[5][o:o + 64, 0:64],
                                       Uc[o:o + 64, ch * 64: ch * 64 + 64],
                                       nb_t[o:o + 64, 0:64],
                                       start=False, stop=True)
                    PE.inc(mmi)
                    mk(f"sa.out.{ci}.{ch}", PE)

                @block.vector
                def _(v, ci=ci, ch=ch):
                    w_(v, f"sa.out.{ci}.{ch}")
                    DV.inc(v.tensor_copy(fm(B16, ch, ci * 64, ci * 64 + 64), PB[5][:, 0:64]))
                    mk(f"sa.ye.{ci}.{ch}", DV)

                @block.vector
                def _(v, ci=ci, ch=ch, c_=c_):
                    w_(v, f"sa.ye.{ci}.{ch}")
                    ecol = A16[:, 4096 + ch * 256 + c_ * 64 + 63: 4096 + ch * 256 + c_ * 64 + 64]
                    ecb = bass.AP(ecol.tensor, ecol.offset, [ecol.ap[0], [0, 64]])
                    DV.inc(v.tensor_tensor(khf[:, ch * 64: ch * 64 + 64],
                                           C16[:, 4096 + ch * 256 + c_ * 64: 4096 + ch * 256 + c_ * 64 + 64],
                                           ecb, OPx.mult))
                    mk(f"sa.kh.{ci}.{ch}", DV)

                @block.tensor
                def _(t, ci=ci, ch=ch):
                    w_(t, f"sa.kh.{ci}.{ch}")
                    t.transpose(PT[0:64, 64:128], khf[0:64, ch * 64: ch * 64 + 64], id16[0:64, 0:64])
                    PE.inc(t.transpose(PT[64:128, 64:128], khf[64:128, ch * 64: ch * 64 + 64],
                                       id16[64:128, 64:128]))
                    mk(f"sa.ktr.{ci}.{ch}", PE)

                @block.vector
                def _(v, ci=ci, ch=ch):
                    w_(v, f"sa.ktr.{ci}.{ch}")
                    DV.inc(v.tensor_copy(khT[:, ch * 64: ch * 64 + 64], PT[:, 64:128]))
                    mk(f"sa.ktre.{ci}.{ch}", DV)

                @block.tensor
                def _(t, ci=ci, ch=ch):
                    w_(t, f"sa.ktre.{ci}.{ch}")
                    for half in range(2):
                        o = half * 64
                        mmi = t.matmul(PB[6][o:o + 64, 0:64], khT[o:o + 64, ch * 64: ch * 64 + 64],
                                       Uc[o:o + 64, ch * 64: ch * 64 + 64], start=True, stop=True)
                    PE.inc(mmi)
                    mk(f"sa.stm.{ci}.{ch}", PE)

                @block.vector
                def _(v, ci=ci, ch=ch, c_=c_):
                    w_(v, f"sa.stm.{ci}.{ch}")
                    cecol = ceg[:, ch * 4 + c_: ch * 4 + c_ + 1]
                    DV.inc(v.scalar_tensor_tensor(STT[:, ch * 64: ch * 64 + 64],
                                                  STT[:, ch * 64: ch * 64 + 64],
                                                  cecol, PB[6][:, 0:64], OPx.mult, OPx.subtract))
                    mk(f"sa.st.{ci}.{ch}", DV)
            mk(f"sa.st.{ci}.7", DV)
        mk(f"sa.grpdone.{gi}", DV)

    y16_v = g("y16_v")

    @block.sync
    def _(q):
        bar(q)
        for c in range(DCH):
            SP.inc(q.dma_start(out=y16_v[:, c, :], in_=fm(B16, c)), 16)
        mk("sa.ysp", SP)


def emit_tail_a(bld, block, env):
    g = lambda k: env[k]
    mk, w_, bar, fn = bld.mk, bld.w, bld.bar, bld.fence
    SP, GP, AC, DV, PE = g("SP"), g("GP"), g("AC"), g("DV"), g("PE")
    SPL, BCS, GL, WS = g("SPL"), g("BCS"), g("GL"), g("WS")
    XR, A16, B16, C16, WBs, FS = g("XR"), g("A16"), g("B16"), g("C16"), g("WBs"), g("FS")
    PB, PT = g("PB"), g("PT")
    W, CCc, row_d, gate_v = g("W"), g("CCc"), g("row_d"), g("gate_v")
    fm, ccol, b2c, ones16 = g("fm"), g("ccol"), g("b2c"), g("ones16")
    stats_rows, row_math = g("stats_rows"), g("row_math")
    rep512, rep_head64 = g("rep512"), g("rep_head64")
    load_whalf, ws, wblk = g("load_whalf"), g("ws"), g("wblk")

    for c in range(DCH):
        for tt in range(4):
            sqs = C16[:, tt * 512:(tt + 1) * 512]

            @block.scalar
            def _(s, c=c, tt=tt, sqs=sqs):
                if c == 0 and tt == 0:
                    bar(s)
                w_(s, f"hn.nmm.{c-1}.{tt}")
                AC.inc(s.activation(sqs, fm(B16, c, tt * TT, (tt + 1) * TT), AF.Square))
                mk(f"hn.sq.{c}.{tt}", AC)

            @block.tensor
            def _(t, c=c, tt=tt, sqs=sqs):
                w_(t, f"hn.sq.{c}.{tt}")
                w_(t, f"hn.rev.{c}.{tt-1}" if tt > 0 else f"hn.rev.{c-1}.3")
                PE.inc(t.matmul(PB[6][0:2, :], b2c, sqs, start=True, stop=True))
                mk(f"hn.nmm.{c}.{tt}", PE)

            @block.scalar
            def _(s, c=c, tt=tt):
                w_(s, f"hn.nmm.{c}.{tt}")
                if tt == 0:
                    w_(s, f"hn.rsp.{c-1}")
                AC.inc(s.activation(FS[0:2, 2048 + tt * 512: 2048 + (tt + 1) * 512],
                                    PB[6][0:2, :], AF.Copy))
                mk(f"hn.rev.{c}.{tt}", AC)

        @block.sync
        def _(q, c=c):
            w_(q, f"hn.rev.{c}.3")
            SPL.inc(q.dma_start(out=row_d[4 + 2 * c: 6 + 2 * c, :], in_=FS[0:2, 2048:4096]), 16)
            mk(f"hn.rsp.{c}", SPL)

    @block.sync
    def _(q):
        bar(q)
        SP.inc(q.dma_start(out=FS[:, 2048:2304],
                           in_=bass.AP(row_d.tensor, 4 * 2048, [[256, 128], [1, 256]])), 16)

    @block.scalar
    def _(s):
        bar(s)
        AC.inc(s.activation(FS[:, 2816:3072], FS[:, 2048:2304], AF.Ln, scale=1.0 / 64, bias=CCc[:, 115:116]))
        fn(s, AC)
        AC.inc(s.activation(FS[:, 2560:2816], FS[:, 2816:3072], AF.Exp, scale=-0.5))
        mk("hn.rm", AC)

    @block.sync
    def _(q):
        bar(q)
        SP.inc(q.dma_start(out=bass.AP(row_d.tensor, 20 * 2048, [[256, 128], [1, 256]]),
                           in_=FS[:, 2560:2816]), 16)
        mk("hn.rqsp", SP)

    rqb = g("WBC")[:, 0:512]
    for c in range(DCH):
        for tt in range(4):
            @block.sync
            def _(q, c=c, tt=tt):
                if c == 0 and tt == 0:
                    bar(q)
                w_(q, f"hn.ap.{c}.{tt-1}" if tt > 0 else f"hn.ap.{c-1}.3")
                rep_head64(q, 20, c, tt, rqb)
                mk(f"hn.bc.{c}.{tt}", BCS)

            @block.vector
            def _(v, c=c, tt=tt):
                w_(v, f"hn.bc.{c}.{tt}")
                DV.inc(v.scalar_tensor_tensor(fm(B16, c, tt * TT, (tt + 1) * TT),
                                              fm(B16, c, tt * TT, (tt + 1) * TT),
                                              ccol("hnw", c), rqb, OP.mult, OP.mult))
                mk(f"hn.ap.{c}.{tt}", DV)
    mk("hn.done", DV)

    @block.gpsimd
    def _(g2):
        bar(g2)
        load_whalf(g2, W["a_out_w"], 0, 0)
        load_whalf(g2, W["a_out_w"], 1, 1)

    gsl = [C16[:, 4096 + k * 512: 4096 + (k + 1) * 512] for k in range(2)]
    t1s = FS[:, 2048:2560]
    u1s = FS[:, 3072:3584]
    mb, cpb = g("WBC")[:, 512:1024], g("WBC")[:, 1024:1536]
    for tt in range(4):
        @block.sync
        def _(q, tt=tt):
            if tt == 0:
                bar(q)
            w_(q, f"op.yi.{DCH-1}.{tt-1}")
            rep512(q, 0, tt, mb)
            rep512(q, 2, tt, cpb)
            mk(f"op.bc.{tt}", BCS)

        for mo in range(DCH):
            @block.tensor
            def _(t, tt=tt, mo=mo):
                if tt == 0 and mo in (0, 4):
                    bld.fence(t, g("WS")[mo // 4])
                w_(t, f"op.yi.{mo-2}.{tt}" if mo >= 2 else (f"op.yi.{6+mo}.{tt-1}" if tt > 0 else "x.-1"))
                for k in range(DCH):
                    mmi = t.matmul(PB[mo % 2][:], wblk(mo // 4, k, mo % 4),
                                   fm(B16, k, tt * TT, (tt + 1) * TT),
                                   start=(k == 0), stop=(k == DCH - 1))
                PE.inc(mmi)
                mk(f"op.mm.{mo}.{tt}", PE)

            @block.sync
            def _(q, tt=tt, mo=mo):
                w_(q, f"op.gld.{mo-1}.{tt}" if mo > 0 else f"op.gld.{DCH-1}.{tt-1}")
                w_(q, f"op.yi.{mo-2}.{tt}" if mo >= 2 else (f"op.yi.{6+mo}.{tt-1}" if tt > 0 else "x.-1"))
                GL.inc(q.dma_start(out=gsl[mo % 2], in_=gate_v[:, mo, tt * TT:(tt + 1) * TT]), 16)
                mk(f"op.gld.{mo}.{tt}", GL)

            @block.vector
            def _(v, tt=tt, mo=mo):
                w_(v, f"op.mm.{mo}.{tt}")
                w_(v, f"op.gld.{mo}.{tt}")
                w_(v, f"op.bc.{tt}")
                fn(v, DV)
                DV.inc(v.tensor_tensor(t1s, XR[:, mo * T + tt * TT: mo * T + (tt + 1) * TT], mb, OP.subtract))
                fn(v, DV)
                DV.inc(v.tensor_tensor(u1s, t1s, cpb, OP.mult))
                fn(v, DV)
                DV.inc(v.tensor_tensor(t1s, PB[mo % 2][:], gsl[mo % 2], OP.mult))
                fn(v, DV)
                DV.inc(v.tensor_tensor(fm(A16, mo, tt * TT, (tt + 1) * TT), u1s, t1s, OP.add))
                mk(f"op.yi.{mo}.{tt}", DV)
    mk("op.done", DV)

    stats_rows("rms1", A16, 0, 1, want_mean=False)

    def rms1_math():
        @block.scalar
        def _(s):
            bar(s)
            AC.inc(s.activation(FS[:, 2144:2160], FS[:, 2048:2064], AF.Ln, scale=1.0 / D, bias=CCc[:, 115:116]))
            fn(s, AC)
            AC.inc(s.activation(FS[:, 2560:2576], FS[:, 2144:2160], AF.Exp, scale=-0.5))
            mk("rms1.c", AC)
    row_math("rms1m", rms1_math, [1], 1, [2], 1)

    c2b = g("WBC")[:, 0:512]
    for tt in range(4):
        @block.sync
        def _(q, tt=tt):
            if tt == 0:
                bar(q)
            w_(q, f"rms1.ap.{DCH-1}.{tt-1}")
            rep512(q, 2, tt, c2b)
            mk(f"rms1.bc.{tt}", BCS)

        for c in range(DCH):
            @block.vector
            def _(v, tt=tt, c=c):
                w_(v, f"rms1.bc.{tt}")
                DV.inc(v.scalar_tensor_tensor(fm(B16, c, tt * TT, (tt + 1) * TT),
                                              fm(A16, c, tt * TT, (tt + 1) * TT),
                                              ccol("fnw", c), c2b, OP.mult, OP.mult))
                mk(f"rms1.ap.{c}.{tt}", DV)

    w1s, w3s, w2s = WBs[:, 0:4096], WBs[:, 4096:8192], WBs[:, 8192:12288]
    sil = C16[:, 0:512]
    h2 = lambda fc: C16[:, 2048 + fc * 512: 2048 + (fc + 1) * 512]
    for fb in range(8):
        @block.gpsimd
        def _(g2, fb=fb):
            if fb == 0:
                bar(g2)
            else:
                w_(g2, f"ff.mm2.{fb-1}.3.{DCH-1}")
            WS[0].inc(g2.dma_start(
                out=w1s.rearrange("p (k n) -> p k n", k=8),
                in_=W["a_ff_w1"].rearrange("(k p) n -> p k n", p=128)[:, :, fb * 512:(fb + 1) * 512]), 16)
            WS[1].inc(g2.dma_start(
                out=w3s.rearrange("p (k n) -> p k n", k=8),
                in_=W["a_ff_w3"].rearrange("(k p) n -> p k n", p=128)[:, :, fb * 512:(fb + 1) * 512]), 16)
            WS[2].inc(g2.dma_start(
                out=w2s.rearrange("p (k n) -> p k n", k=4),
                in_=W["a_ff_w2"].rearrange("(k p) n -> p k n", p=128)[:, 4 * fb: 4 * fb + 4, :]), 16)
            mk(f"ff.w.{fb}", GP)

        for tt in range(4):
            for fc in range(4):
                @block.tensor
                def _(t, fb=fb, tt=tt, fc=fc):
                    if tt == 0 and fc == 0:
                        bld.fence(t, WS[0])
                        bld.fence(t, WS[1])
                    w_(t, f"ff.sil.{fb}.{tt}.{fc-1}" if fc > 0 else f"ff.sil.{fb}.{tt-1}.3")
                    w_(t, f"ff.gu.{fb}.{tt}.{fc-1}" if fc > 0 else f"ff.gu.{fb}.{tt-1}.3")
                    for k in range(DCH):
                        t.matmul(PB[0][:], w1s[:, k * 512 + fc * 128: k * 512 + fc * 128 + 128],
                                 fm(B16, k, tt * TT, (tt + 1) * TT), start=(k == 0), stop=(k == DCH - 1))
                    for k in range(DCH):
                        mmi = t.matmul(PB[1][:], w3s[:, k * 512 + fc * 128: k * 512 + fc * 128 + 128],
                                       fm(B16, k, tt * TT, (tt + 1) * TT), start=(k == 0), stop=(k == DCH - 1))
                    PE.inc(mmi)
                    mk(f"ff.mm13.{fb}.{tt}.{fc}", PE)

                @block.scalar
                def _(s, fb=fb, tt=tt, fc=fc):
                    w_(s, f"ff.mm13.{fb}.{tt}.{fc}")
                    AC.inc(s.activation(sil, PB[0][:], AF.Silu))
                    mk(f"ff.sil.{fb}.{tt}.{fc}", AC)

                @block.vector
                def _(v, fb=fb, tt=tt, fc=fc):
                    w_(v, f"ff.sil.{fb}.{tt}.{fc}")
                    w_(v, f"ff.mm2.{fb-1}.{tt}.{DCH-1}" if fb > 0 else "x.-1")
                    DV.inc(v.tensor_tensor(h2(fc), sil, PB[1][:], OP.mult))
                    mk(f"ff.gu.{fb}.{tt}.{fc}", DV)

            for mo in range(DCH):
                @block.tensor
                def _(t, fb=fb, tt=tt, mo=mo):
                    if mo == 0:
                        bld.fence(t, WS[2])
                    w_(t, f"ff.gu.{fb}.{tt}.3")
                    w_(t, f"ff.ev.{fb}.{tt}.{mo-2}" if mo >= 2 else (f"ff.ev.{fb}.{tt-1}.{6+mo}" if tt > 0 else f"ff.ev.{fb-1}.3.{6+mo}"))
                    for fc in range(4):
                        mmi = t.matmul(PB[2 + mo % 2][:], w2s[:, fc * 1024 + mo * 128: fc * 1024 + mo * 128 + 128],
                                       h2(fc), start=(fc == 0), stop=(fc == 3))
                    PE.inc(mmi)
                    mk(f"ff.mm2.{fb}.{tt}.{mo}", PE)

                @block.vector
                def _(v, fb=fb, tt=tt, mo=mo):
                    w_(v, f"ff.mm2.{fb}.{tt}.{mo}")
                    if fb == 0:
                        fn(v, DV)
                        DV.inc(v.tensor_tensor(t1s, PB[2 + mo % 2][:], fm(A16, mo, tt * TT, (tt + 1) * TT), OP.add))
                        fn(v, DV)
                        DV.inc(v.tensor_tensor(XR[:, mo * T + tt * TT: mo * T + (tt + 1) * TT],
                                               XR[:, mo * T + tt * TT: mo * T + (tt + 1) * TT], t1s, OP.add))
                    else:
                        DV.inc(v.tensor_tensor(XR[:, mo * T + tt * TT: mo * T + (tt + 1) * TT],
                                               XR[:, mo * T + tt * TT: mo * T + (tt + 1) * TT],
                                               PB[2 + mo % 2][:], OP.add))
                    mk(f"ff.ev.{fb}.{tt}.{mo}", DV)
    mk("tail.done", DV)


def emit_ffn(bld, block, env, round_):
    g = lambda k: env[k]
    mk, w_, bar, fn = bld.mk, bld.w, bld.bar, bld.fence
    R = f"f{round_}"
    SP, GP, AC, DV, PE = g("SP"), g("GP"), g("AC"), g("DV"), g("PE")
    SPL, BCS, GL, WS = g("SPL"), g("BCS"), g("GL"), g("WS")
    XR, A16, B16, C16, WBs, FS = g("XR"), g("A16"), g("B16"), g("C16"), g("WBs"), g("FS")
    PB, PT = g("PB"), g("PT")
    W, CCc, row_d = g("W"), g("CCc"), g("row_d")
    gates_d, id16 = g("gates_d"), g("id16")
    fm, ccol, ones16 = g("fm"), g("ccol"), g("ones16")
    stats_rows, row_math, rep512 = g("stats_rows"), g("row_math"), g("rep512")

    @block.vector
    def _(v):
        bar(v)
        for c in range(DCH):
            DV.inc(v.tensor_copy(fm(B16, c), fm(XR, c)))
    stats_rows(R + "ln", B16, 0, 1, want_mean=True)

    def lnm():
        @block.vector
        def _(v):
            bar(v)
            DV.inc(v.tensor_scalar(FS[:, 2560:2576], FS[:, 2048:2064], 1.0 / D, None, OP.mult))
            DV.inc(v.tensor_scalar(FS[:, 2080:2096], FS[:, 2064:2080], 1.0 / D, None, OP.mult))
            fn(v, DV)
            DV.inc(v.tensor_tensor(FS[:, 2096:2112], FS[:, 2560:2576], FS[:, 2560:2576], OP.mult))
            fn(v, DV)
            DV.inc(v.tensor_tensor(FS[:, 2112:2128], FS[:, 2080:2096], FS[:, 2096:2112], OP.subtract))
            mk(R + "ln.var", DV)

        @block.scalar
        def _(s):
            w_(s, R + "ln.var")
            AC.inc(s.activation(FS[:, 2128:2144], FS[:, 2112:2128], AF.Ln, bias=CCc[:, 114:115]))
            fn(s, AC)
            AC.inc(s.activation(FS[:, 2576:2592], FS[:, 2128:2144], AF.Exp, scale=-0.5))
            mk(R + "ln.c", AC)
    row_math(R + "lnm", lnm, [0, 1], 2, [0, 2], 2)

    mb, cpb = g("WBC")[:, 0:512], g("WBC")[:, 512:1024]
    t1s = FS[:, 2048:2560]
    for tt in range(4):
        @block.sync
        def _(q, tt=tt):
            if tt == 0:
                bar(q)
            w_(q, R + f"ln.ap.{DCH-1}.{tt-1}")
            rep512(q, 0, tt, mb)
            rep512(q, 2, tt, cpb)
            mk(R + f"ln.bc.{tt}", BCS)

        for c in range(DCH):
            @block.vector
            def _(v, tt=tt, c=c):
                w_(v, R + f"ln.bc.{tt}")
                fn(v, DV)
                DV.inc(v.tensor_tensor(t1s, XR[:, c * T + tt * TT: c * T + (tt + 1) * TT], mb, OP.subtract))
                fn(v, DV)
                DV.inc(v.tensor_tensor(fm(A16, c, tt * TT, (tt + 1) * TT), t1s, cpb, OP.mult))
                mk(R + f"ln.ap.{c}.{tt}", DV)
    mk(R + "h16", DV)

    @block.gpsimd
    def _(g2):
        bar(g2)
        GP.inc(g2.dma_start(out=WBs[:, 0:64].rearrange("p (k n) -> p k n", k=8),
                            in_=W["f_Wr"].rearrange("(k p) n -> p k n", p=128)), 16)
        mk(R + "wr", GP)

    gtile = C16[:, 5120:5128]
    stile = FS[:, 8:16]
    for w in range(16):
        @block.tensor
        def _(t, w=w):
            if w == 0:
                w_(t, R + "wr")
                w_(t, R + "h16")
            w_(t, R + f"g.sm.{w-1}" if w > 0 else "x.-1")
            for k in range(DCH):
                mmi = t.matmul(PB[4][0:128, 0:8], fm(A16, k, w * 128, (w + 1) * 128),
                               WBs[:, k * 8:(k + 1) * 8], start=(k == 0), stop=(k == DCH - 1))
            PE.inc(mmi)
            mk(R + f"g.mm.{w}", PE)

        @block.vector
        def _(v, w=w):
            w_(v, R + f"g.mm.{w}")
            w_(v, R + f"g.tr.{w-1}" if w > 0 else "x.-1")
            DV.inc(v.tensor_reduce(FS[:, 16:17], PB[4][0:128, 0:8], AX, OP.max))
            fn(v, DV)
            DV.inc(v.tensor_scalar(stile, PB[4][0:128, 0:8], FS[:, 16:17], None, OP.subtract))
            mk(R + f"g.sm.{w}", DV)

        @block.scalar
        def _(s, w=w):
            w_(s, R + f"g.sm.{w}")
            AC.inc(s.activation(stile, stile, AF.Exp))
            mk(R + f"g.ex.{w}", AC)

        @block.vector
        def _(v, w=w):
            w_(v, R + f"g.ex.{w}")
            DV.inc(v.tensor_reduce(FS[:, 16:17], stile, AX, OP.add))
            fn(v, DV)
            DV.inc(v.tensor_reduce(FS[:, 17:18], stile, AX, OP.max))
            fn(v, DV)
            DV.inc(v.tensor_scalar(FS[:, 24:32], stile, FS[:, 17:18], None, OP.is_ge))
            fn(v, DV)
            DV.inc(v.scalar_tensor_tensor(FS[:, 32:40], FS[:, 24:32], -1e30, stile, OP.mult, OP.add))
            fn(v, DV)
            DV.inc(v.tensor_reduce(FS[:, 18:19], FS[:, 32:40], AX, OP.max))
            fn(v, DV)
            DV.inc(v.tensor_tensor(FS[:, 19:20], FS[:, 17:18], FS[:, 18:19], OP.add))
            mk(R + f"g.t2.{w}", DV)

        @block.scalar
        def _(s, w=w):
            w_(s, R + f"g.t2.{w}")
            AC.inc(s.activation(FS[:, 20:21], FS[:, 19:20], AF.Ln))
            fn(s, AC)
            AC.inc(s.activation(FS[:, 21:22], FS[:, 20:21], AF.Exp, scale=-1.0))
            mk(R + f"g.rd.{w}", AC)

        @block.vector
        def _(v, w=w):
            w_(v, R + f"g.rd.{w}")
            DV.inc(v.tensor_tensor(FS[:, 22:23], FS[:, 17:18], FS[:, 21:22], OP.mult))
            DV.inc(v.tensor_tensor(FS[:, 23:24], FS[:, 18:19], FS[:, 21:22], OP.mult))
            fn(v, DV)
            DV.inc(v.tensor_scalar(FS[:, 24:32], stile, FS[:, 17:18], None, OP.is_ge))
            DV.inc(v.tensor_scalar(FS[:, 32:40], stile, FS[:, 18:19], None, OP.is_ge))
            fn(v, DV)
            DV.inc(v.tensor_tensor(FS[:, 32:40], FS[:, 32:40], FS[:, 24:32], OP.subtract))
            fn(v, DV)
            DV.inc(v.tensor_scalar(FS[:, 24:32], FS[:, 24:32], FS[:, 22:23], None, OP.mult))
            DV.inc(v.tensor_scalar(FS[:, 32:40], FS[:, 32:40], FS[:, 23:24], None, OP.mult))
            fn(v, DV)
            DV.inc(v.tensor_tensor(gtile, FS[:, 24:32], FS[:, 32:40], OP.add))
            mk(R + f"g.gt.{w}", DV)

        @block.tensor
        def _(t, w=w):
            w_(t, R + f"g.gt.{w}")
            w_(t, R + f"g.rev.{w-1}" if w > 0 else "x.-1")
            PE.inc(t.transpose(PT[0:8, 0:128], gtile[0:128, :], id16))
            mk(R + f"g.tr.{w}", PE)

        @block.scalar
        def _(s, w=w):
            w_(s, R + f"g.tr.{w}")
            AC.inc(s.activation(FS[0:8, 2048 + w * 128: 2048 + (w + 1) * 128], PT[0:8, 0:128], AF.Copy))
            mk(R + f"g.rev.{w}", AC)

    @block.sync
    def _(q):
        bar(q)
        SPL.inc(q.dma_start(out=gates_d, in_=FS[0:8, 2048:4096]), 16)
        mk(R + "g.sp", SPL)

    w1s, w3s, w2s = WBs[:, 0:4096], WBs[:, 4096:8192], WBs[:, 8192:12288]
    sil = C16[:, 0:512]
    gu = lambda dc: C16[:, 1024 + dc * 512: 1024 + (dc + 1) * 512]
    wgb = g("WBC")[:, 0:512]
    for e in range(9):
        gated = e < 8
        wg_ap = W["f_Wg"][e] if gated else W["f_Sg"]
        wu_ap = W["f_Wu"][e] if gated else W["f_Su"]
        wd_ap = W["f_Wd"][e] if gated else W["f_Sd"]
        for q_ in range(4):
            @block.gpsimd
            def _(g2, e=e, q_=q_, wg_ap=wg_ap, wu_ap=wu_ap, wd_ap=wd_ap):
                if e == 0 and q_ == 0:
                    bar(g2)
                else:
                    pe_, qp = (e, q_ - 1) if q_ > 0 else (e - 1, 3)
                    w_(g2, R + f"x.mm2.{pe_}.{qp}.3.{DCH-1}")
                WS[0].inc(g2.dma_start(
                    out=w1s.rearrange("p (k n) -> p k n", k=8),
                    in_=wg_ap.rearrange("(k p) n -> p k n", p=128)[:, :, q_ * 512:(q_ + 1) * 512]), 16)
                WS[1].inc(g2.dma_start(
                    out=w3s.rearrange("p (k n) -> p k n", k=8),
                    in_=wu_ap.rearrange("(k p) n -> p k n", p=128)[:, :, q_ * 512:(q_ + 1) * 512]), 16)
                WS[2].inc(g2.dma_start(
                    out=w2s.rearrange("p (k n) -> p k n", k=4),
                    in_=wd_ap.rearrange("(k p) n -> p k n", p=128)[:, 4 * q_: 4 * q_ + 4, :]), 16)
                mk(R + f"x.w.{e}.{q_}", GP)

            for tt in range(4):
                @block.sync
                def _(q, e=e, q_=q_, tt=tt):
                    if e < 8 and q_ == 0:
                        if e == 0 and tt == 0:
                            bar(q)
                        w_(q, R + f"x.he.{e-1}.3" if e > 0 else "x.-1")
                        if tt > 0:
                            w_(q, R + f"x.he.{e}.{tt-1}")
                        BCS.inc(q.dma_start(out=wgb, in_=bass.AP(
                            gates_d.tensor, e * T + tt * TT, [[0, 128], [1, 512]])), 16)
                        mk(R + f"x.gb.{e}.{q_}.{tt}", BCS)

                @block.vector
                def _(v, e=e, q_=q_, tt=tt):
                    if e < 8 and q_ == 0:
                        w_(v, R + f"x.gb.{e}.{q_}.{tt}")
                        w_(v, R + f"x.heu.{e-1}.{tt}" if e > 0 else "x.-1")
                        for k in range(DCH):
                            DV.inc(v.tensor_tensor(fm(B16, k, tt * TT, (tt + 1) * TT),
                                                   fm(A16, k, tt * TT, (tt + 1) * TT), wgb, OP.mult))
                        mk(R + f"x.he.{e}.{tt}", DV)

                for dc in range(4):
                    @block.tensor
                    def _(t, e=e, q_=q_, tt=tt, dc=dc, gated=gated):
                        if tt == 0 and dc == 0:
                            bld.fence(t, WS[0])
                            bld.fence(t, WS[1])
                        w_(t, R + f"x.sil.{e}.{q_}.{tt}.{dc-1}" if dc > 0 else (R + f"x.sil.{e}.{q_}.{tt-1}.3" if tt > 0 else (R + f"x.sil.{e}.{q_-1}.3.3" if q_ > 0 else (R + f"x.sil.{e-1}.3.3.3" if e > 0 else "x.-1"))))
                        w_(t, R + f"x.gu.{e}.{q_}.{tt}.{dc-1}" if dc > 0 else (R + f"x.gu.{e}.{q_}.{tt-1}.3" if tt > 0 else (R + f"x.gu.{e}.{q_-1}.3.3" if q_ > 0 else (R + f"x.gu.{e-1}.3.3.3" if e > 0 else "x.-1"))))
                        if gated:
                            w_(t, R + f"x.he.{e}.{tt}")
                        for k in range(DCH):
                            t.matmul(PB[0][:], w1s[:, k * 512 + dc * 128: k * 512 + dc * 128 + 128],
                                     fm(A16, k, tt * TT, (tt + 1) * TT),
                                     start=(k == 0), stop=(k == DCH - 1))
                        for k in range(DCH):
                            mmi = t.matmul(PB[1][:], w3s[:, k * 512 + dc * 128: k * 512 + dc * 128 + 128],
                                           fm(B16, k, tt * TT, (tt + 1) * TT) if gated else fm(A16, k, tt * TT, (tt + 1) * TT),
                                           start=(k == 0), stop=(k == DCH - 1))
                        PE.inc(mmi)
                        mk(R + f"x.mm13.{e}.{q_}.{tt}.{dc}", PE)
                        if q_ == 3 and dc == 3:
                            mk(R + f"x.heu.{e}.{tt}", PE)

                    @block.scalar
                    def _(s, e=e, q_=q_, tt=tt, dc=dc):
                        w_(s, R + f"x.mm13.{e}.{q_}.{tt}.{dc}")
                        AC.inc(s.activation(sil, PB[0][:], AF.Silu))
                        mk(R + f"x.sil.{e}.{q_}.{tt}.{dc}", AC)

                    @block.vector
                    def _(v, e=e, q_=q_, tt=tt, dc=dc, gated=gated):
                        w_(v, R + f"x.sil.{e}.{q_}.{tt}.{dc}")
                        w_(v, R + f"x.mm2g.{e}.{q_}.{tt-1}" if tt > 0 else (R + f"x.mm2g.{e}.{q_-1}.3" if q_ > 0 else (R + f"x.mm2g.{e-1}.3.3" if e > 0 else "x.-1")))
                        DV.inc(v.tensor_tensor(gu(dc), sil, PB[1][:], OP.mult))
                        mk(R + f"x.gu.{e}.{q_}.{tt}.{dc}", DV)

                for mo in range(DCH):
                    @block.tensor
                    def _(t, e=e, q_=q_, tt=tt, mo=mo):
                        if mo == 0:
                            bld.fence(t, WS[2])
                            w_(t, R + f"x.gu.{e}.{q_}.{tt}.3")
                        w_(t, R + f"x.ev.{e}.{q_}.{tt}.{mo-2}" if mo >= 2 else (R + f"x.ev.{e}.{q_}.{tt-1}.{6+mo}" if tt > 0 else (R + f"x.ev.{e}.{q_-1}.3.{6+mo}" if q_ > 0 else (R + f"x.ev.{e-1}.3.3.{6+mo}" if e > 0 else "x.-1"))))
                        for fc in range(4):
                            mmi = t.matmul(PB[2 + mo % 2][:], w2s[:, fc * 1024 + mo * 128: fc * 1024 + mo * 128 + 128],
                                           gu(fc), start=(fc == 0), stop=(fc == 3))
                        PE.inc(mmi)
                        mk(R + f"x.mm2.{e}.{q_}.{tt}.{mo}", PE)
                        if mo == DCH - 1:
                            mk(R + f"x.mm2g.{e}.{q_}.{tt}", PE)

                    @block.vector
                    def _(v, e=e, q_=q_, tt=tt, mo=mo):
                        w_(v, R + f"x.mm2.{e}.{q_}.{tt}.{mo}")
                        DV.inc(v.tensor_tensor(XR[:, mo * T + tt * TT: mo * T + (tt + 1) * TT],
                                               XR[:, mo * T + tt * TT: mo * T + (tt + 1) * TT],
                                               PB[2 + mo % 2][:], OP.add))
                        mk(R + f"x.ev.{e}.{q_}.{tt}.{mo}", DV)
    mk(R + ".done", DV)


def emit_attn_b(bld, block, env):
    g = lambda k: env[k]
    mk, w_, bar, fn = bld.mk, bld.w, bld.bar, bld.fence
    SP, GP, AC, DV, PE = g("SP"), g("GP"), g("AC"), g("DV"), g("PE")
    SPL, BCS, GL, WS = g("SPL"), g("BCS"), g("GL"), g("WT")
    XR, A16, B16, C16, WBs, FS, SS = g("XR"), g("A16"), g("B16"), g("C16"), g("WBs"), g("FS"), g("SS")
    PB, PT = g("PB"), g("PT")
    W, CCc, row_d = g("W"), g("CCc"), g("row_d")
    wbT_d, wbar_d, vtmB_d, vfm_d, grow_d = g("wbT_d"), g("wbar_d"), g("vtmB_d"), g("vfm_d"), g("grow_d")
    MKB, RT16, id16, mhi, mlo = g("MKB"), g("RT16"), g("id16"), g("mhi"), g("mlo")
    fm, ccol = g("fm"), g("ccol")
    stats_rows, row_math, rep512 = g("stats_rows"), g("row_math"), g("rep512")
    ep_d = g("ep_d")
    ytm_d = g("ytm_d")

    @block.vector
    def _(v):
        bar(v)
        for c in range(DCH):
            DV.inc(v.tensor_copy(fm(B16, c), fm(XR, c)))
    stats_rows("bln", B16, 0, 1, want_mean=True)

    def lnm():
        @block.vector
        def _(v):
            bar(v)
            DV.inc(v.tensor_scalar(FS[:, 2560:2576], FS[:, 2048:2064], 1.0 / D, None, OP.mult))
            DV.inc(v.tensor_scalar(FS[:, 2080:2096], FS[:, 2064:2080], 1.0 / D, None, OP.mult))
            fn(v, DV)
            DV.inc(v.tensor_tensor(FS[:, 2096:2112], FS[:, 2560:2576], FS[:, 2560:2576], OP.mult))
            fn(v, DV)
            DV.inc(v.tensor_tensor(FS[:, 2112:2128], FS[:, 2080:2096], FS[:, 2096:2112], OP.subtract))
            mk("bln.var", DV)

        @block.scalar
        def _(s):
            w_(s, "bln.var")
            AC.inc(s.activation(FS[:, 2128:2144], FS[:, 2112:2128], AF.Ln, bias=CCc[:, 114:115]))
            fn(s, AC)
            AC.inc(s.activation(FS[:, 2576:2592], FS[:, 2128:2144], AF.Exp, scale=-0.5))
            mk("bln.c", AC)
    row_math("blnm", lnm, [0, 1], 2, [0, 2], 2)

    mb, cpb = g("WBC")[:, 0:512], g("WBC")[:, 512:1024]
    t1s = FS[:, 2048:2560]
    for tt in range(4):
        @block.sync
        def _(q, tt=tt):
            if tt == 0:
                bar(q)
            w_(q, f"bln.ap.{DCH-1}.{tt-1}")
            rep512(q, 0, tt, mb)
            rep512(q, 2, tt, cpb)
            mk(f"bln.bc.{tt}", BCS)

        for c in range(DCH):
            @block.vector
            def _(v, tt=tt, c=c):
                w_(v, f"bln.bc.{tt}")
                fn(v, DV)
                DV.inc(v.tensor_tensor(t1s, XR[:, c * T + tt * TT: c * T + (tt + 1) * TT], mb, OP.subtract))
                fn(v, DV)
                DV.inc(v.tensor_tensor(fm(A16, c, tt * TT, (tt + 1) * TT), t1s, cpb, OP.mult))
                mk(f"bln.ap.{c}.{tt}", DV)
    mk("bh16", DV)

    ws = g("ws")
    WGB, BGB, WLB, BLB = g("WGB"), g("BGB"), g("WLB"), g("BLB")
    Cin = g("C")

    @block.sync
    def _(q):
        bar(q)
        SP.inc(q.dma_start(out=WGB, in_=Cin["wg_bc"]), 16)
        SP.inc(q.dma_start(out=BGB, in_=Cin["bg_bc"]), 16)
        SP.inc(q.dma_start(out=WLB, in_=Cin["wl_bc"]), 16)
        SP.inc(q.dma_start(out=BLB, in_=Cin["bl_bc"]), 16)
        mk("bgl", SP)

    def load_whalfT(q, widx, half, slot):
        for k in range(DCH):
            WS[slot].inc(q.dma_start(
                out=ws[slot][:, k * 512:(k + 1) * 512],
                in_=wbT_d[widx][half * 512:(half + 1) * 512, k * 128:(k + 1) * 128],
                transpose=True), 16)

    def projT(tag, widx, dest_pool):
        @block.scalar
        def _(s):
            bar(s)
            load_whalfT(s, widx, 0, 0)
            load_whalfT(s, widx, 1, 1)
            mk(f"{tag}.wT", AC)

        for mo in range(DCH):
            for tt in range(4):
                @block.tensor
                def _(t, mo=mo, tt=tt):
                    if mo == 0 and tt == 0:
                        bld.fence(t, WS[0])
                        bld.fence(t, WS[1])
                    w_(t, f"{tag}.ev.{mo}.{tt-1}" if tt > 0 else f"{tag}.ev.{mo-2}.3")
                    half, m4 = mo // 4, mo % 4
                    for k in range(DCH):
                        mmi = t.matmul(PB[mo % 2][:], ws[half][:, k * 512 + m4 * 128: k * 512 + m4 * 128 + 128],
                                       fm(A16, k, tt * TT, (tt + 1) * TT),
                                       start=(k == 0), stop=(k == DCH - 1))
                    PE.inc(mmi)
                    mk(f"{tag}.mm.{mo}.{tt}", PE)

                @block.vector
                def _(v, mo=mo, tt=tt):
                    w_(v, f"{tag}.mm.{mo}.{tt}")
                    DV.inc(v.tensor_copy(fm(dest_pool, mo, tt * TT, (tt + 1) * TT), PB[mo % 2][:]))
                    mk(f"{tag}.ev.{mo}.{tt}", DV)
    projT("bq", 0, B16)

    vsl = [C16[:, 0:512], C16[:, 512:1024]]

    @block.scalar
    def _(s):
        bar(s)
        load_whalfT(s, 2, 0, 0)
        load_whalfT(s, 2, 1, 1)
        mk("bv.wT", AC)

    for mo in range(DCH):
        for tt in range(4):
            i_ = mo * 4 + tt

            @block.tensor
            def _(t, mo=mo, tt=tt, i_=i_):
                if mo == 0 and tt == 0:
                    bld.fence(t, WS[0])
                    bld.fence(t, WS[1])
                w_(t, f"bv.ev.{i_-2}")
                half, m4 = mo // 4, mo % 4
                for k in range(DCH):
                    mmi = t.matmul(PB[i_ % 2][:], ws[half][:, k * 512 + m4 * 128: k * 512 + m4 * 128 + 128],
                                   fm(A16, k, tt * TT, (tt + 1) * TT),
                                   start=(k == 0), stop=(k == DCH - 1))
                PE.inc(mmi)
                mk(f"bv.mm.{i_}", PE)

            @block.vector
            def _(v, i_=i_):
                w_(v, f"bv.mm.{i_}")
                w_(v, f"bv.sp.{i_-2}")
                DV.inc(v.tensor_copy(vsl[i_ % 2], PB[i_ % 2][:]))
                mk(f"bv.ev.{i_}", DV)

            @block.sync
            def _(q, mo=mo, tt=tt, i_=i_):
                w_(q, f"bv.ev.{i_}")
                SPL.inc(q.dma_start(out=vfm_d.rearrange("(c p) t -> p c t", p=128)[:, mo, tt * TT:(tt + 1) * TT],
                                    in_=vsl[i_ % 2]), 16)
                mk(f"bv.sp.{i_}", SPL)

    for nh in range(2):
        @block.scalar
        def _(s, nh=nh):
            bar(s)
            load_whalfT(s, 2, nh, 2)
            mk(f"bvt.wT.{nh}", AC)

        for w in range(16):
            i_ = nh * 16 + w

            @block.tensor
            def _(t, nh=nh, w=w, i_=i_):
                if w == 0:
                    bld.fence(t, WS[2])
                w_(t, f"bvt.ev.{i_-2}")
                for k in range(DCH):
                    mmi = t.matmul(PB[i_ % 2][:], fm(A16, k, w * 128, (w + 1) * 128),
                                   ws[2][:, k * 512:(k + 1) * 512],
                                   start=(k == 0), stop=(k == DCH - 1))
                PE.inc(mmi)
                mk(f"bvt.mm.{i_}", PE)

            @block.vector
            def _(v, i_=i_):
                w_(v, f"bvt.mm.{i_}")
                w_(v, f"bvt.sp.{i_-2}")
                DV.inc(v.tensor_copy(vsl[i_ % 2], PB[i_ % 2][:]))
                mk(f"bvt.ev.{i_}", DV)

            @block.sync
            def _(q, nh=nh, w=w, i_=i_):
                w_(q, f"bvt.ev.{i_}")
                SPL.inc(q.dma_start(out=vtmB_d[nh][w * 128:(w + 1) * 128, :], in_=vsl[i_ % 2]), 16)
                mk(f"bvt.sp.{i_}", SPL)

    for nh in range(2):
        @block.scalar
        def _(s, nh=nh):
            bar(s)
            load_whalfT(s, 1, nh, 2)
            mk(f"bk.wT.{nh}", AC)

        for w in range(16):
            i_ = nh * 16 + w
            ktl = C16[:, 1024:2048]
            sqt = C16[:, 2048:3072]
            knt = C16[:, 3072:3584]
            wbt = C16[:, 3584:4096]
            ept = C16[:, 4096:4608]
            lrt = FS[:, 2048:2560]
            rkc = FS[:, 2560:2562]

            @block.tensor
            def _(t, nh=nh, w=w, i_=i_):
                if w == 0:
                    bld.fence(t, WS[2])
                w_(t, f"bk.kn.{i_-1}")
                for k in range(DCH):
                    mmi = t.matmul(PB[0][:], fm(A16, k, w * 128, (w + 1) * 128),
                                   ws[2][:, k * 512:(k + 1) * 512],
                                   start=(k == 0), stop=(k == DCH - 1))
                PE.inc(mmi)
                mk(f"bk.mm.{i_}", PE)

            @block.scalar
            def _(s, nh=nh, i_=i_):
                w_(s, f"bk.mm.{i_}")
                w_(s, f"bk.wb.{i_-1}")
                AC.inc(s.activation(sqt[:, 0:512], PB[0][:], AF.Square))
                mk(f"bk.sq.{i_}", AC)

            @block.vector
            def _(v, nh=nh, i_=i_):
                w_(v, f"bk.sq.{i_}")
                DV.inc(v.tensor_reduce(rkc[:, 0:1], sqt[:, 0:512], AX, OP.add))
                mk(f"bk.ss.{i_}", DV)

            @block.scalar
            def _(s, nh=nh, i_=i_):
                w_(s, f"bk.ss.{i_}")
                AC.inc(s.activation(FS[:, 2562:2563], rkc[:, 0:1], AF.Sqrt))
                mk(f"bk.sr.{i_}", AC)

            @block.vector
            def _(v, nh=nh, i_=i_):
                w_(v, f"bk.sr.{i_}")
                DV.inc(v.tensor_scalar(FS[:, 2563:2564], FS[:, 2562:2563], 1e-12, None, OP.max))
                mk(f"bk.mx.{i_}", DV)

            @block.scalar
            def _(s, nh=nh, i_=i_):
                w_(s, f"bk.mx.{i_}")
                AC.inc(s.activation(FS[:, 2564:2565], FS[:, 2563:2564], AF.Ln))
                fn(s, AC)
                AC.inc(s.activation(FS[:, 2565:2566], FS[:, 2564:2565], AF.Exp, scale=-1.0))
                mk(f"bk.rk.{i_}", AC)

            @block.vector
            def _(v, nh=nh, i_=i_):
                w_(v, f"bk.rk.{i_}")
                DV.inc(v.tensor_scalar(knt[:, 0:512], PB[0][:], FS[:, 2565:2566], None, OP.mult))
                mk(f"bk.kn.{i_}", DV)

            @block.vector
            def _(v, nh=nh, i_=i_):
                w_(v, f"bk.kn.{i_}")
                w_(v, f"bk.wb.{i_-1}")
                DV.inc(v.tensor_tensor(sqt[:, 0:512], knt[:, 0:512], WGB[:, nh * 512:(nh + 1) * 512], OP.mult))
                DV.inc(v.tensor_tensor(sqt[:, 512:1024], knt[:, 0:512], WLB[:, nh * 512:(nh + 1) * 512], OP.mult))
                fn(v, DV)
                DV.inc(v.tensor_tensor(sqt[:, 0:512], sqt[:, 0:512], BGB[:, nh * 512:(nh + 1) * 512], OP.add))
                DV.inc(v.tensor_tensor(sqt[:, 512:1024], sqt[:, 512:1024], BLB[:, nh * 512:(nh + 1) * 512], OP.add))
                mk(f"bk.lg.{i_}", DV)

            @block.scalar
            def _(s, nh=nh, i_=i_):
                w_(s, f"bk.lg.{i_}")
                AC.inc(s.activation(sqt[:, 0:512], sqt[:, 0:512], AF.Sigmoid))
                AC.inc(s.activation(sqt[:, 512:1024], sqt[:, 512:1024], AF.Sigmoid))
                mk(f"bk.sg.{i_}", AC)

            @block.vector
            def _(v, nh=nh, i_=i_):
                w_(v, f"bk.sg.{i_}")
                DV.inc(v.tensor_scalar(sqt[:, 512:1024], sqt[:, 512:1024], -1.0, 1.0, OP.mult, OP.add))
                fn(v, DV)
                DV.inc(v.tensor_tensor(sqt[:, 0:512], sqt[:, 0:512], sqt[:, 512:1024], OP.mult))
                fn(v, DV)
                DV.inc(v.tensor_scalar(sqt[:, 0:512], sqt[:, 0:512], RFLOOR, None, OP.max))
                mk(f"bk.rho.{i_}", DV)

            @block.scalar
            def _(s, nh=nh, i_=i_):
                w_(s, f"bk.rho.{i_}")
                AC.inc(s.activation(lrt[:, 0:512], sqt[:, 0:512], AF.Ln))
                mk(f"bk.lr.{i_}", AC)

            @block.vector
            def _(v, nh=nh, i_=i_):
                w_(v, f"bk.lr.{i_}")
                DV.inc(v.tensor_copy(C16[:, 4608:5120], lrt[:, 0:512]))
                mk(f"bk.lr16.{i_}", DV)

            @block.tensor
            def _(t, nh=nh, i_=i_):
                w_(t, f"bk.lr16.{i_}")
                w_(t, f"bk.ep.{i_-1}")
                t.matmul(PB[1][:], RT16[:], C16[:, 4608:5120], start=True, stop=True)
                PE.inc(t.matmul(PB[2][0:2, :], g("KON")[:, 131:133], C16[:, 4608:5120], start=True, stop=True))
                mk(f"bk.lrmm.{i_}", PE)

            @block.scalar
            def _(s, nh=nh, i_=i_, w=w):
                w_(s, f"bk.lrmm.{i_}")
                w_(s, f"bk.wsp.{i_-1}")
                AC.inc(s.activation(ept[:, 0:512], PB[1][:], AF.Exp))
                fn(s, AC)
                AC.inc(s.activation(sqt[:, 512:1024], PB[1][:], AF.Exp, scale=-1.0))
                AC.inc(s.activation(FS[0:2, 2566 + 0:2566 + 512][0:2, 0:512] if False else FS[0:2, 2566:3078], PB[2][0:2, :], AF.Copy))
                mk(f"bk.ep.{i_}", AC)

            @block.vector
            def _(v, nh=nh, i_=i_):
                w_(v, f"bk.ep.{i_}")
                w_(v, f"bk.wsp.{i_-2}")
                DV.inc(v.tensor_scalar(wbt[:, 0:512], sqt[:, 0:512], -1.0, 1.0, OP.mult, OP.add))
                fn(v, DV)
                DV.inc(v.tensor_tensor(wbt[:, 0:512], wbt[:, 0:512], knt[:, 0:512], OP.mult))
                fn(v, DV)
                DV.inc(v.tensor_tensor(wbt[:, 0:512], wbt[:, 0:512], sqt[:, 512:1024], OP.mult))
                mk(f"bk.wb.{i_}", DV)

            @block.sync
            def _(q, nh=nh, w=w, i_=i_):
                w_(q, f"bk.wb.{i_}")
                SPL.inc(q.dma_start(out=wbar_d[nh][w * 128:(w + 1) * 128, :], in_=wbt[:, 0:512]), 16)
                SPL.inc(q.dma_start(out=ep_d[nh][w * 128:(w + 1) * 128, :], in_=ept[:, 0:512]), 16)
                SPL.inc(q.dma_start(out=grow_d[nh][0:2, w * 512:(w + 1) * 512] if False else
                        bass.AP(grow_d.tensor, nh * 16 * 1024 + w * 1024, [[512, 2], [1, 512]]),
                        in_=FS[0:2, 2566:3078]), 16)
                mk(f"bk.wsp.{i_}", SPL)

    for nh in range(2):
        for c in range(16):
            @block.sync
            def _(q, nh=nh, c=c):
                if nh == 0 and c == 0:
                    bar(q)
                w_(q, f"bg.sp.{nh}.{c-1}" if c > 0 else (f"bg.sp.{nh-1}.15" if nh > 0 else "x.-1"))
                SP.inc(q.dma_start(out=FS[0:1, 2048:2560],
                                   in_=bass.AP(grow_d.tensor, nh * 16384 + c * 1024 + 512, [[512, 1], [1, 512]])), 16)
                if c < 15:
                    SP.inc(q.dma_start(out=FS[0:1, 2560:3072],
                                       in_=bass.AP(grow_d.tensor, nh * 16384 + (c + 1) * 1024, [[512, 1], [1, 512]])), 16)
                mk(f"bg.ld.{nh}.{c}", SP)

            @block.vector
            def _(v, nh=nh, c=c):
                w_(v, f"bg.ld.{nh}.{c}")
                if c < 15:
                    DV.inc(v.tensor_tensor(FS[0:1, 3072:3584], FS[0:1, 2048:2560], FS[0:1, 2560:3072], OP.add))
                else:
                    DV.inc(v.tensor_copy(FS[0:1, 3072:3584], FS[0:1, 2048:2560]))
                mk(f"bg.add.{nh}.{c}", DV)

            @block.scalar
            def _(s, nh=nh, c=c):
                w_(s, f"bg.add.{nh}.{c}")
                AC.inc(s.activation(FS[0:1, 3584:4096], FS[0:1, 3072:3584], AF.Exp))
                mk(f"bg.exp.{nh}.{c}", AC)

            @block.sync
            def _(q2, nh=nh, c=c):
                w_(q2, f"bg.exp.{nh}.{c}")
                SPL.inc(q2.dma_start(out=bass.AP(grow_d.tensor, 32768 + nh * 16384 + c * 512, [[512, 1], [1, 512]]),
                                     in_=FS[0:1, 3584:4096]), 16)
                mk(f"bg.sp.{nh}.{c}", SPL)

    S16 = lambda vc: A16[:, 8192 + vc * 512: 8192 + (vc + 1) * 512]
    qsl = lambda kc, c: B16[:, 0:0]
    am16 = WBs[:, 0:128]
    wbc_t = C16[:, 0:512]
    vtc = C16[:, 512:1024]
    epc = C16[:, 1024:1536]
    vfc = C16[:, 1536:2048]
    ytc = C16[:, 2048:2560]
    gb_t = g("WBC")[:, 0:512]

    @block.vector
    def _(v):
        bar(v)
        DV.inc(v.memset(SS[:, 0:4096], 0.0))
        mk("bs.s.-1", DV)

    for h in range(2):
        for c in range(16):
            @block.sync
            def _(q, h=h, c=c):
                if h == 0 and c == 0:
                    bar(q)
                w_(q, f"bs.sdone.{h}.{c-1}" if c > 0 else (f"bs.sdone.{h-1}.15" if h > 0 else "x.-1"))
                GL.inc(q.dma_start(out=wbc_t, in_=wbar_d[h][c * 128:(c + 1) * 128, :]), 16)
                GL.inc(q.dma_start(out=vtc, in_=vtmB_d[h][c * 128:(c + 1) * 128, :]), 16)
                GL.inc(q.dma_start(out=epc, in_=ep_d[h][c * 128:(c + 1) * 128, :]), 16)
                GL.inc(q.dma_start(out=vfc.rearrange("p (a b) -> p a b", a=4),
                                   in_=vfm_d.rearrange("(c p) t -> p c t", p=128)[:, 4 * h: 4 * h + 4, c * 128:(c + 1) * 128]), 16)
                BCS.inc(q.dma_start(out=gb_t, in_=bass.AP(
                    grow_d.tensor, 32768 + h * 16384 + c * 512, [[0, 128], [1, 512]])), 16)
                mk(f"bs.ld.{h}.{c}", GL)

            @block.vector
            def _(v, h=h, c=c):
                w_(v, f"bs.ld.{h}.{c}")
                w_(v, f"bs.s16u.{h}.{c-1}" if c > 0 else (f"bs.s16u.{h-1}.15" if h > 0 else "x.-1"))
                for vc in range(4):
                    DV.inc(v.tensor_copy(S16(vc), SS[:, h * 2048 + vc * 512: h * 2048 + (vc + 1) * 512]))
                mk(f"bs.s16.{h}.{c}", DV)
                mk(f"bs.s16u.{h}.{c}", DV)

            @block.tensor
            def _(t, h=h, c=c):
                w_(t, f"bs.s16.{h}.{c}")
                w_(t, f"bs.am.{h}.{c-1}" if c > 0 else (f"bs.am.{h-1}.15" if h > 0 else "x.-1"))
                for kc in range(4):
                    mmi = t.matmul(PB[0][:, 0:128], vfc[:, kc * 128:(kc + 1) * 128],
                                   B16[:, (4 * h + kc) * T + c * 128: (4 * h + kc) * T + (c + 1) * 128],
                                   start=(kc == 0), stop=(kc == 3))
                PE.inc(mmi)
                mk(f"bs.amm.{h}.{c}", PE)

            @block.vector
            def _(v, h=h, c=c):
                w_(v, f"bs.amm.{h}.{c}")
                DV.inc(v.tensor_tensor(am16[:, 0:128], PB[0][:, 0:128], MKB[:], OP.mult))
                mk(f"bs.am.{h}.{c}", DV)

            @block.tensor
            def _(t, h=h, c=c):
                w_(t, f"bs.am.{h}.{c}")
                w_(t, f"bs.ye.{h}.{c-1}" if c > 0 else (f"bs.ye.{h-1}.15" if h > 0 else "x.-1"))
                t.matmul(PB[1][:], am16[:, 0:128], wbc_t, start=True, stop=False)
                for kc in range(4):
                    mmi = t.matmul(PB[1][:], B16[:, (4 * h + kc) * T + c * 128: (4 * h + kc) * T + (c + 1) * 128],
                                   S16(kc), start=False, stop=(kc == 3))
                PE.inc(mmi)
                mk(f"bs.ymm.{h}.{c}", PE)

            @block.vector
            def _(v, h=h, c=c):
                w_(v, f"bs.ymm.{h}.{c}")
                w_(v, f"bs.ysp.{h}.{c-1}" if c > 0 else (f"bs.ysp.{h-1}.15" if h > 0 else "x.-1"))
                DV.inc(v.tensor_tensor(ytc, PB[1][:], epc, OP.mult))
                mk(f"bs.ye.{h}.{c}", DV)

            @block.sync
            def _(q, h=h, c=c):
                w_(q, f"bs.ye.{h}.{c}")
                SPL.inc(q.dma_start(out=ytm_d[h][c * 128:(c + 1) * 128, :], in_=ytc), 16)
                mk(f"bs.ysp.{h}.{c}", SPL)

            for vc in range(4):
                @block.tensor
                def _(t, h=h, c=c, vc=vc):
                    w_(t, f"bs.su.{h}.{c}.{vc-1}" if vc > 0 else f"bs.ye.{h}.{c}")
                    PE.inc(t.matmul(PB[2][:], vtc[:, vc * 128:(vc + 1) * 128], wbc_t, start=True, stop=True))
                    mk(f"bs.smm.{h}.{c}.{vc}", PE)

                @block.vector
                def _(v, h=h, c=c, vc=vc):
                    w_(v, f"bs.smm.{h}.{c}.{vc}")
                    w_(v, "bs.ld." + f"{h}.{c}")
                    bld.fence(v, g("BCS"))
                    sl = SS[:, h * 2048 + vc * 512: h * 2048 + (vc + 1) * 512]
                    DV.inc(v.tensor_tensor(sl, sl, PB[2][:], OP.add))
                    fn(v, DV)
                    DV.inc(v.tensor_tensor(sl, sl, gb_t, OP.mult))
                    mk(f"bs.su.{h}.{c}.{vc}", DV)
            mk(f"bs.sdone.{h}.{c}", DV)

    @block.scalar
    def _(s):
        bar(s)
        for h in range(2):
            for kc in range(4):
                WS[0].inc(s.dma_start(out=fm(A16, 4 * h + kc),
                                      in_=ytm_d[h][0:T, kc * 128:(kc + 1) * 128], transpose=True), 16)
        load_whalfT(s, 3, 0, 1)
        mk("bo.w0", AC)

    for mo in range(DCH):
        if mo == 4:
            @block.scalar
            def _(s):
                bar(s)
                load_whalfT(s, 3, 1, 1)
                mk("bo.w1", AC)
        for tt in range(4):
            @block.tensor
            def _(t, mo=mo, tt=tt):
                if tt == 0:
                    bld.fence(t, WS[1])
                    bld.fence(t, WS[0])
                w_(t, f"bo.ev.{mo}.{tt-1}" if tt > 0 else f"bo.ev.{mo-2}.3")
                m4 = mo % 4
                for k in range(DCH):
                    mmi = t.matmul(PB[mo % 2][:], ws[1][:, k * 512 + m4 * 128: k * 512 + m4 * 128 + 128],
                                   fm(A16, k, tt * TT, (tt + 1) * TT),
                                   start=(k == 0), stop=(k == DCH - 1))
                PE.inc(mmi)
                mk(f"bo.mm.{mo}.{tt}", PE)

            @block.vector
            def _(v, mo=mo, tt=tt):
                w_(v, f"bo.mm.{mo}.{tt}")
                DV.inc(v.tensor_tensor(XR[:, mo * T + tt * TT: mo * T + (tt + 1) * TT],
                                       XR[:, mo * T + tt * TT: mo * T + (tt + 1) * TT],
                                       PB[mo % 2][:], OP.add))
                mk(f"bo.ev.{mo}.{tt}", DV)
    mk("attnb.done", DV)


def kernel(**inputs):
    in_maps = host_prep(inputs)
    nc = build(stage=STAGE, dumps=DUMPS)
    res = run_bass_kernel_spmd(nc, in_maps, core_ids=list(range(8)))
    return np.stack([np.asarray(r["out"], np.float32).T for r in res.results], axis=0)


# version 03
# speedup vs baseline: 1.1448x; 1.1448x over previous
import contextlib
import os as _os
import numpy as np
import ml_dtypes
import concourse.bass as bass
import concourse.mybir as mybir
from concourse.bass_utils import run_bass_kernel_spmd
from concourse.alu_op_type import AluOpType as OP

F32 = mybir.dt.float32
BF16 = mybir.dt.bfloat16
AF = mybir.ActivationFunctionType
AX = mybir.AxisListType.X

T = 2048
D = 1024
DCH = 8
TT = 512
CA = 64
CB = 128
FF = 4096
NE = 8
DEXP = 2048
AFLOOR = 0.09
RFLOOR = 0.30

STAGE = 99
DUMPS = ()


def _f32(a):
    return np.ascontiguousarray(np.asarray(a, dtype=np.float32))


def host_constants():
    c = {}
    c["id128"] = np.ascontiguousarray(np.eye(128).astype(ml_dtypes.bfloat16))
    c["ones_col"] = np.ones((128, 1), dtype=ml_dtypes.bfloat16)
    b2 = np.zeros((128, 2), np.float32)
    b2[0:64, 0] = 1.0
    b2[64:128, 1] = 1.0
    c["b2"] = np.ascontiguousarray(b2.astype(ml_dtypes.bfloat16))
    jj, tt_ = np.meshgrid(np.arange(64), np.arange(64), indexing="ij")
    m = np.zeros((64, 128), np.float32)
    m[:, 0:64] = -(jj <= tt_).astype(np.float32)
    m[:, 64:128] = -(jj < tt_).astype(np.float32)
    c["maskA"] = np.ascontiguousarray(np.tile(m, (2, 1)).astype(ml_dtypes.bfloat16))
    jj, tt_ = np.meshgrid(np.arange(CB), np.arange(CB), indexing="ij")
    c["maskB"] = np.ascontiguousarray((jj <= tt_).astype(ml_dtypes.bfloat16))
    rt = (jj <= tt_).astype(np.float32) - (jj <= 63).astype(np.float32)
    c["rbT"] = np.ascontiguousarray(rt.astype(ml_dtypes.bfloat16))
    mh = np.zeros((128, 1), np.float32)
    mh[64:128] = 1.0
    c["mcol_hi"] = np.ascontiguousarray(mh.astype(ml_dtypes.bfloat16))
    ml_ = np.zeros((128, 1), np.float32)
    ml_[0:64] = 1.0
    c["mcol_lo"] = np.ascontiguousarray(ml_.astype(ml_dtypes.bfloat16))
    return c


def host_prep(inputs):
    ins = {k: np.asarray(v) for k, v in inputs.items()}
    common = dict(host_constants())

    def col8(v):
        return _f32(np.asarray(v, np.float32).reshape(8, 128).T)

    A = "L0_attn_a_"
    common["nw_col"] = col8(ins[A + "norm_in_w"])
    common["cb_col"] = col8(ins[A + "conv_b"])
    cw = np.asarray(ins[A + "conv_w"], np.float32)[:, 0, :].reshape(8, 128, 4)
    common["cw_col"] = _f32(cw.transpose(1, 0, 2).reshape(128, 32))
    common["ab_col"] = col8(ins[A + "alpha_b"])
    common["hnw_col"] = col8(np.asarray(ins[A + "head_norm_w"], np.float32).reshape(-1))
    common["fnw_col"] = col8(ins[A + "ff_norm_w"])
    common["u2b_col"] = col8(ins[A + "u2_b"])
    common["u1b_col"] = _f32(np.asarray(ins[A + "u1_b"], np.float32).reshape(64, 1))
    common["bb_col"] = _f32(np.asarray(ins[A + "beta_b"], np.float32).reshape(16, 1))
    for nm in ["q_w", "k_w", "v_w", "alpha_w", "out_w", "u1_w", "u2_w",
               "ff_w1", "ff_w3", "ff_w2", "beta_w"]:
        common["a_" + nm] = _f32(ins[A + nm])
    B = "L0_attn_b_"
    for nm in ["Wq", "Wk", "Wv", "Wo"]:
        common["b_" + nm] = _f32(ins[B + nm])

    def colb(v):
        w = np.asarray(v, np.float32).reshape(2, 4, 128)
        return _f32(w.transpose(2, 0, 1).reshape(128, 8))

    common["wg_col"] = colb(ins[B + "W_gamma"])
    common["bg_col"] = colb(ins[B + "b_gamma"])
    common["wl_col"] = colb(ins[B + "W_lambda"])
    common["bl_col"] = colb(ins[B + "b_lambda"])

    def bcast_tm(v):
        w = np.asarray(v, np.float32).reshape(1, 1024)
        return _f32(np.repeat(w, 128, axis=0))

    common["wg_bc"] = bcast_tm(ins[B + "W_gamma"])
    common["bg_bc"] = bcast_tm(ins[B + "b_gamma"])
    common["wl_bc"] = bcast_tm(ins[B + "W_lambda"])
    common["bl_bc"] = bcast_tm(ins[B + "b_lambda"])
    Fp = "L0_ffn_"
    for nm in ["Wr", "Wg", "Wu", "Wd", "Sg", "Su", "Sd"]:
        common["f_" + nm] = _f32(ins[Fp + nm])
    x = np.asarray(ins[A + "x"], np.float32)
    return [dict(common, x=_f32(x[b].T)) for b in range(8)]


class Sem:
    def __init__(self, handle):
        self.h = handle
        self.n = 0

    def inc(self, instr, k=1):
        instr.then_inc(self.h, k)
        self.n += k
        return instr


class Builder:

    def __init__(self, nc, stage, dumps):
        self.nc = nc
        self.stage = stage
        self.dumps = dumps
        self.marks = {}

    def mk(self, name, sem):
        self.marks[name] = (sem, sem.n)

    def w(self, eng, name):
        if name not in self.marks:
            if ".-" in name or name.endswith(".-1") or ".-1." in name:
                return
            raise KeyError(f"wait on missing mark: {name}")
        sem, n = self.marks[name]
        if sem in getattr(self, "dma_sems", ()):
            n = sem.n
        if n > 0:
            eng.wait_ge(sem.h, n)

    def bar(self, eng):
        for s in self.sems:
            if s.n > 0:
                eng.wait_ge(s.h, s.n)

    def fence(self, eng, sem):
        if sem.n > 0:
            eng.wait_ge(sem.h, sem.n)


def build(stage=99, dumps=()):
    nc = bass.Bass("TRN2", target_bir_lowering=False)
    bld = Builder(nc, stage, dumps)
    dt = nc.dram_tensor

    X = dt("x", [D, T], F32, kind="ExternalInput").ap()
    W = {}
    for nm, shp in [("a_q_w", [D, D]), ("a_k_w", [D, D]), ("a_v_w", [D, D]),
                    ("a_alpha_w", [D, D]), ("a_out_w", [D, D]),
                    ("a_beta_w", [D, 16]), ("a_u1_w", [D, 64]), ("a_u2_w", [64, D]),
                    ("a_ff_w1", [D, FF]), ("a_ff_w3", [D, FF]), ("a_ff_w2", [FF, D]),
                    ("b_Wq", [D, D]), ("b_Wk", [D, D]), ("b_Wv", [D, D]), ("b_Wo", [D, D]),
                    ("f_Wr", [D, NE]), ("f_Wg", [NE, D, DEXP]), ("f_Wu", [NE, D, DEXP]),
                    ("f_Wd", [NE, DEXP, D]), ("f_Sg", [D, DEXP]), ("f_Su", [D, DEXP]),
                    ("f_Sd", [DEXP, D])]:
        W[nm] = dt(nm, shp, F32, kind="ExternalInput").ap()
    C = {}
    for nm, shp, d_ in [("id128", [128, 128], BF16), ("ones_col", [128, 1], BF16),
                        ("b2", [128, 2], BF16), ("maskA", [128, 128], BF16),
                        ("maskB", [CB, CB], BF16), ("rbT", [CB, CB], BF16),
                        ("mcol_hi", [128, 1], BF16), ("mcol_lo", [128, 1], BF16),
                        ("nw_col", [128, 8], F32), ("cb_col", [128, 8], F32),
                        ("cw_col", [128, 32], F32), ("ab_col", [128, 8], F32),
                        ("hnw_col", [128, 8], F32), ("fnw_col", [128, 8], F32),
                        ("u2b_col", [128, 8], F32), ("u1b_col", [64, 1], F32),
                        ("bb_col", [16, 1], F32), ("wg_col", [128, 8], F32),
                        ("bg_col", [128, 8], F32), ("wl_col", [128, 8], F32),
                        ("bl_col", [128, 8], F32), ("wg_bc", [128, 1024], F32),
                        ("bg_bc", [128, 1024], F32), ("wl_bc", [128, 1024], F32),
                        ("bl_bc", [128, 1024], F32)]:
        C[nm] = dt(nm, shp, d_, kind="ExternalInput").ap()
    OUT = dt("out", [D, T], F32, kind="ExternalOutput").ap()

    def scr(nm, shp, d_):
        kind = "ExternalOutput" if nm in dumps else "Internal"
        return dt(nm, shp, d_, kind=kind).ap()

    qn_d = scr("qn_d", [D, T], BF16)
    kn_d = scr("kn_d", [D, T], BF16)
    vtm_d = scr("vtm_d", [T, D], BF16)
    L_d = scr("L_d", [D, T + 64], F32)
    btm_d = scr("btm_d", [T, 16], F32)
    gate_d = scr("gate_d", [D, T], BF16)
    row_d = scr("row_d", [64, T], F32)
    y16_d = scr("y16_d", [D, T], BF16)
    wbT_d = scr("wbT_d", [4, D, D], BF16)
    gates_d = scr("gates_d", [NE, T], F32)
    wbar_d = scr("wbar_d", [2, T, 512], BF16)
    vtmB_d = scr("vtmB_d", [2, T, 512], BF16)
    grow_d = scr("grow_d", [64, 1024], F32)
    ep_d = scr("ep_d", [2, T, 512], BF16)
    ytm_d = scr("ytm_d", [2, T, 512], BF16)
    vfm_d = scr("vfm_d", [D, T], BF16)
    dbg_d = scr("dbg_d", [128, 8192], F32)

    Xv = X.rearrange("(c p) t -> p c t", p=128)
    OUTv = OUT.rearrange("(c p) t -> p c t", p=128)
    qn_v = qn_d.rearrange("(c p) t -> p c t", p=128)
    kn_v = kn_d.rearrange("(c p) t -> p c t", p=128)
    L_v = L_d.rearrange("(c p) t -> p c t", p=128)
    gate_v = gate_d.rearrange("(c p) t -> p c t", p=128)
    y16_v = y16_d.rearrange("(c p) t -> p c t", p=128)

    es = contextlib.ExitStack()
    with es:
        sbuf = lambda nm, shp, d_: es.enter_context(nc.sbuf_tensor(nm, shp, d_))
        XR = sbuf("XR", [128, DCH * T], F32)
        A16 = sbuf("A16", [128, DCH * T], BF16)
        B16 = sbuf("B16", [128, DCH * T], BF16)
        C16 = sbuf("C16", [128, 8192], BF16)
        WBs = sbuf("WBs", [128, 12288], BF16)
        FS = sbuf("FS", [128, 4096], F32)
        WBC = sbuf("WBC", [128, 1536], F32)
        SS = sbuf("SS", [128, 4096], F32)
        KON = sbuf("KON", [128, 136], BF16)
        MKA = sbuf("MKA", [128, 128], BF16)
        MKB = sbuf("MKB", [128, 128], BF16)
        RT16 = sbuf("RT16", [128, 128], BF16)
        CCc = sbuf("CCc", [128, 128], F32)
        psum = lambda nm, shp, d_: es.enter_context(nc.psum_tensor(nm, shp, d_))
        PB = [psum(f"pb{i}", [128, 512], F32) for i in range(7)]
        PT = psum("pt16", [128, 1024], BF16)

        SP = Sem(es.enter_context(nc.semaphore("sp")))
        GP = Sem(es.enter_context(nc.semaphore("gp")))
        AC = Sem(es.enter_context(nc.semaphore("ac")))
        DV = Sem(es.enter_context(nc.semaphore("dv")))
        PE = Sem(es.enter_context(nc.semaphore("pe")))
        SPL = Sem(es.enter_context(nc.semaphore("spl")))
        BCS = Sem(es.enter_context(nc.semaphore("bcs")))
        GL = Sem(es.enter_context(nc.semaphore("gl")))
        WS = [Sem(es.enter_context(nc.semaphore("ws0"))),
              Sem(es.enter_context(nc.semaphore("ws1"))),
              Sem(es.enter_context(nc.semaphore("ws2")))]
        WT = [Sem(es.enter_context(nc.semaphore("wt0"))),
              Sem(es.enter_context(nc.semaphore("wt1"))),
              Sem(es.enter_context(nc.semaphore("wt2")))]
        bld.sems = [SP, GP, AC, DV, PE, SPL, BCS, GL] + WS + WT
        bld.dma_sems = (SP, GP, SPL, BCS, GL, WS[0], WS[1], WS[2], WT[0], WT[1], WT[2])
        block = es.enter_context(nc.Block())
        mk, w_, bar = bld.mk, bld.w, bld.bar

        def fm(pool, c, t0=0, t1=None):
            t1 = T if t1 is None else t1
            return pool[:, c * T + t0: c * T + t1]

        id16 = KON[:, 0:128]
        ones16 = KON[:, 128:129]
        b2c = KON[:, 129:131]
        mhi = KON[:, 131:132]
        mlo = KON[:, 132:133]
        ccbase = {"nw": 0, "cb": 8, "ab": 16, "hnw": 24, "fnw": 32, "u2b": 40,
                  "wg": 48, "bg": 56, "wl": 64, "bl": 72, "cw": 80, "u1b": 112, "bb": 113}

        def ccol(name, i=0):
            b = ccbase[name] + i
            return CCc[:, b:b + 1]

        @block.sync
        def _(q):
            SP.inc(q.dma_start(out=XR[:].rearrange("p (c t) -> p c t", c=DCH), in_=Xv), 16)
            SP.inc(q.dma_start(out=KON[:, 0:128], in_=C["id128"]), 16)
            SP.inc(q.dma_start(out=KON[:, 128:129], in_=C["ones_col"]), 16)
            SP.inc(q.dma_start(out=KON[:, 129:131], in_=C["b2"]), 16)
            SP.inc(q.dma_start(out=KON[:, 131:132], in_=C["mcol_hi"]), 16)
            SP.inc(q.dma_start(out=KON[:, 132:133], in_=C["mcol_lo"]), 16)
            SP.inc(q.dma_start(out=MKA[:], in_=C["maskA"]), 16)
            SP.inc(q.dma_start(out=MKB[:], in_=C["maskB"]), 16)
            SP.inc(q.dma_start(out=RT16[:], in_=C["rbT"]), 16)
            for i, nm in enumerate(["nw_col", "cb_col", "ab_col", "hnw_col",
                                    "fnw_col", "u2b_col", "wg_col", "bg_col",
                                    "wl_col", "bl_col"]):
                SP.inc(q.dma_start(out=CCc[:, 8 * i:8 * i + 8], in_=C[nm]), 16)
            SP.inc(q.dma_start(out=CCc[:, 80:112], in_=C["cw_col"]), 16)
            SP.inc(q.dma_start(out=CCc[0:64, 112:113], in_=C["u1b_col"]), 16)
            SP.inc(q.dma_start(out=CCc[0:16, 113:114], in_=C["bb_col"]), 16)

        @block.vector
        def _(v):
            v.memset(FS[:, 0:64], 0.0)
            v.memset(CCc[:, 114:115], 1e-5)
            v.memset(CCc[:, 115:116], 1e-6)
            DV.inc(v.memset(CCc[:, 116:117], 1e-5 * 1e-6))

        @block.gpsimd
        def _(g):
            bar(g)
            for c in range(DCH):
                GP.inc(g.dma_start(out=L_v[:, c, 0:64], in_=FS[:, 0:64]), 16)
            for i, nm in enumerate(["b_Wq", "b_Wk", "b_Wv", "b_Wo"]):
                GP.inc(g.dma_start(out=wbT_d[i], in_=W[nm]), 16)
            GP.inc(g.dma_start(
                out=WBs[:, 11264:11776].rearrange("p (k n) -> p k n", k=8),
                in_=W["a_u1_w"].rearrange("(k p) n -> p k n", p=128)), 16)
            GP.inc(g.dma_start(out=WBs[0:64, 10240:11264], in_=W["a_u2_w"]), 16)
            GP.inc(g.dma_start(
                out=WBs[:, 11776:11904].rearrange("p (k n) -> p k n", k=8),
                in_=W["a_beta_w"].rearrange("(k p) n -> p k n", p=128)), 16)
        u1w16 = WBs[:, 11264:11776]
        u2w16 = WBs[0:64, 10240:11264]
        bw16 = WBs[:, 11776:11904]
        ws = [WBs[:, 0:4096], WBs[:, 4096:8192], WBs[:, 8192:12288]]

        def load_whalf(g, wap, half, slot):
            WS[slot].inc(g.dma_start(
                out=ws[slot].rearrange("p (k n) -> p k n", k=8),
                in_=wap.rearrange("(k p) n -> p k n", p=128)[:, :, half * 512:(half + 1) * 512]), 16)

        def wblk(slot, k, mo4):
            return ws[slot][:, k * 512 + mo4 * 128: k * 512 + mo4 * 128 + 128]

        def stats_rows(tag, src_pool, row_x, row_s, want_mean=True):
            def sqs(c):
                return C16[:, c * 512:(c + 1) * 512]

            for tt in range(4):
                @block.scalar
                def _(s, tt=tt):
                    if tt == 0:
                        bar(s)
                    for c in range(DCH):
                        w_(s, f"{tag}.mm.{tt-1}.{c}")
                        AC.inc(s.activation(sqs(c), fm(src_pool, c, tt * TT, (tt + 1) * TT), AF.Square))
                        mk(f"{tag}.sq.{tt}.{c}", AC)

                @block.tensor
                def _(t, tt=tt):
                    w_(t, f"{tag}.rows.{tt-1}")
                    for c in range(DCH):
                        w_(t, f"{tag}.sq.{tt}.{c}")
                        if want_mean:
                            t.matmul(PB[4][0:1, :], ones16, fm(src_pool, c, tt * TT, (tt + 1) * TT),
                                     start=(c == 0), stop=(c == DCH - 1))
                        PE.inc(t.matmul(PB[5][0:1, :], ones16, sqs(c),
                                        start=(c == 0), stop=(c == DCH - 1)))
                        mk(f"{tag}.mm.{tt}.{c}", PE)
                    mk(f"{tag}.mmtt.{tt}", PE)

                @block.scalar
                def _(s, tt=tt):
                    w_(s, f"{tag}.mmtt.{tt}")
                    if want_mean:
                        s.activation(FS[0:1, tt * 512:(tt + 1) * 512], PB[4][0:1, :], AF.Copy)
                    AC.inc(s.activation(FS[0:1, 2048 + tt * 512: 2048 + (tt + 1) * 512], PB[5][0:1, :], AF.Copy))
                    mk(f"{tag}.rows.{tt}", AC)

            @block.sync
            def _(q):
                w_(q, f"{tag}.rows.3")
                if want_mean:
                    SP.inc(q.dma_start(out=row_d[row_x:row_x + 1, :], in_=FS[0:1, 0:2048]), 16)
                SP.inc(q.dma_start(out=row_d[row_s:row_s + 1, :], in_=FS[0:1, 2048:4096]), 16)
                mk(f"{tag}.rsp", SP)

        def row_math(tag, emit_ops, src_rows, n_src, dst_rows, n_dst):
            @block.sync
            def _(q):
                bar(q)
                for i, r in enumerate(src_rows):
                    SP.inc(q.dma_start(out=FS[:, 2048 + 16 * i: 2048 + 16 * (i + 1)],
                                       in_=bass.AP(row_d.tensor, r * 2048, [[16, 128], [1, 16]])), 16)
                mk(f"{tag}.rl", SP)

            emit_ops()

            @block.sync
            def _(q):
                bar(q)
                for i, r in enumerate(dst_rows):
                    SP.inc(q.dma_start(out=bass.AP(row_d.tensor, r * 2048, [[16, 128], [1, 16]]),
                                       in_=FS[:, 2560 + 16 * i: 2560 + 16 * (i + 1)]), 16)
                mk(f"{tag}.msp", SP)

        def rep512(q, row, tt, dst):
            BCS.inc(q.dma_start(out=dst, in_=bass.AP(
                row_d.tensor, row * 2048 + tt * 512, [[0, 128], [1, 512]])), 16)

        def rep_head64(q, rowbase, c, tt, dst):
            BCS.inc(q.dma_start(out=dst, in_=bass.AP(
                row_d.tensor, (rowbase + 2 * c) * 2048 + tt * 512,
                [[2048, 2], [0, 64], [1, 512]])), 16)

        if stage >= 1:
            @block.vector
            def _(v):
                bar(v)
                for c in range(DCH):
                    DV.inc(v.tensor_copy(fm(B16, c), fm(XR, c)))

            stats_rows("ln1", B16, 0, 1, want_mean=True)

            def ln1_math():
                fn = bld.fence

                @block.vector
                def _(v):
                    bar(v)
                    DV.inc(v.tensor_scalar(FS[:, 2560:2576], FS[:, 2048:2064], 1.0 / D, None, OP.mult))
                    DV.inc(v.tensor_scalar(FS[:, 2080:2096], FS[:, 2064:2080], 1.0 / D, None, OP.mult))
                    fn(v, DV)
                    DV.inc(v.tensor_tensor(FS[:, 2096:2112], FS[:, 2560:2576], FS[:, 2560:2576], OP.mult))
                    fn(v, DV)
                    DV.inc(v.tensor_tensor(FS[:, 2112:2128], FS[:, 2080:2096], FS[:, 2096:2112], OP.subtract))
                    mk("ln1.var", DV)

                @block.scalar
                def _(s):
                    w_(s, "ln1.var")
                    AC.inc(s.activation(FS[:, 2128:2144], FS[:, 2112:2128], AF.Ln, bias=CCc[:, 114:115]))
                    fn(s, AC)
                    AC.inc(s.activation(FS[:, 2576:2592], FS[:, 2128:2144], AF.Exp, scale=-0.5))
                    AC.inc(s.activation(FS[:, 2144:2160], FS[:, 2112:2128], AF.Ln, scale=1.0 + 1e-6, bias=CCc[:, 116:117]))
                    fn(s, AC)
                    AC.inc(s.activation(FS[:, 2592:2608], FS[:, 2144:2160], AF.Exp, scale=-0.5))
                    mk("ln1.c", AC)
            row_math("ln1m", ln1_math, [0, 1], 2, [0, 2, 3], 3)

            mb, cpb, cfb = WBC[:, 0:512], WBC[:, 512:1024], WBC[:, 1024:1536]
            xst = C16[:, 1024:1536]
            s1t = C16[0:64, 1536:2048]
            gsl = [C16[:, 2048:2560], C16[:, 2560:3072]]
            t1s = FS[:, 2048:2560]

            for tt in range(4):
                @block.sync
                def _(q, tt=tt):
                    bar(q)
                    rep512(q, 0, tt, mb)
                    rep512(q, 2, tt, cpb)
                    rep512(q, 3, tt, cfb)
                    mk("p1.bc", BCS)

                for c in range(DCH):
                    @block.vector
                    def _(v, tt=tt, c=c):
                        if c == 0:
                            w_(v, "p1.bc")
                        w_(v, f"p1.u1.{tt}.{c-1}" if c > 0 else f"p1.u1.{tt-1}.{DCH-1}")
                        fn(v, DV) if False else bld.fence(v, DV)
                        DV.inc(v.tensor_tensor(t1s, XR[:, c * T + tt * TT: c * T + (tt + 1) * TT], mb, OP.subtract))
                        bld.fence(v, DV)
                        v.scalar_tensor_tensor(fm(A16, c, tt * TT, (tt + 1) * TT),
                                               t1s, ccol("nw", c), cfb, OP.mult, OP.mult)
                        DV.inc(v.tensor_tensor(xst, t1s, cpb, OP.mult))
                        mk(f"p1.xs.{tt}.{c}", DV)

                    @block.tensor
                    def _(t, tt=tt, c=c):
                        w_(t, f"p1.xs.{tt}.{c}")
                        if c == 0:
                            w_(t, f"p1.s1.{tt-1}")
                        PE.inc(t.matmul(PB[6][0:64, :], u1w16[:, c * 64:(c + 1) * 64], xst,
                                        start=(c == 0), stop=(c == DCH - 1)))
                        mk(f"p1.u1.{tt}.{c}", PE)

                @block.scalar
                def _(s, tt=tt):
                    w_(s, f"p1.u1.{tt}.{DCH-1}")
                    w_(s, f"p1.gmm.{tt-1}.{DCH-1}")
                    AC.inc(s.activation(s1t, PB[6][0:64, :], AF.Silu, bias=CCc[0:64, 112:113]))
                    mk(f"p1.s1.{tt}", AC)

                for mo in range(DCH):
                    @block.tensor
                    def _(t, tt=tt, mo=mo):
                        if mo == 0:
                            w_(t, f"p1.s1.{tt}")
                        w_(t, f"p1.gact.{tt}.{mo-2}" if mo >= 2 else f"p1.gact.{tt-1}.{6+mo}")
                        PE.inc(t.matmul(PB[4 + (mo % 2)][:, :], u2w16[:, mo * 128:(mo + 1) * 128],
                                        s1t, start=True, stop=True))
                        mk(f"p1.gmm.{tt}.{mo}", PE)

                    @block.scalar
                    def _(s, tt=tt, mo=mo):
                        w_(s, f"p1.gmm.{tt}.{mo}")
                        w_(s, f"p1.gsp.{tt}.{mo-2}" if mo >= 2 else f"p1.gsp.{tt-1}.{6+mo}")
                        AC.inc(s.activation(gsl[mo % 2], PB[4 + (mo % 2)][:, :], AF.Sigmoid, bias=ccol("u2b", mo)))
                        mk(f"p1.gact.{tt}.{mo}", AC)

                    @block.sync
                    def _(q, tt=tt, mo=mo):
                        w_(q, f"p1.gact.{tt}.{mo}")
                        SPL.inc(q.dma_start(out=gate_v[:, mo, tt * TT:(tt + 1) * TT], in_=gsl[mo % 2]), 16)
                        mk(f"p1.gsp.{tt}.{mo}", SPL)

            hcv = FS[:, 0:2048]
            for c in range(DCH):
                @block.vector
                def _(v, c=c):
                    if c == 0:
                        bar(v)
                    w_(v, f"p1.hs.{c-1}")
                    DV.inc(v.tensor_scalar(hcv, fm(A16, c), ccol("cw", 4 * c + 3), None, OP.mult))
                    bld.fence(v, DV)
                    DV.inc(v.scalar_tensor_tensor(hcv[:, 1:2048], fm(A16, c, 0, 2047), ccol("cw", 4 * c + 2), hcv[:, 1:2048], OP.mult, OP.add))
                    bld.fence(v, DV)
                    DV.inc(v.scalar_tensor_tensor(hcv[:, 2:2048], fm(A16, c, 0, 2046), ccol("cw", 4 * c + 1), hcv[:, 2:2048], OP.mult, OP.add))
                    bld.fence(v, DV)
                    DV.inc(v.scalar_tensor_tensor(hcv[:, 3:2048], fm(A16, c, 0, 2045), ccol("cw", 4 * c + 0), hcv[:, 3:2048], OP.mult, OP.add))
                    mk(f"p1.hc.{c}", DV)

                @block.scalar
                def _(s, c=c):
                    w_(s, f"p1.hc.{c}")
                    AC.inc(s.activation(fm(B16, c), hcv, AF.Silu, bias=ccol("cb", c)))
                    mk(f"p1.hs.{c}", AC)

        def proj_fm(tag, wap, tile_cb, n_mo=8):
            @block.gpsimd
            def _(g):
                bar(g)
                load_whalf(g, wap, 0, 0)
                mk(f"{tag}.w0", GP)
                load_whalf(g, wap, 1, 1)
                mk(f"{tag}.w1", GP)

            for mo in range(n_mo):
                half = mo // 4
                for tt in range(4):
                    @block.tensor
                    def _(t, mo=mo, tt=tt, half=half):
                        if mo == 0 and tt == 0:
                            bar(t)
                        if tt == 0:
                            bld.fence(t, WS[half])
                        w_(t, f"{tag}.ev.{mo}.{tt-1}" if tt > 0 else f"{tag}.ev.{mo-2}.3")
                        for k in range(DCH):
                            mmi = t.matmul(PB[mo % 2][:], wblk(half, k, mo % 4),
                                           fm(B16, k, tt * TT, (tt + 1) * TT),
                                           start=(k == 0), stop=(k == DCH - 1))
                        PE.inc(mmi)
                        mk(f"{tag}.mm.{mo}.{tt}", PE)
                    tile_cb(mo, tt)

        def qk_sink(tag):
            def cb(mo, tt):
                sqs = C16[:, tt * 512:(tt + 1) * 512]

                @block.vector
                def _(v, mo=mo, tt=tt, sqs=sqs):
                    w_(v, f"{tag}.mm.{mo}.{tt}")
                    DV.inc(v.tensor_copy(fm(A16, mo, tt * TT, (tt + 1) * TT), PB[mo % 2][:]))
                    mk(f"{tag}.ev.{mo}.{tt}", DV)

                @block.scalar
                def _(s, mo=mo, tt=tt, sqs=sqs):
                    w_(s, f"{tag}.ev.{mo}.{tt}")
                    w_(s, f"{tag}.nmm.{mo-1}.{tt}")
                    AC.inc(s.activation(sqs, fm(A16, mo, tt * TT, (tt + 1) * TT), AF.Square))
                    mk(f"{tag}.sq.{mo}.{tt}", AC)

                @block.tensor
                def _(t, mo=mo, tt=tt, sqs=sqs):
                    w_(t, f"{tag}.sq.{mo}.{tt}")
                    w_(t, f"{tag}.rev.{mo}.{tt-1}" if tt > 0 else f"{tag}.rev.{mo-1}.3")
                    PE.inc(t.matmul(PB[6][0:2, :], b2c, sqs, start=True, stop=True))
                    mk(f"{tag}.nmm.{mo}.{tt}", PE)

                @block.scalar
                def _(s, mo=mo, tt=tt):
                    w_(s, f"{tag}.nmm.{mo}.{tt}")
                    if tt == 0:
                        w_(s, f"{tag}.rsp.{mo-1}")
                    AC.inc(s.activation(FS[0:2, 2048 + tt * 512: 2048 + (tt + 1) * 512],
                                        PB[6][0:2, :], AF.Copy))
                    mk(f"{tag}.rev.{mo}.{tt}", AC)

                if tt == 3:
                    @block.sync
                    def _(q, mo=mo):
                        w_(q, f"{tag}.rev.{mo}.3")
                        SPL.inc(q.dma_start(out=row_d[4 + 2 * mo: 6 + 2 * mo, :], in_=FS[0:2, 2048:4096]), 16)
                        mk(f"{tag}.rsp.{mo}", SPL)
            return cb

        def qk_normmath(tag):
            def ops():
                @block.scalar
                def _(s):
                    bar(s)
                    AC.inc(s.activation(FS[:, 2816:3072], FS[:, 2048:2304], AF.Sqrt))
                    fn0 = bld.fence
                    fn0(s, AC)
                    AC.inc(s.activation(FS[:, 3072:3328], FS[:, 2816:3072], AF.Ln, bias=CCc[:, 115:116]))
                    fn0(s, AC)
                    AC.inc(s.activation(FS[:, 2560:2816], FS[:, 3072:3328], AF.Exp, scale=-1.0))
                    mk(f"{tag}.rmath", AC)

            @block.sync
            def _(q):
                bar(q)
                SP.inc(q.dma_start(out=FS[:, 2048:2304],
                                   in_=bass.AP(row_d.tensor, 4 * 2048, [[256, 128], [1, 256]])), 16)
            ops()

            @block.sync
            def _(q):
                bar(q)
                SP.inc(q.dma_start(out=bass.AP(row_d.tensor, 20 * 2048, [[256, 128], [1, 256]]),
                                   in_=FS[:, 2560:2816]), 16)
                mk(f"{tag}.rqsp", SP)

        def qk_apply(tag, dest_v):
            rqb = WBC[:, 0:512]
            qnt = [C16[:, 1024:1536], C16[:, 1536:2048]]

            for c in range(DCH):
                for tt in range(4):
                    i = c * 4 + tt

                    @block.sync
                    def _(q, c=c, tt=tt):
                        if c == 0 and tt == 0:
                            bar(q)
                        w_(q, f"{tag}.ap.{c}.{tt-1}" if tt > 0 else f"{tag}.ap.{c-1}.3")
                        rep_head64(q, 20, c, tt, rqb)
                        mk(f"{tag}.bc.{c}.{tt}", BCS)

                    @block.vector
                    def _(v, c=c, tt=tt, i=i):
                        w_(v, f"{tag}.bc.{c}.{tt}")
                        w_(v, f"{tag}.sp.{c}.{tt-2}" if tt >= 2 else f"{tag}.sp.{c-1}.{2+tt}")
                        DV.inc(v.tensor_tensor(qnt[i % 2], fm(A16, c, tt * TT, (tt + 1) * TT), rqb, OP.mult))
                        mk(f"{tag}.ap.{c}.{tt}", DV)

                    @block.sync
                    def _(q, c=c, tt=tt, i=i):
                        w_(q, f"{tag}.ap.{c}.{tt}")
                        SPL.inc(q.dma_start(out=dest_v[:, c, tt * TT:(tt + 1) * TT], in_=qnt[i % 2]), 16)
                        mk(f"{tag}.sp.{c}.{tt}", SPL)

        if stage >= 2:
            proj_fm("q", W["a_q_w"], qk_sink("q"))
            qk_normmath("q")
            qk_apply("q", qn_v)
            proj_fm("k", W["a_k_w"], qk_sink("k"))
            qk_normmath("k")
            qk_apply("k", kn_v)

            lsl_a = [FS[:, 2560:3072], FS[:, 3072:3584]]
            onct = WBC[:, 0:512]

            @block.vector
            def _(v):
                bar(v)
                DV.inc(v.memset(onct, 1.0))
                mk("al.ones", DV)

            def a_cb(mo, tt):
                asl = FS[:, 2048:2560]

                @block.scalar
                def _(s, mo=mo, tt=tt, asl=asl):
                    if mo == 0 and tt == 0:
                        w_(s, "al.ones")
                    w_(s, f"al.mm.{mo}.{tt}")
                    w_(s, f"al.sc.{mo}.{tt-1}" if tt > 0 else f"al.sc.{mo-1}.3")
                    AC.inc(s.activation(asl, PB[mo % 2][:], AF.Sigmoid, bias=ccol("ab", mo)))
                    mk(f"al.sg.{mo}.{tt}", AC)

                @block.vector
                def _(v, mo=mo, tt=tt, asl=asl):
                    w_(v, f"al.sg.{mo}.{tt}")
                    DV.inc(v.tensor_scalar(asl, asl, AFLOOR, None, OP.max))
                    mk(f"al.cl.{mo}.{tt}", DV)
                    mk(f"al.ev.{mo}.{tt}", DV)

                @block.scalar
                def _(s, mo=mo, tt=tt, asl=asl):
                    w_(s, f"al.cl.{mo}.{tt}")
                    AC.inc(s.activation(asl, asl, AF.Ln))
                    mk(f"al.ln.{mo}.{tt}", AC)

                @block.vector
                def _(v, mo=mo, tt=tt, asl=asl):
                    w_(v, f"al.ln.{mo}.{tt}")
                    w_(v, f"al.lsp.{mo}.{tt-2}" if tt >= 2 else f"al.lsp.{mo-1}.{2+tt}")
                    init = 0.0 if tt == 0 else lsl_a[(tt - 1) % 2][:, 511:512]
                    DV.inc(v.tensor_tensor_scan(lsl_a[tt % 2], onct, asl, init, OP.mult, OP.add))
                    mk(f"al.sc.{mo}.{tt}", DV)

                @block.sync
                def _(q, mo=mo, tt=tt):
                    w_(q, f"al.sc.{mo}.{tt}")
                    SPL.inc(q.dma_start(out=L_v[:, mo, 64 + tt * TT: 64 + (tt + 1) * TT], in_=lsl_a[tt % 2]), 16)
                    mk(f"al.lsp.{mo}.{tt}", SPL)
            proj_fm("al", W["a_alpha_w"], a_cb)

            @block.gpsimd
            def _(g):
                bar(g)
                load_whalf(g, W["a_v_w"], 0, 0)
                mk("v.w0", GP)
                load_whalf(g, W["a_v_w"], 1, 1)
                mk("v.w1", GP)

            vsl = [C16[:, 0:512], C16[:, 512:1024]]
            for i_ in range(32):
                nh, ww = i_ // 16, i_ % 16

                @block.tensor
                def _(t, i_=i_, nh=nh, ww=ww):
                    if i_ == 0:
                        bar(t)
                    if ww == 0:
                        bld.fence(t, WS[nh])
                    w_(t, f"v.ev.{i_-2}")
                    for k in range(DCH):
                        mmi = t.matmul(PB[i_ % 2][:], fm(B16, k, ww * 128, (ww + 1) * 128),
                                       ws[nh][:, k * 512:(k + 1) * 512],
                                       start=(k == 0), stop=(k == DCH - 1))
                    PE.inc(mmi)
                    mk(f"v.mm.{i_}", PE)

                @block.vector
                def _(v, i_=i_):
                    w_(v, f"v.mm.{i_}")
                    w_(v, f"v.sp.{i_-2}")
                    DV.inc(v.tensor_copy(vsl[i_ % 2], PB[i_ % 2][:]))
                    mk(f"v.ev.{i_}", DV)

                @block.sync
                def _(q, i_=i_, nh=nh, ww=ww):
                    w_(q, f"v.ev.{i_}")
                    SPL.inc(q.dma_start(out=vtm_d[ww * 128:(ww + 1) * 128, nh * 512:(nh + 1) * 512],
                                       in_=vsl[i_ % 2]), 16)
                    mk(f"v.sp.{i_}", SPL)

            bsl = C16[0:16, 3584:4096]
            bts = [FS[:, 3584:3600], FS[:, 3600:3616]]
            for tt in range(4):
                @block.tensor
                def _(t, tt=tt):
                    if tt == 0:
                        bar(t)
                    w_(t, f"b.sg.{tt-1}")
                    for k in range(DCH):
                        mmi = t.matmul(PB[6][0:16, :], bw16[:, k * 16:(k + 1) * 16],
                                       fm(B16, k, tt * TT, (tt + 1) * TT), start=(k == 0), stop=(k == DCH - 1))
                    PE.inc(mmi)
                    mk(f"b.mm.{tt}", PE)

                @block.scalar
                def _(s, tt=tt):
                    w_(s, f"b.mm.{tt}")
                    w_(s, f"b.tr.{tt-1}.3")
                    AC.inc(s.activation(bsl, PB[6][0:16, :], AF.Sigmoid, bias=CCc[0:16, 113:114]))
                    mk(f"b.sg.{tt}", AC)

                for blk in range(4):
                    i_ = tt * 4 + blk

                    @block.tensor
                    def _(t, tt=tt, blk=blk):
                        if blk == 0:
                            w_(t, f"b.sg.{tt}")
                        w_(t, f"b.ev.{tt}.{blk-1}" if blk > 0 else f"b.ev.{tt-1}.3")
                        PE.inc(t.transpose(PT[0:128, 0:16], bsl[:, blk * 128:(blk + 1) * 128], id16[0:16, 0:16]))
                        mk(f"b.tr.{tt}.{blk}", PE)

                    @block.vector
                    def _(v, tt=tt, blk=blk, i_=i_):
                        w_(v, f"b.tr.{tt}.{blk}")
                        w_(v, f"b.sp.{i_-2}")
                        DV.inc(v.tensor_copy(bts[i_ % 2], PT[:, 0:16]))
                        mk(f"b.ev.{tt}.{blk}", DV)

                    @block.sync
                    def _(q, tt=tt, blk=blk, i_=i_):
                        w_(q, f"b.ev.{tt}.{blk}")
                        SPL.inc(q.dma_start(out=btm_d[tt * 512 + blk * 128: tt * 512 + (blk + 1) * 128, :],
                                           in_=bts[i_ % 2]), 16)
                        mk(f"b.sp.{i_}", SPL)

        if stage >= 3:
            emit_scan_a(bld, block, dict(locals()))

        if stage >= 4:
            emit_tail_a(bld, block, dict(locals()))

        if stage >= 5:
            emit_ffn(bld, block, dict(locals()), round_=0)

        WGB = SS[:, 0:1024]
        BGB = SS[:, 1024:2048]
        WLB = SS[:, 2048:3072]
        BLB = SS[:, 3072:4096]

        if stage >= 6:
            emit_attn_b(bld, block, dict(locals()))

        if stage >= 7:
            emit_ffn(bld, block, dict(locals()), round_=1)

        @block.sync
        def _(q):
            bar(q)
            for c in range(DCH):
                SP.inc(q.dma_start(out=OUTv[:, c, :], in_=XR[:, c * T:(c + 1) * T]), 16)
            q.wait_ge(SP.h, SP.n)
    cluster_blocks(nc)
    return nc


ENG_ORDER = ("EngineType.SP", "EngineType.Pool", "EngineType.PE",
             "EngineType.DVE", "EngineType.Activation")


def _block_engine(bb):
    for ins in bb.instructions:
        if type(ins).__name__ == "InstUnconditionalBranch":
            continue
        return str(getattr(ins, "engine", None))
    return "empty"


def cluster_blocks(nc):
    fn = nc.m.functions[0]
    bbs = list(fn.blocks)
    head, tail = bbs[0], bbs[-1]
    groups = {}
    for bb in bbs[1:-1]:
        groups.setdefault(_block_engine(bb), []).append(bb)
    new_mid = []
    for eng in ENG_ORDER:
        new_mid.extend(groups.pop(eng, []))
    for eng in sorted(groups):
        new_mid.extend(groups[eng])
    chain = [head] + new_mid + [tail]
    for i, bb in enumerate(chain[:-1]):
        ins = list(bb.instructions)
        assert type(ins[-1]).__name__ == "InstUnconditionalBranch", bb.name
        br = mybir.InstUnconditionalBranch(
            name=ins[-1].name, target=chain[i + 1].name, ins=[], outs=[],
            engine=ins[-1].engine)
        bb.instructions = ins[:-1] + [br]
    fn.blocks = chain
    return nc


def emit_scan_a(bld, block, env):
    g = lambda k: env[k]
    mk, w_, bar = bld.mk, bld.w, bld.bar
    SP, GP, AC, DV, PE = g("SP"), g("GP"), g("AC"), g("DV"), g("PE")
    SPL, BCS, GL, WS = g("SPL"), g("BCS"), g("GL"), g("WS")
    A16, B16, C16, WBs, FS, SS = g("A16"), g("B16"), g("C16"), g("WBs"), g("FS"), g("SS")
    PB, PT = g("PB"), g("PT")
    MKA, id16 = g("MKA"), g("id16")
    qn_v, kn_v, L_v = g("qn_v"), g("kn_v"), g("L_v")
    vtm_d, btm_d = g("vtm_d"), g("btm_d")
    fm = g("fm")
    OPx = OP
    import os as _os2

    NG = T // 256
    vtmg = WBs[:, 0:4096]
    nb_t = WBs[:, 4224:4352]
    Zp = [WBs[:, 4352 + i * 64: 4352 + (i + 1) * 64] for i in range(6)]
    ntr = WBs[:, 4736:4800]
    Wt_ = WBs[:, 5248:5760]
    yc = [WBs[:, 5760:6272], WBs[:, 6272:6784]]
    Uc = WBs[:, 6784:7296]
    khf = WBs[:, 9344:9856]
    khT = WBs[:, 9856:10880]
    zps = WBs[:, 10880:11392]
    Lg = lambda ch, a, b: FS[:, ch * 257 + a: ch * 257 + b]
    cmg = FS[:, 2056:2088]
    ceg = FS[:, 2088:2120]
    nmg = FS[:, 2120:2152]
    STT = FS[:, 2184:2696]
    btg = FS[:, 2696:2760]

    def apx(sl, *dims):
        return bass.AP(sl.tensor, sl.offset, [sl.ap[0]] + list(dims))

    @block.vector
    def _(v):
        bar(v)
        DV.inc(v.memset(STT, 0.0))
        mk("sa.st.-1.7", DV)

    for gi in range(NG):
        t0 = gi * 256

        @block.sync
        def _(q, gi=gi, t0=t0):
            if gi == 0:
                bar(q)
            else:
                w_(q, f"sa.grpdone.{gi-1}")
            GL.inc(q.dma_start(out=C16[:, 0:2048].rearrange("p (c t) -> p c t", c=8),
                               in_=qn_v[:, :, t0:t0 + 256]), 16)
            GL.inc(q.dma_start(out=C16[:, 2048:4096].rearrange("p (c t) -> p c t", c=8),
                               in_=kn_v[:, :, t0:t0 + 256]), 16)
            GL.inc(q.dma_start(out=FS[:, 0:2056].rearrange("p (c t) -> p c t", c=8),
                               in_=L_v[:, :, 64 + t0 - 1: 64 + t0 + 256]), 16)
            for half in range(2):
                o = half * 64
                GL.inc(q.dma_start(out=vtmg[o:o + 64, :].rearrange("p (c n) -> p c n", c=4),
                                   in_=bass.AP(vtm_d.tensor, t0 * 1024,
                                               [[1024, 64], [65536, 4], [1, 1024]])), 16)
                GL.inc(q.dma_start(out=btg[o:o + 64, :].rearrange("p (c n) -> p c n", c=4),
                                   in_=bass.AP(btm_d.tensor, t0 * 16,
                                               [[16, 64], [1024, 4], [1, 16]])), 16)
            mk(f"sa.ld.{gi}", GL)

        @block.vector
        def _(v, gi=gi):
            w_(v, f"sa.ld.{gi}")
            Lv_ = FS[:, 0:2056]
            midAP = apx(Lv_[:, 32:33], [257, 8], [64, 4])
            startAP = apx(Lv_[:, 0:1], [257, 8], [64, 4])
            endAP = apx(Lv_[:, 64:65], [257, 8], [64, 4])
            nmv = nmg.rearrange("p (a b) -> p a b", a=8)
            cmv = cmg.rearrange("p (a b) -> p a b", a=8)
            cev = ceg.rearrange("p (a b) -> p a b", a=8)
            v.tensor_scalar(nmv, midAP, -1.0, None, OPx.mult)
            v.tensor_tensor(cmv, midAP, startAP, OPx.subtract)
            DV.inc(v.tensor_tensor(cev, endAP, startAP, OPx.subtract))
            mk(f"sa.nm.{gi}", DV)

        @block.scalar
        def _(s, gi=gi):
            w_(s, f"sa.nm.{gi}")
            s.activation(cmg, cmg, AF.Exp)
            s.activation(ceg, ceg, AF.Exp)
            for ch in range(8):
                for c_ in range(4):
                    src = Lg(ch, 1 + c_ * 64, 1 + c_ * 64 + 64)
                    nmcol = nmg[:, ch * 4 + c_: ch * 4 + c_ + 1]
                    midcol = Lg(ch, 1 + c_ * 64 + 31, 1 + c_ * 64 + 32)
                    s.activation(A16[:, 4096 + ch * 256 + c_ * 64: 4096 + ch * 256 + c_ * 64 + 64],
                                 src, AF.Exp, bias=nmcol)
                    aci = s.activation(A16[:, 6144 + ch * 256 + c_ * 64: 6144 + ch * 256 + c_ * 64 + 64],
                                       src, AF.Exp, bias=midcol, scale=-1.0)
            AC.inc(aci)
            mk(f"sa.exp.{gi}", AC)

        @block.vector
        def _(v, gi=gi):
            w_(v, f"sa.exp.{gi}")
            for ch in range(8):
                v.tensor_tensor(A16[:, ch * 512: ch * 512 + 256],
                                C16[:, ch * 256: ch * 256 + 256],
                                A16[:, 4096 + ch * 256: 4096 + ch * 256 + 256], OPx.mult)
                v.tensor_tensor(A16[:, ch * 512 + 256: ch * 512 + 512],
                                C16[:, 2048 + ch * 256: 2048 + ch * 256 + 256],
                                A16[:, 4096 + ch * 256: 4096 + ch * 256 + 256], OPx.mult)
                wbi = v.tensor_tensor(C16[:, 4096 + ch * 256: 4096 + ch * 256 + 256],
                                      C16[:, 2048 + ch * 256: 2048 + ch * 256 + 256],
                                      A16[:, 6144 + ch * 256: 6144 + ch * 256 + 256], OPx.mult)
            DV.inc(wbi)
            mk(f"sa.wb.{gi}", DV)

        SA_LIM = int(_os2.environ.get("SA_LIM", "9999"))
        for c_ in range(4):
            ci = gi * 4 + c_
            if ci >= SA_LIM:
                mk(f"sa.st.{ci}.7", DV)
                continue

            @block.vector
            def _(v, gi=gi, c_=c_, ci=ci):
                w_(v, f"sa.wb.{gi}")
                w_(v, f"sa.st.{ci-1}.7")
                for ch2 in range(8):
                    cmAP = apx(cmg[:, ch2 * 4 + c_: ch2 * 4 + c_ + 1], [0, 64])
                    DV.inc(v.tensor_tensor(zps[:, ch2 * 64:(ch2 + 1) * 64],
                                           STT[:, ch2 * 64:(ch2 + 1) * 64], cmAP, OPx.mult))
                mk(f"sa.z0p.{ci}", DV)

            @block.tensor
            def _(t, c_=c_, ci=ci):
                w_(t, f"sa.z0p.{ci}")
                w_(t, f"sa.wt.{ci-1}")
                for h in range(16):
                    ch, o = h // 2, (h % 2) * 64
                    mmi = t.matmul(PB[2][o:o + 64, ch * 64: ch * 64 + 64],
                                   A16[o:o + 64, ch * 512 + 256 + c_ * 64: ch * 512 + 256 + c_ * 64 + 64],
                                   zps[o:o + 64, ch * 64:ch * 64 + 64],
                                   start=True, stop=True)
                PE.inc(mmi)
                mk(f"sa.rk.{ci}", PE)

            @block.vector
            def _(v, c_=c_, ci=ci):
                w_(v, f"sa.rk.{ci}")
                ev = bass.AP(vtmg.tensor, vtmg.offset + c_ * 1024,
                             [vtmg[0:64, 0:1].ap[0], [128, 8], [1, 64]])
                od = bass.AP(vtmg.tensor, vtmg[64:128, 0:1].offset + c_ * 1024 + 64,
                             [vtmg[64:128, 0:1].ap[0], [128, 8], [1, 64]])
                v.tensor_tensor(Wt_[0:64, :].rearrange("p (a b) -> p a b", a=8),
                                PB[2][0:64, :].rearrange("p (a b) -> p a b", a=8), ev, OPx.subtract)
                DV.inc(v.tensor_tensor(Wt_[64:128, :].rearrange("p (a b) -> p a b", a=8),
                                       PB[2][64:128, :].rearrange("p (a b) -> p a b", a=8), od, OPx.subtract))
                mk(f"sa.wt.{ci}", DV)

            for ch in range(8):
                @block.tensor
                def _(t, c_=c_, ci=ci, ch=ch):
                    w_(t, f"sa.wt.{ci}")
                    w_(t, f"sa.z1.{ci}.{ch-1}")
                    for half in range(2):
                        o = half * 64
                        rhs = A16[o:o + 64, ch * 512: ch * 512 + 512].rearrange(
                            "p (qk t) -> p qk t", qk=2)[:, :, c_ * 64:(c_ + 1) * 64]
                        mmi = t.matmul(PB[0][o:o + 64, 0:128],
                                       C16[o:o + 64, 4096 + ch * 256 + c_ * 64: 4096 + ch * 256 + c_ * 64 + 64],
                                       rhs, start=True, stop=True)
                    PE.inc(mmi)
                    mk(f"sa.pa.{ci}.{ch}", PE)

                @block.vector
                def _(v, c_=c_, ci=ci, ch=ch):
                    w_(v, f"sa.pa.{ci}.{ch}")
                    DV.inc(v.tensor_tensor(nb_t[:], PB[0][:, 0:128], MKA[:], OPx.mult))
                    bld.fence(v, DV)
                    for half in range(2):
                        o = half * 64
                        bcol = btg[o:o + 64, c_ * 16 + 2 * ch + half: c_ * 16 + 2 * ch + half + 1]
                        zli = v.tensor_scalar(Zp[0][o:o + 64, :], nb_t[o:o + 64, 64:128],
                                              bcol, None, OPx.mult)
                    DV.inc(zli)
                    mk(f"sa.z1.{ci}.{ch}", DV)

                for lev in range(1, 6):
                    @block.tensor
                    def _(t, ci=ci, ch=ch, lev=lev):
                        w_(t, f"sa.z1.{ci}.{ch}" if lev == 1 else f"sa.sqe.{ci}.{ch}.{lev-1}")
                        t.transpose(PT[0:64, 0:64], Zp[lev - 1][0:64, :], id16[0:64, 0:64])
                        PE.inc(t.transpose(PT[64:128, 0:64], Zp[lev - 1][64:128, :], id16[64:128, 64:128]))
                        mk(f"sa.tr.{ci}.{ch}.{lev}", PE)

                    @block.vector
                    def _(v, ci=ci, ch=ch, lev=lev):
                        w_(v, f"sa.tr.{ci}.{ch}.{lev}")
                        DV.inc(v.tensor_copy(ntr[:], PT[:, 0:64]))
                        mk(f"sa.tre.{ci}.{ch}.{lev}", DV)

                    @block.tensor
                    def _(t, ci=ci, ch=ch, lev=lev):
                        w_(t, f"sa.tre.{ci}.{ch}.{lev}")
                        for half in range(2):
                            o = half * 64
                            mmi = t.matmul(PB[1][o:o + 64, 0:64], ntr[o:o + 64, :],
                                           Zp[lev - 1][o:o + 64, :], start=True, stop=True)
                        PE.inc(mmi)
                        mk(f"sa.sq.{ci}.{ch}.{lev}", PE)

                    @block.vector
                    def _(v, ci=ci, ch=ch, lev=lev):
                        w_(v, f"sa.sq.{ci}.{ch}.{lev}")
                        DV.inc(v.tensor_copy(Zp[lev][:], PB[1][:, 0:64]))
                        mk(f"sa.sqe.{ci}.{ch}.{lev}", DV)

                for li, zi in enumerate([5, 4, 3, 2, 1, 0]):
                    @block.tensor
                    def _(t, ci=ci, ch=ch, li=li, zi=zi):
                        w_(t, f"sa.sqe.{ci}.{ch}.5" if li == 0 else f"sa.app.{ci}.{ch}.{li-1}")
                        src = Wt_ if li == 0 else yc[(li - 1) % 2]
                        for half in range(2):
                            o = half * 64
                            mmi = t.matmul(PB[4][o:o + 64, 0:64], Zp[zi][o:o + 64, :],
                                           src[o:o + 64, ch * 64: ch * 64 + 64],
                                           start=True, stop=True)
                        PE.inc(mmi)
                        mk(f"sa.appm.{ci}.{ch}.{li}", PE)

                    @block.vector
                    def _(v, ci=ci, ch=ch, li=li):
                        w_(v, f"sa.appm.{ci}.{ch}.{li}")
                        if li > 0:
                            w_(v, f"sa.app.{ci}.{ch}.{li-1}")
                        src = Wt_ if li == 0 else yc[(li - 1) % 2]
                        DV.inc(v.tensor_tensor(yc[li % 2][:, ch * 64: ch * 64 + 64],
                                               src[:, ch * 64: ch * 64 + 64],
                                               PB[4][:, 0:64], OPx.add))
                        mk(f"sa.app.{ci}.{ch}.{li}", DV)

                @block.vector
                def _(v, ci=ci, ch=ch, c_=c_):
                    w_(v, f"sa.app.{ci}.{ch}.5")
                    for half in range(2):
                        o = half * 64
                        bcol = btg[o:o + 64, c_ * 16 + 2 * ch + half: c_ * 16 + 2 * ch + half + 1]
                        uci = v.tensor_scalar(Uc[o:o + 64, ch * 64: ch * 64 + 64],
                                              yc[1][o:o + 64, ch * 64: ch * 64 + 64],
                                              bcol, None, OPx.mult)
                    DV.inc(uci)
                    mk(f"sa.uc.{ci}.{ch}", DV)

                @block.tensor
                def _(t, ci=ci, ch=ch, c_=c_):
                    w_(t, f"sa.uc.{ci}.{ch}")
                    w_(t, f"sa.ye.{ci}.{ch-1}")
                    for half in range(2):
                        o = half * 64
                        t.matmul(PB[5][o:o + 64, 0:64],
                                 zps[o:o + 64, ch * 64: ch * 64 + 64],
                                 A16[o:o + 64, ch * 512 + c_ * 64: ch * 512 + c_ * 64 + 64],
                                 start=True, stop=False)
                        mmi = t.matmul(PB[5][o:o + 64, 0:64],
                                       Uc[o:o + 64, ch * 64: ch * 64 + 64],
                                       nb_t[o:o + 64, 0:64],
                                       start=False, stop=True)
                    PE.inc(mmi)
                    mk(f"sa.out.{ci}.{ch}", PE)

                @block.vector
                def _(v, ci=ci, ch=ch):
                    w_(v, f"sa.out.{ci}.{ch}")
                    DV.inc(v.tensor_copy(fm(B16, ch, ci * 64, ci * 64 + 64), PB[5][:, 0:64]))
                    mk(f"sa.ye.{ci}.{ch}", DV)

                @block.vector
                def _(v, ci=ci, ch=ch, c_=c_):
                    w_(v, f"sa.ye.{ci}.{ch}")
                    ecol = A16[:, 4096 + ch * 256 + c_ * 64 + 63: 4096 + ch * 256 + c_ * 64 + 64]
                    ecb = bass.AP(ecol.tensor, ecol.offset, [ecol.ap[0], [0, 64]])
                    DV.inc(v.tensor_tensor(khf[:, ch * 64: ch * 64 + 64],
                                           C16[:, 4096 + ch * 256 + c_ * 64: 4096 + ch * 256 + c_ * 64 + 64],
                                           ecb, OPx.mult))
                    mk(f"sa.kh.{ci}.{ch}", DV)

                @block.tensor
                def _(t, ci=ci, ch=ch):
                    w_(t, f"sa.kh.{ci}.{ch}")
                    t.transpose(PT[0:64, 64:128], khf[0:64, ch * 64: ch * 64 + 64], id16[0:64, 0:64])
                    PE.inc(t.transpose(PT[64:128, 64:128], khf[64:128, ch * 64: ch * 64 + 64],
                                       id16[64:128, 64:128]))
                    mk(f"sa.ktr.{ci}.{ch}", PE)

                @block.vector
                def _(v, ci=ci, ch=ch):
                    w_(v, f"sa.ktr.{ci}.{ch}")
                    DV.inc(v.tensor_copy(khT[:, ch * 64: ch * 64 + 64], PT[:, 64:128]))
                    mk(f"sa.ktre.{ci}.{ch}", DV)

                @block.tensor
                def _(t, ci=ci, ch=ch):
                    w_(t, f"sa.ktre.{ci}.{ch}")
                    for half in range(2):
                        o = half * 64
                        mmi = t.matmul(PB[6][o:o + 64, 0:64], khT[o:o + 64, ch * 64: ch * 64 + 64],
                                       Uc[o:o + 64, ch * 64: ch * 64 + 64], start=True, stop=True)
                    PE.inc(mmi)
                    mk(f"sa.stm.{ci}.{ch}", PE)

                @block.vector
                def _(v, ci=ci, ch=ch, c_=c_):
                    w_(v, f"sa.stm.{ci}.{ch}")
                    cecol = ceg[:, ch * 4 + c_: ch * 4 + c_ + 1]
                    DV.inc(v.scalar_tensor_tensor(STT[:, ch * 64: ch * 64 + 64],
                                                  STT[:, ch * 64: ch * 64 + 64],
                                                  cecol, PB[6][:, 0:64], OPx.mult, OPx.subtract))
                    mk(f"sa.st.{ci}.{ch}", DV)
            mk(f"sa.st.{ci}.7", DV)
        mk(f"sa.grpdone.{gi}", DV)

    y16_v = g("y16_v")

    @block.sync
    def _(q):
        bar(q)
        for c in range(DCH):
            SP.inc(q.dma_start(out=y16_v[:, c, :], in_=fm(B16, c)), 16)
        mk("sa.ysp", SP)


def emit_tail_a(bld, block, env):
    g = lambda k: env[k]
    mk, w_, bar, fn = bld.mk, bld.w, bld.bar, bld.fence
    SP, GP, AC, DV, PE = g("SP"), g("GP"), g("AC"), g("DV"), g("PE")
    SPL, BCS, GL, WS = g("SPL"), g("BCS"), g("GL"), g("WS")
    XR, A16, B16, C16, WBs, FS = g("XR"), g("A16"), g("B16"), g("C16"), g("WBs"), g("FS")
    PB, PT = g("PB"), g("PT")
    W, CCc, row_d, gate_v = g("W"), g("CCc"), g("row_d"), g("gate_v")
    fm, ccol, b2c, ones16 = g("fm"), g("ccol"), g("b2c"), g("ones16")
    stats_rows, row_math = g("stats_rows"), g("row_math")
    rep512, rep_head64 = g("rep512"), g("rep_head64")
    load_whalf, ws, wblk = g("load_whalf"), g("ws"), g("wblk")

    for c in range(DCH):
        for tt in range(4):
            sqs = C16[:, tt * 512:(tt + 1) * 512]

            @block.scalar
            def _(s, c=c, tt=tt, sqs=sqs):
                if c == 0 and tt == 0:
                    bar(s)
                w_(s, f"hn.nmm.{c-1}.{tt}")
                AC.inc(s.activation(sqs, fm(B16, c, tt * TT, (tt + 1) * TT), AF.Square))
                mk(f"hn.sq.{c}.{tt}", AC)

            @block.tensor
            def _(t, c=c, tt=tt, sqs=sqs):
                w_(t, f"hn.sq.{c}.{tt}")
                w_(t, f"hn.rev.{c}.{tt-1}" if tt > 0 else f"hn.rev.{c-1}.3")
                PE.inc(t.matmul(PB[6][0:2, :], b2c, sqs, start=True, stop=True))
                mk(f"hn.nmm.{c}.{tt}", PE)

            @block.scalar
            def _(s, c=c, tt=tt):
                w_(s, f"hn.nmm.{c}.{tt}")
                if tt == 0:
                    w_(s, f"hn.rsp.{c-1}")
                AC.inc(s.activation(FS[0:2, 2048 + tt * 512: 2048 + (tt + 1) * 512],
                                    PB[6][0:2, :], AF.Copy))
                mk(f"hn.rev.{c}.{tt}", AC)

        @block.sync
        def _(q, c=c):
            w_(q, f"hn.rev.{c}.3")
            SPL.inc(q.dma_start(out=row_d[4 + 2 * c: 6 + 2 * c, :], in_=FS[0:2, 2048:4096]), 16)
            mk(f"hn.rsp.{c}", SPL)

    @block.sync
    def _(q):
        bar(q)
        SP.inc(q.dma_start(out=FS[:, 2048:2304],
                           in_=bass.AP(row_d.tensor, 4 * 2048, [[256, 128], [1, 256]])), 16)

    @block.scalar
    def _(s):
        bar(s)
        AC.inc(s.activation(FS[:, 2816:3072], FS[:, 2048:2304], AF.Ln, scale=1.0 / 64, bias=CCc[:, 115:116]))
        fn(s, AC)
        AC.inc(s.activation(FS[:, 2560:2816], FS[:, 2816:3072], AF.Exp, scale=-0.5))
        mk("hn.rm", AC)

    @block.sync
    def _(q):
        bar(q)
        SP.inc(q.dma_start(out=bass.AP(row_d.tensor, 20 * 2048, [[256, 128], [1, 256]]),
                           in_=FS[:, 2560:2816]), 16)
        mk("hn.rqsp", SP)

    rqb = g("WBC")[:, 0:512]
    for c in range(DCH):
        for tt in range(4):
            @block.sync
            def _(q, c=c, tt=tt):
                if c == 0 and tt == 0:
                    bar(q)
                w_(q, f"hn.ap.{c}.{tt-1}" if tt > 0 else f"hn.ap.{c-1}.3")
                rep_head64(q, 20, c, tt, rqb)
                mk(f"hn.bc.{c}.{tt}", BCS)

            @block.vector
            def _(v, c=c, tt=tt):
                w_(v, f"hn.bc.{c}.{tt}")
                DV.inc(v.scalar_tensor_tensor(fm(B16, c, tt * TT, (tt + 1) * TT),
                                              fm(B16, c, tt * TT, (tt + 1) * TT),
                                              ccol("hnw", c), rqb, OP.mult, OP.mult))
                mk(f"hn.ap.{c}.{tt}", DV)
    mk("hn.done", DV)

    @block.gpsimd
    def _(g2):
        bar(g2)
        load_whalf(g2, W["a_out_w"], 0, 0)
        load_whalf(g2, W["a_out_w"], 1, 1)

    gsl = [C16[:, 4096 + k * 512: 4096 + (k + 1) * 512] for k in range(2)]
    t1s = FS[:, 2048:2560]
    u1s = FS[:, 3072:3584]
    mb, cpb = g("WBC")[:, 512:1024], g("WBC")[:, 1024:1536]
    for tt in range(4):
        @block.sync
        def _(q, tt=tt):
            if tt == 0:
                bar(q)
            w_(q, f"op.yi.{DCH-1}.{tt-1}")
            rep512(q, 0, tt, mb)
            rep512(q, 2, tt, cpb)
            mk(f"op.bc.{tt}", BCS)

        for mo in range(DCH):
            @block.tensor
            def _(t, tt=tt, mo=mo):
                if tt == 0 and mo in (0, 4):
                    bld.fence(t, g("WS")[mo // 4])
                w_(t, f"op.yi.{mo-2}.{tt}" if mo >= 2 else (f"op.yi.{6+mo}.{tt-1}" if tt > 0 else "x.-1"))
                for k in range(DCH):
                    mmi = t.matmul(PB[mo % 2][:], wblk(mo // 4, k, mo % 4),
                                   fm(B16, k, tt * TT, (tt + 1) * TT),
                                   start=(k == 0), stop=(k == DCH - 1))
                PE.inc(mmi)
                mk(f"op.mm.{mo}.{tt}", PE)

            @block.sync
            def _(q, tt=tt, mo=mo):
                w_(q, f"op.gld.{mo-1}.{tt}" if mo > 0 else f"op.gld.{DCH-1}.{tt-1}")
                w_(q, f"op.yi.{mo-2}.{tt}" if mo >= 2 else (f"op.yi.{6+mo}.{tt-1}" if tt > 0 else "x.-1"))
                GL.inc(q.dma_start(out=gsl[mo % 2], in_=gate_v[:, mo, tt * TT:(tt + 1) * TT]), 16)
                mk(f"op.gld.{mo}.{tt}", GL)

            @block.vector
            def _(v, tt=tt, mo=mo):
                w_(v, f"op.mm.{mo}.{tt}")
                w_(v, f"op.gld.{mo}.{tt}")
                w_(v, f"op.bc.{tt}")
                fn(v, DV)
                DV.inc(v.tensor_tensor(t1s, XR[:, mo * T + tt * TT: mo * T + (tt + 1) * TT], mb, OP.subtract))
                fn(v, DV)
                DV.inc(v.tensor_tensor(u1s, t1s, cpb, OP.mult))
                fn(v, DV)
                DV.inc(v.tensor_tensor(t1s, PB[mo % 2][:], gsl[mo % 2], OP.mult))
                fn(v, DV)
                DV.inc(v.tensor_tensor(fm(A16, mo, tt * TT, (tt + 1) * TT), u1s, t1s, OP.add))
                mk(f"op.yi.{mo}.{tt}", DV)
    mk("op.done", DV)

    stats_rows("rms1", A16, 0, 1, want_mean=False)

    def rms1_math():
        @block.scalar
        def _(s):
            bar(s)
            AC.inc(s.activation(FS[:, 2144:2160], FS[:, 2048:2064], AF.Ln, scale=1.0 / D, bias=CCc[:, 115:116]))
            fn(s, AC)
            AC.inc(s.activation(FS[:, 2560:2576], FS[:, 2144:2160], AF.Exp, scale=-0.5))
            mk("rms1.c", AC)
    row_math("rms1m", rms1_math, [1], 1, [2], 1)

    c2b = g("WBC")[:, 0:512]
    for tt in range(4):
        @block.sync
        def _(q, tt=tt):
            if tt == 0:
                bar(q)
            w_(q, f"rms1.ap.{DCH-1}.{tt-1}")
            rep512(q, 2, tt, c2b)
            mk(f"rms1.bc.{tt}", BCS)

        for c in range(DCH):
            @block.vector
            def _(v, tt=tt, c=c):
                w_(v, f"rms1.bc.{tt}")
                DV.inc(v.scalar_tensor_tensor(fm(B16, c, tt * TT, (tt + 1) * TT),
                                              fm(A16, c, tt * TT, (tt + 1) * TT),
                                              ccol("fnw", c), c2b, OP.mult, OP.mult))
                mk(f"rms1.ap.{c}.{tt}", DV)

    w1s, w3s, w2s = WBs[:, 0:4096], WBs[:, 4096:8192], WBs[:, 8192:12288]
    sil = C16[:, 0:512]
    h2 = lambda fc: C16[:, 2048 + fc * 512: 2048 + (fc + 1) * 512]
    for fb in range(8):
        @block.gpsimd
        def _(g2, fb=fb):
            if fb == 0:
                bar(g2)
            else:
                w_(g2, f"ff.mm2.{fb-1}.3.{DCH-1}")
            WS[0].inc(g2.dma_start(
                out=w1s.rearrange("p (k n) -> p k n", k=8),
                in_=W["a_ff_w1"].rearrange("(k p) n -> p k n", p=128)[:, :, fb * 512:(fb + 1) * 512]), 16)
            WS[1].inc(g2.dma_start(
                out=w3s.rearrange("p (k n) -> p k n", k=8),
                in_=W["a_ff_w3"].rearrange("(k p) n -> p k n", p=128)[:, :, fb * 512:(fb + 1) * 512]), 16)
            WS[2].inc(g2.dma_start(
                out=w2s.rearrange("p (k n) -> p k n", k=4),
                in_=W["a_ff_w2"].rearrange("(k p) n -> p k n", p=128)[:, 4 * fb: 4 * fb + 4, :]), 16)
            mk(f"ff.w.{fb}", GP)

        for tt in range(4):
            for fc in range(4):
                @block.tensor
                def _(t, fb=fb, tt=tt, fc=fc):
                    if tt == 0 and fc == 0:
                        bld.fence(t, WS[0])
                        bld.fence(t, WS[1])
                    w_(t, f"ff.sil.{fb}.{tt}.{fc-1}" if fc > 0 else f"ff.sil.{fb}.{tt-1}.3")
                    w_(t, f"ff.gu.{fb}.{tt}.{fc-1}" if fc > 0 else f"ff.gu.{fb}.{tt-1}.3")
                    for k in range(DCH):
                        t.matmul(PB[0][:], w1s[:, k * 512 + fc * 128: k * 512 + fc * 128 + 128],
                                 fm(B16, k, tt * TT, (tt + 1) * TT), start=(k == 0), stop=(k == DCH - 1))
                    for k in range(DCH):
                        mmi = t.matmul(PB[1][:], w3s[:, k * 512 + fc * 128: k * 512 + fc * 128 + 128],
                                       fm(B16, k, tt * TT, (tt + 1) * TT), start=(k == 0), stop=(k == DCH - 1))
                    PE.inc(mmi)
                    mk(f"ff.mm13.{fb}.{tt}.{fc}", PE)

                @block.scalar
                def _(s, fb=fb, tt=tt, fc=fc):
                    w_(s, f"ff.mm13.{fb}.{tt}.{fc}")
                    AC.inc(s.activation(sil, PB[0][:], AF.Silu))
                    mk(f"ff.sil.{fb}.{tt}.{fc}", AC)

                @block.vector
                def _(v, fb=fb, tt=tt, fc=fc):
                    w_(v, f"ff.sil.{fb}.{tt}.{fc}")
                    w_(v, f"ff.mm2.{fb-1}.{tt}.{DCH-1}" if fb > 0 else "x.-1")
                    DV.inc(v.tensor_tensor(h2(fc), sil, PB[1][:], OP.mult))
                    mk(f"ff.gu.{fb}.{tt}.{fc}", DV)

            for mo in range(DCH):
                @block.tensor
                def _(t, fb=fb, tt=tt, mo=mo):
                    if mo == 0:
                        bld.fence(t, WS[2])
                    w_(t, f"ff.gu.{fb}.{tt}.3")
                    w_(t, f"ff.ev.{fb}.{tt}.{mo-2}" if mo >= 2 else (f"ff.ev.{fb}.{tt-1}.{6+mo}" if tt > 0 else f"ff.ev.{fb-1}.3.{6+mo}"))
                    for fc in range(4):
                        mmi = t.matmul(PB[2 + mo % 2][:], w2s[:, fc * 1024 + mo * 128: fc * 1024 + mo * 128 + 128],
                                       h2(fc), start=(fc == 0), stop=(fc == 3))
                    PE.inc(mmi)
                    mk(f"ff.mm2.{fb}.{tt}.{mo}", PE)

                @block.vector
                def _(v, fb=fb, tt=tt, mo=mo):
                    w_(v, f"ff.mm2.{fb}.{tt}.{mo}")
                    if fb == 0:
                        fn(v, DV)
                        DV.inc(v.tensor_tensor(t1s, PB[2 + mo % 2][:], fm(A16, mo, tt * TT, (tt + 1) * TT), OP.add))
                        fn(v, DV)
                        DV.inc(v.tensor_tensor(XR[:, mo * T + tt * TT: mo * T + (tt + 1) * TT],
                                               XR[:, mo * T + tt * TT: mo * T + (tt + 1) * TT], t1s, OP.add))
                    else:
                        DV.inc(v.tensor_tensor(XR[:, mo * T + tt * TT: mo * T + (tt + 1) * TT],
                                               XR[:, mo * T + tt * TT: mo * T + (tt + 1) * TT],
                                               PB[2 + mo % 2][:], OP.add))
                    mk(f"ff.ev.{fb}.{tt}.{mo}", DV)
    mk("tail.done", DV)


def emit_ffn(bld, block, env, round_):
    g = lambda k: env[k]
    mk, w_, bar, fn = bld.mk, bld.w, bld.bar, bld.fence
    R = f"f{round_}"
    SP, GP, AC, DV, PE = g("SP"), g("GP"), g("AC"), g("DV"), g("PE")
    SPL, BCS, GL, WS = g("SPL"), g("BCS"), g("GL"), g("WS")
    XR, A16, B16, C16, WBs, FS = g("XR"), g("A16"), g("B16"), g("C16"), g("WBs"), g("FS")
    PB, PT = g("PB"), g("PT")
    W, CCc, row_d = g("W"), g("CCc"), g("row_d")
    gates_d, id16 = g("gates_d"), g("id16")
    fm, ccol, ones16 = g("fm"), g("ccol"), g("ones16")
    stats_rows, row_math, rep512 = g("stats_rows"), g("row_math"), g("rep512")

    @block.vector
    def _(v):
        bar(v)
        for c in range(DCH):
            DV.inc(v.tensor_copy(fm(B16, c), fm(XR, c)))
    stats_rows(R + "ln", B16, 0, 1, want_mean=True)

    def lnm():
        @block.vector
        def _(v):
            bar(v)
            DV.inc(v.tensor_scalar(FS[:, 2560:2576], FS[:, 2048:2064], 1.0 / D, None, OP.mult))
            DV.inc(v.tensor_scalar(FS[:, 2080:2096], FS[:, 2064:2080], 1.0 / D, None, OP.mult))
            fn(v, DV)
            DV.inc(v.tensor_tensor(FS[:, 2096:2112], FS[:, 2560:2576], FS[:, 2560:2576], OP.mult))
            fn(v, DV)
            DV.inc(v.tensor_tensor(FS[:, 2112:2128], FS[:, 2080:2096], FS[:, 2096:2112], OP.subtract))
            mk(R + "ln.var", DV)

        @block.scalar
        def _(s):
            w_(s, R + "ln.var")
            AC.inc(s.activation(FS[:, 2128:2144], FS[:, 2112:2128], AF.Ln, bias=CCc[:, 114:115]))
            fn(s, AC)
            AC.inc(s.activation(FS[:, 2576:2592], FS[:, 2128:2144], AF.Exp, scale=-0.5))
            mk(R + "ln.c", AC)
    row_math(R + "lnm", lnm, [0, 1], 2, [0, 2], 2)

    mb, cpb = g("WBC")[:, 0:512], g("WBC")[:, 512:1024]
    t1s = FS[:, 2048:2560]
    for tt in range(4):
        @block.sync
        def _(q, tt=tt):
            if tt == 0:
                bar(q)
            w_(q, R + f"ln.ap.{DCH-1}.{tt-1}")
            rep512(q, 0, tt, mb)
            rep512(q, 2, tt, cpb)
            mk(R + f"ln.bc.{tt}", BCS)

        for c in range(DCH):
            @block.vector
            def _(v, tt=tt, c=c):
                w_(v, R + f"ln.bc.{tt}")
                fn(v, DV)
                DV.inc(v.tensor_tensor(t1s, XR[:, c * T + tt * TT: c * T + (tt + 1) * TT], mb, OP.subtract))
                fn(v, DV)
                DV.inc(v.tensor_tensor(fm(A16, c, tt * TT, (tt + 1) * TT), t1s, cpb, OP.mult))
                mk(R + f"ln.ap.{c}.{tt}", DV)
    mk(R + "h16", DV)

    @block.gpsimd
    def _(g2):
        bar(g2)
        GP.inc(g2.dma_start(out=WBs[:, 0:64].rearrange("p (k n) -> p k n", k=8),
                            in_=W["f_Wr"].rearrange("(k p) n -> p k n", p=128)), 16)
        mk(R + "wr", GP)

    gtile = C16[:, 5120:5128]
    stile = FS[:, 8:16]
    for w in range(16):
        @block.tensor
        def _(t, w=w):
            if w == 0:
                w_(t, R + "wr")
                w_(t, R + "h16")
            w_(t, R + f"g.sm.{w-1}" if w > 0 else "x.-1")
            for k in range(DCH):
                mmi = t.matmul(PB[4][0:128, 0:8], fm(A16, k, w * 128, (w + 1) * 128),
                               WBs[:, k * 8:(k + 1) * 8], start=(k == 0), stop=(k == DCH - 1))
            PE.inc(mmi)
            mk(R + f"g.mm.{w}", PE)

        @block.vector
        def _(v, w=w):
            w_(v, R + f"g.mm.{w}")
            w_(v, R + f"g.tr.{w-1}" if w > 0 else "x.-1")
            DV.inc(v.tensor_reduce(FS[:, 16:17], PB[4][0:128, 0:8], AX, OP.max))
            fn(v, DV)
            DV.inc(v.tensor_scalar(stile, PB[4][0:128, 0:8], FS[:, 16:17], None, OP.subtract))
            mk(R + f"g.sm.{w}", DV)

        @block.scalar
        def _(s, w=w):
            w_(s, R + f"g.sm.{w}")
            AC.inc(s.activation(stile, stile, AF.Exp))
            mk(R + f"g.ex.{w}", AC)

        @block.vector
        def _(v, w=w):
            w_(v, R + f"g.ex.{w}")
            DV.inc(v.tensor_reduce(FS[:, 16:17], stile, AX, OP.add))
            fn(v, DV)
            DV.inc(v.tensor_reduce(FS[:, 17:18], stile, AX, OP.max))
            fn(v, DV)
            DV.inc(v.tensor_scalar(FS[:, 24:32], stile, FS[:, 17:18], None, OP.is_ge))
            fn(v, DV)
            DV.inc(v.scalar_tensor_tensor(FS[:, 32:40], FS[:, 24:32], -1e30, stile, OP.mult, OP.add))
            fn(v, DV)
            DV.inc(v.tensor_reduce(FS[:, 18:19], FS[:, 32:40], AX, OP.max))
            fn(v, DV)
            DV.inc(v.tensor_tensor(FS[:, 19:20], FS[:, 17:18], FS[:, 18:19], OP.add))
            mk(R + f"g.t2.{w}", DV)

        @block.scalar
        def _(s, w=w):
            w_(s, R + f"g.t2.{w}")
            AC.inc(s.activation(FS[:, 20:21], FS[:, 19:20], AF.Ln))
            fn(s, AC)
            AC.inc(s.activation(FS[:, 21:22], FS[:, 20:21], AF.Exp, scale=-1.0))
            mk(R + f"g.rd.{w}", AC)

        @block.vector
        def _(v, w=w):
            w_(v, R + f"g.rd.{w}")
            DV.inc(v.tensor_tensor(FS[:, 22:23], FS[:, 17:18], FS[:, 21:22], OP.mult))
            DV.inc(v.tensor_tensor(FS[:, 23:24], FS[:, 18:19], FS[:, 21:22], OP.mult))
            fn(v, DV)
            DV.inc(v.tensor_scalar(FS[:, 24:32], stile, FS[:, 17:18], None, OP.is_ge))
            DV.inc(v.tensor_scalar(FS[:, 32:40], stile, FS[:, 18:19], None, OP.is_ge))
            fn(v, DV)
            DV.inc(v.tensor_tensor(FS[:, 32:40], FS[:, 32:40], FS[:, 24:32], OP.subtract))
            fn(v, DV)
            DV.inc(v.tensor_scalar(FS[:, 24:32], FS[:, 24:32], FS[:, 22:23], None, OP.mult))
            DV.inc(v.tensor_scalar(FS[:, 32:40], FS[:, 32:40], FS[:, 23:24], None, OP.mult))
            fn(v, DV)
            DV.inc(v.tensor_tensor(gtile, FS[:, 24:32], FS[:, 32:40], OP.add))
            mk(R + f"g.gt.{w}", DV)

        @block.tensor
        def _(t, w=w):
            w_(t, R + f"g.gt.{w}")
            w_(t, R + f"g.rev.{w-1}" if w > 0 else "x.-1")
            PE.inc(t.transpose(PT[0:8, 0:128], gtile[0:128, :], id16))
            mk(R + f"g.tr.{w}", PE)

        @block.scalar
        def _(s, w=w):
            w_(s, R + f"g.tr.{w}")
            AC.inc(s.activation(FS[0:8, 2048 + w * 128: 2048 + (w + 1) * 128], PT[0:8, 0:128], AF.Copy))
            mk(R + f"g.rev.{w}", AC)

    @block.sync
    def _(q):
        bar(q)
        SPL.inc(q.dma_start(out=gates_d, in_=FS[0:8, 2048:4096]), 16)
        mk(R + "g.sp", SPL)

    w1s, w3s, w2s = WBs[:, 0:4096], WBs[:, 4096:8192], WBs[:, 8192:12288]
    sil = C16[:, 0:512]
    gu = lambda dc: C16[:, 1024 + dc * 512: 1024 + (dc + 1) * 512]
    wgb = g("WBC")[:, 0:512]
    for e in range(9):
        gated = e < 8
        wg_ap = W["f_Wg"][e] if gated else W["f_Sg"]
        wu_ap = W["f_Wu"][e] if gated else W["f_Su"]
        wd_ap = W["f_Wd"][e] if gated else W["f_Sd"]
        for q_ in range(4):
            @block.gpsimd
            def _(g2, e=e, q_=q_, wg_ap=wg_ap, wu_ap=wu_ap, wd_ap=wd_ap):
                if e == 0 and q_ == 0:
                    bar(g2)
                else:
                    pe_, qp = (e, q_ - 1) if q_ > 0 else (e - 1, 3)
                    w_(g2, R + f"x.mm2.{pe_}.{qp}.3.{DCH-1}")
                WS[0].inc(g2.dma_start(
                    out=w1s.rearrange("p (k n) -> p k n", k=8),
                    in_=wg_ap.rearrange("(k p) n -> p k n", p=128)[:, :, q_ * 512:(q_ + 1) * 512]), 16)
                WS[1].inc(g2.dma_start(
                    out=w3s.rearrange("p (k n) -> p k n", k=8),
                    in_=wu_ap.rearrange("(k p) n -> p k n", p=128)[:, :, q_ * 512:(q_ + 1) * 512]), 16)
                WS[2].inc(g2.dma_start(
                    out=w2s.rearrange("p (k n) -> p k n", k=4),
                    in_=wd_ap.rearrange("(k p) n -> p k n", p=128)[:, 4 * q_: 4 * q_ + 4, :]), 16)
                mk(R + f"x.w.{e}.{q_}", GP)

            for tt in range(4):
                @block.sync
                def _(q, e=e, q_=q_, tt=tt):
                    if e < 8 and q_ == 0:
                        if e == 0 and tt == 0:
                            bar(q)
                        w_(q, R + f"x.he.{e-1}.3" if e > 0 else "x.-1")
                        if tt > 0:
                            w_(q, R + f"x.he.{e}.{tt-1}")
                        BCS.inc(q.dma_start(out=wgb, in_=bass.AP(
                            gates_d.tensor, e * T + tt * TT, [[0, 128], [1, 512]])), 16)
                        mk(R + f"x.gb.{e}.{q_}.{tt}", BCS)

                @block.vector
                def _(v, e=e, q_=q_, tt=tt):
                    if e < 8 and q_ == 0:
                        w_(v, R + f"x.gb.{e}.{q_}.{tt}")
                        w_(v, R + f"x.heu.{e-1}.{tt}" if e > 0 else "x.-1")
                        for k in range(DCH):
                            DV.inc(v.tensor_tensor(fm(B16, k, tt * TT, (tt + 1) * TT),
                                                   fm(A16, k, tt * TT, (tt + 1) * TT), wgb, OP.mult))
                        mk(R + f"x.he.{e}.{tt}", DV)

                for dc in range(4):
                    @block.tensor
                    def _(t, e=e, q_=q_, tt=tt, dc=dc, gated=gated):
                        if tt == 0 and dc == 0:
                            bld.fence(t, WS[0])
                            bld.fence(t, WS[1])
                        w_(t, R + f"x.sil.{e}.{q_}.{tt}.{dc-1}" if dc > 0 else (R + f"x.sil.{e}.{q_}.{tt-1}.3" if tt > 0 else (R + f"x.sil.{e}.{q_-1}.3.3" if q_ > 0 else (R + f"x.sil.{e-1}.3.3.3" if e > 0 else "x.-1"))))
                        w_(t, R + f"x.gu.{e}.{q_}.{tt}.{dc-1}" if dc > 0 else (R + f"x.gu.{e}.{q_}.{tt-1}.3" if tt > 0 else (R + f"x.gu.{e}.{q_-1}.3.3" if q_ > 0 else (R + f"x.gu.{e-1}.3.3.3" if e > 0 else "x.-1"))))
                        if gated:
                            w_(t, R + f"x.he.{e}.{tt}")
                        for k in range(DCH):
                            t.matmul(PB[0][:], w1s[:, k * 512 + dc * 128: k * 512 + dc * 128 + 128],
                                     fm(A16, k, tt * TT, (tt + 1) * TT),
                                     start=(k == 0), stop=(k == DCH - 1))
                        for k in range(DCH):
                            mmi = t.matmul(PB[1][:], w3s[:, k * 512 + dc * 128: k * 512 + dc * 128 + 128],
                                           fm(B16, k, tt * TT, (tt + 1) * TT) if gated else fm(A16, k, tt * TT, (tt + 1) * TT),
                                           start=(k == 0), stop=(k == DCH - 1))
                        PE.inc(mmi)
                        mk(R + f"x.mm13.{e}.{q_}.{tt}.{dc}", PE)
                        if q_ == 3 and dc == 3:
                            mk(R + f"x.heu.{e}.{tt}", PE)

                    @block.scalar
                    def _(s, e=e, q_=q_, tt=tt, dc=dc):
                        w_(s, R + f"x.mm13.{e}.{q_}.{tt}.{dc}")
                        AC.inc(s.activation(sil, PB[0][:], AF.Silu))
                        mk(R + f"x.sil.{e}.{q_}.{tt}.{dc}", AC)

                    @block.vector
                    def _(v, e=e, q_=q_, tt=tt, dc=dc, gated=gated):
                        w_(v, R + f"x.sil.{e}.{q_}.{tt}.{dc}")
                        w_(v, R + f"x.mm2g.{e}.{q_}.{tt-1}" if tt > 0 else (R + f"x.mm2g.{e}.{q_-1}.3" if q_ > 0 else (R + f"x.mm2g.{e-1}.3.3" if e > 0 else "x.-1")))
                        DV.inc(v.tensor_tensor(gu(dc), sil, PB[1][:], OP.mult))
                        mk(R + f"x.gu.{e}.{q_}.{tt}.{dc}", DV)

                for mo in range(DCH):
                    @block.tensor
                    def _(t, e=e, q_=q_, tt=tt, mo=mo):
                        if mo == 0:
                            bld.fence(t, WS[2])
                            w_(t, R + f"x.gu.{e}.{q_}.{tt}.3")
                        w_(t, R + f"x.ev.{e}.{q_}.{tt}.{mo-2}" if mo >= 2 else (R + f"x.ev.{e}.{q_}.{tt-1}.{6+mo}" if tt > 0 else (R + f"x.ev.{e}.{q_-1}.3.{6+mo}" if q_ > 0 else (R + f"x.ev.{e-1}.3.3.{6+mo}" if e > 0 else "x.-1"))))
                        for fc in range(4):
                            mmi = t.matmul(PB[2 + mo % 2][:], w2s[:, fc * 1024 + mo * 128: fc * 1024 + mo * 128 + 128],
                                           gu(fc), start=(fc == 0), stop=(fc == 3))
                        PE.inc(mmi)
                        mk(R + f"x.mm2.{e}.{q_}.{tt}.{mo}", PE)
                        if mo == DCH - 1:
                            mk(R + f"x.mm2g.{e}.{q_}.{tt}", PE)

                    @block.vector
                    def _(v, e=e, q_=q_, tt=tt, mo=mo):
                        w_(v, R + f"x.mm2.{e}.{q_}.{tt}.{mo}")
                        DV.inc(v.tensor_tensor(XR[:, mo * T + tt * TT: mo * T + (tt + 1) * TT],
                                               XR[:, mo * T + tt * TT: mo * T + (tt + 1) * TT],
                                               PB[2 + mo % 2][:], OP.add))
                        mk(R + f"x.ev.{e}.{q_}.{tt}.{mo}", DV)
    mk(R + ".done", DV)


def emit_attn_b(bld, block, env):
    g = lambda k: env[k]
    mk, w_, bar, fn = bld.mk, bld.w, bld.bar, bld.fence
    SP, GP, AC, DV, PE = g("SP"), g("GP"), g("AC"), g("DV"), g("PE")
    SPL, BCS, GL, WS = g("SPL"), g("BCS"), g("GL"), g("WT")
    XR, A16, B16, C16, WBs, FS, SS = g("XR"), g("A16"), g("B16"), g("C16"), g("WBs"), g("FS"), g("SS")
    PB, PT = g("PB"), g("PT")
    W, CCc, row_d = g("W"), g("CCc"), g("row_d")
    wbT_d, wbar_d, vtmB_d, vfm_d, grow_d = g("wbT_d"), g("wbar_d"), g("vtmB_d"), g("vfm_d"), g("grow_d")
    MKB, RT16, id16, mhi, mlo = g("MKB"), g("RT16"), g("id16"), g("mhi"), g("mlo")
    fm, ccol = g("fm"), g("ccol")
    stats_rows, row_math, rep512 = g("stats_rows"), g("row_math"), g("rep512")
    ep_d = g("ep_d")
    ytm_d = g("ytm_d")

    @block.vector
    def _(v):
        bar(v)
        for c in range(DCH):
            DV.inc(v.tensor_copy(fm(B16, c), fm(XR, c)))
    stats_rows("bln", B16, 0, 1, want_mean=True)

    def lnm():
        @block.vector
        def _(v):
            bar(v)
            DV.inc(v.tensor_scalar(FS[:, 2560:2576], FS[:, 2048:2064], 1.0 / D, None, OP.mult))
            DV.inc(v.tensor_scalar(FS[:, 2080:2096], FS[:, 2064:2080], 1.0 / D, None, OP.mult))
            fn(v, DV)
            DV.inc(v.tensor_tensor(FS[:, 2096:2112], FS[:, 2560:2576], FS[:, 2560:2576], OP.mult))
            fn(v, DV)
            DV.inc(v.tensor_tensor(FS[:, 2112:2128], FS[:, 2080:2096], FS[:, 2096:2112], OP.subtract))
            mk("bln.var", DV)

        @block.scalar
        def _(s):
            w_(s, "bln.var")
            AC.inc(s.activation(FS[:, 2128:2144], FS[:, 2112:2128], AF.Ln, bias=CCc[:, 114:115]))
            fn(s, AC)
            AC.inc(s.activation(FS[:, 2576:2592], FS[:, 2128:2144], AF.Exp, scale=-0.5))
            mk("bln.c", AC)
    row_math("blnm", lnm, [0, 1], 2, [0, 2], 2)

    mb, cpb = g("WBC")[:, 0:512], g("WBC")[:, 512:1024]
    t1s = FS[:, 2048:2560]
    for tt in range(4):
        @block.sync
        def _(q, tt=tt):
            if tt == 0:
                bar(q)
            w_(q, f"bln.ap.{DCH-1}.{tt-1}")
            rep512(q, 0, tt, mb)
            rep512(q, 2, tt, cpb)
            mk(f"bln.bc.{tt}", BCS)

        for c in range(DCH):
            @block.vector
            def _(v, tt=tt, c=c):
                w_(v, f"bln.bc.{tt}")
                fn(v, DV)
                DV.inc(v.tensor_tensor(t1s, XR[:, c * T + tt * TT: c * T + (tt + 1) * TT], mb, OP.subtract))
                fn(v, DV)
                DV.inc(v.tensor_tensor(fm(A16, c, tt * TT, (tt + 1) * TT), t1s, cpb, OP.mult))
                mk(f"bln.ap.{c}.{tt}", DV)
    mk("bh16", DV)

    ws = g("ws")
    WGB, BGB, WLB, BLB = g("WGB"), g("BGB"), g("WLB"), g("BLB")
    Cin = g("C")

    @block.sync
    def _(q):
        bar(q)
        SP.inc(q.dma_start(out=WGB, in_=Cin["wg_bc"]), 16)
        SP.inc(q.dma_start(out=BGB, in_=Cin["bg_bc"]), 16)
        SP.inc(q.dma_start(out=WLB, in_=Cin["wl_bc"]), 16)
        SP.inc(q.dma_start(out=BLB, in_=Cin["bl_bc"]), 16)
        mk("bgl", SP)

    def load_whalfT(q, widx, half, slot):
        for k in range(DCH):
            WS[slot].inc(q.dma_start(
                out=ws[slot][:, k * 512:(k + 1) * 512],
                in_=wbT_d[widx][half * 512:(half + 1) * 512, k * 128:(k + 1) * 128],
                transpose=True), 16)

    def projT(tag, widx, dest_pool):
        @block.scalar
        def _(s):
            bar(s)
            load_whalfT(s, widx, 0, 0)
            load_whalfT(s, widx, 1, 1)
            mk(f"{tag}.wT", AC)

        for mo in range(DCH):
            for tt in range(4):
                @block.tensor
                def _(t, mo=mo, tt=tt):
                    if mo == 0 and tt == 0:
                        bld.fence(t, WS[0])
                        bld.fence(t, WS[1])
                    w_(t, f"{tag}.ev.{mo}.{tt-1}" if tt > 0 else f"{tag}.ev.{mo-2}.3")
                    half, m4 = mo // 4, mo % 4
                    for k in range(DCH):
                        mmi = t.matmul(PB[mo % 2][:], ws[half][:, k * 512 + m4 * 128: k * 512 + m4 * 128 + 128],
                                       fm(A16, k, tt * TT, (tt + 1) * TT),
                                       start=(k == 0), stop=(k == DCH - 1))
                    PE.inc(mmi)
                    mk(f"{tag}.mm.{mo}.{tt}", PE)

                @block.vector
                def _(v, mo=mo, tt=tt):
                    w_(v, f"{tag}.mm.{mo}.{tt}")
                    DV.inc(v.tensor_copy(fm(dest_pool, mo, tt * TT, (tt + 1) * TT), PB[mo % 2][:]))
                    mk(f"{tag}.ev.{mo}.{tt}", DV)
    projT("bq", 0, B16)

    vsl = [C16[:, 0:512], C16[:, 512:1024]]

    @block.scalar
    def _(s):
        bar(s)
        load_whalfT(s, 2, 0, 0)
        load_whalfT(s, 2, 1, 1)
        mk("bv.wT", AC)

    for mo in range(DCH):
        for tt in range(4):
            i_ = mo * 4 + tt

            @block.tensor
            def _(t, mo=mo, tt=tt, i_=i_):
                if mo == 0 and tt == 0:
                    bld.fence(t, WS[0])
                    bld.fence(t, WS[1])
                w_(t, f"bv.ev.{i_-2}")
                half, m4 = mo // 4, mo % 4
                for k in range(DCH):
                    mmi = t.matmul(PB[i_ % 2][:], ws[half][:, k * 512 + m4 * 128: k * 512 + m4 * 128 + 128],
                                   fm(A16, k, tt * TT, (tt + 1) * TT),
                                   start=(k == 0), stop=(k == DCH - 1))
                PE.inc(mmi)
                mk(f"bv.mm.{i_}", PE)

            @block.vector
            def _(v, i_=i_):
                w_(v, f"bv.mm.{i_}")
                w_(v, f"bv.sp.{i_-2}")
                DV.inc(v.tensor_copy(vsl[i_ % 2], PB[i_ % 2][:]))
                mk(f"bv.ev.{i_}", DV)

            @block.sync
            def _(q, mo=mo, tt=tt, i_=i_):
                w_(q, f"bv.ev.{i_}")
                SPL.inc(q.dma_start(out=vfm_d.rearrange("(c p) t -> p c t", p=128)[:, mo, tt * TT:(tt + 1) * TT],
                                    in_=vsl[i_ % 2]), 16)
                mk(f"bv.sp.{i_}", SPL)

    for nh in range(2):
        @block.scalar
        def _(s, nh=nh):
            bar(s)
            load_whalfT(s, 2, nh, 2)
            mk(f"bvt.wT.{nh}", AC)

        for w in range(16):
            i_ = nh * 16 + w

            @block.tensor
            def _(t, nh=nh, w=w, i_=i_):
                if w == 0:
                    bld.fence(t, WS[2])
                w_(t, f"bvt.ev.{i_-2}")
                for k in range(DCH):
                    mmi = t.matmul(PB[i_ % 2][:], fm(A16, k, w * 128, (w + 1) * 128),
                                   ws[2][:, k * 512:(k + 1) * 512],
                                   start=(k == 0), stop=(k == DCH - 1))
                PE.inc(mmi)
                mk(f"bvt.mm.{i_}", PE)

            @block.vector
            def _(v, i_=i_):
                w_(v, f"bvt.mm.{i_}")
                w_(v, f"bvt.sp.{i_-2}")
                DV.inc(v.tensor_copy(vsl[i_ % 2], PB[i_ % 2][:]))
                mk(f"bvt.ev.{i_}", DV)

            @block.sync
            def _(q, nh=nh, w=w, i_=i_):
                w_(q, f"bvt.ev.{i_}")
                SPL.inc(q.dma_start(out=vtmB_d[nh][w * 128:(w + 1) * 128, :], in_=vsl[i_ % 2]), 16)
                mk(f"bvt.sp.{i_}", SPL)

    for nh in range(2):
        @block.scalar
        def _(s, nh=nh):
            bar(s)
            load_whalfT(s, 1, nh, 2)
            mk(f"bk.wT.{nh}", AC)

        for w in range(16):
            i_ = nh * 16 + w
            ktl = C16[:, 1024:2048]
            sqt = C16[:, 2048:3072]
            knt = C16[:, 3072:3584]
            wbt = C16[:, 3584:4096]
            ept = C16[:, 4096:4608]
            lrt = FS[:, 2048:2560]
            rkc = FS[:, 2560:2562]

            @block.tensor
            def _(t, nh=nh, w=w, i_=i_):
                if w == 0:
                    bld.fence(t, WS[2])
                w_(t, f"bk.kn.{i_-1}")
                for k in range(DCH):
                    mmi = t.matmul(PB[0][:], fm(A16, k, w * 128, (w + 1) * 128),
                                   ws[2][:, k * 512:(k + 1) * 512],
                                   start=(k == 0), stop=(k == DCH - 1))
                PE.inc(mmi)
                mk(f"bk.mm.{i_}", PE)

            @block.scalar
            def _(s, nh=nh, i_=i_):
                w_(s, f"bk.mm.{i_}")
                w_(s, f"bk.wb.{i_-1}")
                AC.inc(s.activation(sqt[:, 0:512], PB[0][:], AF.Square))
                mk(f"bk.sq.{i_}", AC)

            @block.vector
            def _(v, nh=nh, i_=i_):
                w_(v, f"bk.sq.{i_}")
                DV.inc(v.tensor_reduce(rkc[:, 0:1], sqt[:, 0:512], AX, OP.add))
                mk(f"bk.ss.{i_}", DV)

            @block.scalar
            def _(s, nh=nh, i_=i_):
                w_(s, f"bk.ss.{i_}")
                AC.inc(s.activation(FS[:, 2562:2563], rkc[:, 0:1], AF.Sqrt))
                mk(f"bk.sr.{i_}", AC)

            @block.vector
            def _(v, nh=nh, i_=i_):
                w_(v, f"bk.sr.{i_}")
                DV.inc(v.tensor_scalar(FS[:, 2563:2564], FS[:, 2562:2563], 1e-12, None, OP.max))
                mk(f"bk.mx.{i_}", DV)

            @block.scalar
            def _(s, nh=nh, i_=i_):
                w_(s, f"bk.mx.{i_}")
                AC.inc(s.activation(FS[:, 2564:2565], FS[:, 2563:2564], AF.Ln))
                fn(s, AC)
                AC.inc(s.activation(FS[:, 2565:2566], FS[:, 2564:2565], AF.Exp, scale=-1.0))
                mk(f"bk.rk.{i_}", AC)

            @block.vector
            def _(v, nh=nh, i_=i_):
                w_(v, f"bk.rk.{i_}")
                DV.inc(v.tensor_scalar(knt[:, 0:512], PB[0][:], FS[:, 2565:2566], None, OP.mult))
                mk(f"bk.kn.{i_}", DV)

            @block.vector
            def _(v, nh=nh, i_=i_):
                w_(v, f"bk.kn.{i_}")
                w_(v, f"bk.wb.{i_-1}")
                DV.inc(v.tensor_tensor(sqt[:, 0:512], knt[:, 0:512], WGB[:, nh * 512:(nh + 1) * 512], OP.mult))
                DV.inc(v.tensor_tensor(sqt[:, 512:1024], knt[:, 0:512], WLB[:, nh * 512:(nh + 1) * 512], OP.mult))
                fn(v, DV)
                DV.inc(v.tensor_tensor(sqt[:, 0:512], sqt[:, 0:512], BGB[:, nh * 512:(nh + 1) * 512], OP.add))
                DV.inc(v.tensor_tensor(sqt[:, 512:1024], sqt[:, 512:1024], BLB[:, nh * 512:(nh + 1) * 512], OP.add))
                mk(f"bk.lg.{i_}", DV)

            @block.scalar
            def _(s, nh=nh, i_=i_):
                w_(s, f"bk.lg.{i_}")
                AC.inc(s.activation(sqt[:, 0:512], sqt[:, 0:512], AF.Sigmoid))
                AC.inc(s.activation(sqt[:, 512:1024], sqt[:, 512:1024], AF.Sigmoid))
                mk(f"bk.sg.{i_}", AC)

            @block.vector
            def _(v, nh=nh, i_=i_):
                w_(v, f"bk.sg.{i_}")
                DV.inc(v.tensor_scalar(sqt[:, 512:1024], sqt[:, 512:1024], -1.0, 1.0, OP.mult, OP.add))
                fn(v, DV)
                DV.inc(v.tensor_tensor(sqt[:, 0:512], sqt[:, 0:512], sqt[:, 512:1024], OP.mult))
                fn(v, DV)
                DV.inc(v.tensor_scalar(sqt[:, 0:512], sqt[:, 0:512], RFLOOR, None, OP.max))
                mk(f"bk.rho.{i_}", DV)

            @block.scalar
            def _(s, nh=nh, i_=i_):
                w_(s, f"bk.rho.{i_}")
                AC.inc(s.activation(lrt[:, 0:512], sqt[:, 0:512], AF.Ln))
                mk(f"bk.lr.{i_}", AC)

            @block.vector
            def _(v, nh=nh, i_=i_):
                w_(v, f"bk.lr.{i_}")
                DV.inc(v.tensor_copy(C16[:, 4608:5120], lrt[:, 0:512]))
                mk(f"bk.lr16.{i_}", DV)

            @block.tensor
            def _(t, nh=nh, i_=i_):
                w_(t, f"bk.lr16.{i_}")
                w_(t, f"bk.ep.{i_-1}")
                t.matmul(PB[1][:], RT16[:], C16[:, 4608:5120], start=True, stop=True)
                PE.inc(t.matmul(PB[2][0:2, :], g("KON")[:, 131:133], C16[:, 4608:5120], start=True, stop=True))
                mk(f"bk.lrmm.{i_}", PE)

            @block.scalar
            def _(s, nh=nh, i_=i_, w=w):
                w_(s, f"bk.lrmm.{i_}")
                w_(s, f"bk.wsp.{i_-1}")
                AC.inc(s.activation(ept[:, 0:512], PB[1][:], AF.Exp))
                fn(s, AC)
                AC.inc(s.activation(sqt[:, 512:1024], PB[1][:], AF.Exp, scale=-1.0))
                AC.inc(s.activation(FS[0:2, 2566 + 0:2566 + 512][0:2, 0:512] if False else FS[0:2, 2566:3078], PB[2][0:2, :], AF.Copy))
                mk(f"bk.ep.{i_}", AC)

            @block.vector
            def _(v, nh=nh, i_=i_):
                w_(v, f"bk.ep.{i_}")
                w_(v, f"bk.wsp.{i_-2}")
                DV.inc(v.tensor_scalar(wbt[:, 0:512], sqt[:, 0:512], -1.0, 1.0, OP.mult, OP.add))
                fn(v, DV)
                DV.inc(v.tensor_tensor(wbt[:, 0:512], wbt[:, 0:512], knt[:, 0:512], OP.mult))
                fn(v, DV)
                DV.inc(v.tensor_tensor(wbt[:, 0:512], wbt[:, 0:512], sqt[:, 512:1024], OP.mult))
                mk(f"bk.wb.{i_}", DV)

            @block.sync
            def _(q, nh=nh, w=w, i_=i_):
                w_(q, f"bk.wb.{i_}")
                SPL.inc(q.dma_start(out=wbar_d[nh][w * 128:(w + 1) * 128, :], in_=wbt[:, 0:512]), 16)
                SPL.inc(q.dma_start(out=ep_d[nh][w * 128:(w + 1) * 128, :], in_=ept[:, 0:512]), 16)
                SPL.inc(q.dma_start(out=grow_d[nh][0:2, w * 512:(w + 1) * 512] if False else
                        bass.AP(grow_d.tensor, nh * 16 * 1024 + w * 1024, [[512, 2], [1, 512]]),
                        in_=FS[0:2, 2566:3078]), 16)
                mk(f"bk.wsp.{i_}", SPL)

    for nh in range(2):
        for c in range(16):
            @block.sync
            def _(q, nh=nh, c=c):
                if nh == 0 and c == 0:
                    bar(q)
                w_(q, f"bg.sp.{nh}.{c-1}" if c > 0 else (f"bg.sp.{nh-1}.15" if nh > 0 else "x.-1"))
                SP.inc(q.dma_start(out=FS[0:1, 2048:2560],
                                   in_=bass.AP(grow_d.tensor, nh * 16384 + c * 1024 + 512, [[512, 1], [1, 512]])), 16)
                if c < 15:
                    SP.inc(q.dma_start(out=FS[0:1, 2560:3072],
                                       in_=bass.AP(grow_d.tensor, nh * 16384 + (c + 1) * 1024, [[512, 1], [1, 512]])), 16)
                mk(f"bg.ld.{nh}.{c}", SP)

            @block.vector
            def _(v, nh=nh, c=c):
                w_(v, f"bg.ld.{nh}.{c}")
                if c < 15:
                    DV.inc(v.tensor_tensor(FS[0:1, 3072:3584], FS[0:1, 2048:2560], FS[0:1, 2560:3072], OP.add))
                else:
                    DV.inc(v.tensor_copy(FS[0:1, 3072:3584], FS[0:1, 2048:2560]))
                mk(f"bg.add.{nh}.{c}", DV)

            @block.scalar
            def _(s, nh=nh, c=c):
                w_(s, f"bg.add.{nh}.{c}")
                AC.inc(s.activation(FS[0:1, 3584:4096], FS[0:1, 3072:3584], AF.Exp))
                mk(f"bg.exp.{nh}.{c}", AC)

            @block.sync
            def _(q2, nh=nh, c=c):
                w_(q2, f"bg.exp.{nh}.{c}")
                SPL.inc(q2.dma_start(out=bass.AP(grow_d.tensor, 32768 + nh * 16384 + c * 512, [[512, 1], [1, 512]]),
                                     in_=FS[0:1, 3584:4096]), 16)
                mk(f"bg.sp.{nh}.{c}", SPL)

    S16 = lambda vc: A16[:, 8192 + vc * 512: 8192 + (vc + 1) * 512]
    qsl = lambda kc, c: B16[:, 0:0]
    am16 = WBs[:, 0:128]
    wbc_t = C16[:, 0:512]
    vtc = C16[:, 512:1024]
    epc = C16[:, 1024:1536]
    vfc = C16[:, 1536:2048]
    ytc = C16[:, 2048:2560]
    gb_t = g("WBC")[:, 0:512]

    @block.vector
    def _(v):
        bar(v)
        DV.inc(v.memset(SS[:, 0:4096], 0.0))
        mk("bs.s.-1", DV)

    for h in range(2):
        for c in range(16):
            @block.sync
            def _(q, h=h, c=c):
                if h == 0 and c == 0:
                    bar(q)
                w_(q, f"bs.sdone.{h}.{c-1}" if c > 0 else (f"bs.sdone.{h-1}.15" if h > 0 else "x.-1"))
                GL.inc(q.dma_start(out=wbc_t, in_=wbar_d[h][c * 128:(c + 1) * 128, :]), 16)
                GL.inc(q.dma_start(out=vtc, in_=vtmB_d[h][c * 128:(c + 1) * 128, :]), 16)
                GL.inc(q.dma_start(out=epc, in_=ep_d[h][c * 128:(c + 1) * 128, :]), 16)
                GL.inc(q.dma_start(out=vfc.rearrange("p (a b) -> p a b", a=4),
                                   in_=vfm_d.rearrange("(c p) t -> p c t", p=128)[:, 4 * h: 4 * h + 4, c * 128:(c + 1) * 128]), 16)
                BCS.inc(q.dma_start(out=gb_t, in_=bass.AP(
                    grow_d.tensor, 32768 + h * 16384 + c * 512, [[0, 128], [1, 512]])), 16)
                mk(f"bs.ld.{h}.{c}", GL)

            @block.vector
            def _(v, h=h, c=c):
                w_(v, f"bs.ld.{h}.{c}")
                w_(v, f"bs.s16u.{h}.{c-1}" if c > 0 else (f"bs.s16u.{h-1}.15" if h > 0 else "x.-1"))
                for vc in range(4):
                    DV.inc(v.tensor_copy(S16(vc), SS[:, h * 2048 + vc * 512: h * 2048 + (vc + 1) * 512]))
                mk(f"bs.s16.{h}.{c}", DV)
                mk(f"bs.s16u.{h}.{c}", DV)

            @block.tensor
            def _(t, h=h, c=c):
                w_(t, f"bs.s16.{h}.{c}")
                w_(t, f"bs.am.{h}.{c-1}" if c > 0 else (f"bs.am.{h-1}.15" if h > 0 else "x.-1"))
                for kc in range(4):
                    mmi = t.matmul(PB[0][:, 0:128], vfc[:, kc * 128:(kc + 1) * 128],
                                   B16[:, (4 * h + kc) * T + c * 128: (4 * h + kc) * T + (c + 1) * 128],
                                   start=(kc == 0), stop=(kc == 3))
                PE.inc(mmi)
                mk(f"bs.amm.{h}.{c}", PE)

            @block.vector
            def _(v, h=h, c=c):
                w_(v, f"bs.amm.{h}.{c}")
                DV.inc(v.tensor_tensor(am16[:, 0:128], PB[0][:, 0:128], MKB[:], OP.mult))
                mk(f"bs.am.{h}.{c}", DV)

            @block.tensor
            def _(t, h=h, c=c):
                w_(t, f"bs.am.{h}.{c}")
                w_(t, f"bs.ye.{h}.{c-1}" if c > 0 else (f"bs.ye.{h-1}.15" if h > 0 else "x.-1"))
                t.matmul(PB[1][:], am16[:, 0:128], wbc_t, start=True, stop=False)
                for kc in range(4):
                    mmi = t.matmul(PB[1][:], B16[:, (4 * h + kc) * T + c * 128: (4 * h + kc) * T + (c + 1) * 128],
                                   S16(kc), start=False, stop=(kc == 3))
                PE.inc(mmi)
                mk(f"bs.ymm.{h}.{c}", PE)

            @block.vector
            def _(v, h=h, c=c):
                w_(v, f"bs.ymm.{h}.{c}")
                w_(v, f"bs.ysp.{h}.{c-1}" if c > 0 else (f"bs.ysp.{h-1}.15" if h > 0 else "x.-1"))
                DV.inc(v.tensor_tensor(ytc, PB[1][:], epc, OP.mult))
                mk(f"bs.ye.{h}.{c}", DV)

            @block.sync
            def _(q, h=h, c=c):
                w_(q, f"bs.ye.{h}.{c}")
                SPL.inc(q.dma_start(out=ytm_d[h][c * 128:(c + 1) * 128, :], in_=ytc), 16)
                mk(f"bs.ysp.{h}.{c}", SPL)

            for vc in range(4):
                @block.tensor
                def _(t, h=h, c=c, vc=vc):
                    w_(t, f"bs.su.{h}.{c}.{vc-1}" if vc > 0 else f"bs.ye.{h}.{c}")
                    PE.inc(t.matmul(PB[2][:], vtc[:, vc * 128:(vc + 1) * 128], wbc_t, start=True, stop=True))
                    mk(f"bs.smm.{h}.{c}.{vc}", PE)

                @block.vector
                def _(v, h=h, c=c, vc=vc):
                    w_(v, f"bs.smm.{h}.{c}.{vc}")
                    w_(v, "bs.ld." + f"{h}.{c}")
                    bld.fence(v, g("BCS"))
                    sl = SS[:, h * 2048 + vc * 512: h * 2048 + (vc + 1) * 512]
                    DV.inc(v.tensor_tensor(sl, sl, PB[2][:], OP.add))
                    fn(v, DV)
                    DV.inc(v.tensor_tensor(sl, sl, gb_t, OP.mult))
                    mk(f"bs.su.{h}.{c}.{vc}", DV)
            mk(f"bs.sdone.{h}.{c}", DV)

    @block.scalar
    def _(s):
        bar(s)
        for h in range(2):
            for kc in range(4):
                WS[0].inc(s.dma_start(out=fm(A16, 4 * h + kc),
                                      in_=ytm_d[h][0:T, kc * 128:(kc + 1) * 128], transpose=True), 16)
        load_whalfT(s, 3, 0, 1)
        mk("bo.w0", AC)

    for mo in range(DCH):
        if mo == 4:
            @block.scalar
            def _(s):
                bar(s)
                load_whalfT(s, 3, 1, 1)
                mk("bo.w1", AC)
        for tt in range(4):
            @block.tensor
            def _(t, mo=mo, tt=tt):
                if tt == 0:
                    bld.fence(t, WS[1])
                    bld.fence(t, WS[0])
                w_(t, f"bo.ev.{mo}.{tt-1}" if tt > 0 else f"bo.ev.{mo-2}.3")
                m4 = mo % 4
                for k in range(DCH):
                    mmi = t.matmul(PB[mo % 2][:], ws[1][:, k * 512 + m4 * 128: k * 512 + m4 * 128 + 128],
                                   fm(A16, k, tt * TT, (tt + 1) * TT),
                                   start=(k == 0), stop=(k == DCH - 1))
                PE.inc(mmi)
                mk(f"bo.mm.{mo}.{tt}", PE)

            @block.vector
            def _(v, mo=mo, tt=tt):
                w_(v, f"bo.mm.{mo}.{tt}")
                DV.inc(v.tensor_tensor(XR[:, mo * T + tt * TT: mo * T + (tt + 1) * TT],
                                       XR[:, mo * T + tt * TT: mo * T + (tt + 1) * TT],
                                       PB[mo % 2][:], OP.add))
                mk(f"bo.ev.{mo}.{tt}", DV)
    mk("attnb.done", DV)


def kernel(**inputs):
    in_maps = host_prep(inputs)
    nc = build(stage=STAGE, dumps=DUMPS)
    res = run_bass_kernel_spmd(nc, in_maps, core_ids=list(range(8)))
    return np.stack([np.asarray(r["out"], np.float32).T for r in res.results], axis=0)


# version 04
# speedup vs baseline: 1.3303x; 1.1620x over previous
import contextlib
import os as _os
import os
import numpy as np
import ml_dtypes
import concourse.bass as bass
import concourse.mybir as mybir
from concourse.bass_utils import run_bass_kernel_spmd
from concourse.alu_op_type import AluOpType as OP

F32 = mybir.dt.float32
BF16 = mybir.dt.bfloat16
AF = mybir.ActivationFunctionType
AX = mybir.AxisListType.X

T = 2048
D = 1024
DCH = 8
TT = 512
CA = 64
CB = 128
FF = 4096
NE = 8
DEXP = 2048
AFLOOR = 0.09
RFLOOR = 0.30

STAGE = 99
DUMPS = ()


def _f32(a):
    return np.ascontiguousarray(np.asarray(a, dtype=np.float32))


def host_constants():
    c = {}
    c["id128"] = np.ascontiguousarray(np.eye(128).astype(ml_dtypes.bfloat16))
    c["ones_col"] = np.ones((128, 1), dtype=ml_dtypes.bfloat16)
    b2 = np.zeros((128, 2), np.float32)
    b2[0:64, 0] = 1.0
    b2[64:128, 1] = 1.0
    c["b2"] = np.ascontiguousarray(b2.astype(ml_dtypes.bfloat16))
    jj, tt_ = np.meshgrid(np.arange(64), np.arange(64), indexing="ij")
    m = np.zeros((64, 128), np.float32)
    m[:, 0:64] = -(jj <= tt_).astype(np.float32)
    m[:, 64:128] = -(jj < tt_).astype(np.float32)
    c["maskA"] = np.ascontiguousarray(np.tile(m, (2, 1)).astype(ml_dtypes.bfloat16))
    jj, tt_ = np.meshgrid(np.arange(CB), np.arange(CB), indexing="ij")
    c["maskB"] = np.ascontiguousarray((jj <= tt_).astype(ml_dtypes.bfloat16))
    rt = (jj <= tt_).astype(np.float32) - (jj <= 63).astype(np.float32)
    c["rbT"] = np.ascontiguousarray(rt.astype(ml_dtypes.bfloat16))
    mh = np.zeros((128, 1), np.float32)
    mh[64:128] = 1.0
    c["mcol_hi"] = np.ascontiguousarray(mh.astype(ml_dtypes.bfloat16))
    ml_ = np.zeros((128, 1), np.float32)
    ml_[0:64] = 1.0
    c["mcol_lo"] = np.ascontiguousarray(ml_.astype(ml_dtypes.bfloat16))
    return c


def host_prep(inputs):
    ins = {k: np.asarray(v) for k, v in inputs.items()}
    common = dict(host_constants())

    def col8(v):
        return _f32(np.asarray(v, np.float32).reshape(8, 128).T)

    A = "L0_attn_a_"
    common["nw_col"] = col8(ins[A + "norm_in_w"])
    common["cb_col"] = col8(ins[A + "conv_b"])
    cw = np.asarray(ins[A + "conv_w"], np.float32)[:, 0, :].reshape(8, 128, 4)
    common["cw_col"] = _f32(cw.transpose(1, 0, 2).reshape(128, 32))
    common["ab_col"] = col8(ins[A + "alpha_b"])
    common["hnw_col"] = col8(np.asarray(ins[A + "head_norm_w"], np.float32).reshape(-1))
    common["fnw_col"] = col8(ins[A + "ff_norm_w"])
    common["u2b_col"] = col8(ins[A + "u2_b"])
    common["u1b_col"] = _f32(np.asarray(ins[A + "u1_b"], np.float32).reshape(64, 1))
    common["bb_col"] = _f32(np.asarray(ins[A + "beta_b"], np.float32).reshape(16, 1))
    for nm in ["q_w", "k_w", "v_w", "alpha_w", "out_w", "u1_w", "u2_w",
               "ff_w1", "ff_w3", "ff_w2", "beta_w"]:
        common["a_" + nm] = _f32(ins[A + nm])
    B = "L0_attn_b_"
    for nm in ["Wq", "Wk", "Wv", "Wo"]:
        common["b_" + nm] = _f32(ins[B + nm])

    def colb(v):
        w = np.asarray(v, np.float32).reshape(2, 4, 128)
        return _f32(w.transpose(2, 0, 1).reshape(128, 8))

    common["wg_col"] = colb(ins[B + "W_gamma"])
    common["bg_col"] = colb(ins[B + "b_gamma"])
    common["wl_col"] = colb(ins[B + "W_lambda"])
    common["bl_col"] = colb(ins[B + "b_lambda"])

    def bcast_tm(v):
        w = np.asarray(v, np.float32).reshape(1, 1024)
        return _f32(np.repeat(w, 128, axis=0))

    common["wg_bc"] = bcast_tm(ins[B + "W_gamma"])
    common["bg_bc"] = bcast_tm(ins[B + "b_gamma"])
    common["wl_bc"] = bcast_tm(ins[B + "W_lambda"])
    common["bl_bc"] = bcast_tm(ins[B + "b_lambda"])
    Fp = "L0_ffn_"
    for nm in ["Wr", "Wg", "Wu", "Wd", "Sg", "Su", "Sd"]:
        common["f_" + nm] = _f32(ins[Fp + nm])
    x = np.asarray(ins[A + "x"], np.float32)
    return [dict(common, x=_f32(x[b].T)) for b in range(8)]


class Sem:
    def __init__(self, handle):
        self.h = handle
        self.n = 0

    def inc(self, instr, k=1):
        instr.then_inc(self.h, k)
        self.n += k
        return instr


class Builder:

    def __init__(self, nc, stage, dumps):
        self.nc = nc
        self.stage = stage
        self.dumps = dumps
        self.marks = {}

    def mk(self, name, sem):
        self.marks[name] = (sem, sem.n)

    def w(self, eng, name):
        if name not in self.marks:
            if ".-" in name or name.endswith(".-1") or ".-1." in name:
                return
            raise KeyError(f"wait on missing mark: {name}")
        sem, n = self.marks[name]
        if sem in getattr(self, "dma_sems", ()):
            n = sem.n
        if n > 0:
            eng.wait_ge(sem.h, n)

    def bar(self, eng):
        for s in self.sems:
            if s.n > 0:
                eng.wait_ge(s.h, s.n)

    def fence(self, eng, sem):
        if sem.n > 0:
            eng.wait_ge(sem.h, sem.n)


def build(stage=99, dumps=()):
    nc = bass.Bass("TRN2", target_bir_lowering=False)
    bld = Builder(nc, stage, dumps)
    dt = nc.dram_tensor

    X = dt("x", [D, T], F32, kind="ExternalInput").ap()
    W = {}
    for nm, shp in [("a_q_w", [D, D]), ("a_k_w", [D, D]), ("a_v_w", [D, D]),
                    ("a_alpha_w", [D, D]), ("a_out_w", [D, D]),
                    ("a_beta_w", [D, 16]), ("a_u1_w", [D, 64]), ("a_u2_w", [64, D]),
                    ("a_ff_w1", [D, FF]), ("a_ff_w3", [D, FF]), ("a_ff_w2", [FF, D]),
                    ("b_Wq", [D, D]), ("b_Wk", [D, D]), ("b_Wv", [D, D]), ("b_Wo", [D, D]),
                    ("f_Wr", [D, NE]), ("f_Wg", [NE, D, DEXP]), ("f_Wu", [NE, D, DEXP]),
                    ("f_Wd", [NE, DEXP, D]), ("f_Sg", [D, DEXP]), ("f_Su", [D, DEXP]),
                    ("f_Sd", [DEXP, D])]:
        W[nm] = dt(nm, shp, F32, kind="ExternalInput").ap()
    C = {}
    for nm, shp, d_ in [("id128", [128, 128], BF16), ("ones_col", [128, 1], BF16),
                        ("b2", [128, 2], BF16), ("maskA", [128, 128], BF16),
                        ("maskB", [CB, CB], BF16), ("rbT", [CB, CB], BF16),
                        ("mcol_hi", [128, 1], BF16), ("mcol_lo", [128, 1], BF16),
                        ("nw_col", [128, 8], F32), ("cb_col", [128, 8], F32),
                        ("cw_col", [128, 32], F32), ("ab_col", [128, 8], F32),
                        ("hnw_col", [128, 8], F32), ("fnw_col", [128, 8], F32),
                        ("u2b_col", [128, 8], F32), ("u1b_col", [64, 1], F32),
                        ("bb_col", [16, 1], F32), ("wg_col", [128, 8], F32),
                        ("bg_col", [128, 8], F32), ("wl_col", [128, 8], F32),
                        ("bl_col", [128, 8], F32), ("wg_bc", [128, 1024], F32),
                        ("bg_bc", [128, 1024], F32), ("wl_bc", [128, 1024], F32),
                        ("bl_bc", [128, 1024], F32)]:
        C[nm] = dt(nm, shp, d_, kind="ExternalInput").ap()
    OUT = dt("out", [D, T], F32, kind="ExternalOutput").ap()

    def scr(nm, shp, d_):
        kind = "ExternalOutput" if nm in dumps else "Internal"
        return dt(nm, shp, d_, kind=kind).ap()

    qn_d = scr("qn_d", [D, T], BF16)
    kn_d = scr("kn_d", [D, T], BF16)
    vtm_d = scr("vtm_d", [T, D], BF16)
    L_d = scr("L_d", [D, T + 64], F32)
    btm_d = scr("btm_d", [T, 16], F32)
    gate_d = scr("gate_d", [D, T], BF16)
    row_d = scr("row_d", [64, T], F32)
    y16_d = scr("y16_d", [D, T], BF16)
    wbT_d = scr("wbT_d", [4, D, D], BF16)
    gates_d = scr("gates_d", [NE, T], F32)
    wbar_d = scr("wbar_d", [2, T, 512], BF16)
    vtmB_d = scr("vtmB_d", [2, T, 512], BF16)
    grow_d = scr("grow_d", [64, 1024], F32)
    ep_d = scr("ep_d", [2, T, 512], BF16)
    ytm_d = scr("ytm_d", [2, T, 512], BF16)
    vfm_d = scr("vfm_d", [D, T], BF16)
    dbg_d = scr("dbg_d", [128, 8192], F32)

    Xv = X.rearrange("(c p) t -> p c t", p=128)
    OUTv = OUT.rearrange("(c p) t -> p c t", p=128)
    qn_v = qn_d.rearrange("(c p) t -> p c t", p=128)
    kn_v = kn_d.rearrange("(c p) t -> p c t", p=128)
    L_v = L_d.rearrange("(c p) t -> p c t", p=128)
    gate_v = gate_d.rearrange("(c p) t -> p c t", p=128)
    y16_v = y16_d.rearrange("(c p) t -> p c t", p=128)

    es = contextlib.ExitStack()
    with es:
        sbuf = lambda nm, shp, d_: es.enter_context(nc.sbuf_tensor(nm, shp, d_))
        XR = sbuf("XR", [128, DCH * T], F32)
        A16 = sbuf("A16", [128, DCH * T], BF16)
        B16 = sbuf("B16", [128, DCH * T], BF16)
        C16 = sbuf("C16", [128, 8192], BF16)
        WBs = sbuf("WBs", [128, 12288], BF16)
        FS = sbuf("FS", [128, 4096], F32)
        WBC = sbuf("WBC", [128, 1536], F32)
        SS = sbuf("SS", [128, 4096], F32)
        KON = sbuf("KON", [128, 136], BF16)
        MKA = sbuf("MKA", [128, 128], BF16)
        MKB = sbuf("MKB", [128, 128], BF16)
        RT16 = sbuf("RT16", [128, 128], BF16)
        CCc = sbuf("CCc", [128, 128], F32)
        psum = lambda nm, shp, d_: es.enter_context(nc.psum_tensor(nm, shp, d_))
        PB = [psum(f"pb{i}", [128, 512], F32) for i in range(7)]
        PT = psum("pt16", [128, 1024], BF16)

        SP = Sem(es.enter_context(nc.semaphore("sp")))
        GP = Sem(es.enter_context(nc.semaphore("gp")))
        AC = Sem(es.enter_context(nc.semaphore("ac")))
        DV = Sem(es.enter_context(nc.semaphore("dv")))
        PE = Sem(es.enter_context(nc.semaphore("pe")))
        SPL = Sem(es.enter_context(nc.semaphore("spl")))
        BCS = Sem(es.enter_context(nc.semaphore("bcs")))
        GL = Sem(es.enter_context(nc.semaphore("gl")))
        WS = [Sem(es.enter_context(nc.semaphore("ws0"))),
              Sem(es.enter_context(nc.semaphore("ws1"))),
              Sem(es.enter_context(nc.semaphore("ws2")))]
        WT = [Sem(es.enter_context(nc.semaphore("wt0"))),
              Sem(es.enter_context(nc.semaphore("wt1"))),
              Sem(es.enter_context(nc.semaphore("wt2")))]
        WS2 = [Sem(es.enter_context(nc.semaphore("wsb0"))),
               Sem(es.enter_context(nc.semaphore("wsb1")))]
        bld.sems = [SP, GP, AC, DV, PE, SPL, BCS, GL] + WS + WT + WS2
        bld.dma_sems = (SP, GP, SPL, BCS, GL, WS[0], WS[1], WS[2], WT[0], WT[1], WT[2],
                        WS2[0], WS2[1])
        block = es.enter_context(nc.Block())
        mk, w_, bar = bld.mk, bld.w, bld.bar

        def fm(pool, c, t0=0, t1=None):
            t1 = T if t1 is None else t1
            return pool[:, c * T + t0: c * T + t1]

        id16 = KON[:, 0:128]
        ones16 = KON[:, 128:129]
        b2c = KON[:, 129:131]
        mhi = KON[:, 131:132]
        mlo = KON[:, 132:133]
        ccbase = {"nw": 0, "cb": 8, "ab": 16, "hnw": 24, "fnw": 32, "u2b": 40,
                  "wg": 48, "bg": 56, "wl": 64, "bl": 72, "cw": 80, "u1b": 112, "bb": 113}

        def ccol(name, i=0):
            b = ccbase[name] + i
            return CCc[:, b:b + 1]

        @block.sync
        def _(q):
            SP.inc(q.dma_start(out=XR[:].rearrange("p (c t) -> p c t", c=DCH), in_=Xv), 16)
            SP.inc(q.dma_start(out=KON[:, 0:128], in_=C["id128"]), 16)
            SP.inc(q.dma_start(out=KON[:, 128:129], in_=C["ones_col"]), 16)
            SP.inc(q.dma_start(out=KON[:, 129:131], in_=C["b2"]), 16)
            SP.inc(q.dma_start(out=KON[:, 131:132], in_=C["mcol_hi"]), 16)
            SP.inc(q.dma_start(out=KON[:, 132:133], in_=C["mcol_lo"]), 16)
            SP.inc(q.dma_start(out=MKA[:], in_=C["maskA"]), 16)
            SP.inc(q.dma_start(out=MKB[:], in_=C["maskB"]), 16)
            SP.inc(q.dma_start(out=RT16[:], in_=C["rbT"]), 16)
            for i, nm in enumerate(["nw_col", "cb_col", "ab_col", "hnw_col",
                                    "fnw_col", "u2b_col", "wg_col", "bg_col",
                                    "wl_col", "bl_col"]):
                SP.inc(q.dma_start(out=CCc[:, 8 * i:8 * i + 8], in_=C[nm]), 16)
            SP.inc(q.dma_start(out=CCc[:, 80:112], in_=C["cw_col"]), 16)
            SP.inc(q.dma_start(out=CCc[0:64, 112:113], in_=C["u1b_col"]), 16)
            SP.inc(q.dma_start(out=CCc[0:16, 113:114], in_=C["bb_col"]), 16)

        @block.vector
        def _(v):
            v.memset(FS[:, 0:64], 0.0)
            v.memset(CCc[:, 114:115], 1e-5)
            v.memset(CCc[:, 115:116], 1e-6)
            DV.inc(v.memset(CCc[:, 116:117], 1e-5 * 1e-6))

        @block.gpsimd
        def _(g):
            bar(g)
            for c in range(DCH):
                GP.inc(g.dma_start(out=L_v[:, c, 0:64], in_=FS[:, 0:64]), 16)
            for i, nm in enumerate(["b_Wq", "b_Wk", "b_Wv", "b_Wo"]):
                GP.inc(g.dma_start(out=wbT_d[i], in_=W[nm]), 16)
            GP.inc(g.dma_start(
                out=WBs[:, 11264:11776].rearrange("p (k n) -> p k n", k=8),
                in_=W["a_u1_w"].rearrange("(k p) n -> p k n", p=128)), 16)
            GP.inc(g.dma_start(out=WBs[0:64, 10240:11264], in_=W["a_u2_w"]), 16)
            GP.inc(g.dma_start(
                out=WBs[:, 11776:11904].rearrange("p (k n) -> p k n", k=8),
                in_=W["a_beta_w"].rearrange("(k p) n -> p k n", p=128)), 16)
        u1w16 = WBs[:, 11264:11776]
        u2w16 = WBs[0:64, 10240:11264]
        bw16 = WBs[:, 11776:11904]
        ws = [WBs[:, 0:4096], WBs[:, 4096:8192], WBs[:, 8192:12288]]

        def load_whalf(g, wap, half, slot):
            WS[slot].inc(g.dma_start(
                out=ws[slot].rearrange("p (k n) -> p k n", k=8),
                in_=wap.rearrange("(k p) n -> p k n", p=128)[:, :, half * 512:(half + 1) * 512]), 16)

        def wblk(slot, k, mo4):
            return ws[slot][:, k * 512 + mo4 * 128: k * 512 + mo4 * 128 + 128]

        def stats_rows(tag, src_pool, row_x, row_s, want_mean=True):
            def sqs(c):
                return C16[:, c * 512:(c + 1) * 512]

            for tt in range(4):
                @block.scalar
                def _(s, tt=tt):
                    if tt == 0:
                        bar(s)
                    for c in range(DCH):
                        w_(s, f"{tag}.mm.{tt-1}.{c}")
                        AC.inc(s.activation(sqs(c), fm(src_pool, c, tt * TT, (tt + 1) * TT), AF.Square))
                        mk(f"{tag}.sq.{tt}.{c}", AC)

                @block.tensor
                def _(t, tt=tt):
                    w_(t, f"{tag}.rows.{tt-1}")
                    for c in range(DCH):
                        w_(t, f"{tag}.sq.{tt}.{c}")
                        if want_mean:
                            t.matmul(PB[4][0:1, :], ones16, fm(src_pool, c, tt * TT, (tt + 1) * TT),
                                     start=(c == 0), stop=(c == DCH - 1))
                        PE.inc(t.matmul(PB[5][0:1, :], ones16, sqs(c),
                                        start=(c == 0), stop=(c == DCH - 1)))
                        mk(f"{tag}.mm.{tt}.{c}", PE)
                    mk(f"{tag}.mmtt.{tt}", PE)

                @block.scalar
                def _(s, tt=tt):
                    w_(s, f"{tag}.mmtt.{tt}")
                    if want_mean:
                        s.activation(FS[0:1, tt * 512:(tt + 1) * 512], PB[4][0:1, :], AF.Copy)
                    AC.inc(s.activation(FS[0:1, 2048 + tt * 512: 2048 + (tt + 1) * 512], PB[5][0:1, :], AF.Copy))
                    mk(f"{tag}.rows.{tt}", AC)

            @block.sync
            def _(q):
                w_(q, f"{tag}.rows.3")
                if want_mean:
                    SP.inc(q.dma_start(out=row_d[row_x:row_x + 1, :], in_=FS[0:1, 0:2048]), 16)
                SP.inc(q.dma_start(out=row_d[row_s:row_s + 1, :], in_=FS[0:1, 2048:4096]), 16)
                mk(f"{tag}.rsp", SP)

        def row_math(tag, emit_ops, src_rows, n_src, dst_rows, n_dst):
            @block.sync
            def _(q):
                bar(q)
                for i, r in enumerate(src_rows):
                    SP.inc(q.dma_start(out=FS[:, 2048 + 16 * i: 2048 + 16 * (i + 1)],
                                       in_=bass.AP(row_d.tensor, r * 2048, [[16, 128], [1, 16]])), 16)
                mk(f"{tag}.rl", SP)

            emit_ops()

            @block.sync
            def _(q):
                bar(q)
                for i, r in enumerate(dst_rows):
                    SP.inc(q.dma_start(out=bass.AP(row_d.tensor, r * 2048, [[16, 128], [1, 16]]),
                                       in_=FS[:, 2560 + 16 * i: 2560 + 16 * (i + 1)]), 16)
                mk(f"{tag}.msp", SP)

        def rep512(q, row, tt, dst):
            BCS.inc(q.dma_start(out=dst, in_=bass.AP(
                row_d.tensor, row * 2048 + tt * 512, [[0, 128], [1, 512]])), 16)

        def rep_head64(q, rowbase, c, tt, dst):
            BCS.inc(q.dma_start(out=dst, in_=bass.AP(
                row_d.tensor, (rowbase + 2 * c) * 2048 + tt * 512,
                [[2048, 2], [0, 64], [1, 512]])), 16)

        if stage >= 1:
            @block.vector
            def _(v):
                bar(v)
                for c in range(DCH):
                    DV.inc(v.tensor_copy(fm(B16, c), fm(XR, c)))

            stats_rows("ln1", B16, 0, 1, want_mean=True)

            def ln1_math():
                fn = bld.fence

                @block.vector
                def _(v):
                    bar(v)
                    DV.inc(v.tensor_scalar(FS[:, 2560:2576], FS[:, 2048:2064], 1.0 / D, None, OP.mult))
                    DV.inc(v.tensor_scalar(FS[:, 2080:2096], FS[:, 2064:2080], 1.0 / D, None, OP.mult))
                    fn(v, DV)
                    DV.inc(v.tensor_tensor(FS[:, 2096:2112], FS[:, 2560:2576], FS[:, 2560:2576], OP.mult))
                    fn(v, DV)
                    DV.inc(v.tensor_tensor(FS[:, 2112:2128], FS[:, 2080:2096], FS[:, 2096:2112], OP.subtract))
                    mk("ln1.var", DV)

                @block.scalar
                def _(s):
                    w_(s, "ln1.var")
                    AC.inc(s.activation(FS[:, 2128:2144], FS[:, 2112:2128], AF.Ln, bias=CCc[:, 114:115]))
                    fn(s, AC)
                    AC.inc(s.activation(FS[:, 2576:2592], FS[:, 2128:2144], AF.Exp, scale=-0.5))
                    AC.inc(s.activation(FS[:, 2144:2160], FS[:, 2112:2128], AF.Ln, scale=1.0 + 1e-6, bias=CCc[:, 116:117]))
                    fn(s, AC)
                    AC.inc(s.activation(FS[:, 2592:2608], FS[:, 2144:2160], AF.Exp, scale=-0.5))
                    mk("ln1.c", AC)
            row_math("ln1m", ln1_math, [0, 1], 2, [0, 2, 3], 3)

            mb, cpb, cfb = WBC[:, 0:512], WBC[:, 512:1024], WBC[:, 1024:1536]
            xst = C16[:, 1024:1536]
            s1t = C16[0:64, 1536:2048]
            gsl = [C16[:, 2048:2560], C16[:, 2560:3072]]
            t1s = FS[:, 2048:2560]

            for tt in range(4):
                @block.sync
                def _(q, tt=tt):
                    bar(q)
                    rep512(q, 0, tt, mb)
                    rep512(q, 2, tt, cpb)
                    rep512(q, 3, tt, cfb)
                    mk("p1.bc", BCS)

                for c in range(DCH):
                    @block.vector
                    def _(v, tt=tt, c=c):
                        if c == 0:
                            w_(v, "p1.bc")
                        w_(v, f"p1.u1.{tt}.{c-1}" if c > 0 else f"p1.u1.{tt-1}.{DCH-1}")
                        fn(v, DV) if False else bld.fence(v, DV)
                        DV.inc(v.tensor_tensor(t1s, XR[:, c * T + tt * TT: c * T + (tt + 1) * TT], mb, OP.subtract))
                        bld.fence(v, DV)
                        v.scalar_tensor_tensor(fm(A16, c, tt * TT, (tt + 1) * TT),
                                               t1s, ccol("nw", c), cfb, OP.mult, OP.mult)
                        DV.inc(v.tensor_tensor(xst, t1s, cpb, OP.mult))
                        mk(f"p1.xs.{tt}.{c}", DV)

                    @block.tensor
                    def _(t, tt=tt, c=c):
                        w_(t, f"p1.xs.{tt}.{c}")
                        if c == 0:
                            w_(t, f"p1.s1.{tt-1}")
                        PE.inc(t.matmul(PB[6][0:64, :], u1w16[:, c * 64:(c + 1) * 64], xst,
                                        start=(c == 0), stop=(c == DCH - 1)))
                        mk(f"p1.u1.{tt}.{c}", PE)

                @block.scalar
                def _(s, tt=tt):
                    w_(s, f"p1.u1.{tt}.{DCH-1}")
                    w_(s, f"p1.gmm.{tt-1}.{DCH-1}")
                    AC.inc(s.activation(s1t, PB[6][0:64, :], AF.Silu, bias=CCc[0:64, 112:113]))
                    mk(f"p1.s1.{tt}", AC)

                for mo in range(DCH):
                    @block.tensor
                    def _(t, tt=tt, mo=mo):
                        if mo == 0:
                            w_(t, f"p1.s1.{tt}")
                        w_(t, f"p1.gact.{tt}.{mo-2}" if mo >= 2 else f"p1.gact.{tt-1}.{6+mo}")
                        PE.inc(t.matmul(PB[4 + (mo % 2)][:, :], u2w16[:, mo * 128:(mo + 1) * 128],
                                        s1t, start=True, stop=True))
                        mk(f"p1.gmm.{tt}.{mo}", PE)

                    @block.scalar
                    def _(s, tt=tt, mo=mo):
                        w_(s, f"p1.gmm.{tt}.{mo}")
                        w_(s, f"p1.gsp.{tt}.{mo-2}" if mo >= 2 else f"p1.gsp.{tt-1}.{6+mo}")
                        AC.inc(s.activation(gsl[mo % 2], PB[4 + (mo % 2)][:, :], AF.Sigmoid, bias=ccol("u2b", mo)))
                        mk(f"p1.gact.{tt}.{mo}", AC)

                    @block.sync
                    def _(q, tt=tt, mo=mo):
                        w_(q, f"p1.gact.{tt}.{mo}")
                        SPL.inc(q.dma_start(out=gate_v[:, mo, tt * TT:(tt + 1) * TT], in_=gsl[mo % 2]), 16)
                        mk(f"p1.gsp.{tt}.{mo}", SPL)

            hcv = FS[:, 0:2048]
            for c in range(DCH):
                @block.vector
                def _(v, c=c):
                    if c == 0:
                        bar(v)
                    w_(v, f"p1.hs.{c-1}")
                    DV.inc(v.tensor_scalar(hcv, fm(A16, c), ccol("cw", 4 * c + 3), None, OP.mult))
                    bld.fence(v, DV)
                    DV.inc(v.scalar_tensor_tensor(hcv[:, 1:2048], fm(A16, c, 0, 2047), ccol("cw", 4 * c + 2), hcv[:, 1:2048], OP.mult, OP.add))
                    bld.fence(v, DV)
                    DV.inc(v.scalar_tensor_tensor(hcv[:, 2:2048], fm(A16, c, 0, 2046), ccol("cw", 4 * c + 1), hcv[:, 2:2048], OP.mult, OP.add))
                    bld.fence(v, DV)
                    DV.inc(v.scalar_tensor_tensor(hcv[:, 3:2048], fm(A16, c, 0, 2045), ccol("cw", 4 * c + 0), hcv[:, 3:2048], OP.mult, OP.add))
                    mk(f"p1.hc.{c}", DV)

                @block.scalar
                def _(s, c=c):
                    w_(s, f"p1.hc.{c}")
                    AC.inc(s.activation(fm(B16, c), hcv, AF.Silu, bias=ccol("cb", c)))
                    mk(f"p1.hs.{c}", AC)

        def proj_fm(tag, wap, tile_cb, n_mo=8):
            @block.gpsimd
            def _(g):
                bar(g)
                load_whalf(g, wap, 0, 0)
                mk(f"{tag}.w0", GP)
                load_whalf(g, wap, 1, 1)
                mk(f"{tag}.w1", GP)

            for mo in range(n_mo):
                half = mo // 4
                for tt in range(4):
                    @block.tensor
                    def _(t, mo=mo, tt=tt, half=half):
                        if mo == 0 and tt == 0:
                            bar(t)
                        if tt == 0:
                            bld.fence(t, WS[half])
                        w_(t, f"{tag}.ev.{mo}.{tt-1}" if tt > 0 else f"{tag}.ev.{mo-2}.3")
                        for k in range(DCH):
                            mmi = t.matmul(PB[mo % 2][:], wblk(half, k, mo % 4),
                                           fm(B16, k, tt * TT, (tt + 1) * TT),
                                           start=(k == 0), stop=(k == DCH - 1))
                        PE.inc(mmi)
                        mk(f"{tag}.mm.{mo}.{tt}", PE)
                    tile_cb(mo, tt)

        def qk_sink(tag):
            def cb(mo, tt):
                sqs = C16[:, tt * 512:(tt + 1) * 512]

                @block.vector
                def _(v, mo=mo, tt=tt, sqs=sqs):
                    w_(v, f"{tag}.mm.{mo}.{tt}")
                    DV.inc(v.tensor_copy(fm(A16, mo, tt * TT, (tt + 1) * TT), PB[mo % 2][:]))
                    mk(f"{tag}.ev.{mo}.{tt}", DV)

                @block.scalar
                def _(s, mo=mo, tt=tt, sqs=sqs):
                    w_(s, f"{tag}.ev.{mo}.{tt}")
                    w_(s, f"{tag}.nmm.{mo-1}.{tt}")
                    AC.inc(s.activation(sqs, fm(A16, mo, tt * TT, (tt + 1) * TT), AF.Square))
                    mk(f"{tag}.sq.{mo}.{tt}", AC)

                @block.tensor
                def _(t, mo=mo, tt=tt, sqs=sqs):
                    w_(t, f"{tag}.sq.{mo}.{tt}")
                    w_(t, f"{tag}.rev.{mo}.{tt-1}" if tt > 0 else f"{tag}.rev.{mo-1}.3")
                    PE.inc(t.matmul(PB[6][0:2, :], b2c, sqs, start=True, stop=True))
                    mk(f"{tag}.nmm.{mo}.{tt}", PE)

                @block.scalar
                def _(s, mo=mo, tt=tt):
                    w_(s, f"{tag}.nmm.{mo}.{tt}")
                    if tt == 0:
                        w_(s, f"{tag}.rsp.{mo-1}")
                    AC.inc(s.activation(FS[0:2, 2048 + tt * 512: 2048 + (tt + 1) * 512],
                                        PB[6][0:2, :], AF.Copy))
                    mk(f"{tag}.rev.{mo}.{tt}", AC)

                if tt == 3:
                    @block.sync
                    def _(q, mo=mo):
                        w_(q, f"{tag}.rev.{mo}.3")
                        SPL.inc(q.dma_start(out=row_d[4 + 2 * mo: 6 + 2 * mo, :], in_=FS[0:2, 2048:4096]), 16)
                        mk(f"{tag}.rsp.{mo}", SPL)
            return cb

        def qk_normmath(tag):
            def ops():
                @block.scalar
                def _(s):
                    bar(s)
                    AC.inc(s.activation(FS[:, 2816:3072], FS[:, 2048:2304], AF.Sqrt))
                    fn0 = bld.fence
                    fn0(s, AC)
                    AC.inc(s.activation(FS[:, 3072:3328], FS[:, 2816:3072], AF.Ln, bias=CCc[:, 115:116]))
                    fn0(s, AC)
                    AC.inc(s.activation(FS[:, 2560:2816], FS[:, 3072:3328], AF.Exp, scale=-1.0))
                    mk(f"{tag}.rmath", AC)

            @block.sync
            def _(q):
                bar(q)
                SP.inc(q.dma_start(out=FS[:, 2048:2304],
                                   in_=bass.AP(row_d.tensor, 4 * 2048, [[256, 128], [1, 256]])), 16)
            ops()

            @block.sync
            def _(q):
                bar(q)
                SP.inc(q.dma_start(out=bass.AP(row_d.tensor, 20 * 2048, [[256, 128], [1, 256]]),
                                   in_=FS[:, 2560:2816]), 16)
                mk(f"{tag}.rqsp", SP)

        def qk_apply(tag, dest_v):
            rqb = WBC[:, 0:512]
            qnt = [C16[:, 1024:1536], C16[:, 1536:2048]]

            for c in range(DCH):
                for tt in range(4):
                    i = c * 4 + tt

                    @block.sync
                    def _(q, c=c, tt=tt):
                        if c == 0 and tt == 0:
                            bar(q)
                        w_(q, f"{tag}.ap.{c}.{tt-1}" if tt > 0 else f"{tag}.ap.{c-1}.3")
                        rep_head64(q, 20, c, tt, rqb)
                        mk(f"{tag}.bc.{c}.{tt}", BCS)

                    @block.vector
                    def _(v, c=c, tt=tt, i=i):
                        w_(v, f"{tag}.bc.{c}.{tt}")
                        w_(v, f"{tag}.sp.{c}.{tt-2}" if tt >= 2 else f"{tag}.sp.{c-1}.{2+tt}")
                        DV.inc(v.tensor_tensor(qnt[i % 2], fm(A16, c, tt * TT, (tt + 1) * TT), rqb, OP.mult))
                        mk(f"{tag}.ap.{c}.{tt}", DV)

                    @block.sync
                    def _(q, c=c, tt=tt, i=i):
                        w_(q, f"{tag}.ap.{c}.{tt}")
                        SPL.inc(q.dma_start(out=dest_v[:, c, tt * TT:(tt + 1) * TT], in_=qnt[i % 2]), 16)
                        mk(f"{tag}.sp.{c}.{tt}", SPL)

        if stage >= 2:
            proj_fm("q", W["a_q_w"], qk_sink("q"))
            qk_normmath("q")
            qk_apply("q", qn_v)
            proj_fm("k", W["a_k_w"], qk_sink("k"))
            qk_normmath("k")
            qk_apply("k", kn_v)

            lsl_a = [FS[:, 2560:3072], FS[:, 3072:3584]]
            onct = WBC[:, 0:512]

            @block.vector
            def _(v):
                bar(v)
                DV.inc(v.memset(onct, 1.0))
                mk("al.ones", DV)

            def a_cb(mo, tt):
                asl = FS[:, 2048:2560]

                @block.scalar
                def _(s, mo=mo, tt=tt, asl=asl):
                    if mo == 0 and tt == 0:
                        w_(s, "al.ones")
                    w_(s, f"al.mm.{mo}.{tt}")
                    w_(s, f"al.sc.{mo}.{tt-1}" if tt > 0 else f"al.sc.{mo-1}.3")
                    AC.inc(s.activation(asl, PB[mo % 2][:], AF.Sigmoid, bias=ccol("ab", mo)))
                    mk(f"al.sg.{mo}.{tt}", AC)

                @block.vector
                def _(v, mo=mo, tt=tt, asl=asl):
                    w_(v, f"al.sg.{mo}.{tt}")
                    DV.inc(v.tensor_scalar(asl, asl, AFLOOR, None, OP.max))
                    mk(f"al.cl.{mo}.{tt}", DV)
                    mk(f"al.ev.{mo}.{tt}", DV)

                @block.scalar
                def _(s, mo=mo, tt=tt, asl=asl):
                    w_(s, f"al.cl.{mo}.{tt}")
                    AC.inc(s.activation(asl, asl, AF.Ln))
                    mk(f"al.ln.{mo}.{tt}", AC)

                @block.vector
                def _(v, mo=mo, tt=tt, asl=asl):
                    w_(v, f"al.ln.{mo}.{tt}")
                    w_(v, f"al.lsp.{mo}.{tt-2}" if tt >= 2 else f"al.lsp.{mo-1}.{2+tt}")
                    init = 0.0 if tt == 0 else lsl_a[(tt - 1) % 2][:, 511:512]
                    DV.inc(v.tensor_tensor_scan(lsl_a[tt % 2], onct, asl, init, OP.mult, OP.add))
                    mk(f"al.sc.{mo}.{tt}", DV)

                @block.sync
                def _(q, mo=mo, tt=tt):
                    w_(q, f"al.sc.{mo}.{tt}")
                    SPL.inc(q.dma_start(out=L_v[:, mo, 64 + tt * TT: 64 + (tt + 1) * TT], in_=lsl_a[tt % 2]), 16)
                    mk(f"al.lsp.{mo}.{tt}", SPL)
            proj_fm("al", W["a_alpha_w"], a_cb)

            @block.gpsimd
            def _(g):
                bar(g)
                load_whalf(g, W["a_v_w"], 0, 0)
                mk("v.w0", GP)
                load_whalf(g, W["a_v_w"], 1, 1)
                mk("v.w1", GP)

            vsl = [C16[:, 0:512], C16[:, 512:1024]]
            for i_ in range(32):
                nh, ww = i_ // 16, i_ % 16

                @block.tensor
                def _(t, i_=i_, nh=nh, ww=ww):
                    if i_ == 0:
                        bar(t)
                    if ww == 0:
                        bld.fence(t, WS[nh])
                    w_(t, f"v.ev.{i_-2}")
                    for k in range(DCH):
                        mmi = t.matmul(PB[i_ % 2][:], fm(B16, k, ww * 128, (ww + 1) * 128),
                                       ws[nh][:, k * 512:(k + 1) * 512],
                                       start=(k == 0), stop=(k == DCH - 1))
                    PE.inc(mmi)
                    mk(f"v.mm.{i_}", PE)

                @block.vector
                def _(v, i_=i_):
                    w_(v, f"v.mm.{i_}")
                    w_(v, f"v.sp.{i_-2}")
                    DV.inc(v.tensor_copy(vsl[i_ % 2], PB[i_ % 2][:]))
                    mk(f"v.ev.{i_}", DV)

                @block.sync
                def _(q, i_=i_, nh=nh, ww=ww):
                    w_(q, f"v.ev.{i_}")
                    SPL.inc(q.dma_start(out=vtm_d[ww * 128:(ww + 1) * 128, nh * 512:(nh + 1) * 512],
                                       in_=vsl[i_ % 2]), 16)
                    mk(f"v.sp.{i_}", SPL)

            bsl = C16[0:16, 3584:4096]
            bts = [FS[:, 3584:3600], FS[:, 3600:3616]]
            for tt in range(4):
                @block.tensor
                def _(t, tt=tt):
                    if tt == 0:
                        bar(t)
                    w_(t, f"b.sg.{tt-1}")
                    for k in range(DCH):
                        mmi = t.matmul(PB[6][0:16, :], bw16[:, k * 16:(k + 1) * 16],
                                       fm(B16, k, tt * TT, (tt + 1) * TT), start=(k == 0), stop=(k == DCH - 1))
                    PE.inc(mmi)
                    mk(f"b.mm.{tt}", PE)

                @block.scalar
                def _(s, tt=tt):
                    w_(s, f"b.mm.{tt}")
                    w_(s, f"b.tr.{tt-1}.3")
                    AC.inc(s.activation(bsl, PB[6][0:16, :], AF.Sigmoid, bias=CCc[0:16, 113:114]))
                    mk(f"b.sg.{tt}", AC)

                for blk in range(4):
                    i_ = tt * 4 + blk

                    @block.tensor
                    def _(t, tt=tt, blk=blk):
                        if blk == 0:
                            w_(t, f"b.sg.{tt}")
                        w_(t, f"b.ev.{tt}.{blk-1}" if blk > 0 else f"b.ev.{tt-1}.3")
                        PE.inc(t.transpose(PT[0:128, 0:16], bsl[:, blk * 128:(blk + 1) * 128], id16[0:16, 0:16]))
                        mk(f"b.tr.{tt}.{blk}", PE)

                    @block.vector
                    def _(v, tt=tt, blk=blk, i_=i_):
                        w_(v, f"b.tr.{tt}.{blk}")
                        w_(v, f"b.sp.{i_-2}")
                        DV.inc(v.tensor_copy(bts[i_ % 2], PT[:, 0:16]))
                        mk(f"b.ev.{tt}.{blk}", DV)

                    @block.sync
                    def _(q, tt=tt, blk=blk, i_=i_):
                        w_(q, f"b.ev.{tt}.{blk}")
                        SPL.inc(q.dma_start(out=btm_d[tt * 512 + blk * 128: tt * 512 + (blk + 1) * 128, :],
                                           in_=bts[i_ % 2]), 16)
                        mk(f"b.sp.{i_}", SPL)

        if stage >= 3:
            emit_scan_a(bld, block, dict(locals()))

        if stage >= 4:
            emit_tail_a(bld, block, dict(locals()))

        if stage >= 5:
            emit_ffn(bld, block, dict(locals()), round_=0)

        WGB = SS[:, 0:1024]
        BGB = SS[:, 1024:2048]
        WLB = SS[:, 2048:3072]
        BLB = SS[:, 3072:4096]

        if stage >= 6:
            emit_attn_b(bld, block, dict(locals()))

        if stage >= 7:
            emit_ffn(bld, block, dict(locals()), round_=1)

        @block.sync
        def _(q):
            bar(q)
            for c in range(DCH):
                SP.inc(q.dma_start(out=OUTv[:, c, :], in_=XR[:, c * T:(c + 1) * T]), 16)
            q.wait_ge(SP.h, SP.n)
    if os.environ.get("NO_CLUSTER", "0") != "1":
        cluster_blocks(nc)
    return nc


ENG_ORDER = ("EngineType.SP", "EngineType.Pool", "EngineType.PE",
             "EngineType.DVE", "EngineType.Activation")


def _block_engine(bb):
    for ins in bb.instructions:
        if type(ins).__name__ == "InstUnconditionalBranch":
            continue
        return str(getattr(ins, "engine", None))
    return "empty"


def cluster_blocks(nc):
    fn = nc.m.functions[0]
    bbs = list(fn.blocks)
    head, tail = bbs[0], bbs[-1]
    groups = {}
    for bb in bbs[1:-1]:
        groups.setdefault(_block_engine(bb), []).append(bb)
    new_mid = []
    for eng in ENG_ORDER:
        new_mid.extend(groups.pop(eng, []))
    for eng in sorted(groups):
        new_mid.extend(groups[eng])
    chain = [head] + new_mid + [tail]
    for i, bb in enumerate(chain[:-1]):
        ins = list(bb.instructions)
        assert type(ins[-1]).__name__ == "InstUnconditionalBranch", bb.name
        br = mybir.InstUnconditionalBranch(
            name=ins[-1].name, target=chain[i + 1].name, ins=[], outs=[],
            engine=ins[-1].engine)
        bb.instructions = ins[:-1] + [br]
    fn.blocks = chain
    return nc


def emit_scan_a(bld, block, env):
    g = lambda k: env[k]
    mk, w_, bar = bld.mk, bld.w, bld.bar
    SP, GP, AC, DV, PE = g("SP"), g("GP"), g("AC"), g("DV"), g("PE")
    SPL, BCS, GL, WS = g("SPL"), g("BCS"), g("GL"), g("WS")
    A16, B16, C16, WBs, FS, SS = g("A16"), g("B16"), g("C16"), g("WBs"), g("FS"), g("SS")
    PB, PT = g("PB"), g("PT")
    MKA, id16 = g("MKA"), g("id16")
    qn_v, kn_v, L_v = g("qn_v"), g("kn_v"), g("L_v")
    vtm_d, btm_d = g("vtm_d"), g("btm_d")
    fm = g("fm")
    OPx = OP
    import os as _os2

    NG = T // 256
    vtmg = WBs[:, 0:4096]
    nb_t = WBs[:, 4224:4352]
    Zp = [WBs[:, 4352 + i * 64: 4352 + (i + 1) * 64] for i in range(6)]
    ntr = WBs[:, 4736:4800]
    Wt_ = WBs[:, 5248:5760]
    yc = [WBs[:, 5760:6272], WBs[:, 6272:6784]]
    Uc = WBs[:, 6784:7296]
    khf = WBs[:, 9344:9856]
    khT = WBs[:, 9856:10880]
    zps = WBs[:, 10880:11392]
    Lg = lambda ch, a, b: FS[:, ch * 257 + a: ch * 257 + b]
    cmg = FS[:, 2056:2088]
    ceg = FS[:, 2088:2120]
    nmg = FS[:, 2120:2152]
    STT = FS[:, 2184:2696]
    btg = FS[:, 2696:2760]

    def apx(sl, *dims):
        return bass.AP(sl.tensor, sl.offset, [sl.ap[0]] + list(dims))

    @block.vector
    def _(v):
        bar(v)
        DV.inc(v.memset(STT, 0.0))
        mk("sa.st.-1.7", DV)

    for gi in range(NG):
        t0 = gi * 256

        @block.sync
        def _(q, gi=gi, t0=t0):
            if gi == 0:
                bar(q)
            else:
                w_(q, f"sa.grpdone.{gi-1}")
            GL.inc(q.dma_start(out=C16[:, 0:2048].rearrange("p (c t) -> p c t", c=8),
                               in_=qn_v[:, :, t0:t0 + 256]), 16)
            GL.inc(q.dma_start(out=C16[:, 2048:4096].rearrange("p (c t) -> p c t", c=8),
                               in_=kn_v[:, :, t0:t0 + 256]), 16)
            GL.inc(q.dma_start(out=FS[:, 0:2056].rearrange("p (c t) -> p c t", c=8),
                               in_=L_v[:, :, 64 + t0 - 1: 64 + t0 + 256]), 16)
            for half in range(2):
                o = half * 64
                GL.inc(q.dma_start(out=vtmg[o:o + 64, :].rearrange("p (c n) -> p c n", c=4),
                                   in_=bass.AP(vtm_d.tensor, t0 * 1024,
                                               [[1024, 64], [65536, 4], [1, 1024]])), 16)
                GL.inc(q.dma_start(out=btg[o:o + 64, :].rearrange("p (c n) -> p c n", c=4),
                                   in_=bass.AP(btm_d.tensor, t0 * 16,
                                               [[16, 64], [1024, 4], [1, 16]])), 16)
            mk(f"sa.ld.{gi}", GL)

        @block.vector
        def _(v, gi=gi):
            w_(v, f"sa.ld.{gi}")
            Lv_ = FS[:, 0:2056]
            midAP = apx(Lv_[:, 32:33], [257, 8], [64, 4])
            startAP = apx(Lv_[:, 0:1], [257, 8], [64, 4])
            endAP = apx(Lv_[:, 64:65], [257, 8], [64, 4])
            nmv = nmg.rearrange("p (a b) -> p a b", a=8)
            cmv = cmg.rearrange("p (a b) -> p a b", a=8)
            cev = ceg.rearrange("p (a b) -> p a b", a=8)
            v.tensor_scalar(nmv, midAP, -1.0, None, OPx.mult)
            v.tensor_tensor(cmv, midAP, startAP, OPx.subtract)
            DV.inc(v.tensor_tensor(cev, endAP, startAP, OPx.subtract))
            mk(f"sa.nm.{gi}", DV)

        @block.scalar
        def _(s, gi=gi):
            w_(s, f"sa.nm.{gi}")
            s.activation(cmg, cmg, AF.Exp)
            s.activation(ceg, ceg, AF.Exp)
            for ch in range(8):
                for c_ in range(4):
                    src = Lg(ch, 1 + c_ * 64, 1 + c_ * 64 + 64)
                    nmcol = nmg[:, ch * 4 + c_: ch * 4 + c_ + 1]
                    midcol = Lg(ch, 1 + c_ * 64 + 31, 1 + c_ * 64 + 32)
                    s.activation(A16[:, 4096 + ch * 256 + c_ * 64: 4096 + ch * 256 + c_ * 64 + 64],
                                 src, AF.Exp, bias=nmcol)
                    aci = s.activation(A16[:, 6144 + ch * 256 + c_ * 64: 6144 + ch * 256 + c_ * 64 + 64],
                                       src, AF.Exp, bias=midcol, scale=-1.0)
            AC.inc(aci)
            mk(f"sa.exp.{gi}", AC)

        @block.vector
        def _(v, gi=gi):
            w_(v, f"sa.exp.{gi}")
            for ch in range(8):
                v.tensor_tensor(A16[:, ch * 512: ch * 512 + 256],
                                C16[:, ch * 256: ch * 256 + 256],
                                A16[:, 4096 + ch * 256: 4096 + ch * 256 + 256], OPx.mult)
                v.tensor_tensor(A16[:, ch * 512 + 256: ch * 512 + 512],
                                C16[:, 2048 + ch * 256: 2048 + ch * 256 + 256],
                                A16[:, 4096 + ch * 256: 4096 + ch * 256 + 256], OPx.mult)
                wbi = v.tensor_tensor(C16[:, 4096 + ch * 256: 4096 + ch * 256 + 256],
                                      C16[:, 2048 + ch * 256: 2048 + ch * 256 + 256],
                                      A16[:, 6144 + ch * 256: 6144 + ch * 256 + 256], OPx.mult)
            DV.inc(wbi)
            mk(f"sa.wb.{gi}", DV)

        SA_LIM = int(_os2.environ.get("SA_LIM", "9999"))
        for c_ in range(4):
            ci = gi * 4 + c_
            if ci >= SA_LIM:
                mk(f"sa.st.{ci}.7", DV)
                continue

            @block.vector
            def _(v, gi=gi, c_=c_, ci=ci):
                w_(v, f"sa.wb.{gi}")
                w_(v, f"sa.st.{ci-1}.7")
                for ch2 in range(8):
                    cmAP = apx(cmg[:, ch2 * 4 + c_: ch2 * 4 + c_ + 1], [0, 64])
                    DV.inc(v.tensor_tensor(zps[:, ch2 * 64:(ch2 + 1) * 64],
                                           STT[:, ch2 * 64:(ch2 + 1) * 64], cmAP, OPx.mult))
                mk(f"sa.z0p.{ci}", DV)

            @block.tensor
            def _(t, c_=c_, ci=ci):
                w_(t, f"sa.z0p.{ci}")
                w_(t, f"sa.wt.{ci-1}")
                for h in range(16):
                    ch, o = h // 2, (h % 2) * 64
                    mmi = t.matmul(PB[2][o:o + 64, ch * 64: ch * 64 + 64],
                                   A16[o:o + 64, ch * 512 + 256 + c_ * 64: ch * 512 + 256 + c_ * 64 + 64],
                                   zps[o:o + 64, ch * 64:ch * 64 + 64],
                                   start=True, stop=True)
                PE.inc(mmi)
                mk(f"sa.rk.{ci}", PE)

            @block.vector
            def _(v, c_=c_, ci=ci):
                w_(v, f"sa.rk.{ci}")
                ev = bass.AP(vtmg.tensor, vtmg.offset + c_ * 1024,
                             [vtmg[0:64, 0:1].ap[0], [128, 8], [1, 64]])
                od = bass.AP(vtmg.tensor, vtmg[64:128, 0:1].offset + c_ * 1024 + 64,
                             [vtmg[64:128, 0:1].ap[0], [128, 8], [1, 64]])
                v.tensor_tensor(Wt_[0:64, :].rearrange("p (a b) -> p a b", a=8),
                                PB[2][0:64, :].rearrange("p (a b) -> p a b", a=8), ev, OPx.subtract)
                DV.inc(v.tensor_tensor(Wt_[64:128, :].rearrange("p (a b) -> p a b", a=8),
                                       PB[2][64:128, :].rearrange("p (a b) -> p a b", a=8), od, OPx.subtract))
                mk(f"sa.wt.{ci}", DV)

            for ch in range(8):
                @block.tensor
                def _(t, c_=c_, ci=ci, ch=ch):
                    w_(t, f"sa.wt.{ci}")
                    w_(t, f"sa.z1.{ci}.{ch-1}")
                    for half in range(2):
                        o = half * 64
                        rhs = A16[o:o + 64, ch * 512: ch * 512 + 512].rearrange(
                            "p (qk t) -> p qk t", qk=2)[:, :, c_ * 64:(c_ + 1) * 64]
                        mmi = t.matmul(PB[0][o:o + 64, 0:128],
                                       C16[o:o + 64, 4096 + ch * 256 + c_ * 64: 4096 + ch * 256 + c_ * 64 + 64],
                                       rhs, start=True, stop=True)
                    PE.inc(mmi)
                    mk(f"sa.pa.{ci}.{ch}", PE)

                @block.vector
                def _(v, c_=c_, ci=ci, ch=ch):
                    w_(v, f"sa.pa.{ci}.{ch}")
                    DV.inc(v.tensor_tensor(nb_t[:], PB[0][:, 0:128], MKA[:], OPx.mult))
                    bld.fence(v, DV)
                    for half in range(2):
                        o = half * 64
                        bcol = btg[o:o + 64, c_ * 16 + 2 * ch + half: c_ * 16 + 2 * ch + half + 1]
                        zli = v.tensor_scalar(Zp[0][o:o + 64, :], nb_t[o:o + 64, 64:128],
                                              bcol, None, OPx.mult)
                    DV.inc(zli)
                    mk(f"sa.z1.{ci}.{ch}", DV)

                for lev in range(1, 6):
                    @block.tensor
                    def _(t, ci=ci, ch=ch, lev=lev):
                        w_(t, f"sa.z1.{ci}.{ch}" if lev == 1 else f"sa.sqe.{ci}.{ch}.{lev-1}")
                        t.transpose(PT[0:64, 0:64], Zp[lev - 1][0:64, :], id16[0:64, 0:64])
                        PE.inc(t.transpose(PT[64:128, 0:64], Zp[lev - 1][64:128, :], id16[64:128, 64:128]))
                        mk(f"sa.tr.{ci}.{ch}.{lev}", PE)

                    @block.vector
                    def _(v, ci=ci, ch=ch, lev=lev):
                        w_(v, f"sa.tr.{ci}.{ch}.{lev}")
                        DV.inc(v.tensor_copy(ntr[:], PT[:, 0:64]))
                        mk(f"sa.tre.{ci}.{ch}.{lev}", DV)

                    @block.tensor
                    def _(t, ci=ci, ch=ch, lev=lev):
                        w_(t, f"sa.tre.{ci}.{ch}.{lev}")
                        for half in range(2):
                            o = half * 64
                            mmi = t.matmul(PB[1][o:o + 64, 0:64], ntr[o:o + 64, :],
                                           Zp[lev - 1][o:o + 64, :], start=True, stop=True)
                        PE.inc(mmi)
                        mk(f"sa.sq.{ci}.{ch}.{lev}", PE)

                    @block.vector
                    def _(v, ci=ci, ch=ch, lev=lev):
                        w_(v, f"sa.sq.{ci}.{ch}.{lev}")
                        DV.inc(v.tensor_copy(Zp[lev][:], PB[1][:, 0:64]))
                        mk(f"sa.sqe.{ci}.{ch}.{lev}", DV)

                for li, zi in enumerate([5, 4, 3, 2, 1, 0]):
                    @block.tensor
                    def _(t, ci=ci, ch=ch, li=li, zi=zi):
                        w_(t, f"sa.sqe.{ci}.{ch}.5" if li == 0 else f"sa.app.{ci}.{ch}.{li-1}")
                        src = Wt_ if li == 0 else yc[(li - 1) % 2]
                        for half in range(2):
                            o = half * 64
                            mmi = t.matmul(PB[4][o:o + 64, 0:64], Zp[zi][o:o + 64, :],
                                           src[o:o + 64, ch * 64: ch * 64 + 64],
                                           start=True, stop=True)
                        PE.inc(mmi)
                        mk(f"sa.appm.{ci}.{ch}.{li}", PE)

                    @block.vector
                    def _(v, ci=ci, ch=ch, li=li):
                        w_(v, f"sa.appm.{ci}.{ch}.{li}")
                        if li > 0:
                            w_(v, f"sa.app.{ci}.{ch}.{li-1}")
                        src = Wt_ if li == 0 else yc[(li - 1) % 2]
                        DV.inc(v.tensor_tensor(yc[li % 2][:, ch * 64: ch * 64 + 64],
                                               src[:, ch * 64: ch * 64 + 64],
                                               PB[4][:, 0:64], OPx.add))
                        mk(f"sa.app.{ci}.{ch}.{li}", DV)

                @block.vector
                def _(v, ci=ci, ch=ch, c_=c_):
                    w_(v, f"sa.app.{ci}.{ch}.5")
                    for half in range(2):
                        o = half * 64
                        bcol = btg[o:o + 64, c_ * 16 + 2 * ch + half: c_ * 16 + 2 * ch + half + 1]
                        uci = v.tensor_scalar(Uc[o:o + 64, ch * 64: ch * 64 + 64],
                                              yc[1][o:o + 64, ch * 64: ch * 64 + 64],
                                              bcol, None, OPx.mult)
                    DV.inc(uci)
                    mk(f"sa.uc.{ci}.{ch}", DV)

                @block.tensor
                def _(t, ci=ci, ch=ch, c_=c_):
                    w_(t, f"sa.uc.{ci}.{ch}")
                    w_(t, f"sa.ye.{ci}.{ch-1}")
                    for half in range(2):
                        o = half * 64
                        t.matmul(PB[5][o:o + 64, 0:64],
                                 zps[o:o + 64, ch * 64: ch * 64 + 64],
                                 A16[o:o + 64, ch * 512 + c_ * 64: ch * 512 + c_ * 64 + 64],
                                 start=True, stop=False)
                        mmi = t.matmul(PB[5][o:o + 64, 0:64],
                                       Uc[o:o + 64, ch * 64: ch * 64 + 64],
                                       nb_t[o:o + 64, 0:64],
                                       start=False, stop=True)
                    PE.inc(mmi)
                    mk(f"sa.out.{ci}.{ch}", PE)

                @block.vector
                def _(v, ci=ci, ch=ch):
                    w_(v, f"sa.out.{ci}.{ch}")
                    DV.inc(v.tensor_copy(fm(B16, ch, ci * 64, ci * 64 + 64), PB[5][:, 0:64]))
                    mk(f"sa.ye.{ci}.{ch}", DV)

                @block.vector
                def _(v, ci=ci, ch=ch, c_=c_):
                    w_(v, f"sa.ye.{ci}.{ch}")
                    ecol = A16[:, 4096 + ch * 256 + c_ * 64 + 63: 4096 + ch * 256 + c_ * 64 + 64]
                    ecb = bass.AP(ecol.tensor, ecol.offset, [ecol.ap[0], [0, 64]])
                    DV.inc(v.tensor_tensor(khf[:, ch * 64: ch * 64 + 64],
                                           C16[:, 4096 + ch * 256 + c_ * 64: 4096 + ch * 256 + c_ * 64 + 64],
                                           ecb, OPx.mult))
                    mk(f"sa.kh.{ci}.{ch}", DV)

                @block.tensor
                def _(t, ci=ci, ch=ch):
                    w_(t, f"sa.kh.{ci}.{ch}")
                    t.transpose(PT[0:64, 64:128], khf[0:64, ch * 64: ch * 64 + 64], id16[0:64, 0:64])
                    PE.inc(t.transpose(PT[64:128, 64:128], khf[64:128, ch * 64: ch * 64 + 64],
                                       id16[64:128, 64:128]))
                    mk(f"sa.ktr.{ci}.{ch}", PE)

                @block.vector
                def _(v, ci=ci, ch=ch):
                    w_(v, f"sa.ktr.{ci}.{ch}")
                    DV.inc(v.tensor_copy(khT[:, ch * 64: ch * 64 + 64], PT[:, 64:128]))
                    mk(f"sa.ktre.{ci}.{ch}", DV)

                @block.tensor
                def _(t, ci=ci, ch=ch):
                    w_(t, f"sa.ktre.{ci}.{ch}")
                    for half in range(2):
                        o = half * 64
                        mmi = t.matmul(PB[6][o:o + 64, 0:64], khT[o:o + 64, ch * 64: ch * 64 + 64],
                                       Uc[o:o + 64, ch * 64: ch * 64 + 64], start=True, stop=True)
                    PE.inc(mmi)
                    mk(f"sa.stm.{ci}.{ch}", PE)

                @block.vector
                def _(v, ci=ci, ch=ch, c_=c_):
                    w_(v, f"sa.stm.{ci}.{ch}")
                    cecol = ceg[:, ch * 4 + c_: ch * 4 + c_ + 1]
                    DV.inc(v.scalar_tensor_tensor(STT[:, ch * 64: ch * 64 + 64],
                                                  STT[:, ch * 64: ch * 64 + 64],
                                                  cecol, PB[6][:, 0:64], OPx.mult, OPx.subtract))
                    mk(f"sa.st.{ci}.{ch}", DV)
            mk(f"sa.st.{ci}.7", DV)
        mk(f"sa.grpdone.{gi}", DV)

    y16_v = g("y16_v")

    @block.sync
    def _(q):
        bar(q)
        for c in range(DCH):
            SP.inc(q.dma_start(out=y16_v[:, c, :], in_=fm(B16, c)), 16)
        mk("sa.ysp", SP)


def emit_tail_a(bld, block, env):
    g = lambda k: env[k]
    mk, w_, bar, fn = bld.mk, bld.w, bld.bar, bld.fence
    SP, GP, AC, DV, PE = g("SP"), g("GP"), g("AC"), g("DV"), g("PE")
    SPL, BCS, GL, WS = g("SPL"), g("BCS"), g("GL"), g("WS")
    XR, A16, B16, C16, WBs, FS = g("XR"), g("A16"), g("B16"), g("C16"), g("WBs"), g("FS")
    PB, PT = g("PB"), g("PT")
    W, CCc, row_d, gate_v = g("W"), g("CCc"), g("row_d"), g("gate_v")
    fm, ccol, b2c, ones16 = g("fm"), g("ccol"), g("b2c"), g("ones16")
    stats_rows, row_math = g("stats_rows"), g("row_math")
    rep512, rep_head64 = g("rep512"), g("rep_head64")
    load_whalf, ws, wblk = g("load_whalf"), g("ws"), g("wblk")

    for c in range(DCH):
        for tt in range(4):
            sqs = C16[:, tt * 512:(tt + 1) * 512]

            @block.scalar
            def _(s, c=c, tt=tt, sqs=sqs):
                if c == 0 and tt == 0:
                    bar(s)
                w_(s, f"hn.nmm.{c-1}.{tt}")
                AC.inc(s.activation(sqs, fm(B16, c, tt * TT, (tt + 1) * TT), AF.Square))
                mk(f"hn.sq.{c}.{tt}", AC)

            @block.tensor
            def _(t, c=c, tt=tt, sqs=sqs):
                w_(t, f"hn.sq.{c}.{tt}")
                w_(t, f"hn.rev.{c}.{tt-1}" if tt > 0 else f"hn.rev.{c-1}.3")
                PE.inc(t.matmul(PB[6][0:2, :], b2c, sqs, start=True, stop=True))
                mk(f"hn.nmm.{c}.{tt}", PE)

            @block.scalar
            def _(s, c=c, tt=tt):
                w_(s, f"hn.nmm.{c}.{tt}")
                if tt == 0:
                    w_(s, f"hn.rsp.{c-1}")
                AC.inc(s.activation(FS[0:2, 2048 + tt * 512: 2048 + (tt + 1) * 512],
                                    PB[6][0:2, :], AF.Copy))
                mk(f"hn.rev.{c}.{tt}", AC)

        @block.sync
        def _(q, c=c):
            w_(q, f"hn.rev.{c}.3")
            SPL.inc(q.dma_start(out=row_d[4 + 2 * c: 6 + 2 * c, :], in_=FS[0:2, 2048:4096]), 16)
            mk(f"hn.rsp.{c}", SPL)

    @block.sync
    def _(q):
        bar(q)
        SP.inc(q.dma_start(out=FS[:, 2048:2304],
                           in_=bass.AP(row_d.tensor, 4 * 2048, [[256, 128], [1, 256]])), 16)

    @block.scalar
    def _(s):
        bar(s)
        AC.inc(s.activation(FS[:, 2816:3072], FS[:, 2048:2304], AF.Ln, scale=1.0 / 64, bias=CCc[:, 115:116]))
        fn(s, AC)
        AC.inc(s.activation(FS[:, 2560:2816], FS[:, 2816:3072], AF.Exp, scale=-0.5))
        mk("hn.rm", AC)

    @block.sync
    def _(q):
        bar(q)
        SP.inc(q.dma_start(out=bass.AP(row_d.tensor, 20 * 2048, [[256, 128], [1, 256]]),
                           in_=FS[:, 2560:2816]), 16)
        mk("hn.rqsp", SP)

    rqb = g("WBC")[:, 0:512]
    for c in range(DCH):
        for tt in range(4):
            @block.sync
            def _(q, c=c, tt=tt):
                if c == 0 and tt == 0:
                    bar(q)
                w_(q, f"hn.ap.{c}.{tt-1}" if tt > 0 else f"hn.ap.{c-1}.3")
                rep_head64(q, 20, c, tt, rqb)
                mk(f"hn.bc.{c}.{tt}", BCS)

            @block.vector
            def _(v, c=c, tt=tt):
                w_(v, f"hn.bc.{c}.{tt}")
                DV.inc(v.scalar_tensor_tensor(fm(B16, c, tt * TT, (tt + 1) * TT),
                                              fm(B16, c, tt * TT, (tt + 1) * TT),
                                              ccol("hnw", c), rqb, OP.mult, OP.mult))
                mk(f"hn.ap.{c}.{tt}", DV)
    mk("hn.done", DV)

    @block.gpsimd
    def _(g2):
        bar(g2)
        load_whalf(g2, W["a_out_w"], 0, 0)
        load_whalf(g2, W["a_out_w"], 1, 1)

    gsl = [C16[:, 4096 + k * 512: 4096 + (k + 1) * 512] for k in range(2)]
    t1s = FS[:, 2048:2560]
    u1s = FS[:, 3072:3584]
    mb, cpb = g("WBC")[:, 512:1024], g("WBC")[:, 1024:1536]
    for tt in range(4):
        @block.sync
        def _(q, tt=tt):
            if tt == 0:
                bar(q)
            w_(q, f"op.yi.{DCH-1}.{tt-1}")
            rep512(q, 0, tt, mb)
            rep512(q, 2, tt, cpb)
            mk(f"op.bc.{tt}", BCS)

        for mo in range(DCH):
            @block.tensor
            def _(t, tt=tt, mo=mo):
                if tt == 0 and mo in (0, 4):
                    bld.fence(t, g("WS")[mo // 4])
                w_(t, f"op.yi.{mo-2}.{tt}" if mo >= 2 else (f"op.yi.{6+mo}.{tt-1}" if tt > 0 else "x.-1"))
                for k in range(DCH):
                    mmi = t.matmul(PB[mo % 2][:], wblk(mo // 4, k, mo % 4),
                                   fm(B16, k, tt * TT, (tt + 1) * TT),
                                   start=(k == 0), stop=(k == DCH - 1))
                PE.inc(mmi)
                mk(f"op.mm.{mo}.{tt}", PE)

            @block.sync
            def _(q, tt=tt, mo=mo):
                w_(q, f"op.gld.{mo-1}.{tt}" if mo > 0 else f"op.gld.{DCH-1}.{tt-1}")
                w_(q, f"op.yi.{mo-2}.{tt}" if mo >= 2 else (f"op.yi.{6+mo}.{tt-1}" if tt > 0 else "x.-1"))
                GL.inc(q.dma_start(out=gsl[mo % 2], in_=gate_v[:, mo, tt * TT:(tt + 1) * TT]), 16)
                mk(f"op.gld.{mo}.{tt}", GL)

            @block.vector
            def _(v, tt=tt, mo=mo):
                w_(v, f"op.mm.{mo}.{tt}")
                w_(v, f"op.gld.{mo}.{tt}")
                w_(v, f"op.bc.{tt}")
                fn(v, DV)
                DV.inc(v.tensor_tensor(t1s, XR[:, mo * T + tt * TT: mo * T + (tt + 1) * TT], mb, OP.subtract))
                fn(v, DV)
                DV.inc(v.tensor_tensor(u1s, t1s, cpb, OP.mult))
                fn(v, DV)
                DV.inc(v.tensor_tensor(t1s, PB[mo % 2][:], gsl[mo % 2], OP.mult))
                fn(v, DV)
                DV.inc(v.tensor_tensor(fm(A16, mo, tt * TT, (tt + 1) * TT), u1s, t1s, OP.add))
                mk(f"op.yi.{mo}.{tt}", DV)
    mk("op.done", DV)

    stats_rows("rms1", A16, 0, 1, want_mean=False)

    def rms1_math():
        @block.scalar
        def _(s):
            bar(s)
            AC.inc(s.activation(FS[:, 2144:2160], FS[:, 2048:2064], AF.Ln, scale=1.0 / D, bias=CCc[:, 115:116]))
            fn(s, AC)
            AC.inc(s.activation(FS[:, 2560:2576], FS[:, 2144:2160], AF.Exp, scale=-0.5))
            mk("rms1.c", AC)
    row_math("rms1m", rms1_math, [1], 1, [2], 1)

    c2b = g("WBC")[:, 0:512]
    for tt in range(4):
        @block.sync
        def _(q, tt=tt):
            if tt == 0:
                bar(q)
            w_(q, f"rms1.ap.{DCH-1}.{tt-1}")
            rep512(q, 2, tt, c2b)
            mk(f"rms1.bc.{tt}", BCS)

        for c in range(DCH):
            @block.vector
            def _(v, tt=tt, c=c):
                w_(v, f"rms1.bc.{tt}")
                DV.inc(v.scalar_tensor_tensor(fm(B16, c, tt * TT, (tt + 1) * TT),
                                              fm(A16, c, tt * TT, (tt + 1) * TT),
                                              ccol("fnw", c), c2b, OP.mult, OP.mult))
                mk(f"rms1.ap.{c}.{tt}", DV)

    w1s, w3s, w2s = WBs[:, 0:4096], WBs[:, 4096:8192], WBs[:, 8192:12288]
    sil = C16[:, 0:512]
    h2 = lambda fc: C16[:, 2048 + fc * 512: 2048 + (fc + 1) * 512]
    for fb in range(8):
        @block.gpsimd
        def _(g2, fb=fb):
            if fb == 0:
                bar(g2)
            else:
                w_(g2, f"ff.mm2.{fb-1}.3.{DCH-1}")
            WS[0].inc(g2.dma_start(
                out=w1s.rearrange("p (k n) -> p k n", k=8),
                in_=W["a_ff_w1"].rearrange("(k p) n -> p k n", p=128)[:, :, fb * 512:(fb + 1) * 512]), 16)
            WS[1].inc(g2.dma_start(
                out=w3s.rearrange("p (k n) -> p k n", k=8),
                in_=W["a_ff_w3"].rearrange("(k p) n -> p k n", p=128)[:, :, fb * 512:(fb + 1) * 512]), 16)
            WS[2].inc(g2.dma_start(
                out=w2s.rearrange("p (k n) -> p k n", k=4),
                in_=W["a_ff_w2"].rearrange("(k p) n -> p k n", p=128)[:, 4 * fb: 4 * fb + 4, :]), 16)
            mk(f"ff.w.{fb}", GP)

        for tt in range(4):
            for fc in range(4):
                @block.tensor
                def _(t, fb=fb, tt=tt, fc=fc):
                    if tt == 0 and fc == 0:
                        bld.fence(t, WS[0])
                        bld.fence(t, WS[1])
                    w_(t, f"ff.sil.{fb}.{tt}.{fc-1}" if fc > 0 else f"ff.sil.{fb}.{tt-1}.3")
                    w_(t, f"ff.gu.{fb}.{tt}.{fc-1}" if fc > 0 else f"ff.gu.{fb}.{tt-1}.3")
                    for k in range(DCH):
                        t.matmul(PB[0][:], w1s[:, k * 512 + fc * 128: k * 512 + fc * 128 + 128],
                                 fm(B16, k, tt * TT, (tt + 1) * TT), start=(k == 0), stop=(k == DCH - 1))
                    for k in range(DCH):
                        mmi = t.matmul(PB[1][:], w3s[:, k * 512 + fc * 128: k * 512 + fc * 128 + 128],
                                       fm(B16, k, tt * TT, (tt + 1) * TT), start=(k == 0), stop=(k == DCH - 1))
                    PE.inc(mmi)
                    mk(f"ff.mm13.{fb}.{tt}.{fc}", PE)

                @block.scalar
                def _(s, fb=fb, tt=tt, fc=fc):
                    w_(s, f"ff.mm13.{fb}.{tt}.{fc}")
                    AC.inc(s.activation(sil, PB[0][:], AF.Silu))
                    mk(f"ff.sil.{fb}.{tt}.{fc}", AC)

                @block.vector
                def _(v, fb=fb, tt=tt, fc=fc):
                    w_(v, f"ff.sil.{fb}.{tt}.{fc}")
                    w_(v, f"ff.mm2.{fb-1}.{tt}.{DCH-1}" if fb > 0 else "x.-1")
                    DV.inc(v.tensor_tensor(h2(fc), sil, PB[1][:], OP.mult))
                    mk(f"ff.gu.{fb}.{tt}.{fc}", DV)

            for mo in range(DCH):
                @block.tensor
                def _(t, fb=fb, tt=tt, mo=mo):
                    if mo == 0:
                        bld.fence(t, WS[2])
                    w_(t, f"ff.gu.{fb}.{tt}.3")
                    w_(t, f"ff.ev.{fb}.{tt}.{mo-2}" if mo >= 2 else (f"ff.ev.{fb}.{tt-1}.{6+mo}" if tt > 0 else f"ff.ev.{fb-1}.3.{6+mo}"))
                    for fc in range(4):
                        mmi = t.matmul(PB[2 + mo % 2][:], w2s[:, fc * 1024 + mo * 128: fc * 1024 + mo * 128 + 128],
                                       h2(fc), start=(fc == 0), stop=(fc == 3))
                    PE.inc(mmi)
                    mk(f"ff.mm2.{fb}.{tt}.{mo}", PE)

                @block.vector
                def _(v, fb=fb, tt=tt, mo=mo):
                    w_(v, f"ff.mm2.{fb}.{tt}.{mo}")
                    if fb == 0:
                        fn(v, DV)
                        DV.inc(v.tensor_tensor(t1s, PB[2 + mo % 2][:], fm(A16, mo, tt * TT, (tt + 1) * TT), OP.add))
                        fn(v, DV)
                        DV.inc(v.tensor_tensor(XR[:, mo * T + tt * TT: mo * T + (tt + 1) * TT],
                                               XR[:, mo * T + tt * TT: mo * T + (tt + 1) * TT], t1s, OP.add))
                    else:
                        DV.inc(v.tensor_tensor(XR[:, mo * T + tt * TT: mo * T + (tt + 1) * TT],
                                               XR[:, mo * T + tt * TT: mo * T + (tt + 1) * TT],
                                               PB[2 + mo % 2][:], OP.add))
                    mk(f"ff.ev.{fb}.{tt}.{mo}", DV)
    mk("tail.done", DV)


def emit_ffn(bld, block, env, round_):
    g = lambda k: env[k]
    mk, w_, bar, fn = bld.mk, bld.w, bld.bar, bld.fence
    R = f"f{round_}"
    SP, GP, AC, DV, PE = g("SP"), g("GP"), g("AC"), g("DV"), g("PE")
    SPL, BCS, GL, WS = g("SPL"), g("BCS"), g("GL"), g("WS")
    XR, A16, B16, C16, WBs, FS = g("XR"), g("A16"), g("B16"), g("C16"), g("WBs"), g("FS")
    PB, PT = g("PB"), g("PT")
    W, CCc, row_d = g("W"), g("CCc"), g("row_d")
    gates_d, id16 = g("gates_d"), g("id16")
    fm, ccol, ones16 = g("fm"), g("ccol"), g("ones16")
    stats_rows, row_math, rep512 = g("stats_rows"), g("row_math"), g("rep512")

    @block.vector
    def _(v):
        bar(v)
        for c in range(DCH):
            DV.inc(v.tensor_copy(fm(B16, c), fm(XR, c)))
    stats_rows(R + "ln", B16, 0, 1, want_mean=True)

    def lnm():
        @block.vector
        def _(v):
            bar(v)
            DV.inc(v.tensor_scalar(FS[:, 2560:2576], FS[:, 2048:2064], 1.0 / D, None, OP.mult))
            DV.inc(v.tensor_scalar(FS[:, 2080:2096], FS[:, 2064:2080], 1.0 / D, None, OP.mult))
            fn(v, DV)
            DV.inc(v.tensor_tensor(FS[:, 2096:2112], FS[:, 2560:2576], FS[:, 2560:2576], OP.mult))
            fn(v, DV)
            DV.inc(v.tensor_tensor(FS[:, 2112:2128], FS[:, 2080:2096], FS[:, 2096:2112], OP.subtract))
            mk(R + "ln.var", DV)

        @block.scalar
        def _(s):
            w_(s, R + "ln.var")
            AC.inc(s.activation(FS[:, 2128:2144], FS[:, 2112:2128], AF.Ln, bias=CCc[:, 114:115]))
            fn(s, AC)
            AC.inc(s.activation(FS[:, 2576:2592], FS[:, 2128:2144], AF.Exp, scale=-0.5))
            mk(R + "ln.c", AC)
    row_math(R + "lnm", lnm, [0, 1], 2, [0, 2], 2)

    mb, cpb = g("WBC")[:, 0:512], g("WBC")[:, 512:1024]
    t1s = FS[:, 2048:2560]
    for tt in range(4):
        @block.sync
        def _(q, tt=tt):
            if tt == 0:
                bar(q)
            w_(q, R + f"ln.ap.{DCH-1}.{tt-1}")
            rep512(q, 0, tt, mb)
            rep512(q, 2, tt, cpb)
            mk(R + f"ln.bc.{tt}", BCS)

        for c in range(DCH):
            @block.vector
            def _(v, tt=tt, c=c):
                w_(v, R + f"ln.bc.{tt}")
                fn(v, DV)
                DV.inc(v.tensor_tensor(t1s, XR[:, c * T + tt * TT: c * T + (tt + 1) * TT], mb, OP.subtract))
                fn(v, DV)
                DV.inc(v.tensor_tensor(fm(A16, c, tt * TT, (tt + 1) * TT), t1s, cpb, OP.mult))
                mk(R + f"ln.ap.{c}.{tt}", DV)
    mk(R + "h16", DV)

    WS2 = g("WS2")
    WBCp = g("WBC")

    @block.gpsimd
    def _(g2):
        bar(g2)
        GP.inc(g2.dma_start(out=WBs[:, 0:64].rearrange("p (k n) -> p k n", k=8),
                            in_=W["f_Wr"].rearrange("(k p) n -> p k n", p=128)), 16)
        mk(R + "wr", GP)

    gtile = C16[:, 8000:8008]
    stile = WBCp[:, 1152:1160]
    gstage = WBCp[0:8, 1024:1152]
    rc = lambda a, b: WBCp[:, 1152 + a: 1152 + b]
    for w in range(16):
        @block.tensor
        def _(t, w=w):
            if w == 0:
                w_(t, R + "wr")
                w_(t, R + "h16")
            w_(t, R + f"g.sm.{w-1}" if w > 0 else "x.-1")
            for k in range(DCH):
                mmi = t.matmul(PB[4][0:128, 0:8], fm(A16, k, w * 128, (w + 1) * 128),
                               WBs[:, k * 8:(k + 1) * 8], start=(k == 0), stop=(k == DCH - 1))
            PE.inc(mmi)
            mk(R + f"g.mm.{w}", PE)

        @block.vector
        def _(v, w=w):
            w_(v, R + f"g.mm.{w}")
            w_(v, R + f"g.tr.{w-1}" if w > 0 else "x.-1")
            DV.inc(v.tensor_reduce(rc(8, 9), PB[4][0:128, 0:8], AX, OP.max))
            fn(v, DV)
            DV.inc(v.tensor_scalar(stile, PB[4][0:128, 0:8], rc(8, 9), None, OP.subtract))
            mk(R + f"g.sm.{w}", DV)

        @block.scalar
        def _(s, w=w):
            w_(s, R + f"g.sm.{w}")
            AC.inc(s.activation(stile, stile, AF.Exp))
            mk(R + f"g.ex.{w}", AC)

        @block.vector
        def _(v, w=w):
            w_(v, R + f"g.ex.{w}")
            DV.inc(v.tensor_reduce(rc(9, 10), stile, AX, OP.max))
            fn(v, DV)
            DV.inc(v.tensor_scalar(rc(16, 24), stile, rc(9, 10), None, OP.is_ge))
            fn(v, DV)
            DV.inc(v.scalar_tensor_tensor(rc(24, 32), rc(16, 24), -1e30, stile, OP.mult, OP.add))
            fn(v, DV)
            DV.inc(v.tensor_reduce(rc(10, 11), rc(24, 32), AX, OP.max))
            fn(v, DV)
            DV.inc(v.tensor_tensor(rc(11, 12), rc(9, 10), rc(10, 11), OP.add))
            mk(R + f"g.t2.{w}", DV)

        @block.scalar
        def _(s, w=w):
            w_(s, R + f"g.t2.{w}")
            AC.inc(s.activation(rc(12, 13), rc(11, 12), AF.Ln))
            fn(s, AC)
            AC.inc(s.activation(rc(13, 14), rc(12, 13), AF.Exp, scale=-1.0))
            mk(R + f"g.rd.{w}", AC)

        @block.vector
        def _(v, w=w):
            w_(v, R + f"g.rd.{w}")
            DV.inc(v.tensor_tensor(rc(14, 15), rc(9, 10), rc(13, 14), OP.mult))
            DV.inc(v.tensor_tensor(rc(15, 16), rc(10, 11), rc(13, 14), OP.mult))
            fn(v, DV)
            DV.inc(v.tensor_scalar(rc(16, 24), stile, rc(9, 10), None, OP.is_ge))
            DV.inc(v.tensor_scalar(rc(24, 32), stile, rc(10, 11), None, OP.is_ge))
            fn(v, DV)
            DV.inc(v.tensor_tensor(rc(24, 32), rc(24, 32), rc(16, 24), OP.subtract))
            fn(v, DV)
            DV.inc(v.tensor_scalar(rc(16, 24), rc(16, 24), rc(14, 15), None, OP.mult))
            DV.inc(v.tensor_scalar(rc(24, 32), rc(24, 32), rc(15, 16), None, OP.mult))
            fn(v, DV)
            DV.inc(v.tensor_tensor(gtile, rc(16, 24), rc(24, 32), OP.add))
            mk(R + f"g.gt.{w}", DV)

        @block.tensor
        def _(t, w=w):
            w_(t, R + f"g.gt.{w}")
            w_(t, R + f"g.rev.{w-1}" if w > 0 else "x.-1")
            PE.inc(t.transpose(PT[0:8, 0:128], gtile[0:128, :], id16))
            mk(R + f"g.tr.{w}", PE)

        @block.scalar
        def _(s, w=w):
            w_(s, R + f"g.tr.{w}")
            if w > 0:
                w_(s, R + f"g.sp.{w-1}")
            AC.inc(s.activation(gstage, PT[0:8, 0:128], AF.Copy))
            mk(R + f"g.rev.{w}", AC)

        @block.sync
        def _(q, w=w):
            w_(q, R + f"g.rev.{w}")
            SPL.inc(q.dma_start(out=gates_d[:, w * 128:(w + 1) * 128], in_=gstage), 16)
            mk(R + f"g.sp.{w}", SPL)
    mk(R + "g.sp", SPL)

    FSbf = FS[:, 0:2048].bitcast(BF16)
    w1b = [WBs[:, 0:4096], FSbf]
    w3b = [WBs[:, 4096:8192], C16[:, 3072:7168]]
    w2s = WBs[:, 8192:12288]
    q_w1 = [WS[0], WS2[0]]
    q_w3 = [WS[1], WS2[1]]
    sil_b = [C16[:, 0:512], C16[:, 512:1024]]
    gu = lambda dc: C16[:, 1024 + dc * 512: 1024 + (dc + 1) * 512]
    gbank = [PB[0], PB[4]]
    ubank = [PB[1], PB[5]]
    wgb = g("WBC")[:, 0:512]

    NQ = 36
    exq = [(e, q_) for e in range(9) for q_ in range(4)]

    def wref(wq):
        e, q_ = exq[wq]
        return e, q_

    def ld_w13(g2, Q, first):
        e, q_ = exq[Q]
        gated = e < 8
        wg_ap = W["f_Wg"][e] if gated else W["f_Sg"]
        wu_ap = W["f_Wu"][e] if gated else W["f_Su"]
        p = Q % 2
        if first:
            if Q == 0:
                bar(g2)
        else:
            eW, qW = exq[Q - 2]
            w_(g2, R + f"x.mm13.{eW}.{qW}.3.3")
        q_w1[p].inc(g2.dma_start(
            out=w1b[p].rearrange("p (k n) -> p k n", k=8),
            in_=wg_ap.rearrange("(k p) n -> p k n", p=128)[:, :, q_ * 512:(q_ + 1) * 512]), 16)
        q_w3[p].inc(g2.dma_start(
            out=w3b[p].rearrange("p (k n) -> p k n", k=8),
            in_=wu_ap.rearrange("(k p) n -> p k n", p=128)[:, :, q_ * 512:(q_ + 1) * 512]), 16)
        mk(R + f"x.w13.{Q}", GP)

    def ld_w2(g2, Q, first):
        e, q_ = exq[Q]
        wd_ap = W["f_Wd"][e] if e < 8 else W["f_Sd"]
        if not first:
            eW, qW = exq[Q - 1]
            w_(g2, R + f"x.mm2.{eW}.{qW}.3.{DCH-1}")
        WS[2].inc(g2.dma_start(
            out=w2s.rearrange("p (k n) -> p k n", k=4),
            in_=wd_ap.rearrange("(k p) n -> p k n", p=128)[:, 4 * q_: 4 * q_ + 4, :]), 16)
        mk(R + f"x.w2.{Q}", GP)

    for Q in range(NQ):
        e, q_ = exq[Q]
        gated = e < 8
        p = Q % 2

        @block.gpsimd
        def _(g2, Q=Q):
            if Q == 0:
                ld_w13(g2, 0, True)
                ld_w13(g2, 1, True)
                ld_w2(g2, 0, True)
            else:
                if Q + 1 < NQ:
                    ld_w13(g2, Q + 1, False)
                ld_w2(g2, Q, False)

        for tt in range(4):
            @block.sync
            def _(q, e=e, q_=q_, tt=tt):
                if e < 8 and q_ == 0:
                    if e == 0 and tt == 0:
                        w_(q, R + "g.sp.15")
                    w_(q, R + f"x.he.{e-1}.3" if e > 0 else "x.-1")
                    if tt > 0:
                        w_(q, R + f"x.he.{e}.{tt-1}")
                    BCS.inc(q.dma_start(out=wgb, in_=bass.AP(
                        gates_d.tensor, e * T + tt * TT, [[0, 128], [1, 512]])), 16)
                    mk(R + f"x.gb.{e}.{q_}.{tt}", BCS)

            @block.vector
            def _(v, e=e, q_=q_, tt=tt):
                if e < 8 and q_ == 0:
                    w_(v, R + f"x.gb.{e}.{q_}.{tt}")
                    w_(v, R + f"x.heu.{e-1}.{tt}" if e > 0 else "x.-1")
                    for k in range(DCH):
                        DV.inc(v.tensor_tensor(fm(B16, k, tt * TT, (tt + 1) * TT),
                                               fm(A16, k, tt * TT, (tt + 1) * TT), wgb, OP.mult))
                    mk(R + f"x.he.{e}.{tt}", DV)

            for dc in range(4):
                G = (Q * 4 + tt) * 4 + dc
                dp = dc % 2

                @block.tensor
                def _(t, e=e, q_=q_, tt=tt, dc=dc, gated=gated, G=G, dp=dp, p=p, Q=Q):
                    if tt == 0 and dc == 0:
                        bld.fence(t, q_w1[p])
                        bld.fence(t, q_w3[p])
                    if G >= 2:
                        Gp = G - 2
                        eP, qP = exq[Gp // 16]
                        ttP, dcP = (Gp % 16) // 4, Gp % 4
                        w_(t, R + f"x.sil.{eP}.{qP}.{ttP}.{dcP}")
                        w_(t, R + f"x.gu.{eP}.{qP}.{ttP}.{dcP}")
                    if Q == 0 and tt == 0 and dc == 1:
                        w_(t, R + "g.sm.15")
                    if gated:
                        w_(t, R + f"x.he.{e}.{tt}")
                    for k in range(DCH):
                        t.matmul(gbank[dp][:], w1b[p][:, k * 512 + dc * 128: k * 512 + dc * 128 + 128],
                                 fm(A16, k, tt * TT, (tt + 1) * TT),
                                 start=(k == 0), stop=(k == DCH - 1))
                    for k in range(DCH):
                        mmi = t.matmul(ubank[dp][:], w3b[p][:, k * 512 + dc * 128: k * 512 + dc * 128 + 128],
                                       fm(B16, k, tt * TT, (tt + 1) * TT) if gated else fm(A16, k, tt * TT, (tt + 1) * TT),
                                       start=(k == 0), stop=(k == DCH - 1))
                    PE.inc(mmi)
                    mk(R + f"x.mm13.{e}.{q_}.{tt}.{dc}", PE)
                    if q_ == 3 and dc == 3:
                        mk(R + f"x.heu.{e}.{tt}", PE)

                @block.scalar
                def _(s, e=e, q_=q_, tt=tt, dc=dc, G=G, dp=dp):
                    w_(s, R + f"x.mm13.{e}.{q_}.{tt}.{dc}")
                    if G >= 2:
                        Gp = G - 2
                        eP, qP = exq[Gp // 16]
                        ttP, dcP = (Gp % 16) // 4, Gp % 4
                        w_(s, R + f"x.gu.{eP}.{qP}.{ttP}.{dcP}")
                    AC.inc(s.activation(sil_b[dp], gbank[dp][:], AF.Silu))
                    mk(R + f"x.sil.{e}.{q_}.{tt}.{dc}", AC)

                @block.vector
                def _(v, e=e, q_=q_, tt=tt, dc=dc, G=G, dp=dp, Q=Q):
                    w_(v, R + f"x.sil.{e}.{q_}.{tt}.{dc}")
                    TTg = Q * 4 + tt
                    if TTg >= 1:
                        eT, qT = exq[(TTg - 1) // 4]
                        w_(v, R + f"x.mm2g.{eT}.{qT}.{(TTg - 1) % 4}")
                    DV.inc(v.tensor_tensor(gu(dc), sil_b[dp], ubank[dp][:], OP.mult))
                    mk(R + f"x.gu.{e}.{q_}.{tt}.{dc}", DV)

            for mo in range(DCH):
                @block.tensor
                def _(t, e=e, q_=q_, tt=tt, mo=mo):
                    if mo == 0:
                        bld.fence(t, WS[2])
                        w_(t, R + f"x.gu.{e}.{q_}.{tt}.3")
                    w_(t, R + f"x.ev.{e}.{q_}.{tt}.{mo-2}" if mo >= 2 else (R + f"x.ev.{e}.{q_}.{tt-1}.{6+mo}" if tt > 0 else (R + f"x.ev.{e}.{q_-1}.3.{6+mo}" if q_ > 0 else (R + f"x.ev.{e-1}.3.3.{6+mo}" if e > 0 else "x.-1"))))
                    for fc in range(4):
                        mmi = t.matmul(PB[2 + mo % 2][:], w2s[:, fc * 1024 + mo * 128: fc * 1024 + mo * 128 + 128],
                                       gu(fc), start=(fc == 0), stop=(fc == 3))
                    PE.inc(mmi)
                    mk(R + f"x.mm2.{e}.{q_}.{tt}.{mo}", PE)
                    if mo == DCH - 1:
                        mk(R + f"x.mm2g.{e}.{q_}.{tt}", PE)

                @block.vector
                def _(v, e=e, q_=q_, tt=tt, mo=mo):
                    w_(v, R + f"x.mm2.{e}.{q_}.{tt}.{mo}")
                    DV.inc(v.tensor_tensor(XR[:, mo * T + tt * TT: mo * T + (tt + 1) * TT],
                                           XR[:, mo * T + tt * TT: mo * T + (tt + 1) * TT],
                                           PB[2 + mo % 2][:], OP.add))
                    mk(R + f"x.ev.{e}.{q_}.{tt}.{mo}", DV)
    mk(R + ".done", DV)


def emit_attn_b(bld, block, env):
    g = lambda k: env[k]
    mk, w_, bar, fn = bld.mk, bld.w, bld.bar, bld.fence
    SP, GP, AC, DV, PE = g("SP"), g("GP"), g("AC"), g("DV"), g("PE")
    SPL, BCS, GL, WS = g("SPL"), g("BCS"), g("GL"), g("WT")
    XR, A16, B16, C16, WBs, FS, SS = g("XR"), g("A16"), g("B16"), g("C16"), g("WBs"), g("FS"), g("SS")
    PB, PT = g("PB"), g("PT")
    W, CCc, row_d = g("W"), g("CCc"), g("row_d")
    wbT_d, wbar_d, vtmB_d, vfm_d, grow_d = g("wbT_d"), g("wbar_d"), g("vtmB_d"), g("vfm_d"), g("grow_d")
    MKB, RT16, id16, mhi, mlo = g("MKB"), g("RT16"), g("id16"), g("mhi"), g("mlo")
    fm, ccol = g("fm"), g("ccol")
    stats_rows, row_math, rep512 = g("stats_rows"), g("row_math"), g("rep512")
    ep_d = g("ep_d")
    ytm_d = g("ytm_d")

    @block.vector
    def _(v):
        bar(v)
        for c in range(DCH):
            DV.inc(v.tensor_copy(fm(B16, c), fm(XR, c)))
    stats_rows("bln", B16, 0, 1, want_mean=True)

    def lnm():
        @block.vector
        def _(v):
            bar(v)
            DV.inc(v.tensor_scalar(FS[:, 2560:2576], FS[:, 2048:2064], 1.0 / D, None, OP.mult))
            DV.inc(v.tensor_scalar(FS[:, 2080:2096], FS[:, 2064:2080], 1.0 / D, None, OP.mult))
            fn(v, DV)
            DV.inc(v.tensor_tensor(FS[:, 2096:2112], FS[:, 2560:2576], FS[:, 2560:2576], OP.mult))
            fn(v, DV)
            DV.inc(v.tensor_tensor(FS[:, 2112:2128], FS[:, 2080:2096], FS[:, 2096:2112], OP.subtract))
            mk("bln.var", DV)

        @block.scalar
        def _(s):
            w_(s, "bln.var")
            AC.inc(s.activation(FS[:, 2128:2144], FS[:, 2112:2128], AF.Ln, bias=CCc[:, 114:115]))
            fn(s, AC)
            AC.inc(s.activation(FS[:, 2576:2592], FS[:, 2128:2144], AF.Exp, scale=-0.5))
            mk("bln.c", AC)
    row_math("blnm", lnm, [0, 1], 2, [0, 2], 2)

    mb, cpb = g("WBC")[:, 0:512], g("WBC")[:, 512:1024]
    t1s = FS[:, 2048:2560]
    for tt in range(4):
        @block.sync
        def _(q, tt=tt):
            if tt == 0:
                bar(q)
            w_(q, f"bln.ap.{DCH-1}.{tt-1}")
            rep512(q, 0, tt, mb)
            rep512(q, 2, tt, cpb)
            mk(f"bln.bc.{tt}", BCS)

        for c in range(DCH):
            @block.vector
            def _(v, tt=tt, c=c):
                w_(v, f"bln.bc.{tt}")
                fn(v, DV)
                DV.inc(v.tensor_tensor(t1s, XR[:, c * T + tt * TT: c * T + (tt + 1) * TT], mb, OP.subtract))
                fn(v, DV)
                DV.inc(v.tensor_tensor(fm(A16, c, tt * TT, (tt + 1) * TT), t1s, cpb, OP.mult))
                mk(f"bln.ap.{c}.{tt}", DV)
    mk("bh16", DV)

    ws = g("ws")
    WGB, BGB, WLB, BLB = g("WGB"), g("BGB"), g("WLB"), g("BLB")
    Cin = g("C")

    @block.sync
    def _(q):
        bar(q)
        SP.inc(q.dma_start(out=WGB, in_=Cin["wg_bc"]), 16)
        SP.inc(q.dma_start(out=BGB, in_=Cin["bg_bc"]), 16)
        SP.inc(q.dma_start(out=WLB, in_=Cin["wl_bc"]), 16)
        SP.inc(q.dma_start(out=BLB, in_=Cin["bl_bc"]), 16)
        mk("bgl", SP)

    def load_whalfT(q, widx, half, slot):
        for k in range(DCH):
            WS[slot].inc(q.dma_start(
                out=ws[slot][:, k * 512:(k + 1) * 512],
                in_=wbT_d[widx][half * 512:(half + 1) * 512, k * 128:(k + 1) * 128],
                transpose=True), 16)

    def projT(tag, widx, dest_pool):
        @block.scalar
        def _(s):
            bar(s)
            load_whalfT(s, widx, 0, 0)
            load_whalfT(s, widx, 1, 1)
            mk(f"{tag}.wT", AC)

        for mo in range(DCH):
            for tt in range(4):
                @block.tensor
                def _(t, mo=mo, tt=tt):
                    if mo == 0 and tt == 0:
                        bld.fence(t, WS[0])
                        bld.fence(t, WS[1])
                    w_(t, f"{tag}.ev.{mo}.{tt-1}" if tt > 0 else f"{tag}.ev.{mo-2}.3")
                    half, m4 = mo // 4, mo % 4
                    for k in range(DCH):
                        mmi = t.matmul(PB[mo % 2][:], ws[half][:, k * 512 + m4 * 128: k * 512 + m4 * 128 + 128],
                                       fm(A16, k, tt * TT, (tt + 1) * TT),
                                       start=(k == 0), stop=(k == DCH - 1))
                    PE.inc(mmi)
                    mk(f"{tag}.mm.{mo}.{tt}", PE)

                @block.vector
                def _(v, mo=mo, tt=tt):
                    w_(v, f"{tag}.mm.{mo}.{tt}")
                    DV.inc(v.tensor_copy(fm(dest_pool, mo, tt * TT, (tt + 1) * TT), PB[mo % 2][:]))
                    mk(f"{tag}.ev.{mo}.{tt}", DV)
    projT("bq", 0, B16)

    vsl = [C16[:, 0:512], C16[:, 512:1024]]

    @block.scalar
    def _(s):
        bar(s)
        load_whalfT(s, 2, 0, 0)
        load_whalfT(s, 2, 1, 1)
        mk("bv.wT", AC)

    for mo in range(DCH):
        for tt in range(4):
            i_ = mo * 4 + tt

            @block.tensor
            def _(t, mo=mo, tt=tt, i_=i_):
                if mo == 0 and tt == 0:
                    bld.fence(t, WS[0])
                    bld.fence(t, WS[1])
                w_(t, f"bv.ev.{i_-2}")
                half, m4 = mo // 4, mo % 4
                for k in range(DCH):
                    mmi = t.matmul(PB[i_ % 2][:], ws[half][:, k * 512 + m4 * 128: k * 512 + m4 * 128 + 128],
                                   fm(A16, k, tt * TT, (tt + 1) * TT),
                                   start=(k == 0), stop=(k == DCH - 1))
                PE.inc(mmi)
                mk(f"bv.mm.{i_}", PE)

            @block.vector
            def _(v, i_=i_):
                w_(v, f"bv.mm.{i_}")
                w_(v, f"bv.sp.{i_-2}")
                DV.inc(v.tensor_copy(vsl[i_ % 2], PB[i_ % 2][:]))
                mk(f"bv.ev.{i_}", DV)

            @block.sync
            def _(q, mo=mo, tt=tt, i_=i_):
                w_(q, f"bv.ev.{i_}")
                SPL.inc(q.dma_start(out=vfm_d.rearrange("(c p) t -> p c t", p=128)[:, mo, tt * TT:(tt + 1) * TT],
                                    in_=vsl[i_ % 2]), 16)
                mk(f"bv.sp.{i_}", SPL)

    for nh in range(2):
        @block.scalar
        def _(s, nh=nh):
            bar(s)
            load_whalfT(s, 2, nh, 2)
            mk(f"bvt.wT.{nh}", AC)

        for w in range(16):
            i_ = nh * 16 + w

            @block.tensor
            def _(t, nh=nh, w=w, i_=i_):
                if w == 0:
                    bld.fence(t, WS[2])
                w_(t, f"bvt.ev.{i_-2}")
                for k in range(DCH):
                    mmi = t.matmul(PB[i_ % 2][:], fm(A16, k, w * 128, (w + 1) * 128),
                                   ws[2][:, k * 512:(k + 1) * 512],
                                   start=(k == 0), stop=(k == DCH - 1))
                PE.inc(mmi)
                mk(f"bvt.mm.{i_}", PE)

            @block.vector
            def _(v, i_=i_):
                w_(v, f"bvt.mm.{i_}")
                w_(v, f"bvt.sp.{i_-2}")
                DV.inc(v.tensor_copy(vsl[i_ % 2], PB[i_ % 2][:]))
                mk(f"bvt.ev.{i_}", DV)

            @block.sync
            def _(q, nh=nh, w=w, i_=i_):
                w_(q, f"bvt.ev.{i_}")
                SPL.inc(q.dma_start(out=vtmB_d[nh][w * 128:(w + 1) * 128, :], in_=vsl[i_ % 2]), 16)
                mk(f"bvt.sp.{i_}", SPL)

    for nh in range(2):
        @block.scalar
        def _(s, nh=nh):
            bar(s)
            load_whalfT(s, 1, nh, 2)
            mk(f"bk.wT.{nh}", AC)

        for w in range(16):
            i_ = nh * 16 + w
            ktl = C16[:, 1024:2048]
            sqt = C16[:, 2048:3072]
            knt = C16[:, 3072:3584]
            wbt = C16[:, 3584:4096]
            ept = C16[:, 4096:4608]
            lrt = FS[:, 2048:2560]
            rkc = FS[:, 2560:2562]

            @block.tensor
            def _(t, nh=nh, w=w, i_=i_):
                if w == 0:
                    bld.fence(t, WS[2])
                w_(t, f"bk.kn.{i_-1}")
                for k in range(DCH):
                    mmi = t.matmul(PB[0][:], fm(A16, k, w * 128, (w + 1) * 128),
                                   ws[2][:, k * 512:(k + 1) * 512],
                                   start=(k == 0), stop=(k == DCH - 1))
                PE.inc(mmi)
                mk(f"bk.mm.{i_}", PE)

            @block.scalar
            def _(s, nh=nh, i_=i_):
                w_(s, f"bk.mm.{i_}")
                w_(s, f"bk.wb.{i_-1}")
                AC.inc(s.activation(sqt[:, 0:512], PB[0][:], AF.Square))
                mk(f"bk.sq.{i_}", AC)

            @block.vector
            def _(v, nh=nh, i_=i_):
                w_(v, f"bk.sq.{i_}")
                DV.inc(v.tensor_reduce(rkc[:, 0:1], sqt[:, 0:512], AX, OP.add))
                mk(f"bk.ss.{i_}", DV)

            @block.scalar
            def _(s, nh=nh, i_=i_):
                w_(s, f"bk.ss.{i_}")
                AC.inc(s.activation(FS[:, 2562:2563], rkc[:, 0:1], AF.Sqrt))
                mk(f"bk.sr.{i_}", AC)

            @block.vector
            def _(v, nh=nh, i_=i_):
                w_(v, f"bk.sr.{i_}")
                DV.inc(v.tensor_scalar(FS[:, 2563:2564], FS[:, 2562:2563], 1e-12, None, OP.max))
                mk(f"bk.mx.{i_}", DV)

            @block.scalar
            def _(s, nh=nh, i_=i_):
                w_(s, f"bk.mx.{i_}")
                AC.inc(s.activation(FS[:, 2564:2565], FS[:, 2563:2564], AF.Ln))
                fn(s, AC)
                AC.inc(s.activation(FS[:, 2565:2566], FS[:, 2564:2565], AF.Exp, scale=-1.0))
                mk(f"bk.rk.{i_}", AC)

            @block.vector
            def _(v, nh=nh, i_=i_):
                w_(v, f"bk.rk.{i_}")
                DV.inc(v.tensor_scalar(knt[:, 0:512], PB[0][:], FS[:, 2565:2566], None, OP.mult))
                mk(f"bk.kn.{i_}", DV)

            @block.vector
            def _(v, nh=nh, i_=i_):
                w_(v, f"bk.kn.{i_}")
                w_(v, f"bk.wb.{i_-1}")
                DV.inc(v.tensor_tensor(sqt[:, 0:512], knt[:, 0:512], WGB[:, nh * 512:(nh + 1) * 512], OP.mult))
                DV.inc(v.tensor_tensor(sqt[:, 512:1024], knt[:, 0:512], WLB[:, nh * 512:(nh + 1) * 512], OP.mult))
                fn(v, DV)
                DV.inc(v.tensor_tensor(sqt[:, 0:512], sqt[:, 0:512], BGB[:, nh * 512:(nh + 1) * 512], OP.add))
                DV.inc(v.tensor_tensor(sqt[:, 512:1024], sqt[:, 512:1024], BLB[:, nh * 512:(nh + 1) * 512], OP.add))
                mk(f"bk.lg.{i_}", DV)

            @block.scalar
            def _(s, nh=nh, i_=i_):
                w_(s, f"bk.lg.{i_}")
                AC.inc(s.activation(sqt[:, 0:512], sqt[:, 0:512], AF.Sigmoid))
                AC.inc(s.activation(sqt[:, 512:1024], sqt[:, 512:1024], AF.Sigmoid))
                mk(f"bk.sg.{i_}", AC)

            @block.vector
            def _(v, nh=nh, i_=i_):
                w_(v, f"bk.sg.{i_}")
                DV.inc(v.tensor_scalar(sqt[:, 512:1024], sqt[:, 512:1024], -1.0, 1.0, OP.mult, OP.add))
                fn(v, DV)
                DV.inc(v.tensor_tensor(sqt[:, 0:512], sqt[:, 0:512], sqt[:, 512:1024], OP.mult))
                fn(v, DV)
                DV.inc(v.tensor_scalar(sqt[:, 0:512], sqt[:, 0:512], RFLOOR, None, OP.max))
                mk(f"bk.rho.{i_}", DV)

            @block.scalar
            def _(s, nh=nh, i_=i_):
                w_(s, f"bk.rho.{i_}")
                AC.inc(s.activation(lrt[:, 0:512], sqt[:, 0:512], AF.Ln))
                mk(f"bk.lr.{i_}", AC)

            @block.vector
            def _(v, nh=nh, i_=i_):
                w_(v, f"bk.lr.{i_}")
                DV.inc(v.tensor_copy(C16[:, 4608:5120], lrt[:, 0:512]))
                mk(f"bk.lr16.{i_}", DV)

            @block.tensor
            def _(t, nh=nh, i_=i_):
                w_(t, f"bk.lr16.{i_}")
                w_(t, f"bk.ep.{i_-1}")
                t.matmul(PB[1][:], RT16[:], C16[:, 4608:5120], start=True, stop=True)
                PE.inc(t.matmul(PB[2][0:2, :], g("KON")[:, 131:133], C16[:, 4608:5120], start=True, stop=True))
                mk(f"bk.lrmm.{i_}", PE)

            @block.scalar
            def _(s, nh=nh, i_=i_, w=w):
                w_(s, f"bk.lrmm.{i_}")
                w_(s, f"bk.wsp.{i_-1}")
                AC.inc(s.activation(ept[:, 0:512], PB[1][:], AF.Exp))
                fn(s, AC)
                AC.inc(s.activation(sqt[:, 512:1024], PB[1][:], AF.Exp, scale=-1.0))
                AC.inc(s.activation(FS[0:2, 2566 + 0:2566 + 512][0:2, 0:512] if False else FS[0:2, 2566:3078], PB[2][0:2, :], AF.Copy))
                mk(f"bk.ep.{i_}", AC)

            @block.vector
            def _(v, nh=nh, i_=i_):
                w_(v, f"bk.ep.{i_}")
                w_(v, f"bk.wsp.{i_-2}")
                DV.inc(v.tensor_scalar(wbt[:, 0:512], sqt[:, 0:512], -1.0, 1.0, OP.mult, OP.add))
                fn(v, DV)
                DV.inc(v.tensor_tensor(wbt[:, 0:512], wbt[:, 0:512], knt[:, 0:512], OP.mult))
                fn(v, DV)
                DV.inc(v.tensor_tensor(wbt[:, 0:512], wbt[:, 0:512], sqt[:, 512:1024], OP.mult))
                mk(f"bk.wb.{i_}", DV)

            @block.sync
            def _(q, nh=nh, w=w, i_=i_):
                w_(q, f"bk.wb.{i_}")
                SPL.inc(q.dma_start(out=wbar_d[nh][w * 128:(w + 1) * 128, :], in_=wbt[:, 0:512]), 16)
                SPL.inc(q.dma_start(out=ep_d[nh][w * 128:(w + 1) * 128, :], in_=ept[:, 0:512]), 16)
                SPL.inc(q.dma_start(out=grow_d[nh][0:2, w * 512:(w + 1) * 512] if False else
                        bass.AP(grow_d.tensor, nh * 16 * 1024 + w * 1024, [[512, 2], [1, 512]]),
                        in_=FS[0:2, 2566:3078]), 16)
                mk(f"bk.wsp.{i_}", SPL)

    for nh in range(2):
        for c in range(16):
            @block.sync
            def _(q, nh=nh, c=c):
                if nh == 0 and c == 0:
                    bar(q)
                w_(q, f"bg.sp.{nh}.{c-1}" if c > 0 else (f"bg.sp.{nh-1}.15" if nh > 0 else "x.-1"))
                SP.inc(q.dma_start(out=FS[0:1, 2048:2560],
                                   in_=bass.AP(grow_d.tensor, nh * 16384 + c * 1024 + 512, [[512, 1], [1, 512]])), 16)
                if c < 15:
                    SP.inc(q.dma_start(out=FS[0:1, 2560:3072],
                                       in_=bass.AP(grow_d.tensor, nh * 16384 + (c + 1) * 1024, [[512, 1], [1, 512]])), 16)
                mk(f"bg.ld.{nh}.{c}", SP)

            @block.vector
            def _(v, nh=nh, c=c):
                w_(v, f"bg.ld.{nh}.{c}")
                if c < 15:
                    DV.inc(v.tensor_tensor(FS[0:1, 3072:3584], FS[0:1, 2048:2560], FS[0:1, 2560:3072], OP.add))
                else:
                    DV.inc(v.tensor_copy(FS[0:1, 3072:3584], FS[0:1, 2048:2560]))
                mk(f"bg.add.{nh}.{c}", DV)

            @block.scalar
            def _(s, nh=nh, c=c):
                w_(s, f"bg.add.{nh}.{c}")
                AC.inc(s.activation(FS[0:1, 3584:4096], FS[0:1, 3072:3584], AF.Exp))
                mk(f"bg.exp.{nh}.{c}", AC)

            @block.sync
            def _(q2, nh=nh, c=c):
                w_(q2, f"bg.exp.{nh}.{c}")
                SPL.inc(q2.dma_start(out=bass.AP(grow_d.tensor, 32768 + nh * 16384 + c * 512, [[512, 1], [1, 512]]),
                                     in_=FS[0:1, 3584:4096]), 16)
                mk(f"bg.sp.{nh}.{c}", SPL)

    S16 = lambda vc: A16[:, 8192 + vc * 512: 8192 + (vc + 1) * 512]
    qsl = lambda kc, c: B16[:, 0:0]
    am16 = WBs[:, 0:128]
    wbc_t = C16[:, 0:512]
    vtc = C16[:, 512:1024]
    epc = C16[:, 1024:1536]
    vfc = C16[:, 1536:2048]
    ytc = C16[:, 2048:2560]
    gb_t = g("WBC")[:, 0:512]

    @block.vector
    def _(v):
        bar(v)
        DV.inc(v.memset(SS[:, 0:4096], 0.0))
        mk("bs.s.-1", DV)

    for h in range(2):
        for c in range(16):
            @block.sync
            def _(q, h=h, c=c):
                if h == 0 and c == 0:
                    bar(q)
                w_(q, f"bs.sdone.{h}.{c-1}" if c > 0 else (f"bs.sdone.{h-1}.15" if h > 0 else "x.-1"))
                GL.inc(q.dma_start(out=wbc_t, in_=wbar_d[h][c * 128:(c + 1) * 128, :]), 16)
                GL.inc(q.dma_start(out=vtc, in_=vtmB_d[h][c * 128:(c + 1) * 128, :]), 16)
                GL.inc(q.dma_start(out=epc, in_=ep_d[h][c * 128:(c + 1) * 128, :]), 16)
                GL.inc(q.dma_start(out=vfc.rearrange("p (a b) -> p a b", a=4),
                                   in_=vfm_d.rearrange("(c p) t -> p c t", p=128)[:, 4 * h: 4 * h + 4, c * 128:(c + 1) * 128]), 16)
                BCS.inc(q.dma_start(out=gb_t, in_=bass.AP(
                    grow_d.tensor, 32768 + h * 16384 + c * 512, [[0, 128], [1, 512]])), 16)
                mk(f"bs.ld.{h}.{c}", GL)

            @block.vector
            def _(v, h=h, c=c):
                w_(v, f"bs.ld.{h}.{c}")
                w_(v, f"bs.s16u.{h}.{c-1}" if c > 0 else (f"bs.s16u.{h-1}.15" if h > 0 else "x.-1"))
                for vc in range(4):
                    DV.inc(v.tensor_copy(S16(vc), SS[:, h * 2048 + vc * 512: h * 2048 + (vc + 1) * 512]))
                mk(f"bs.s16.{h}.{c}", DV)
                mk(f"bs.s16u.{h}.{c}", DV)

            @block.tensor
            def _(t, h=h, c=c):
                w_(t, f"bs.s16.{h}.{c}")
                w_(t, f"bs.am.{h}.{c-1}" if c > 0 else (f"bs.am.{h-1}.15" if h > 0 else "x.-1"))
                for kc in range(4):
                    mmi = t.matmul(PB[0][:, 0:128], vfc[:, kc * 128:(kc + 1) * 128],
                                   B16[:, (4 * h + kc) * T + c * 128: (4 * h + kc) * T + (c + 1) * 128],
                                   start=(kc == 0), stop=(kc == 3))
                PE.inc(mmi)
                mk(f"bs.amm.{h}.{c}", PE)

            @block.vector
            def _(v, h=h, c=c):
                w_(v, f"bs.amm.{h}.{c}")
                DV.inc(v.tensor_tensor(am16[:, 0:128], PB[0][:, 0:128], MKB[:], OP.mult))
                mk(f"bs.am.{h}.{c}", DV)

            @block.tensor
            def _(t, h=h, c=c):
                w_(t, f"bs.am.{h}.{c}")
                w_(t, f"bs.ye.{h}.{c-1}" if c > 0 else (f"bs.ye.{h-1}.15" if h > 0 else "x.-1"))
                t.matmul(PB[1][:], am16[:, 0:128], wbc_t, start=True, stop=False)
                for kc in range(4):
                    mmi = t.matmul(PB[1][:], B16[:, (4 * h + kc) * T + c * 128: (4 * h + kc) * T + (c + 1) * 128],
                                   S16(kc), start=False, stop=(kc == 3))
                PE.inc(mmi)
                mk(f"bs.ymm.{h}.{c}", PE)

            @block.vector
            def _(v, h=h, c=c):
                w_(v, f"bs.ymm.{h}.{c}")
                w_(v, f"bs.ysp.{h}.{c-1}" if c > 0 else (f"bs.ysp.{h-1}.15" if h > 0 else "x.-1"))
                DV.inc(v.tensor_tensor(ytc, PB[1][:], epc, OP.mult))
                mk(f"bs.ye.{h}.{c}", DV)

            @block.sync
            def _(q, h=h, c=c):
                w_(q, f"bs.ye.{h}.{c}")
                SPL.inc(q.dma_start(out=ytm_d[h][c * 128:(c + 1) * 128, :], in_=ytc), 16)
                mk(f"bs.ysp.{h}.{c}", SPL)

            for vc in range(4):
                @block.tensor
                def _(t, h=h, c=c, vc=vc):
                    w_(t, f"bs.su.{h}.{c}.{vc-1}" if vc > 0 else f"bs.ye.{h}.{c}")
                    PE.inc(t.matmul(PB[2][:], vtc[:, vc * 128:(vc + 1) * 128], wbc_t, start=True, stop=True))
                    mk(f"bs.smm.{h}.{c}.{vc}", PE)

                @block.vector
                def _(v, h=h, c=c, vc=vc):
                    w_(v, f"bs.smm.{h}.{c}.{vc}")
                    w_(v, "bs.ld." + f"{h}.{c}")
                    bld.fence(v, g("BCS"))
                    sl = SS[:, h * 2048 + vc * 512: h * 2048 + (vc + 1) * 512]
                    DV.inc(v.tensor_tensor(sl, sl, PB[2][:], OP.add))
                    fn(v, DV)
                    DV.inc(v.tensor_tensor(sl, sl, gb_t, OP.mult))
                    mk(f"bs.su.{h}.{c}.{vc}", DV)
            mk(f"bs.sdone.{h}.{c}", DV)

    @block.scalar
    def _(s):
        bar(s)
        for h in range(2):
            for kc in range(4):
                WS[0].inc(s.dma_start(out=fm(A16, 4 * h + kc),
                                      in_=ytm_d[h][0:T, kc * 128:(kc + 1) * 128], transpose=True), 16)
        load_whalfT(s, 3, 0, 1)
        mk("bo.w0", AC)

    for mo in range(DCH):
        if mo == 4:
            @block.scalar
            def _(s):
                bar(s)
                load_whalfT(s, 3, 1, 1)
                mk("bo.w1", AC)
        for tt in range(4):
            @block.tensor
            def _(t, mo=mo, tt=tt):
                if tt == 0:
                    bld.fence(t, WS[1])
                    bld.fence(t, WS[0])
                w_(t, f"bo.ev.{mo}.{tt-1}" if tt > 0 else f"bo.ev.{mo-2}.3")
                m4 = mo % 4
                for k in range(DCH):
                    mmi = t.matmul(PB[mo % 2][:], ws[1][:, k * 512 + m4 * 128: k * 512 + m4 * 128 + 128],
                                   fm(A16, k, tt * TT, (tt + 1) * TT),
                                   start=(k == 0), stop=(k == DCH - 1))
                PE.inc(mmi)
                mk(f"bo.mm.{mo}.{tt}", PE)

            @block.vector
            def _(v, mo=mo, tt=tt):
                w_(v, f"bo.mm.{mo}.{tt}")
                DV.inc(v.tensor_tensor(XR[:, mo * T + tt * TT: mo * T + (tt + 1) * TT],
                                       XR[:, mo * T + tt * TT: mo * T + (tt + 1) * TT],
                                       PB[mo % 2][:], OP.add))
                mk(f"bo.ev.{mo}.{tt}", DV)
    mk("attnb.done", DV)


def kernel(**inputs):
    in_maps = host_prep(inputs)
    nc = build(stage=STAGE, dumps=DUMPS)
    res = run_bass_kernel_spmd(nc, in_maps, core_ids=list(range(8)))
    return np.stack([np.asarray(r["out"], np.float32).T for r in res.results], axis=0)


# version 05 (final)
# speedup vs baseline: 1.3819x; 1.0388x over previous
import contextlib
import os as _os
import os
import numpy as np
import ml_dtypes
import concourse.bass as bass
import concourse.mybir as mybir
from concourse.bass_utils import run_bass_kernel_spmd
from concourse.alu_op_type import AluOpType as OP

F32 = mybir.dt.float32
BF16 = mybir.dt.bfloat16
AF = mybir.ActivationFunctionType
AX = mybir.AxisListType.X

T = 2048
D = 1024
DCH = 8
TT = 512
CA = 64
CB = 128
FF = 4096
NE = 8
DEXP = 2048
AFLOOR = 0.09
RFLOOR = 0.30

STAGE = 99
DUMPS = ()


def _f32(a):
    return np.ascontiguousarray(np.asarray(a, dtype=np.float32))


def host_constants():
    c = {}
    c["id128"] = np.ascontiguousarray(np.eye(128).astype(ml_dtypes.bfloat16))
    c["ones_col"] = np.ones((128, 1), dtype=ml_dtypes.bfloat16)
    b2 = np.zeros((128, 2), np.float32)
    b2[0:64, 0] = 1.0
    b2[64:128, 1] = 1.0
    c["b2"] = np.ascontiguousarray(b2.astype(ml_dtypes.bfloat16))
    jj, tt_ = np.meshgrid(np.arange(64), np.arange(64), indexing="ij")
    m = np.zeros((64, 128), np.float32)
    m[:, 0:64] = -(jj <= tt_).astype(np.float32)
    m[:, 64:128] = -(jj < tt_).astype(np.float32)
    c["maskA"] = np.ascontiguousarray(np.tile(m, (2, 1)).astype(ml_dtypes.bfloat16))
    jj, tt_ = np.meshgrid(np.arange(CB), np.arange(CB), indexing="ij")
    c["maskB"] = np.ascontiguousarray((jj <= tt_).astype(ml_dtypes.bfloat16))
    rt = (jj <= tt_).astype(np.float32) - (jj <= 63).astype(np.float32)
    c["rbT"] = np.ascontiguousarray(rt.astype(ml_dtypes.bfloat16))
    mh = np.zeros((128, 1), np.float32)
    mh[64:128] = 1.0
    c["mcol_hi"] = np.ascontiguousarray(mh.astype(ml_dtypes.bfloat16))
    ml_ = np.zeros((128, 1), np.float32)
    ml_[0:64] = 1.0
    c["mcol_lo"] = np.ascontiguousarray(ml_.astype(ml_dtypes.bfloat16))
    return c


def host_prep(inputs):
    ins = {k: np.asarray(v) for k, v in inputs.items()}
    common = dict(host_constants())

    def col8(v):
        return _f32(np.asarray(v, np.float32).reshape(8, 128).T)

    A = "L0_attn_a_"
    common["nw_col"] = col8(ins[A + "norm_in_w"])
    common["cb_col"] = col8(ins[A + "conv_b"])
    cw = np.asarray(ins[A + "conv_w"], np.float32)[:, 0, :].reshape(8, 128, 4)
    common["cw_col"] = _f32(cw.transpose(1, 0, 2).reshape(128, 32))
    common["ab_col"] = col8(ins[A + "alpha_b"])
    common["hnw_col"] = col8(np.asarray(ins[A + "head_norm_w"], np.float32).reshape(-1))
    common["fnw_col"] = col8(ins[A + "ff_norm_w"])
    common["u2b_col"] = col8(ins[A + "u2_b"])
    common["u1b_col"] = _f32(np.asarray(ins[A + "u1_b"], np.float32).reshape(64, 1))
    common["bb_col"] = _f32(np.asarray(ins[A + "beta_b"], np.float32).reshape(16, 1))
    for nm in ["q_w", "k_w", "v_w", "alpha_w", "out_w", "u1_w", "u2_w",
               "ff_w1", "ff_w3", "ff_w2", "beta_w"]:
        common["a_" + nm] = _f32(ins[A + nm])
    B = "L0_attn_b_"
    for nm in ["Wq", "Wk", "Wv", "Wo"]:
        common["b_" + nm] = _f32(ins[B + nm])

    def colb(v):
        w = np.asarray(v, np.float32).reshape(2, 4, 128)
        return _f32(w.transpose(2, 0, 1).reshape(128, 8))

    common["wg_col"] = colb(ins[B + "W_gamma"])
    common["bg_col"] = colb(ins[B + "b_gamma"])
    common["wl_col"] = colb(ins[B + "W_lambda"])
    common["bl_col"] = colb(ins[B + "b_lambda"])

    def bcast_tm(v):
        w = np.asarray(v, np.float32).reshape(1, 1024)
        return _f32(np.repeat(w, 128, axis=0))

    common["wg_bc"] = bcast_tm(ins[B + "W_gamma"])
    common["bg_bc"] = bcast_tm(ins[B + "b_gamma"])
    common["wl_bc"] = bcast_tm(ins[B + "W_lambda"])
    common["bl_bc"] = bcast_tm(ins[B + "b_lambda"])
    Fp = "L0_ffn_"
    for nm in ["Wr", "Wg", "Wu", "Wd", "Sg", "Su", "Sd"]:
        common["f_" + nm] = _f32(ins[Fp + nm])
    x = np.asarray(ins[A + "x"], np.float32)
    return [dict(common, x=_f32(x[b].T)) for b in range(8)]


class Sem:
    def __init__(self, handle):
        self.h = handle
        self.n = 0

    def inc(self, instr, k=1):
        instr.then_inc(self.h, k)
        self.n += k
        return instr


class Builder:

    def __init__(self, nc, stage, dumps):
        self.nc = nc
        self.stage = stage
        self.dumps = dumps
        self.marks = {}

    def mk(self, name, sem):
        self.marks[name] = (sem, sem.n)

    def w(self, eng, name):
        if name not in self.marks:
            if ".-" in name or name.endswith(".-1") or ".-1." in name:
                return
            raise KeyError(f"wait on missing mark: {name}")
        sem, n = self.marks[name]
        if sem in getattr(self, "dma_sems", ()):
            n = sem.n
        if n > 0:
            eng.wait_ge(sem.h, n)

    def bar(self, eng):
        for s in self.sems:
            if s.n > 0:
                eng.wait_ge(s.h, s.n)

    def fence(self, eng, sem):
        if sem.n > 0:
            eng.wait_ge(sem.h, sem.n)


def build(stage=99, dumps=()):
    nc = bass.Bass("TRN2", target_bir_lowering=False)
    bld = Builder(nc, stage, dumps)
    dt = nc.dram_tensor

    X = dt("x", [D, T], F32, kind="ExternalInput").ap()
    W = {}
    for nm, shp in [("a_q_w", [D, D]), ("a_k_w", [D, D]), ("a_v_w", [D, D]),
                    ("a_alpha_w", [D, D]), ("a_out_w", [D, D]),
                    ("a_beta_w", [D, 16]), ("a_u1_w", [D, 64]), ("a_u2_w", [64, D]),
                    ("a_ff_w1", [D, FF]), ("a_ff_w3", [D, FF]), ("a_ff_w2", [FF, D]),
                    ("b_Wq", [D, D]), ("b_Wk", [D, D]), ("b_Wv", [D, D]), ("b_Wo", [D, D]),
                    ("f_Wr", [D, NE]), ("f_Wg", [NE, D, DEXP]), ("f_Wu", [NE, D, DEXP]),
                    ("f_Wd", [NE, DEXP, D]), ("f_Sg", [D, DEXP]), ("f_Su", [D, DEXP]),
                    ("f_Sd", [DEXP, D])]:
        W[nm] = dt(nm, shp, F32, kind="ExternalInput").ap()
    C = {}
    for nm, shp, d_ in [("id128", [128, 128], BF16), ("ones_col", [128, 1], BF16),
                        ("b2", [128, 2], BF16), ("maskA", [128, 128], BF16),
                        ("maskB", [CB, CB], BF16), ("rbT", [CB, CB], BF16),
                        ("mcol_hi", [128, 1], BF16), ("mcol_lo", [128, 1], BF16),
                        ("nw_col", [128, 8], F32), ("cb_col", [128, 8], F32),
                        ("cw_col", [128, 32], F32), ("ab_col", [128, 8], F32),
                        ("hnw_col", [128, 8], F32), ("fnw_col", [128, 8], F32),
                        ("u2b_col", [128, 8], F32), ("u1b_col", [64, 1], F32),
                        ("bb_col", [16, 1], F32), ("wg_col", [128, 8], F32),
                        ("bg_col", [128, 8], F32), ("wl_col", [128, 8], F32),
                        ("bl_col", [128, 8], F32), ("wg_bc", [128, 1024], F32),
                        ("bg_bc", [128, 1024], F32), ("wl_bc", [128, 1024], F32),
                        ("bl_bc", [128, 1024], F32)]:
        C[nm] = dt(nm, shp, d_, kind="ExternalInput").ap()
    OUT = dt("out", [D, T], F32, kind="ExternalOutput").ap()

    def scr(nm, shp, d_):
        kind = "ExternalOutput" if nm in dumps else "Internal"
        return dt(nm, shp, d_, kind=kind).ap()

    qn_d = scr("qn_d", [D, T], BF16)
    kn_d = scr("kn_d", [D, T], BF16)
    vtm_d = scr("vtm_d", [T, D], BF16)
    L_d = scr("L_d", [D, T + 64], F32)
    btm_d = scr("btm_d", [T, 16], F32)
    gate_d = scr("gate_d", [D, T], BF16)
    row_d = scr("row_d", [64, T], F32)
    y16_d = scr("y16_d", [D, T], BF16)
    wbT_d = scr("wbT_d", [4, D, D], BF16)
    gates_d = scr("gates_d", [NE, T], F32)
    wbar_d = scr("wbar_d", [2, T, 512], BF16)
    vtmB_d = scr("vtmB_d", [2, T, 512], BF16)
    grow_d = scr("grow_d", [64, 1024], F32)
    ep_d = scr("ep_d", [2, T, 512], BF16)
    ytm_d = scr("ytm_d", [2, T, 512], BF16)
    vfm_d = scr("vfm_d", [D, T], BF16)
    dbg_d = scr("dbg_d", [128, 8192], F32)

    Xv = X.rearrange("(c p) t -> p c t", p=128)
    OUTv = OUT.rearrange("(c p) t -> p c t", p=128)
    qn_v = qn_d.rearrange("(c p) t -> p c t", p=128)
    kn_v = kn_d.rearrange("(c p) t -> p c t", p=128)
    L_v = L_d.rearrange("(c p) t -> p c t", p=128)
    gate_v = gate_d.rearrange("(c p) t -> p c t", p=128)
    y16_v = y16_d.rearrange("(c p) t -> p c t", p=128)

    es = contextlib.ExitStack()
    with es:
        sbuf = lambda nm, shp, d_: es.enter_context(nc.sbuf_tensor(nm, shp, d_))
        XR = sbuf("XR", [128, DCH * T], F32)
        A16 = sbuf("A16", [128, DCH * T], BF16)
        B16 = sbuf("B16", [128, DCH * T], BF16)
        C16 = sbuf("C16", [128, 8192], BF16)
        WBs = sbuf("WBs", [128, 12288], BF16)
        FS = sbuf("FS", [128, 4096], F32)
        WBC = sbuf("WBC", [128, 1536], F32)
        SS = sbuf("SS", [128, 4096], F32)
        KON = sbuf("KON", [128, 136], BF16)
        MKA = sbuf("MKA", [128, 128], BF16)
        MKB = sbuf("MKB", [128, 128], BF16)
        RT16 = sbuf("RT16", [128, 128], BF16)
        CCc = sbuf("CCc", [128, 128], F32)
        psum = lambda nm, shp, d_: es.enter_context(nc.psum_tensor(nm, shp, d_))
        PB = [psum(f"pb{i}", [128, 512], F32) for i in range(7)]
        PT = psum("pt16", [128, 1024], BF16)

        SP = Sem(es.enter_context(nc.semaphore("sp")))
        GP = Sem(es.enter_context(nc.semaphore("gp")))
        AC = Sem(es.enter_context(nc.semaphore("ac")))
        DV = Sem(es.enter_context(nc.semaphore("dv")))
        PE = Sem(es.enter_context(nc.semaphore("pe")))
        SPL = Sem(es.enter_context(nc.semaphore("spl")))
        BCS = Sem(es.enter_context(nc.semaphore("bcs")))
        GL = Sem(es.enter_context(nc.semaphore("gl")))
        WS = [Sem(es.enter_context(nc.semaphore("ws0"))),
              Sem(es.enter_context(nc.semaphore("ws1"))),
              Sem(es.enter_context(nc.semaphore("ws2")))]
        WT = [Sem(es.enter_context(nc.semaphore("wt0"))),
              Sem(es.enter_context(nc.semaphore("wt1"))),
              Sem(es.enter_context(nc.semaphore("wt2")))]
        WS2 = [Sem(es.enter_context(nc.semaphore("wsb0"))),
               Sem(es.enter_context(nc.semaphore("wsb1")))]
        bld.sems = [SP, GP, AC, DV, PE, SPL, BCS, GL] + WS + WT + WS2
        bld.dma_sems = (SP, GP, SPL, BCS, GL, WS[0], WS[1], WS[2], WT[0], WT[1], WT[2],
                        WS2[0], WS2[1])
        block = es.enter_context(nc.Block())
        mk, w_, bar = bld.mk, bld.w, bld.bar

        def fm(pool, c, t0=0, t1=None):
            t1 = T if t1 is None else t1
            return pool[:, c * T + t0: c * T + t1]

        id16 = KON[:, 0:128]
        ones16 = KON[:, 128:129]
        b2c = KON[:, 129:131]
        mhi = KON[:, 131:132]
        mlo = KON[:, 132:133]
        ccbase = {"nw": 0, "cb": 8, "ab": 16, "hnw": 24, "fnw": 32, "u2b": 40,
                  "wg": 48, "bg": 56, "wl": 64, "bl": 72, "cw": 80, "u1b": 112, "bb": 113}

        def ccol(name, i=0):
            b = ccbase[name] + i
            return CCc[:, b:b + 1]

        @block.sync
        def _(q):
            SP.inc(q.dma_start(out=XR[:].rearrange("p (c t) -> p c t", c=DCH), in_=Xv), 16)
            SP.inc(q.dma_start(out=KON[:, 0:128], in_=C["id128"]), 16)
            SP.inc(q.dma_start(out=KON[:, 128:129], in_=C["ones_col"]), 16)
            SP.inc(q.dma_start(out=KON[:, 129:131], in_=C["b2"]), 16)
            SP.inc(q.dma_start(out=KON[:, 131:132], in_=C["mcol_hi"]), 16)
            SP.inc(q.dma_start(out=KON[:, 132:133], in_=C["mcol_lo"]), 16)
            SP.inc(q.dma_start(out=MKA[:], in_=C["maskA"]), 16)
            SP.inc(q.dma_start(out=MKB[:], in_=C["maskB"]), 16)
            SP.inc(q.dma_start(out=RT16[:], in_=C["rbT"]), 16)
            for i, nm in enumerate(["nw_col", "cb_col", "ab_col", "hnw_col",
                                    "fnw_col", "u2b_col", "wg_col", "bg_col",
                                    "wl_col", "bl_col"]):
                SP.inc(q.dma_start(out=CCc[:, 8 * i:8 * i + 8], in_=C[nm]), 16)
            SP.inc(q.dma_start(out=CCc[:, 80:112], in_=C["cw_col"]), 16)
            SP.inc(q.dma_start(out=CCc[0:64, 112:113], in_=C["u1b_col"]), 16)
            SP.inc(q.dma_start(out=CCc[0:16, 113:114], in_=C["bb_col"]), 16)

        @block.vector
        def _(v):
            v.memset(FS[:, 0:64], 0.0)
            v.memset(CCc[:, 114:115], 1e-5)
            v.memset(CCc[:, 115:116], 1e-6)
            DV.inc(v.memset(CCc[:, 116:117], 1e-5 * 1e-6))

        @block.gpsimd
        def _(g):
            bar(g)
            for c in range(DCH):
                GP.inc(g.dma_start(out=L_v[:, c, 0:64], in_=FS[:, 0:64]), 16)
            for i, nm in enumerate(["b_Wq", "b_Wk", "b_Wv", "b_Wo"]):
                GP.inc(g.dma_start(out=wbT_d[i], in_=W[nm]), 16)
            GP.inc(g.dma_start(
                out=WBs[:, 11264:11776].rearrange("p (k n) -> p k n", k=8),
                in_=W["a_u1_w"].rearrange("(k p) n -> p k n", p=128)), 16)
            GP.inc(g.dma_start(out=WBs[0:64, 10240:11264], in_=W["a_u2_w"]), 16)
            GP.inc(g.dma_start(
                out=WBs[:, 11776:11904].rearrange("p (k n) -> p k n", k=8),
                in_=W["a_beta_w"].rearrange("(k p) n -> p k n", p=128)), 16)
        u1w16 = WBs[:, 11264:11776]
        u2w16 = WBs[0:64, 10240:11264]
        bw16 = WBs[:, 11776:11904]
        ws = [WBs[:, 0:4096], WBs[:, 4096:8192], WBs[:, 8192:12288]]

        def load_whalf(g, wap, half, slot):
            WS[slot].inc(g.dma_start(
                out=ws[slot].rearrange("p (k n) -> p k n", k=8),
                in_=wap.rearrange("(k p) n -> p k n", p=128)[:, :, half * 512:(half + 1) * 512]), 16)

        def wblk(slot, k, mo4):
            return ws[slot][:, k * 512 + mo4 * 128: k * 512 + mo4 * 128 + 128]

        def stats_rows(tag, src_pool, row_x, row_s, want_mean=True):
            def sqs(c):
                return C16[:, c * 512:(c + 1) * 512]

            for tt in range(4):
                @block.scalar
                def _(s, tt=tt):
                    if tt == 0:
                        bar(s)
                    for c in range(DCH):
                        w_(s, f"{tag}.mm.{tt-1}.{c}")
                        AC.inc(s.activation(sqs(c), fm(src_pool, c, tt * TT, (tt + 1) * TT), AF.Square))
                        mk(f"{tag}.sq.{tt}.{c}", AC)

                @block.tensor
                def _(t, tt=tt):
                    w_(t, f"{tag}.rows.{tt-1}")
                    for c in range(DCH):
                        w_(t, f"{tag}.sq.{tt}.{c}")
                        if want_mean:
                            t.matmul(PB[4][0:1, :], ones16, fm(src_pool, c, tt * TT, (tt + 1) * TT),
                                     start=(c == 0), stop=(c == DCH - 1))
                        PE.inc(t.matmul(PB[5][0:1, :], ones16, sqs(c),
                                        start=(c == 0), stop=(c == DCH - 1)))
                        mk(f"{tag}.mm.{tt}.{c}", PE)
                    mk(f"{tag}.mmtt.{tt}", PE)

                @block.scalar
                def _(s, tt=tt):
                    w_(s, f"{tag}.mmtt.{tt}")
                    if want_mean:
                        s.activation(FS[0:1, tt * 512:(tt + 1) * 512], PB[4][0:1, :], AF.Copy)
                    AC.inc(s.activation(FS[0:1, 2048 + tt * 512: 2048 + (tt + 1) * 512], PB[5][0:1, :], AF.Copy))
                    mk(f"{tag}.rows.{tt}", AC)

            @block.sync
            def _(q):
                w_(q, f"{tag}.rows.3")
                if want_mean:
                    SP.inc(q.dma_start(out=row_d[row_x:row_x + 1, :], in_=FS[0:1, 0:2048]), 16)
                SP.inc(q.dma_start(out=row_d[row_s:row_s + 1, :], in_=FS[0:1, 2048:4096]), 16)
                mk(f"{tag}.rsp", SP)

        def row_math(tag, emit_ops, src_rows, n_src, dst_rows, n_dst):
            @block.sync
            def _(q):
                bar(q)
                for i, r in enumerate(src_rows):
                    SP.inc(q.dma_start(out=FS[:, 2048 + 16 * i: 2048 + 16 * (i + 1)],
                                       in_=bass.AP(row_d.tensor, r * 2048, [[16, 128], [1, 16]])), 16)
                mk(f"{tag}.rl", SP)

            emit_ops()

            @block.sync
            def _(q):
                bar(q)
                for i, r in enumerate(dst_rows):
                    SP.inc(q.dma_start(out=bass.AP(row_d.tensor, r * 2048, [[16, 128], [1, 16]]),
                                       in_=FS[:, 2560 + 16 * i: 2560 + 16 * (i + 1)]), 16)
                mk(f"{tag}.msp", SP)

        def rep512(q, row, tt, dst):
            BCS.inc(q.dma_start(out=dst, in_=bass.AP(
                row_d.tensor, row * 2048 + tt * 512, [[0, 128], [1, 512]])), 16)

        def rep_head64(q, rowbase, c, tt, dst):
            BCS.inc(q.dma_start(out=dst, in_=bass.AP(
                row_d.tensor, (rowbase + 2 * c) * 2048 + tt * 512,
                [[2048, 2], [0, 64], [1, 512]])), 16)

        if stage >= 1:
            @block.vector
            def _(v):
                bar(v)
                for c in range(DCH):
                    DV.inc(v.tensor_copy(fm(B16, c), fm(XR, c)))

            stats_rows("ln1", B16, 0, 1, want_mean=True)

            def ln1_math():
                fn = bld.fence

                @block.vector
                def _(v):
                    bar(v)
                    DV.inc(v.tensor_scalar(FS[:, 2560:2576], FS[:, 2048:2064], 1.0 / D, None, OP.mult))
                    DV.inc(v.tensor_scalar(FS[:, 2080:2096], FS[:, 2064:2080], 1.0 / D, None, OP.mult))
                    fn(v, DV)
                    DV.inc(v.tensor_tensor(FS[:, 2096:2112], FS[:, 2560:2576], FS[:, 2560:2576], OP.mult))
                    fn(v, DV)
                    DV.inc(v.tensor_tensor(FS[:, 2112:2128], FS[:, 2080:2096], FS[:, 2096:2112], OP.subtract))
                    mk("ln1.var", DV)

                @block.scalar
                def _(s):
                    w_(s, "ln1.var")
                    AC.inc(s.activation(FS[:, 2128:2144], FS[:, 2112:2128], AF.Ln, bias=CCc[:, 114:115]))
                    fn(s, AC)
                    AC.inc(s.activation(FS[:, 2576:2592], FS[:, 2128:2144], AF.Exp, scale=-0.5))
                    AC.inc(s.activation(FS[:, 2144:2160], FS[:, 2112:2128], AF.Ln, scale=1.0 + 1e-6, bias=CCc[:, 116:117]))
                    fn(s, AC)
                    AC.inc(s.activation(FS[:, 2592:2608], FS[:, 2144:2160], AF.Exp, scale=-0.5))
                    mk("ln1.c", AC)
            row_math("ln1m", ln1_math, [0, 1], 2, [0, 2, 3], 3)

            mb, cpb, cfb = WBC[:, 0:512], WBC[:, 512:1024], WBC[:, 1024:1536]
            xst = C16[:, 1024:1536]
            s1t = C16[0:64, 1536:2048]
            gsl = [C16[:, 2048:2560], C16[:, 2560:3072]]
            t1s = FS[:, 2048:2560]

            for tt in range(4):
                @block.sync
                def _(q, tt=tt):
                    bar(q)
                    rep512(q, 0, tt, mb)
                    rep512(q, 2, tt, cpb)
                    rep512(q, 3, tt, cfb)
                    mk("p1.bc", BCS)

                for c in range(DCH):
                    @block.vector
                    def _(v, tt=tt, c=c):
                        if c == 0:
                            w_(v, "p1.bc")
                        w_(v, f"p1.u1.{tt}.{c-1}" if c > 0 else f"p1.u1.{tt-1}.{DCH-1}")
                        fn(v, DV) if False else bld.fence(v, DV)
                        DV.inc(v.tensor_tensor(t1s, XR[:, c * T + tt * TT: c * T + (tt + 1) * TT], mb, OP.subtract))
                        bld.fence(v, DV)
                        v.scalar_tensor_tensor(fm(A16, c, tt * TT, (tt + 1) * TT),
                                               t1s, ccol("nw", c), cfb, OP.mult, OP.mult)
                        DV.inc(v.tensor_tensor(xst, t1s, cpb, OP.mult))
                        mk(f"p1.xs.{tt}.{c}", DV)

                    @block.tensor
                    def _(t, tt=tt, c=c):
                        w_(t, f"p1.xs.{tt}.{c}")
                        if c == 0:
                            w_(t, f"p1.s1.{tt-1}")
                        PE.inc(t.matmul(PB[6][0:64, :], u1w16[:, c * 64:(c + 1) * 64], xst,
                                        start=(c == 0), stop=(c == DCH - 1)))
                        mk(f"p1.u1.{tt}.{c}", PE)

                @block.scalar
                def _(s, tt=tt):
                    w_(s, f"p1.u1.{tt}.{DCH-1}")
                    w_(s, f"p1.gmm.{tt-1}.{DCH-1}")
                    AC.inc(s.activation(s1t, PB[6][0:64, :], AF.Silu, bias=CCc[0:64, 112:113]))
                    mk(f"p1.s1.{tt}", AC)

                for mo in range(DCH):
                    @block.tensor
                    def _(t, tt=tt, mo=mo):
                        if mo == 0:
                            w_(t, f"p1.s1.{tt}")
                        w_(t, f"p1.gact.{tt}.{mo-2}" if mo >= 2 else f"p1.gact.{tt-1}.{6+mo}")
                        PE.inc(t.matmul(PB[4 + (mo % 2)][:, :], u2w16[:, mo * 128:(mo + 1) * 128],
                                        s1t, start=True, stop=True))
                        mk(f"p1.gmm.{tt}.{mo}", PE)

                    @block.scalar
                    def _(s, tt=tt, mo=mo):
                        w_(s, f"p1.gmm.{tt}.{mo}")
                        w_(s, f"p1.gsp.{tt}.{mo-2}" if mo >= 2 else f"p1.gsp.{tt-1}.{6+mo}")
                        AC.inc(s.activation(gsl[mo % 2], PB[4 + (mo % 2)][:, :], AF.Sigmoid, bias=ccol("u2b", mo)))
                        mk(f"p1.gact.{tt}.{mo}", AC)

                    @block.sync
                    def _(q, tt=tt, mo=mo):
                        w_(q, f"p1.gact.{tt}.{mo}")
                        SPL.inc(q.dma_start(out=gate_v[:, mo, tt * TT:(tt + 1) * TT], in_=gsl[mo % 2]), 16)
                        mk(f"p1.gsp.{tt}.{mo}", SPL)

            hcv = FS[:, 0:2048]
            for c in range(DCH):
                @block.vector
                def _(v, c=c):
                    if c == 0:
                        bar(v)
                    w_(v, f"p1.hs.{c-1}")
                    DV.inc(v.tensor_scalar(hcv, fm(A16, c), ccol("cw", 4 * c + 3), None, OP.mult))
                    bld.fence(v, DV)
                    DV.inc(v.scalar_tensor_tensor(hcv[:, 1:2048], fm(A16, c, 0, 2047), ccol("cw", 4 * c + 2), hcv[:, 1:2048], OP.mult, OP.add))
                    bld.fence(v, DV)
                    DV.inc(v.scalar_tensor_tensor(hcv[:, 2:2048], fm(A16, c, 0, 2046), ccol("cw", 4 * c + 1), hcv[:, 2:2048], OP.mult, OP.add))
                    bld.fence(v, DV)
                    DV.inc(v.scalar_tensor_tensor(hcv[:, 3:2048], fm(A16, c, 0, 2045), ccol("cw", 4 * c + 0), hcv[:, 3:2048], OP.mult, OP.add))
                    mk(f"p1.hc.{c}", DV)

                @block.scalar
                def _(s, c=c):
                    w_(s, f"p1.hc.{c}")
                    AC.inc(s.activation(fm(B16, c), hcv, AF.Silu, bias=ccol("cb", c)))
                    mk(f"p1.hs.{c}", AC)

        def proj_fm(tag, wap, tile_cb, n_mo=8):
            @block.gpsimd
            def _(g):
                bar(g)
                load_whalf(g, wap, 0, 0)
                mk(f"{tag}.w0", GP)
                load_whalf(g, wap, 1, 1)
                mk(f"{tag}.w1", GP)

            for mo in range(n_mo):
                half = mo // 4
                for tt in range(4):
                    @block.tensor
                    def _(t, mo=mo, tt=tt, half=half):
                        if mo == 0 and tt == 0:
                            bar(t)
                        if tt == 0:
                            bld.fence(t, WS[half])
                        w_(t, f"{tag}.ev.{mo}.{tt-1}" if tt > 0 else f"{tag}.ev.{mo-2}.3")
                        for k in range(DCH):
                            mmi = t.matmul(PB[mo % 2][:], wblk(half, k, mo % 4),
                                           fm(B16, k, tt * TT, (tt + 1) * TT),
                                           start=(k == 0), stop=(k == DCH - 1))
                        PE.inc(mmi)
                        mk(f"{tag}.mm.{mo}.{tt}", PE)
                    tile_cb(mo, tt)

        def qk_sink(tag):
            def cb(mo, tt):
                sqs = C16[:, tt * 512:(tt + 1) * 512]

                @block.vector
                def _(v, mo=mo, tt=tt, sqs=sqs):
                    w_(v, f"{tag}.mm.{mo}.{tt}")
                    DV.inc(v.tensor_copy(fm(A16, mo, tt * TT, (tt + 1) * TT), PB[mo % 2][:]))
                    mk(f"{tag}.ev.{mo}.{tt}", DV)

                @block.scalar
                def _(s, mo=mo, tt=tt, sqs=sqs):
                    w_(s, f"{tag}.ev.{mo}.{tt}")
                    w_(s, f"{tag}.nmm.{mo-1}.{tt}")
                    AC.inc(s.activation(sqs, fm(A16, mo, tt * TT, (tt + 1) * TT), AF.Square))
                    mk(f"{tag}.sq.{mo}.{tt}", AC)

                @block.tensor
                def _(t, mo=mo, tt=tt, sqs=sqs):
                    w_(t, f"{tag}.sq.{mo}.{tt}")
                    w_(t, f"{tag}.rev.{mo}.{tt-1}" if tt > 0 else f"{tag}.rev.{mo-1}.3")
                    PE.inc(t.matmul(PB[6][0:2, :], b2c, sqs, start=True, stop=True))
                    mk(f"{tag}.nmm.{mo}.{tt}", PE)

                @block.scalar
                def _(s, mo=mo, tt=tt):
                    w_(s, f"{tag}.nmm.{mo}.{tt}")
                    if tt == 0:
                        w_(s, f"{tag}.rsp.{mo-1}")
                    AC.inc(s.activation(FS[0:2, 2048 + tt * 512: 2048 + (tt + 1) * 512],
                                        PB[6][0:2, :], AF.Copy))
                    mk(f"{tag}.rev.{mo}.{tt}", AC)

                if tt == 3:
                    @block.sync
                    def _(q, mo=mo):
                        w_(q, f"{tag}.rev.{mo}.3")
                        SPL.inc(q.dma_start(out=row_d[4 + 2 * mo: 6 + 2 * mo, :], in_=FS[0:2, 2048:4096]), 16)
                        mk(f"{tag}.rsp.{mo}", SPL)
            return cb

        def qk_normmath(tag):
            def ops():
                @block.scalar
                def _(s):
                    bar(s)
                    AC.inc(s.activation(FS[:, 2816:3072], FS[:, 2048:2304], AF.Sqrt))
                    fn0 = bld.fence
                    fn0(s, AC)
                    AC.inc(s.activation(FS[:, 3072:3328], FS[:, 2816:3072], AF.Ln, bias=CCc[:, 115:116]))
                    fn0(s, AC)
                    AC.inc(s.activation(FS[:, 2560:2816], FS[:, 3072:3328], AF.Exp, scale=-1.0))
                    mk(f"{tag}.rmath", AC)

            @block.sync
            def _(q):
                bar(q)
                SP.inc(q.dma_start(out=FS[:, 2048:2304],
                                   in_=bass.AP(row_d.tensor, 4 * 2048, [[256, 128], [1, 256]])), 16)
            ops()

            @block.sync
            def _(q):
                bar(q)
                SP.inc(q.dma_start(out=bass.AP(row_d.tensor, 20 * 2048, [[256, 128], [1, 256]]),
                                   in_=FS[:, 2560:2816]), 16)
                mk(f"{tag}.rqsp", SP)

        def qk_apply(tag, dest_v):
            rqb = WBC[:, 0:512]
            qnt = [C16[:, 1024:1536], C16[:, 1536:2048]]

            for c in range(DCH):
                for tt in range(4):
                    i = c * 4 + tt

                    @block.sync
                    def _(q, c=c, tt=tt):
                        if c == 0 and tt == 0:
                            bar(q)
                        w_(q, f"{tag}.ap.{c}.{tt-1}" if tt > 0 else f"{tag}.ap.{c-1}.3")
                        rep_head64(q, 20, c, tt, rqb)
                        mk(f"{tag}.bc.{c}.{tt}", BCS)

                    @block.vector
                    def _(v, c=c, tt=tt, i=i):
                        w_(v, f"{tag}.bc.{c}.{tt}")
                        w_(v, f"{tag}.sp.{c}.{tt-2}" if tt >= 2 else f"{tag}.sp.{c-1}.{2+tt}")
                        DV.inc(v.tensor_tensor(qnt[i % 2], fm(A16, c, tt * TT, (tt + 1) * TT), rqb, OP.mult))
                        mk(f"{tag}.ap.{c}.{tt}", DV)

                    @block.sync
                    def _(q, c=c, tt=tt, i=i):
                        w_(q, f"{tag}.ap.{c}.{tt}")
                        SPL.inc(q.dma_start(out=dest_v[:, c, tt * TT:(tt + 1) * TT], in_=qnt[i % 2]), 16)
                        mk(f"{tag}.sp.{c}.{tt}", SPL)

        if stage >= 2:
            proj_fm("q", W["a_q_w"], qk_sink("q"))
            qk_normmath("q")
            qk_apply("q", qn_v)
            proj_fm("k", W["a_k_w"], qk_sink("k"))
            qk_normmath("k")
            qk_apply("k", kn_v)

            lsl_a = [FS[:, 2560:3072], FS[:, 3072:3584]]
            onct = WBC[:, 0:512]

            @block.vector
            def _(v):
                bar(v)
                DV.inc(v.memset(onct, 1.0))
                mk("al.ones", DV)

            def a_cb(mo, tt):
                asl = FS[:, 2048:2560]

                @block.scalar
                def _(s, mo=mo, tt=tt, asl=asl):
                    if mo == 0 and tt == 0:
                        w_(s, "al.ones")
                    w_(s, f"al.mm.{mo}.{tt}")
                    w_(s, f"al.sc.{mo}.{tt-1}" if tt > 0 else f"al.sc.{mo-1}.3")
                    AC.inc(s.activation(asl, PB[mo % 2][:], AF.Sigmoid, bias=ccol("ab", mo)))
                    mk(f"al.sg.{mo}.{tt}", AC)

                @block.vector
                def _(v, mo=mo, tt=tt, asl=asl):
                    w_(v, f"al.sg.{mo}.{tt}")
                    DV.inc(v.tensor_scalar(asl, asl, AFLOOR, None, OP.max))
                    mk(f"al.cl.{mo}.{tt}", DV)
                    mk(f"al.ev.{mo}.{tt}", DV)

                @block.scalar
                def _(s, mo=mo, tt=tt, asl=asl):
                    w_(s, f"al.cl.{mo}.{tt}")
                    AC.inc(s.activation(asl, asl, AF.Ln))
                    mk(f"al.ln.{mo}.{tt}", AC)

                @block.vector
                def _(v, mo=mo, tt=tt, asl=asl):
                    w_(v, f"al.ln.{mo}.{tt}")
                    w_(v, f"al.lsp.{mo}.{tt-2}" if tt >= 2 else f"al.lsp.{mo-1}.{2+tt}")
                    init = 0.0 if tt == 0 else lsl_a[(tt - 1) % 2][:, 511:512]
                    DV.inc(v.tensor_tensor_scan(lsl_a[tt % 2], onct, asl, init, OP.mult, OP.add))
                    mk(f"al.sc.{mo}.{tt}", DV)

                @block.sync
                def _(q, mo=mo, tt=tt):
                    w_(q, f"al.sc.{mo}.{tt}")
                    SPL.inc(q.dma_start(out=L_v[:, mo, 64 + tt * TT: 64 + (tt + 1) * TT], in_=lsl_a[tt % 2]), 16)
                    mk(f"al.lsp.{mo}.{tt}", SPL)
            proj_fm("al", W["a_alpha_w"], a_cb)

            @block.gpsimd
            def _(g):
                bar(g)
                load_whalf(g, W["a_v_w"], 0, 0)
                mk("v.w0", GP)
                load_whalf(g, W["a_v_w"], 1, 1)
                mk("v.w1", GP)

            vsl = [C16[:, 0:512], C16[:, 512:1024]]
            for i_ in range(32):
                nh, ww = i_ // 16, i_ % 16

                @block.tensor
                def _(t, i_=i_, nh=nh, ww=ww):
                    if i_ == 0:
                        bar(t)
                    if ww == 0:
                        bld.fence(t, WS[nh])
                    w_(t, f"v.ev.{i_-2}")
                    for k in range(DCH):
                        mmi = t.matmul(PB[i_ % 2][:], fm(B16, k, ww * 128, (ww + 1) * 128),
                                       ws[nh][:, k * 512:(k + 1) * 512],
                                       start=(k == 0), stop=(k == DCH - 1))
                    PE.inc(mmi)
                    mk(f"v.mm.{i_}", PE)

                @block.vector
                def _(v, i_=i_):
                    w_(v, f"v.mm.{i_}")
                    w_(v, f"v.sp.{i_-2}")
                    DV.inc(v.tensor_copy(vsl[i_ % 2], PB[i_ % 2][:]))
                    mk(f"v.ev.{i_}", DV)

                @block.sync
                def _(q, i_=i_, nh=nh, ww=ww):
                    w_(q, f"v.ev.{i_}")
                    SPL.inc(q.dma_start(out=vtm_d[ww * 128:(ww + 1) * 128, nh * 512:(nh + 1) * 512],
                                       in_=vsl[i_ % 2]), 16)
                    mk(f"v.sp.{i_}", SPL)

            bsl = C16[0:16, 3584:4096]
            bts = [FS[:, 3584:3600], FS[:, 3600:3616]]
            for tt in range(4):
                @block.tensor
                def _(t, tt=tt):
                    if tt == 0:
                        bar(t)
                    w_(t, f"b.sg.{tt-1}")
                    for k in range(DCH):
                        mmi = t.matmul(PB[6][0:16, :], bw16[:, k * 16:(k + 1) * 16],
                                       fm(B16, k, tt * TT, (tt + 1) * TT), start=(k == 0), stop=(k == DCH - 1))
                    PE.inc(mmi)
                    mk(f"b.mm.{tt}", PE)

                @block.scalar
                def _(s, tt=tt):
                    w_(s, f"b.mm.{tt}")
                    w_(s, f"b.tr.{tt-1}.3")
                    AC.inc(s.activation(bsl, PB[6][0:16, :], AF.Sigmoid, bias=CCc[0:16, 113:114]))
                    mk(f"b.sg.{tt}", AC)

                for blk in range(4):
                    i_ = tt * 4 + blk

                    @block.tensor
                    def _(t, tt=tt, blk=blk):
                        if blk == 0:
                            w_(t, f"b.sg.{tt}")
                        w_(t, f"b.ev.{tt}.{blk-1}" if blk > 0 else f"b.ev.{tt-1}.3")
                        PE.inc(t.transpose(PT[0:128, 0:16], bsl[:, blk * 128:(blk + 1) * 128], id16[0:16, 0:16]))
                        mk(f"b.tr.{tt}.{blk}", PE)

                    @block.vector
                    def _(v, tt=tt, blk=blk, i_=i_):
                        w_(v, f"b.tr.{tt}.{blk}")
                        w_(v, f"b.sp.{i_-2}")
                        DV.inc(v.tensor_copy(bts[i_ % 2], PT[:, 0:16]))
                        mk(f"b.ev.{tt}.{blk}", DV)

                    @block.sync
                    def _(q, tt=tt, blk=blk, i_=i_):
                        w_(q, f"b.ev.{tt}.{blk}")
                        SPL.inc(q.dma_start(out=btm_d[tt * 512 + blk * 128: tt * 512 + (blk + 1) * 128, :],
                                           in_=bts[i_ % 2]), 16)
                        mk(f"b.sp.{i_}", SPL)

        if stage >= 3:
            emit_scan_a(bld, block, dict(locals()))

        if stage >= 4:
            emit_tail_a(bld, block, dict(locals()))

        if stage >= 5:
            emit_ffn(bld, block, dict(locals()), round_=0)

        WGB = SS[:, 0:1024]
        BGB = SS[:, 1024:2048]
        WLB = SS[:, 2048:3072]
        BLB = SS[:, 3072:4096]

        if stage >= 6:
            emit_attn_b(bld, block, dict(locals()))

        if stage >= 7:
            emit_ffn(bld, block, dict(locals()), round_=1)

        @block.sync
        def _(q):
            bar(q)
            for c in range(DCH):
                SP.inc(q.dma_start(out=OUTv[:, c, :], in_=XR[:, c * T:(c + 1) * T]), 16)
            q.wait_ge(SP.h, SP.n)
    if os.environ.get("NO_CLUSTER", "0") != "1":
        cluster_blocks(nc)
    return nc


ENG_ORDER = ("EngineType.SP", "EngineType.Pool", "EngineType.PE",
             "EngineType.DVE", "EngineType.Activation")


def _block_engine(bb):
    for ins in bb.instructions:
        if type(ins).__name__ == "InstUnconditionalBranch":
            continue
        return str(getattr(ins, "engine", None))
    return "empty"


def cluster_blocks(nc):
    fn = nc.m.functions[0]
    bbs = list(fn.blocks)
    head, tail = bbs[0], bbs[-1]
    groups = {}
    for bb in bbs[1:-1]:
        groups.setdefault(_block_engine(bb), []).append(bb)
    new_mid = []
    for eng in ENG_ORDER:
        new_mid.extend(groups.pop(eng, []))
    for eng in sorted(groups):
        new_mid.extend(groups[eng])
    chain = [head] + new_mid + [tail]
    for i, bb in enumerate(chain[:-1]):
        ins = list(bb.instructions)
        assert type(ins[-1]).__name__ == "InstUnconditionalBranch", bb.name
        br = mybir.InstUnconditionalBranch(
            name=ins[-1].name, target=chain[i + 1].name, ins=[], outs=[],
            engine=ins[-1].engine)
        bb.instructions = ins[:-1] + [br]
    fn.blocks = chain
    return nc


def emit_scan_a(bld, block, env):
    g = lambda k: env[k]
    mk, w_, bar = bld.mk, bld.w, bld.bar
    SP, GP, AC, DV, PE = g("SP"), g("GP"), g("AC"), g("DV"), g("PE")
    SPL, BCS, GL, WS = g("SPL"), g("BCS"), g("GL"), g("WS")
    A16, B16, C16, WBs, FS, SS = g("A16"), g("B16"), g("C16"), g("WBs"), g("FS"), g("SS")
    PB, PT = g("PB"), g("PT")
    MKA, id16 = g("MKA"), g("id16")
    qn_v, kn_v, L_v = g("qn_v"), g("kn_v"), g("L_v")
    vtm_d, btm_d = g("vtm_d"), g("btm_d")
    fm = g("fm")
    OPx = OP
    import os as _os2

    NG = T // 256
    vtmg = WBs[:, 0:4096]
    nb_t = WBs[:, 4224:4352]
    Zp = [WBs[:, 4352 + i * 64: 4352 + (i + 1) * 64] for i in range(6)]
    ntr = WBs[:, 4736:4800]
    nb_t2 = WBs[:, 7296:7424]
    Zp2 = [WBs[:, 7424 + i * 64: 7424 + (i + 1) * 64] for i in range(6)]
    ntr2 = WBs[:, 7808:7872]
    Wt_ = WBs[:, 5248:5760]
    yc = [WBs[:, 5760:6272], WBs[:, 6272:6784]]
    Uc = WBs[:, 6784:7296]
    khf = WBs[:, 9344:9856]
    khT = WBs[:, 9856:10880]
    zps = WBs[:, 10880:11392]
    Lg = lambda ch, a, b: FS[:, ch * 257 + a: ch * 257 + b]
    cmg = FS[:, 2056:2088]
    ceg = FS[:, 2088:2120]
    nmg = FS[:, 2120:2152]
    STT = FS[:, 2184:2696]
    btg = FS[:, 2696:2760]

    def apx(sl, *dims):
        return bass.AP(sl.tensor, sl.offset, [sl.ap[0]] + list(dims))

    @block.vector
    def _(v):
        bar(v)
        DV.inc(v.memset(STT, 0.0))
        mk("sa.st.-1.7", DV)

    for gi in range(NG):
        t0 = gi * 256

        @block.sync
        def _(q, gi=gi, t0=t0):
            if gi == 0:
                bar(q)
            else:
                w_(q, f"sa.grpdone.{gi-1}")
            GL.inc(q.dma_start(out=C16[:, 0:2048].rearrange("p (c t) -> p c t", c=8),
                               in_=qn_v[:, :, t0:t0 + 256]), 16)
            GL.inc(q.dma_start(out=C16[:, 2048:4096].rearrange("p (c t) -> p c t", c=8),
                               in_=kn_v[:, :, t0:t0 + 256]), 16)
            GL.inc(q.dma_start(out=FS[:, 0:2056].rearrange("p (c t) -> p c t", c=8),
                               in_=L_v[:, :, 64 + t0 - 1: 64 + t0 + 256]), 16)
            for half in range(2):
                o = half * 64
                GL.inc(q.dma_start(out=vtmg[o:o + 64, :].rearrange("p (c n) -> p c n", c=4),
                                   in_=bass.AP(vtm_d.tensor, t0 * 1024,
                                               [[1024, 64], [65536, 4], [1, 1024]])), 16)
                GL.inc(q.dma_start(out=btg[o:o + 64, :].rearrange("p (c n) -> p c n", c=4),
                                   in_=bass.AP(btm_d.tensor, t0 * 16,
                                               [[16, 64], [1024, 4], [1, 16]])), 16)
            mk(f"sa.ld.{gi}", GL)

        @block.vector
        def _(v, gi=gi):
            w_(v, f"sa.ld.{gi}")
            Lv_ = FS[:, 0:2056]
            midAP = apx(Lv_[:, 32:33], [257, 8], [64, 4])
            startAP = apx(Lv_[:, 0:1], [257, 8], [64, 4])
            endAP = apx(Lv_[:, 64:65], [257, 8], [64, 4])
            nmv = nmg.rearrange("p (a b) -> p a b", a=8)
            cmv = cmg.rearrange("p (a b) -> p a b", a=8)
            cev = ceg.rearrange("p (a b) -> p a b", a=8)
            v.tensor_scalar(nmv, midAP, -1.0, None, OPx.mult)
            v.tensor_tensor(cmv, midAP, startAP, OPx.subtract)
            DV.inc(v.tensor_tensor(cev, endAP, startAP, OPx.subtract))
            mk(f"sa.nm.{gi}", DV)

        @block.scalar
        def _(s, gi=gi):
            w_(s, f"sa.nm.{gi}")
            s.activation(cmg, cmg, AF.Exp)
            s.activation(ceg, ceg, AF.Exp)
            for ch in range(8):
                for c_ in range(4):
                    src = Lg(ch, 1 + c_ * 64, 1 + c_ * 64 + 64)
                    nmcol = nmg[:, ch * 4 + c_: ch * 4 + c_ + 1]
                    midcol = Lg(ch, 1 + c_ * 64 + 31, 1 + c_ * 64 + 32)
                    s.activation(A16[:, 4096 + ch * 256 + c_ * 64: 4096 + ch * 256 + c_ * 64 + 64],
                                 src, AF.Exp, bias=nmcol)
                    aci = s.activation(A16[:, 6144 + ch * 256 + c_ * 64: 6144 + ch * 256 + c_ * 64 + 64],
                                       src, AF.Exp, bias=midcol, scale=-1.0)
            AC.inc(aci)
            mk(f"sa.exp.{gi}", AC)

        @block.vector
        def _(v, gi=gi):
            w_(v, f"sa.exp.{gi}")
            for ch in range(8):
                v.tensor_tensor(A16[:, ch * 512: ch * 512 + 256],
                                C16[:, ch * 256: ch * 256 + 256],
                                A16[:, 4096 + ch * 256: 4096 + ch * 256 + 256], OPx.mult)
                v.tensor_tensor(A16[:, ch * 512 + 256: ch * 512 + 512],
                                C16[:, 2048 + ch * 256: 2048 + ch * 256 + 256],
                                A16[:, 4096 + ch * 256: 4096 + ch * 256 + 256], OPx.mult)
                wbi = v.tensor_tensor(C16[:, 4096 + ch * 256: 4096 + ch * 256 + 256],
                                      C16[:, 2048 + ch * 256: 2048 + ch * 256 + 256],
                                      A16[:, 6144 + ch * 256: 6144 + ch * 256 + 256], OPx.mult)
            DV.inc(wbi)
            mk(f"sa.wb.{gi}", DV)

        SA_LIM = int(_os2.environ.get("SA_LIM", "9999"))
        for c_ in range(4):
            ci = gi * 4 + c_
            if ci >= SA_LIM:
                mk(f"sa.st.{ci}.7", DV)
                continue

            @block.vector
            def _(v, gi=gi, c_=c_, ci=ci):
                w_(v, f"sa.wb.{gi}")
                w_(v, f"sa.st.{ci-1}.7")
                for ch2 in range(8):
                    cmAP = apx(cmg[:, ch2 * 4 + c_: ch2 * 4 + c_ + 1], [0, 64])
                    DV.inc(v.tensor_tensor(zps[:, ch2 * 64:(ch2 + 1) * 64],
                                           STT[:, ch2 * 64:(ch2 + 1) * 64], cmAP, OPx.mult))
                mk(f"sa.z0p.{ci}", DV)

            @block.tensor
            def _(t, c_=c_, ci=ci):
                w_(t, f"sa.z0p.{ci}")
                w_(t, f"sa.wt.{ci-1}")
                for h in range(16):
                    ch, o = h // 2, (h % 2) * 64
                    mmi = t.matmul(PB[2][o:o + 64, ch * 64: ch * 64 + 64],
                                   A16[o:o + 64, ch * 512 + 256 + c_ * 64: ch * 512 + 256 + c_ * 64 + 64],
                                   zps[o:o + 64, ch * 64:ch * 64 + 64],
                                   start=True, stop=True)
                PE.inc(mmi)
                mk(f"sa.rk.{ci}", PE)

            @block.vector
            def _(v, c_=c_, ci=ci):
                w_(v, f"sa.rk.{ci}")
                ev = bass.AP(vtmg.tensor, vtmg.offset + c_ * 1024,
                             [vtmg[0:64, 0:1].ap[0], [128, 8], [1, 64]])
                od = bass.AP(vtmg.tensor, vtmg[64:128, 0:1].offset + c_ * 1024 + 64,
                             [vtmg[64:128, 0:1].ap[0], [128, 8], [1, 64]])
                v.tensor_tensor(Wt_[0:64, :].rearrange("p (a b) -> p a b", a=8),
                                PB[2][0:64, :].rearrange("p (a b) -> p a b", a=8), ev, OPx.subtract)
                DV.inc(v.tensor_tensor(Wt_[64:128, :].rearrange("p (a b) -> p a b", a=8),
                                       PB[2][64:128, :].rearrange("p (a b) -> p a b", a=8), od, OPx.subtract))
                mk(f"sa.wt.{ci}", DV)

            def steps(ch):
                p2 = ch % 2
                nbt = nb_t if p2 == 0 else nb_t2
                Zpl = Zp if p2 == 0 else Zp2
                ntrt = ntr if p2 == 0 else ntr2
                trc = slice(0, 64)
                ktc = slice(64, 128)
                sqc = slice(0, 64)
                pwc = slice(0, 128)
                out = []

                def pw(t, ch=ch, c_=c_, ci=ci):
                    w_(t, f"sa.wt.{ci}")
                    w_(t, f"sa.z1.{ci}.{ch-1}")
                    for half in range(2):
                        o = half * 64
                        rhs = A16[o:o + 64, ch * 512: ch * 512 + 512].rearrange(
                            "p (qk t) -> p qk t", qk=2)[:, :, c_ * 64:(c_ + 1) * 64]
                        mmi = t.matmul(PB[0][o:o + 64, pwc],
                                       C16[o:o + 64, 4096 + ch * 256 + c_ * 64: 4096 + ch * 256 + c_ * 64 + 64],
                                       rhs, start=True, stop=True)
                    PE.inc(mmi)
                    mk(f"sa.pa.{ci}.{ch}", PE)
                out.append(("tensor", pw))

                def z1f(v, ch=ch, c_=c_, ci=ci):
                    w_(v, f"sa.pa.{ci}.{ch}")
                    DV.inc(v.tensor_tensor(nbt[:], PB[0][:, pwc], MKA[:], OPx.mult))
                    bld.fence(v, DV)
                    for half in range(2):
                        o = half * 64
                        bcol = btg[o:o + 64, c_ * 16 + 2 * ch + half: c_ * 16 + 2 * ch + half + 1]
                        zli = v.tensor_scalar(Zpl[0][o:o + 64, :], nbt[o:o + 64, 64:128],
                                              bcol, None, OPx.mult)
                    DV.inc(zli)
                    mk(f"sa.z1.{ci}.{ch}", DV)
                out.append(("vector", z1f))

                for lev in range(1, 6):
                    def trf(t, ch=ch, ci=ci, lev=lev, p2=p2):
                        w_(t, f"sa.z1.{ci}.{ch}" if lev == 1 else f"sa.sqe.{ci}.{ch}.{lev-1}")
                        if p2:
                            w_(t, f"sa.tre.{ci}.{ch-1}.{lev}")
                        elif lev >= 2:
                            w_(t, f"sa.tre.{ci}.{ch+1}.{lev-1}")
                        else:
                            w_(t, f"sa.tre.{ci}.{ch-1}.5")
                        t.transpose(PT[0:64, trc], Zpl[lev - 1][0:64, :], id16[0:64, 0:64])
                        PE.inc(t.transpose(PT[64:128, trc], Zpl[lev - 1][64:128, :], id16[64:128, 64:128]))
                        mk(f"sa.tr.{ci}.{ch}.{lev}", PE)
                    out.append(("tensor", trf))

                    def tref(v, ch=ch, ci=ci, lev=lev):
                        w_(v, f"sa.tr.{ci}.{ch}.{lev}")
                        DV.inc(v.tensor_copy(ntrt[:], PT[:, trc]))
                        mk(f"sa.tre.{ci}.{ch}.{lev}", DV)
                    out.append(("vector", tref))

                    def sqf(t, ch=ch, ci=ci, lev=lev, p2=p2):
                        w_(t, f"sa.tre.{ci}.{ch}.{lev}")
                        if p2:
                            w_(t, f"sa.sqe.{ci}.{ch-1}.{lev}")
                        elif lev >= 2:
                            w_(t, f"sa.sqe.{ci}.{ch+1}.{lev-1}")
                        else:
                            w_(t, f"sa.sqe.{ci}.{ch-1}.5")
                        for half in range(2):
                            o = half * 64
                            mmi = t.matmul(PB[1][o:o + 64, sqc], ntrt[o:o + 64, :],
                                           Zpl[lev - 1][o:o + 64, :], start=True, stop=True)
                        PE.inc(mmi)
                        mk(f"sa.sq.{ci}.{ch}.{lev}", PE)
                    out.append(("tensor", sqf))

                    def sqef(v, ch=ch, ci=ci, lev=lev):
                        w_(v, f"sa.sq.{ci}.{ch}.{lev}")
                        DV.inc(v.tensor_copy(Zpl[lev][:], PB[1][:, sqc]))
                        mk(f"sa.sqe.{ci}.{ch}.{lev}", DV)
                    out.append(("vector", sqef))

                for li, zi in enumerate([5, 4, 3, 2, 1, 0]):
                    def appf(t, ch=ch, ci=ci, li=li, zi=zi, p2=p2):
                        w_(t, f"sa.sqe.{ci}.{ch}.5" if li == 0 else f"sa.app.{ci}.{ch}.{li-1}")
                        if li == 0:
                            w_(t, f"sa.app.{ci}.{ch-2}.5")
                        src = Wt_ if li == 0 else yc[(li - 1) % 2]
                        apb = PB[4] if p2 == 0 else PB[3]
                        for half in range(2):
                            o = half * 64
                            mmi = t.matmul(apb[o:o + 64, sqc], Zpl[zi][o:o + 64, :],
                                           src[o:o + 64, ch * 64: ch * 64 + 64],
                                           start=True, stop=True)
                        PE.inc(mmi)
                        mk(f"sa.appm.{ci}.{ch}.{li}", PE)
                    out.append(("tensor", appf))

                    def addf(v, ch=ch, ci=ci, li=li, p2=p2):
                        w_(v, f"sa.appm.{ci}.{ch}.{li}")
                        if li > 0:
                            w_(v, f"sa.app.{ci}.{ch}.{li-1}")
                        src = Wt_ if li == 0 else yc[(li - 1) % 2]
                        apb = PB[4] if p2 == 0 else PB[3]
                        DV.inc(v.tensor_tensor(yc[li % 2][:, ch * 64: ch * 64 + 64],
                                               src[:, ch * 64: ch * 64 + 64],
                                               apb[:, sqc], OPx.add))
                        mk(f"sa.app.{ci}.{ch}.{li}", DV)
                    out.append(("vector", addf))

                def ucf(v, ch=ch, ci=ci, c_=c_):
                    w_(v, f"sa.app.{ci}.{ch}.5")
                    for half in range(2):
                        o = half * 64
                        bcol = btg[o:o + 64, c_ * 16 + 2 * ch + half: c_ * 16 + 2 * ch + half + 1]
                        uci = v.tensor_scalar(Uc[o:o + 64, ch * 64: ch * 64 + 64],
                                              yc[1][o:o + 64, ch * 64: ch * 64 + 64],
                                              bcol, None, OPx.mult)
                    DV.inc(uci)
                    mk(f"sa.uc.{ci}.{ch}", DV)
                out.append(("vector", ucf))

                def outf(t, ch=ch, ci=ci, c_=c_):
                    w_(t, f"sa.uc.{ci}.{ch}")
                    w_(t, f"sa.ye.{ci}.{ch-1}")
                    for half in range(2):
                        o = half * 64
                        t.matmul(PB[5][o:o + 64, sqc],
                                 zps[o:o + 64, ch * 64: ch * 64 + 64],
                                 A16[o:o + 64, ch * 512 + c_ * 64: ch * 512 + c_ * 64 + 64],
                                 start=True, stop=False)
                        mmi = t.matmul(PB[5][o:o + 64, sqc],
                                       Uc[o:o + 64, ch * 64: ch * 64 + 64],
                                       nbt[o:o + 64, 0:64],
                                       start=False, stop=True)
                    PE.inc(mmi)
                    mk(f"sa.out.{ci}.{ch}", PE)
                out.append(("tensor", outf))

                def yef(v, ch=ch, ci=ci):
                    w_(v, f"sa.out.{ci}.{ch}")
                    DV.inc(v.tensor_copy(fm(B16, ch, ci * 64, ci * 64 + 64), PB[5][:, sqc]))
                    mk(f"sa.ye.{ci}.{ch}", DV)
                out.append(("vector", yef))

                def khff(v, ch=ch, ci=ci, c_=c_):
                    w_(v, f"sa.ye.{ci}.{ch}")
                    ecol = A16[:, 4096 + ch * 256 + c_ * 64 + 63: 4096 + ch * 256 + c_ * 64 + 64]
                    ecb = bass.AP(ecol.tensor, ecol.offset, [ecol.ap[0], [0, 64]])
                    DV.inc(v.tensor_tensor(khf[:, ch * 64: ch * 64 + 64],
                                           C16[:, 4096 + ch * 256 + c_ * 64: 4096 + ch * 256 + c_ * 64 + 64],
                                           ecb, OPx.mult))
                    mk(f"sa.kh.{ci}.{ch}", DV)
                out.append(("vector", khff))

                def ktrf(t, ch=ch, ci=ci):
                    w_(t, f"sa.kh.{ci}.{ch}")
                    w_(t, f"sa.ktre.{ci}.{ch-1}")
                    t.transpose(PT[0:64, ktc], khf[0:64, ch * 64: ch * 64 + 64], id16[0:64, 0:64])
                    PE.inc(t.transpose(PT[64:128, ktc], khf[64:128, ch * 64: ch * 64 + 64],
                                       id16[64:128, 64:128]))
                    mk(f"sa.ktr.{ci}.{ch}", PE)
                out.append(("tensor", ktrf))

                def ktref(v, ch=ch, ci=ci):
                    w_(v, f"sa.ktr.{ci}.{ch}")
                    DV.inc(v.tensor_copy(khT[:, ch * 64: ch * 64 + 64], PT[:, ktc]))
                    mk(f"sa.ktre.{ci}.{ch}", DV)
                out.append(("vector", ktref))

                def stmf(t, ch=ch, ci=ci):
                    w_(t, f"sa.ktre.{ci}.{ch}")
                    w_(t, f"sa.st.{ci}.{ch-1}")
                    for half in range(2):
                        o = half * 64
                        mmi = t.matmul(PB[6][o:o + 64, sqc], khT[o:o + 64, ch * 64: ch * 64 + 64],
                                       Uc[o:o + 64, ch * 64: ch * 64 + 64], start=True, stop=True)
                    PE.inc(mmi)
                    mk(f"sa.stm.{ci}.{ch}", PE)
                out.append(("tensor", stmf))

                def stf(v, ch=ch, ci=ci, c_=c_):
                    w_(v, f"sa.stm.{ci}.{ch}")
                    cecol = ceg[:, ch * 4 + c_: ch * 4 + c_ + 1]
                    DV.inc(v.scalar_tensor_tensor(STT[:, ch * 64: ch * 64 + 64],
                                                  STT[:, ch * 64: ch * 64 + 64],
                                                  cecol, PB[6][:, sqc], OPx.mult, OPx.subtract))
                    mk(f"sa.st.{ci}.{ch}", DV)
                out.append(("vector", stf))
                return out

            def emit_step(step):
                eng, fnc = step
                if eng == "tensor":
                    block.tensor(fnc)
                else:
                    block.vector(fnc)

            for cp in range(4):
                sA = steps(2 * cp)
                sB = steps(2 * cp + 1)
                emit_step(sA[0])
                for i in range(1, len(sA)):
                    emit_step(sA[i])
                    emit_step(sB[i - 1])
                emit_step(sB[-1])
            mk(f"sa.st.{ci}.7", DV)
        mk(f"sa.grpdone.{gi}", DV)

    y16_v = g("y16_v")

    @block.sync
    def _(q):
        bar(q)
        for c in range(DCH):
            SP.inc(q.dma_start(out=y16_v[:, c, :], in_=fm(B16, c)), 16)
        mk("sa.ysp", SP)


def emit_tail_a(bld, block, env):
    g = lambda k: env[k]
    mk, w_, bar, fn = bld.mk, bld.w, bld.bar, bld.fence
    SP, GP, AC, DV, PE = g("SP"), g("GP"), g("AC"), g("DV"), g("PE")
    SPL, BCS, GL, WS = g("SPL"), g("BCS"), g("GL"), g("WS")
    XR, A16, B16, C16, WBs, FS = g("XR"), g("A16"), g("B16"), g("C16"), g("WBs"), g("FS")
    PB, PT = g("PB"), g("PT")
    W, CCc, row_d, gate_v = g("W"), g("CCc"), g("row_d"), g("gate_v")
    fm, ccol, b2c, ones16 = g("fm"), g("ccol"), g("b2c"), g("ones16")
    stats_rows, row_math = g("stats_rows"), g("row_math")
    rep512, rep_head64 = g("rep512"), g("rep_head64")
    load_whalf, ws, wblk = g("load_whalf"), g("ws"), g("wblk")

    for c in range(DCH):
        for tt in range(4):
            sqs = C16[:, tt * 512:(tt + 1) * 512]

            @block.scalar
            def _(s, c=c, tt=tt, sqs=sqs):
                if c == 0 and tt == 0:
                    bar(s)
                w_(s, f"hn.nmm.{c-1}.{tt}")
                AC.inc(s.activation(sqs, fm(B16, c, tt * TT, (tt + 1) * TT), AF.Square))
                mk(f"hn.sq.{c}.{tt}", AC)

            @block.tensor
            def _(t, c=c, tt=tt, sqs=sqs):
                w_(t, f"hn.sq.{c}.{tt}")
                w_(t, f"hn.rev.{c}.{tt-1}" if tt > 0 else f"hn.rev.{c-1}.3")
                PE.inc(t.matmul(PB[6][0:2, :], b2c, sqs, start=True, stop=True))
                mk(f"hn.nmm.{c}.{tt}", PE)

            @block.scalar
            def _(s, c=c, tt=tt):
                w_(s, f"hn.nmm.{c}.{tt}")
                if tt == 0:
                    w_(s, f"hn.rsp.{c-1}")
                AC.inc(s.activation(FS[0:2, 2048 + tt * 512: 2048 + (tt + 1) * 512],
                                    PB[6][0:2, :], AF.Copy))
                mk(f"hn.rev.{c}.{tt}", AC)

        @block.sync
        def _(q, c=c):
            w_(q, f"hn.rev.{c}.3")
            SPL.inc(q.dma_start(out=row_d[4 + 2 * c: 6 + 2 * c, :], in_=FS[0:2, 2048:4096]), 16)
            mk(f"hn.rsp.{c}", SPL)

    @block.sync
    def _(q):
        bar(q)
        SP.inc(q.dma_start(out=FS[:, 2048:2304],
                           in_=bass.AP(row_d.tensor, 4 * 2048, [[256, 128], [1, 256]])), 16)

    @block.scalar
    def _(s):
        bar(s)
        AC.inc(s.activation(FS[:, 2816:3072], FS[:, 2048:2304], AF.Ln, scale=1.0 / 64, bias=CCc[:, 115:116]))
        fn(s, AC)
        AC.inc(s.activation(FS[:, 2560:2816], FS[:, 2816:3072], AF.Exp, scale=-0.5))
        mk("hn.rm", AC)

    @block.sync
    def _(q):
        bar(q)
        SP.inc(q.dma_start(out=bass.AP(row_d.tensor, 20 * 2048, [[256, 128], [1, 256]]),
                           in_=FS[:, 2560:2816]), 16)
        mk("hn.rqsp", SP)

    rqb = g("WBC")[:, 0:512]
    for c in range(DCH):
        for tt in range(4):
            @block.sync
            def _(q, c=c, tt=tt):
                if c == 0 and tt == 0:
                    bar(q)
                w_(q, f"hn.ap.{c}.{tt-1}" if tt > 0 else f"hn.ap.{c-1}.3")
                rep_head64(q, 20, c, tt, rqb)
                mk(f"hn.bc.{c}.{tt}", BCS)

            @block.vector
            def _(v, c=c, tt=tt):
                w_(v, f"hn.bc.{c}.{tt}")
                DV.inc(v.scalar_tensor_tensor(fm(B16, c, tt * TT, (tt + 1) * TT),
                                              fm(B16, c, tt * TT, (tt + 1) * TT),
                                              ccol("hnw", c), rqb, OP.mult, OP.mult))
                mk(f"hn.ap.{c}.{tt}", DV)
    mk("hn.done", DV)

    @block.gpsimd
    def _(g2):
        bar(g2)
        load_whalf(g2, W["a_out_w"], 0, 0)
        load_whalf(g2, W["a_out_w"], 1, 1)

    gsl = [C16[:, 4096 + k * 512: 4096 + (k + 1) * 512] for k in range(2)]
    t1s = FS[:, 2048:2560]
    u1s = FS[:, 3072:3584]
    mb, cpb = g("WBC")[:, 512:1024], g("WBC")[:, 1024:1536]
    for tt in range(4):
        @block.sync
        def _(q, tt=tt):
            if tt == 0:
                bar(q)
            w_(q, f"op.yi.{DCH-1}.{tt-1}")
            rep512(q, 0, tt, mb)
            rep512(q, 2, tt, cpb)
            mk(f"op.bc.{tt}", BCS)

        for mo in range(DCH):
            @block.tensor
            def _(t, tt=tt, mo=mo):
                if tt == 0 and mo in (0, 4):
                    bld.fence(t, g("WS")[mo // 4])
                w_(t, f"op.yi.{mo-2}.{tt}" if mo >= 2 else (f"op.yi.{6+mo}.{tt-1}" if tt > 0 else "x.-1"))
                for k in range(DCH):
                    mmi = t.matmul(PB[mo % 2][:], wblk(mo // 4, k, mo % 4),
                                   fm(B16, k, tt * TT, (tt + 1) * TT),
                                   start=(k == 0), stop=(k == DCH - 1))
                PE.inc(mmi)
                mk(f"op.mm.{mo}.{tt}", PE)

            @block.sync
            def _(q, tt=tt, mo=mo):
                w_(q, f"op.gld.{mo-1}.{tt}" if mo > 0 else f"op.gld.{DCH-1}.{tt-1}")
                w_(q, f"op.yi.{mo-2}.{tt}" if mo >= 2 else (f"op.yi.{6+mo}.{tt-1}" if tt > 0 else "x.-1"))
                GL.inc(q.dma_start(out=gsl[mo % 2], in_=gate_v[:, mo, tt * TT:(tt + 1) * TT]), 16)
                mk(f"op.gld.{mo}.{tt}", GL)

            @block.vector
            def _(v, tt=tt, mo=mo):
                w_(v, f"op.mm.{mo}.{tt}")
                w_(v, f"op.gld.{mo}.{tt}")
                w_(v, f"op.bc.{tt}")
                fn(v, DV)
                DV.inc(v.tensor_tensor(t1s, XR[:, mo * T + tt * TT: mo * T + (tt + 1) * TT], mb, OP.subtract))
                fn(v, DV)
                DV.inc(v.tensor_tensor(u1s, t1s, cpb, OP.mult))
                fn(v, DV)
                DV.inc(v.tensor_tensor(t1s, PB[mo % 2][:], gsl[mo % 2], OP.mult))
                fn(v, DV)
                DV.inc(v.tensor_tensor(fm(A16, mo, tt * TT, (tt + 1) * TT), u1s, t1s, OP.add))
                mk(f"op.yi.{mo}.{tt}", DV)
    mk("op.done", DV)

    stats_rows("rms1", A16, 0, 1, want_mean=False)

    def rms1_math():
        @block.scalar
        def _(s):
            bar(s)
            AC.inc(s.activation(FS[:, 2144:2160], FS[:, 2048:2064], AF.Ln, scale=1.0 / D, bias=CCc[:, 115:116]))
            fn(s, AC)
            AC.inc(s.activation(FS[:, 2560:2576], FS[:, 2144:2160], AF.Exp, scale=-0.5))
            mk("rms1.c", AC)
    row_math("rms1m", rms1_math, [1], 1, [2], 1)

    c2b = g("WBC")[:, 0:512]
    for tt in range(4):
        @block.sync
        def _(q, tt=tt):
            if tt == 0:
                bar(q)
            w_(q, f"rms1.ap.{DCH-1}.{tt-1}")
            rep512(q, 2, tt, c2b)
            mk(f"rms1.bc.{tt}", BCS)

        for c in range(DCH):
            @block.vector
            def _(v, tt=tt, c=c):
                w_(v, f"rms1.bc.{tt}")
                DV.inc(v.scalar_tensor_tensor(fm(B16, c, tt * TT, (tt + 1) * TT),
                                              fm(A16, c, tt * TT, (tt + 1) * TT),
                                              ccol("fnw", c), c2b, OP.mult, OP.mult))
                mk(f"rms1.ap.{c}.{tt}", DV)

    w1s, w3s, w2s = WBs[:, 0:4096], WBs[:, 4096:8192], WBs[:, 8192:12288]
    sil = C16[:, 0:512]
    h2 = lambda fc: C16[:, 2048 + fc * 512: 2048 + (fc + 1) * 512]
    for fb in range(8):
        @block.gpsimd
        def _(g2, fb=fb):
            if fb == 0:
                bar(g2)
            else:
                w_(g2, f"ff.mm2.{fb-1}.3.{DCH-1}")
            WS[0].inc(g2.dma_start(
                out=w1s.rearrange("p (k n) -> p k n", k=8),
                in_=W["a_ff_w1"].rearrange("(k p) n -> p k n", p=128)[:, :, fb * 512:(fb + 1) * 512]), 16)
            WS[1].inc(g2.dma_start(
                out=w3s.rearrange("p (k n) -> p k n", k=8),
                in_=W["a_ff_w3"].rearrange("(k p) n -> p k n", p=128)[:, :, fb * 512:(fb + 1) * 512]), 16)
            WS[2].inc(g2.dma_start(
                out=w2s.rearrange("p (k n) -> p k n", k=4),
                in_=W["a_ff_w2"].rearrange("(k p) n -> p k n", p=128)[:, 4 * fb: 4 * fb + 4, :]), 16)
            mk(f"ff.w.{fb}", GP)

        for tt in range(4):
            for fc in range(4):
                @block.tensor
                def _(t, fb=fb, tt=tt, fc=fc):
                    if tt == 0 and fc == 0:
                        bld.fence(t, WS[0])
                        bld.fence(t, WS[1])
                    w_(t, f"ff.sil.{fb}.{tt}.{fc-1}" if fc > 0 else f"ff.sil.{fb}.{tt-1}.3")
                    w_(t, f"ff.gu.{fb}.{tt}.{fc-1}" if fc > 0 else f"ff.gu.{fb}.{tt-1}.3")
                    for k in range(DCH):
                        t.matmul(PB[0][:], w1s[:, k * 512 + fc * 128: k * 512 + fc * 128 + 128],
                                 fm(B16, k, tt * TT, (tt + 1) * TT), start=(k == 0), stop=(k == DCH - 1))
                    for k in range(DCH):
                        mmi = t.matmul(PB[1][:], w3s[:, k * 512 + fc * 128: k * 512 + fc * 128 + 128],
                                       fm(B16, k, tt * TT, (tt + 1) * TT), start=(k == 0), stop=(k == DCH - 1))
                    PE.inc(mmi)
                    mk(f"ff.mm13.{fb}.{tt}.{fc}", PE)

                @block.scalar
                def _(s, fb=fb, tt=tt, fc=fc):
                    w_(s, f"ff.mm13.{fb}.{tt}.{fc}")
                    AC.inc(s.activation(sil, PB[0][:], AF.Silu))
                    mk(f"ff.sil.{fb}.{tt}.{fc}", AC)

                @block.vector
                def _(v, fb=fb, tt=tt, fc=fc):
                    w_(v, f"ff.sil.{fb}.{tt}.{fc}")
                    w_(v, f"ff.mm2.{fb-1}.{tt}.{DCH-1}" if fb > 0 else "x.-1")
                    DV.inc(v.tensor_tensor(h2(fc), sil, PB[1][:], OP.mult))
                    mk(f"ff.gu.{fb}.{tt}.{fc}", DV)

            for mo in range(DCH):
                @block.tensor
                def _(t, fb=fb, tt=tt, mo=mo):
                    if mo == 0:
                        bld.fence(t, WS[2])
                    w_(t, f"ff.gu.{fb}.{tt}.3")
                    w_(t, f"ff.ev.{fb}.{tt}.{mo-2}" if mo >= 2 else (f"ff.ev.{fb}.{tt-1}.{6+mo}" if tt > 0 else f"ff.ev.{fb-1}.3.{6+mo}"))
                    for fc in range(4):
                        mmi = t.matmul(PB[2 + mo % 2][:], w2s[:, fc * 1024 + mo * 128: fc * 1024 + mo * 128 + 128],
                                       h2(fc), start=(fc == 0), stop=(fc == 3))
                    PE.inc(mmi)
                    mk(f"ff.mm2.{fb}.{tt}.{mo}", PE)

                @block.vector
                def _(v, fb=fb, tt=tt, mo=mo):
                    w_(v, f"ff.mm2.{fb}.{tt}.{mo}")
                    if fb == 0:
                        fn(v, DV)
                        DV.inc(v.tensor_tensor(t1s, PB[2 + mo % 2][:], fm(A16, mo, tt * TT, (tt + 1) * TT), OP.add))
                        fn(v, DV)
                        DV.inc(v.tensor_tensor(XR[:, mo * T + tt * TT: mo * T + (tt + 1) * TT],
                                               XR[:, mo * T + tt * TT: mo * T + (tt + 1) * TT], t1s, OP.add))
                    else:
                        DV.inc(v.tensor_tensor(XR[:, mo * T + tt * TT: mo * T + (tt + 1) * TT],
                                               XR[:, mo * T + tt * TT: mo * T + (tt + 1) * TT],
                                               PB[2 + mo % 2][:], OP.add))
                    mk(f"ff.ev.{fb}.{tt}.{mo}", DV)
    mk("tail.done", DV)


def emit_ffn(bld, block, env, round_):
    g = lambda k: env[k]
    mk, w_, bar, fn = bld.mk, bld.w, bld.bar, bld.fence
    R = f"f{round_}"
    SP, GP, AC, DV, PE = g("SP"), g("GP"), g("AC"), g("DV"), g("PE")
    SPL, BCS, GL, WS = g("SPL"), g("BCS"), g("GL"), g("WS")
    XR, A16, B16, C16, WBs, FS = g("XR"), g("A16"), g("B16"), g("C16"), g("WBs"), g("FS")
    PB, PT = g("PB"), g("PT")
    W, CCc, row_d = g("W"), g("CCc"), g("row_d")
    gates_d, id16 = g("gates_d"), g("id16")
    fm, ccol, ones16 = g("fm"), g("ccol"), g("ones16")
    stats_rows, row_math, rep512 = g("stats_rows"), g("row_math"), g("rep512")

    @block.vector
    def _(v):
        bar(v)
        for c in range(DCH):
            DV.inc(v.tensor_copy(fm(B16, c), fm(XR, c)))
    stats_rows(R + "ln", B16, 0, 1, want_mean=True)

    def lnm():
        @block.vector
        def _(v):
            bar(v)
            DV.inc(v.tensor_scalar(FS[:, 2560:2576], FS[:, 2048:2064], 1.0 / D, None, OP.mult))
            DV.inc(v.tensor_scalar(FS[:, 2080:2096], FS[:, 2064:2080], 1.0 / D, None, OP.mult))
            fn(v, DV)
            DV.inc(v.tensor_tensor(FS[:, 2096:2112], FS[:, 2560:2576], FS[:, 2560:2576], OP.mult))
            fn(v, DV)
            DV.inc(v.tensor_tensor(FS[:, 2112:2128], FS[:, 2080:2096], FS[:, 2096:2112], OP.subtract))
            mk(R + "ln.var", DV)

        @block.scalar
        def _(s):
            w_(s, R + "ln.var")
            AC.inc(s.activation(FS[:, 2128:2144], FS[:, 2112:2128], AF.Ln, bias=CCc[:, 114:115]))
            fn(s, AC)
            AC.inc(s.activation(FS[:, 2576:2592], FS[:, 2128:2144], AF.Exp, scale=-0.5))
            mk(R + "ln.c", AC)
    row_math(R + "lnm", lnm, [0, 1], 2, [0, 2], 2)

    mb, cpb = g("WBC")[:, 0:512], g("WBC")[:, 512:1024]
    t1s = FS[:, 2048:2560]
    for tt in range(4):
        @block.sync
        def _(q, tt=tt):
            if tt == 0:
                bar(q)
            w_(q, R + f"ln.ap.{DCH-1}.{tt-1}")
            rep512(q, 0, tt, mb)
            rep512(q, 2, tt, cpb)
            mk(R + f"ln.bc.{tt}", BCS)

        for c in range(DCH):
            @block.vector
            def _(v, tt=tt, c=c):
                w_(v, R + f"ln.bc.{tt}")
                fn(v, DV)
                DV.inc(v.tensor_tensor(t1s, XR[:, c * T + tt * TT: c * T + (tt + 1) * TT], mb, OP.subtract))
                fn(v, DV)
                DV.inc(v.tensor_tensor(fm(A16, c, tt * TT, (tt + 1) * TT), t1s, cpb, OP.mult))
                mk(R + f"ln.ap.{c}.{tt}", DV)
    mk(R + "h16", DV)

    WS2 = g("WS2")
    WBCp = g("WBC")

    FSbf = FS[:, 0:2048].bitcast(BF16)
    w1b = [WBs[:, 0:4096], FSbf]
    w3b = [WBs[:, 4096:8192], C16[:, 3072:7168]]
    w2s = WBs[:, 8192:12288]
    q_w1 = [WS[0], WS2[0]]
    q_w3 = [WS[1], WS2[1]]
    sil_b = [C16[:, 0:512], C16[:, 512:1024]]
    gu = lambda dc: C16[:, 1024 + dc * 512: 1024 + (dc + 1) * 512]
    gbank = [PB[0], PB[4]]
    ubank = [PB[1], PB[5]]
    wgb = g("WBC")[:, 0:512]

    NQ = 36
    exq = [(e, q_) for e in range(9) for q_ in range(4)]

    def wref(wq):
        e, q_ = exq[wq]
        return e, q_

    def ld_w13(g2, Q, first):
        e, q_ = exq[Q]
        gated = e < 8
        wg_ap = W["f_Wg"][e] if gated else W["f_Sg"]
        wu_ap = W["f_Wu"][e] if gated else W["f_Su"]
        p = Q % 2
        if not first:
            eW, qW = exq[Q - 2]
            w_(g2, R + f"x.mm13.{eW}.{qW}.3.3")
        q_w1[p].inc(g2.dma_start(
            out=w1b[p].rearrange("p (k n) -> p k n", k=8),
            in_=wg_ap.rearrange("(k p) n -> p k n", p=128)[:, :, q_ * 512:(q_ + 1) * 512]), 16)
        q_w3[p].inc(g2.dma_start(
            out=w3b[p].rearrange("p (k n) -> p k n", k=8),
            in_=wu_ap.rearrange("(k p) n -> p k n", p=128)[:, :, q_ * 512:(q_ + 1) * 512]), 16)
        mk(R + f"x.w13.{Q}", GP)

    def ld_w2(g2, Q, first):
        e, q_ = exq[Q]
        wd_ap = W["f_Wd"][e] if e < 8 else W["f_Sd"]
        if not first:
            eW, qW = exq[Q - 1]
            w_(g2, R + f"x.mm2.{eW}.{qW}.3.{DCH-1}")
        WS[2].inc(g2.dma_start(
            out=w2s.rearrange("p (k n) -> p k n", k=4),
            in_=wd_ap.rearrange("(k p) n -> p k n", p=128)[:, 4 * q_: 4 * q_ + 4, :]), 16)
        mk(R + f"x.w2.{Q}", GP)


    @block.gpsimd
    def _(g2):
        bar(g2)
        GP.inc(g2.dma_start(out=C16[:, 8008:8072].rearrange("p (k n) -> p k n", k=8),
                            in_=W["f_Wr"].rearrange("(k p) n -> p k n", p=128)), 16)
        mk(R + "wr", GP)
        ld_w13(g2, 0, True)
        ld_w13(g2, 1, True)
        ld_w2(g2, 0, True)

    gtile = C16[:, 8000:8008]
    stile = WBCp[:, 1152:1160]
    gstage = WBCp[0:8, 1024:1152]
    rc = lambda a, b: WBCp[:, 1152 + a: 1152 + b]
    for w in range(16):
        @block.tensor
        def _(t, w=w):
            if w == 0:
                w_(t, R + "wr")
                w_(t, R + "h16")
            w_(t, R + f"g.sm.{w-1}" if w > 0 else "x.-1")
            for k in range(DCH):
                mmi = t.matmul(PB[4][0:128, 0:8], fm(A16, k, w * 128, (w + 1) * 128),
                               C16[:, 8008 + k * 8: 8008 + (k + 1) * 8],
                               start=(k == 0), stop=(k == DCH - 1))
            PE.inc(mmi)
            mk(R + f"g.mm.{w}", PE)

        @block.vector
        def _(v, w=w):
            w_(v, R + f"g.mm.{w}")
            w_(v, R + f"g.tr.{w-1}" if w > 0 else "x.-1")
            DV.inc(v.tensor_reduce(rc(8, 9), PB[4][0:128, 0:8], AX, OP.max))
            fn(v, DV)
            DV.inc(v.tensor_scalar(stile, PB[4][0:128, 0:8], rc(8, 9), None, OP.subtract))
            mk(R + f"g.sm.{w}", DV)

        @block.scalar
        def _(s, w=w):
            w_(s, R + f"g.sm.{w}")
            AC.inc(s.activation(stile, stile, AF.Exp))
            mk(R + f"g.ex.{w}", AC)

        @block.vector
        def _(v, w=w):
            w_(v, R + f"g.ex.{w}")
            DV.inc(v.tensor_reduce(rc(9, 10), stile, AX, OP.max))
            fn(v, DV)
            DV.inc(v.tensor_scalar(rc(16, 24), stile, rc(9, 10), None, OP.is_ge))
            fn(v, DV)
            DV.inc(v.scalar_tensor_tensor(rc(24, 32), rc(16, 24), -1e30, stile, OP.mult, OP.add))
            fn(v, DV)
            DV.inc(v.tensor_reduce(rc(10, 11), rc(24, 32), AX, OP.max))
            fn(v, DV)
            DV.inc(v.tensor_tensor(rc(11, 12), rc(9, 10), rc(10, 11), OP.add))
            mk(R + f"g.t2.{w}", DV)

        @block.scalar
        def _(s, w=w):
            w_(s, R + f"g.t2.{w}")
            AC.inc(s.activation(rc(12, 13), rc(11, 12), AF.Ln))
            fn(s, AC)
            AC.inc(s.activation(rc(13, 14), rc(12, 13), AF.Exp, scale=-1.0))
            mk(R + f"g.rd.{w}", AC)

        @block.vector
        def _(v, w=w):
            w_(v, R + f"g.rd.{w}")
            DV.inc(v.tensor_tensor(rc(14, 15), rc(9, 10), rc(13, 14), OP.mult))
            DV.inc(v.tensor_tensor(rc(15, 16), rc(10, 11), rc(13, 14), OP.mult))
            fn(v, DV)
            DV.inc(v.tensor_scalar(rc(16, 24), stile, rc(9, 10), None, OP.is_ge))
            DV.inc(v.tensor_scalar(rc(24, 32), stile, rc(10, 11), None, OP.is_ge))
            fn(v, DV)
            DV.inc(v.tensor_tensor(rc(24, 32), rc(24, 32), rc(16, 24), OP.subtract))
            fn(v, DV)
            DV.inc(v.tensor_scalar(rc(16, 24), rc(16, 24), rc(14, 15), None, OP.mult))
            DV.inc(v.tensor_scalar(rc(24, 32), rc(24, 32), rc(15, 16), None, OP.mult))
            fn(v, DV)
            DV.inc(v.tensor_tensor(gtile, rc(16, 24), rc(24, 32), OP.add))
            mk(R + f"g.gt.{w}", DV)

        @block.tensor
        def _(t, w=w):
            w_(t, R + f"g.gt.{w}")
            w_(t, R + f"g.rev.{w-1}" if w > 0 else "x.-1")
            PE.inc(t.transpose(PT[0:8, 0:128], gtile[0:128, :], id16))
            mk(R + f"g.tr.{w}", PE)

        @block.scalar
        def _(s, w=w):
            w_(s, R + f"g.tr.{w}")
            if w > 0:
                w_(s, R + f"g.sp.{w-1}")
            AC.inc(s.activation(gstage, PT[0:8, 0:128], AF.Copy))
            mk(R + f"g.rev.{w}", AC)

        @block.sync
        def _(q, w=w):
            w_(q, R + f"g.rev.{w}")
            SPL.inc(q.dma_start(out=gates_d[:, w * 128:(w + 1) * 128], in_=gstage), 16)
            mk(R + f"g.sp.{w}", SPL)
    mk(R + "g.sp", SPL)

    for Q in range(NQ):
        e, q_ = exq[Q]
        gated = e < 8
        p = Q % 2

        if Q > 0:
            @block.gpsimd
            def _(g2, Q=Q):
                if Q + 1 < NQ:
                    ld_w13(g2, Q + 1, False)
                ld_w2(g2, Q, False)

        for tt in range(4):
            @block.sync
            def _(q, e=e, q_=q_, tt=tt):
                if e < 8 and q_ == 0:
                    if e == 0 and tt == 0:
                        w_(q, R + "g.sp.15")
                    w_(q, R + f"x.he.{e-1}.3" if e > 0 else "x.-1")
                    if tt > 0:
                        w_(q, R + f"x.he.{e}.{tt-1}")
                    BCS.inc(q.dma_start(out=wgb, in_=bass.AP(
                        gates_d.tensor, e * T + tt * TT, [[0, 128], [1, 512]])), 16)
                    mk(R + f"x.gb.{e}.{q_}.{tt}", BCS)

            @block.vector
            def _(v, e=e, q_=q_, tt=tt):
                if e < 8 and q_ == 0:
                    w_(v, R + f"x.gb.{e}.{q_}.{tt}")
                    w_(v, R + f"x.heu.{e-1}.{tt}" if e > 0 else "x.-1")
                    for k in range(DCH):
                        DV.inc(v.tensor_tensor(fm(B16, k, tt * TT, (tt + 1) * TT),
                                               fm(A16, k, tt * TT, (tt + 1) * TT), wgb, OP.mult))
                    mk(R + f"x.he.{e}.{tt}", DV)

            for dc in range(4):
                G = (Q * 4 + tt) * 4 + dc
                dp = dc % 2

                @block.tensor
                def _(t, e=e, q_=q_, tt=tt, dc=dc, gated=gated, G=G, dp=dp, p=p, Q=Q):
                    if tt == 0 and dc == 0:
                        bld.fence(t, q_w1[p])
                        bld.fence(t, q_w3[p])
                    if G >= 2:
                        Gp = G - 2
                        eP, qP = exq[Gp // 16]
                        ttP, dcP = (Gp % 16) // 4, Gp % 4
                        w_(t, R + f"x.sil.{eP}.{qP}.{ttP}.{dcP}")
                        w_(t, R + f"x.gu.{eP}.{qP}.{ttP}.{dcP}")
                    if Q == 0 and tt == 0 and dc == 1:
                        w_(t, R + "g.sm.15")
                    if gated:
                        w_(t, R + f"x.he.{e}.{tt}")
                    for k in range(DCH):
                        t.matmul(gbank[dp][:], w1b[p][:, k * 512 + dc * 128: k * 512 + dc * 128 + 128],
                                 fm(A16, k, tt * TT, (tt + 1) * TT),
                                 start=(k == 0), stop=(k == DCH - 1))
                    for k in range(DCH):
                        mmi = t.matmul(ubank[dp][:], w3b[p][:, k * 512 + dc * 128: k * 512 + dc * 128 + 128],
                                       fm(B16, k, tt * TT, (tt + 1) * TT) if gated else fm(A16, k, tt * TT, (tt + 1) * TT),
                                       start=(k == 0), stop=(k == DCH - 1))
                    PE.inc(mmi)
                    mk(R + f"x.mm13.{e}.{q_}.{tt}.{dc}", PE)
                    if q_ == 3 and dc == 3:
                        mk(R + f"x.heu.{e}.{tt}", PE)

                @block.scalar
                def _(s, e=e, q_=q_, tt=tt, dc=dc, G=G, dp=dp):
                    w_(s, R + f"x.mm13.{e}.{q_}.{tt}.{dc}")
                    if G >= 2:
                        Gp = G - 2
                        eP, qP = exq[Gp // 16]
                        ttP, dcP = (Gp % 16) // 4, Gp % 4
                        w_(s, R + f"x.gu.{eP}.{qP}.{ttP}.{dcP}")
                    AC.inc(s.activation(sil_b[dp], gbank[dp][:], AF.Silu))
                    mk(R + f"x.sil.{e}.{q_}.{tt}.{dc}", AC)

                @block.vector
                def _(v, e=e, q_=q_, tt=tt, dc=dc, G=G, dp=dp, Q=Q):
                    w_(v, R + f"x.sil.{e}.{q_}.{tt}.{dc}")
                    TTg = Q * 4 + tt
                    if TTg >= 1:
                        eT, qT = exq[(TTg - 1) // 4]
                        w_(v, R + f"x.mm2g.{eT}.{qT}.{(TTg - 1) % 4}")
                    DV.inc(v.tensor_tensor(gu(dc), sil_b[dp], ubank[dp][:], OP.mult))
                    mk(R + f"x.gu.{e}.{q_}.{tt}.{dc}", DV)

            for mo in range(DCH):
                @block.tensor
                def _(t, e=e, q_=q_, tt=tt, mo=mo):
                    if mo == 0:
                        bld.fence(t, WS[2])
                        w_(t, R + f"x.gu.{e}.{q_}.{tt}.3")
                    w_(t, R + f"x.ev.{e}.{q_}.{tt}.{mo-2}" if mo >= 2 else (R + f"x.ev.{e}.{q_}.{tt-1}.{6+mo}" if tt > 0 else (R + f"x.ev.{e}.{q_-1}.3.{6+mo}" if q_ > 0 else (R + f"x.ev.{e-1}.3.3.{6+mo}" if e > 0 else "x.-1"))))
                    for fc in range(4):
                        mmi = t.matmul(PB[2 + mo % 2][:], w2s[:, fc * 1024 + mo * 128: fc * 1024 + mo * 128 + 128],
                                       gu(fc), start=(fc == 0), stop=(fc == 3))
                    PE.inc(mmi)
                    mk(R + f"x.mm2.{e}.{q_}.{tt}.{mo}", PE)
                    if mo == DCH - 1:
                        mk(R + f"x.mm2g.{e}.{q_}.{tt}", PE)

                @block.vector
                def _(v, e=e, q_=q_, tt=tt, mo=mo):
                    w_(v, R + f"x.mm2.{e}.{q_}.{tt}.{mo}")
                    DV.inc(v.tensor_tensor(XR[:, mo * T + tt * TT: mo * T + (tt + 1) * TT],
                                           XR[:, mo * T + tt * TT: mo * T + (tt + 1) * TT],
                                           PB[2 + mo % 2][:], OP.add))
                    mk(R + f"x.ev.{e}.{q_}.{tt}.{mo}", DV)
    mk(R + ".done", DV)


def emit_attn_b(bld, block, env):
    g = lambda k: env[k]
    mk, w_, bar, fn = bld.mk, bld.w, bld.bar, bld.fence
    SP, GP, AC, DV, PE = g("SP"), g("GP"), g("AC"), g("DV"), g("PE")
    SPL, BCS, GL, WS = g("SPL"), g("BCS"), g("GL"), g("WT")
    XR, A16, B16, C16, WBs, FS, SS = g("XR"), g("A16"), g("B16"), g("C16"), g("WBs"), g("FS"), g("SS")
    PB, PT = g("PB"), g("PT")
    W, CCc, row_d = g("W"), g("CCc"), g("row_d")
    wbT_d, wbar_d, vtmB_d, vfm_d, grow_d = g("wbT_d"), g("wbar_d"), g("vtmB_d"), g("vfm_d"), g("grow_d")
    MKB, RT16, id16, mhi, mlo = g("MKB"), g("RT16"), g("id16"), g("mhi"), g("mlo")
    fm, ccol = g("fm"), g("ccol")
    stats_rows, row_math, rep512 = g("stats_rows"), g("row_math"), g("rep512")
    ep_d = g("ep_d")
    ytm_d = g("ytm_d")

    @block.vector
    def _(v):
        bar(v)
        for c in range(DCH):
            DV.inc(v.tensor_copy(fm(B16, c), fm(XR, c)))
    stats_rows("bln", B16, 0, 1, want_mean=True)

    def lnm():
        @block.vector
        def _(v):
            bar(v)
            DV.inc(v.tensor_scalar(FS[:, 2560:2576], FS[:, 2048:2064], 1.0 / D, None, OP.mult))
            DV.inc(v.tensor_scalar(FS[:, 2080:2096], FS[:, 2064:2080], 1.0 / D, None, OP.mult))
            fn(v, DV)
            DV.inc(v.tensor_tensor(FS[:, 2096:2112], FS[:, 2560:2576], FS[:, 2560:2576], OP.mult))
            fn(v, DV)
            DV.inc(v.tensor_tensor(FS[:, 2112:2128], FS[:, 2080:2096], FS[:, 2096:2112], OP.subtract))
            mk("bln.var", DV)

        @block.scalar
        def _(s):
            w_(s, "bln.var")
            AC.inc(s.activation(FS[:, 2128:2144], FS[:, 2112:2128], AF.Ln, bias=CCc[:, 114:115]))
            fn(s, AC)
            AC.inc(s.activation(FS[:, 2576:2592], FS[:, 2128:2144], AF.Exp, scale=-0.5))
            mk("bln.c", AC)
    row_math("blnm", lnm, [0, 1], 2, [0, 2], 2)

    mb, cpb = g("WBC")[:, 0:512], g("WBC")[:, 512:1024]
    t1s = FS[:, 2048:2560]
    for tt in range(4):
        @block.sync
        def _(q, tt=tt):
            if tt == 0:
                bar(q)
            w_(q, f"bln.ap.{DCH-1}.{tt-1}")
            rep512(q, 0, tt, mb)
            rep512(q, 2, tt, cpb)
            mk(f"bln.bc.{tt}", BCS)

        for c in range(DCH):
            @block.vector
            def _(v, tt=tt, c=c):
                w_(v, f"bln.bc.{tt}")
                fn(v, DV)
                DV.inc(v.tensor_tensor(t1s, XR[:, c * T + tt * TT: c * T + (tt + 1) * TT], mb, OP.subtract))
                fn(v, DV)
                DV.inc(v.tensor_tensor(fm(A16, c, tt * TT, (tt + 1) * TT), t1s, cpb, OP.mult))
                mk(f"bln.ap.{c}.{tt}", DV)
    mk("bh16", DV)

    ws = g("ws")
    WGB, BGB, WLB, BLB = g("WGB"), g("BGB"), g("WLB"), g("BLB")
    Cin = g("C")

    @block.sync
    def _(q):
        bar(q)
        SP.inc(q.dma_start(out=WGB, in_=Cin["wg_bc"]), 16)
        SP.inc(q.dma_start(out=BGB, in_=Cin["bg_bc"]), 16)
        SP.inc(q.dma_start(out=WLB, in_=Cin["wl_bc"]), 16)
        SP.inc(q.dma_start(out=BLB, in_=Cin["bl_bc"]), 16)
        mk("bgl", SP)

    def load_whalfT(q, widx, half, slot):
        for k in range(DCH):
            WS[slot].inc(q.dma_start(
                out=ws[slot][:, k * 512:(k + 1) * 512],
                in_=wbT_d[widx][half * 512:(half + 1) * 512, k * 128:(k + 1) * 128],
                transpose=True), 16)

    def projT(tag, widx, dest_pool):
        @block.scalar
        def _(s):
            bar(s)
            load_whalfT(s, widx, 0, 0)
            load_whalfT(s, widx, 1, 1)
            mk(f"{tag}.wT", AC)

        for mo in range(DCH):
            for tt in range(4):
                @block.tensor
                def _(t, mo=mo, tt=tt):
                    if mo == 0 and tt == 0:
                        bld.fence(t, WS[0])
                        bld.fence(t, WS[1])
                    w_(t, f"{tag}.ev.{mo}.{tt-1}" if tt > 0 else f"{tag}.ev.{mo-2}.3")
                    half, m4 = mo // 4, mo % 4
                    for k in range(DCH):
                        mmi = t.matmul(PB[mo % 2][:], ws[half][:, k * 512 + m4 * 128: k * 512 + m4 * 128 + 128],
                                       fm(A16, k, tt * TT, (tt + 1) * TT),
                                       start=(k == 0), stop=(k == DCH - 1))
                    PE.inc(mmi)
                    mk(f"{tag}.mm.{mo}.{tt}", PE)

                @block.vector
                def _(v, mo=mo, tt=tt):
                    w_(v, f"{tag}.mm.{mo}.{tt}")
                    DV.inc(v.tensor_copy(fm(dest_pool, mo, tt * TT, (tt + 1) * TT), PB[mo % 2][:]))
                    mk(f"{tag}.ev.{mo}.{tt}", DV)
    projT("bq", 0, B16)

    vsl = [C16[:, 0:512], C16[:, 512:1024]]

    @block.scalar
    def _(s):
        bar(s)
        load_whalfT(s, 2, 0, 0)
        load_whalfT(s, 2, 1, 1)
        mk("bv.wT", AC)

    for mo in range(DCH):
        for tt in range(4):
            i_ = mo * 4 + tt

            @block.tensor
            def _(t, mo=mo, tt=tt, i_=i_):
                if mo == 0 and tt == 0:
                    bld.fence(t, WS[0])
                    bld.fence(t, WS[1])
                w_(t, f"bv.ev.{i_-2}")
                half, m4 = mo // 4, mo % 4
                for k in range(DCH):
                    mmi = t.matmul(PB[i_ % 2][:], ws[half][:, k * 512 + m4 * 128: k * 512 + m4 * 128 + 128],
                                   fm(A16, k, tt * TT, (tt + 1) * TT),
                                   start=(k == 0), stop=(k == DCH - 1))
                PE.inc(mmi)
                mk(f"bv.mm.{i_}", PE)

            @block.vector
            def _(v, i_=i_):
                w_(v, f"bv.mm.{i_}")
                w_(v, f"bv.sp.{i_-2}")
                DV.inc(v.tensor_copy(vsl[i_ % 2], PB[i_ % 2][:]))
                mk(f"bv.ev.{i_}", DV)

            @block.sync
            def _(q, mo=mo, tt=tt, i_=i_):
                w_(q, f"bv.ev.{i_}")
                SPL.inc(q.dma_start(out=vfm_d.rearrange("(c p) t -> p c t", p=128)[:, mo, tt * TT:(tt + 1) * TT],
                                    in_=vsl[i_ % 2]), 16)
                mk(f"bv.sp.{i_}", SPL)

    for nh in range(2):
        @block.scalar
        def _(s, nh=nh):
            bar(s)
            load_whalfT(s, 2, nh, 2)
            mk(f"bvt.wT.{nh}", AC)

        for w in range(16):
            i_ = nh * 16 + w

            @block.tensor
            def _(t, nh=nh, w=w, i_=i_):
                if w == 0:
                    bld.fence(t, WS[2])
                w_(t, f"bvt.ev.{i_-2}")
                for k in range(DCH):
                    mmi = t.matmul(PB[i_ % 2][:], fm(A16, k, w * 128, (w + 1) * 128),
                                   ws[2][:, k * 512:(k + 1) * 512],
                                   start=(k == 0), stop=(k == DCH - 1))
                PE.inc(mmi)
                mk(f"bvt.mm.{i_}", PE)

            @block.vector
            def _(v, i_=i_):
                w_(v, f"bvt.mm.{i_}")
                w_(v, f"bvt.sp.{i_-2}")
                DV.inc(v.tensor_copy(vsl[i_ % 2], PB[i_ % 2][:]))
                mk(f"bvt.ev.{i_}", DV)

            @block.sync
            def _(q, nh=nh, w=w, i_=i_):
                w_(q, f"bvt.ev.{i_}")
                SPL.inc(q.dma_start(out=vtmB_d[nh][w * 128:(w + 1) * 128, :], in_=vsl[i_ % 2]), 16)
                mk(f"bvt.sp.{i_}", SPL)

    for nh in range(2):
        @block.scalar
        def _(s, nh=nh):
            bar(s)
            load_whalfT(s, 1, nh, 2)
            mk(f"bk.wT.{nh}", AC)

        for w in range(16):
            i_ = nh * 16 + w
            ktl = C16[:, 1024:2048]
            sqt = C16[:, 2048:3072]
            knt = C16[:, 3072:3584]
            wbt = C16[:, 3584:4096]
            ept = C16[:, 4096:4608]
            lrt = FS[:, 2048:2560]
            rkc = FS[:, 2560:2562]

            @block.tensor
            def _(t, nh=nh, w=w, i_=i_):
                if w == 0:
                    bld.fence(t, WS[2])
                w_(t, f"bk.kn.{i_-1}")
                for k in range(DCH):
                    mmi = t.matmul(PB[0][:], fm(A16, k, w * 128, (w + 1) * 128),
                                   ws[2][:, k * 512:(k + 1) * 512],
                                   start=(k == 0), stop=(k == DCH - 1))
                PE.inc(mmi)
                mk(f"bk.mm.{i_}", PE)

            @block.scalar
            def _(s, nh=nh, i_=i_):
                w_(s, f"bk.mm.{i_}")
                w_(s, f"bk.wb.{i_-1}")
                AC.inc(s.activation(sqt[:, 0:512], PB[0][:], AF.Square))
                mk(f"bk.sq.{i_}", AC)

            @block.vector
            def _(v, nh=nh, i_=i_):
                w_(v, f"bk.sq.{i_}")
                DV.inc(v.tensor_reduce(rkc[:, 0:1], sqt[:, 0:512], AX, OP.add))
                mk(f"bk.ss.{i_}", DV)

            @block.scalar
            def _(s, nh=nh, i_=i_):
                w_(s, f"bk.ss.{i_}")
                AC.inc(s.activation(FS[:, 2562:2563], rkc[:, 0:1], AF.Sqrt))
                mk(f"bk.sr.{i_}", AC)

            @block.vector
            def _(v, nh=nh, i_=i_):
                w_(v, f"bk.sr.{i_}")
                DV.inc(v.tensor_scalar(FS[:, 2563:2564], FS[:, 2562:2563], 1e-12, None, OP.max))
                mk(f"bk.mx.{i_}", DV)

            @block.scalar
            def _(s, nh=nh, i_=i_):
                w_(s, f"bk.mx.{i_}")
                AC.inc(s.activation(FS[:, 2564:2565], FS[:, 2563:2564], AF.Ln))
                fn(s, AC)
                AC.inc(s.activation(FS[:, 2565:2566], FS[:, 2564:2565], AF.Exp, scale=-1.0))
                mk(f"bk.rk.{i_}", AC)

            @block.vector
            def _(v, nh=nh, i_=i_):
                w_(v, f"bk.rk.{i_}")
                DV.inc(v.tensor_scalar(knt[:, 0:512], PB[0][:], FS[:, 2565:2566], None, OP.mult))
                mk(f"bk.kn.{i_}", DV)

            @block.vector
            def _(v, nh=nh, i_=i_):
                w_(v, f"bk.kn.{i_}")
                w_(v, f"bk.wb.{i_-1}")
                DV.inc(v.tensor_tensor(sqt[:, 0:512], knt[:, 0:512], WGB[:, nh * 512:(nh + 1) * 512], OP.mult))
                DV.inc(v.tensor_tensor(sqt[:, 512:1024], knt[:, 0:512], WLB[:, nh * 512:(nh + 1) * 512], OP.mult))
                fn(v, DV)
                DV.inc(v.tensor_tensor(sqt[:, 0:512], sqt[:, 0:512], BGB[:, nh * 512:(nh + 1) * 512], OP.add))
                DV.inc(v.tensor_tensor(sqt[:, 512:1024], sqt[:, 512:1024], BLB[:, nh * 512:(nh + 1) * 512], OP.add))
                mk(f"bk.lg.{i_}", DV)

            @block.scalar
            def _(s, nh=nh, i_=i_):
                w_(s, f"bk.lg.{i_}")
                AC.inc(s.activation(sqt[:, 0:512], sqt[:, 0:512], AF.Sigmoid))
                AC.inc(s.activation(sqt[:, 512:1024], sqt[:, 512:1024], AF.Sigmoid))
                mk(f"bk.sg.{i_}", AC)

            @block.vector
            def _(v, nh=nh, i_=i_):
                w_(v, f"bk.sg.{i_}")
                DV.inc(v.tensor_scalar(sqt[:, 512:1024], sqt[:, 512:1024], -1.0, 1.0, OP.mult, OP.add))
                fn(v, DV)
                DV.inc(v.tensor_tensor(sqt[:, 0:512], sqt[:, 0:512], sqt[:, 512:1024], OP.mult))
                fn(v, DV)
                DV.inc(v.tensor_scalar(sqt[:, 0:512], sqt[:, 0:512], RFLOOR, None, OP.max))
                mk(f"bk.rho.{i_}", DV)

            @block.scalar
            def _(s, nh=nh, i_=i_):
                w_(s, f"bk.rho.{i_}")
                AC.inc(s.activation(lrt[:, 0:512], sqt[:, 0:512], AF.Ln))
                mk(f"bk.lr.{i_}", AC)

            @block.vector
            def _(v, nh=nh, i_=i_):
                w_(v, f"bk.lr.{i_}")
                DV.inc(v.tensor_copy(C16[:, 4608:5120], lrt[:, 0:512]))
                mk(f"bk.lr16.{i_}", DV)

            @block.tensor
            def _(t, nh=nh, i_=i_):
                w_(t, f"bk.lr16.{i_}")
                w_(t, f"bk.ep.{i_-1}")
                t.matmul(PB[1][:], RT16[:], C16[:, 4608:5120], start=True, stop=True)
                PE.inc(t.matmul(PB[2][0:2, :], g("KON")[:, 131:133], C16[:, 4608:5120], start=True, stop=True))
                mk(f"bk.lrmm.{i_}", PE)

            @block.scalar
            def _(s, nh=nh, i_=i_, w=w):
                w_(s, f"bk.lrmm.{i_}")
                w_(s, f"bk.wsp.{i_-1}")
                AC.inc(s.activation(ept[:, 0:512], PB[1][:], AF.Exp))
                fn(s, AC)
                AC.inc(s.activation(sqt[:, 512:1024], PB[1][:], AF.Exp, scale=-1.0))
                AC.inc(s.activation(FS[0:2, 2566 + 0:2566 + 512][0:2, 0:512] if False else FS[0:2, 2566:3078], PB[2][0:2, :], AF.Copy))
                mk(f"bk.ep.{i_}", AC)

            @block.vector
            def _(v, nh=nh, i_=i_):
                w_(v, f"bk.ep.{i_}")
                w_(v, f"bk.wsp.{i_-2}")
                DV.inc(v.tensor_scalar(wbt[:, 0:512], sqt[:, 0:512], -1.0, 1.0, OP.mult, OP.add))
                fn(v, DV)
                DV.inc(v.tensor_tensor(wbt[:, 0:512], wbt[:, 0:512], knt[:, 0:512], OP.mult))
                fn(v, DV)
                DV.inc(v.tensor_tensor(wbt[:, 0:512], wbt[:, 0:512], sqt[:, 512:1024], OP.mult))
                mk(f"bk.wb.{i_}", DV)

            @block.sync
            def _(q, nh=nh, w=w, i_=i_):
                w_(q, f"bk.wb.{i_}")
                SPL.inc(q.dma_start(out=wbar_d[nh][w * 128:(w + 1) * 128, :], in_=wbt[:, 0:512]), 16)
                SPL.inc(q.dma_start(out=ep_d[nh][w * 128:(w + 1) * 128, :], in_=ept[:, 0:512]), 16)
                SPL.inc(q.dma_start(out=grow_d[nh][0:2, w * 512:(w + 1) * 512] if False else
                        bass.AP(grow_d.tensor, nh * 16 * 1024 + w * 1024, [[512, 2], [1, 512]]),
                        in_=FS[0:2, 2566:3078]), 16)
                mk(f"bk.wsp.{i_}", SPL)

    for nh in range(2):
        for c in range(16):
            @block.sync
            def _(q, nh=nh, c=c):
                if nh == 0 and c == 0:
                    bar(q)
                w_(q, f"bg.sp.{nh}.{c-1}" if c > 0 else (f"bg.sp.{nh-1}.15" if nh > 0 else "x.-1"))
                SP.inc(q.dma_start(out=FS[0:1, 2048:2560],
                                   in_=bass.AP(grow_d.tensor, nh * 16384 + c * 1024 + 512, [[512, 1], [1, 512]])), 16)
                if c < 15:
                    SP.inc(q.dma_start(out=FS[0:1, 2560:3072],
                                       in_=bass.AP(grow_d.tensor, nh * 16384 + (c + 1) * 1024, [[512, 1], [1, 512]])), 16)
                mk(f"bg.ld.{nh}.{c}", SP)

            @block.vector
            def _(v, nh=nh, c=c):
                w_(v, f"bg.ld.{nh}.{c}")
                if c < 15:
                    DV.inc(v.tensor_tensor(FS[0:1, 3072:3584], FS[0:1, 2048:2560], FS[0:1, 2560:3072], OP.add))
                else:
                    DV.inc(v.tensor_copy(FS[0:1, 3072:3584], FS[0:1, 2048:2560]))
                mk(f"bg.add.{nh}.{c}", DV)

            @block.scalar
            def _(s, nh=nh, c=c):
                w_(s, f"bg.add.{nh}.{c}")
                AC.inc(s.activation(FS[0:1, 3584:4096], FS[0:1, 3072:3584], AF.Exp))
                mk(f"bg.exp.{nh}.{c}", AC)

            @block.sync
            def _(q2, nh=nh, c=c):
                w_(q2, f"bg.exp.{nh}.{c}")
                SPL.inc(q2.dma_start(out=bass.AP(grow_d.tensor, 32768 + nh * 16384 + c * 512, [[512, 1], [1, 512]]),
                                     in_=FS[0:1, 3584:4096]), 16)
                mk(f"bg.sp.{nh}.{c}", SPL)

    S16 = lambda vc: A16[:, 8192 + vc * 512: 8192 + (vc + 1) * 512]
    qsl = lambda kc, c: B16[:, 0:0]
    am16 = WBs[:, 0:128]
    wbc_t = C16[:, 0:512]
    vtc = C16[:, 512:1024]
    epc = C16[:, 1024:1536]
    vfc = C16[:, 1536:2048]
    ytc = C16[:, 2048:2560]
    gb_t = g("WBC")[:, 0:512]

    @block.vector
    def _(v):
        bar(v)
        DV.inc(v.memset(SS[:, 0:4096], 0.0))
        mk("bs.s.-1", DV)

    for h in range(2):
        for c in range(16):
            @block.sync
            def _(q, h=h, c=c):
                if h == 0 and c == 0:
                    bar(q)
                w_(q, f"bs.sdone.{h}.{c-1}" if c > 0 else (f"bs.sdone.{h-1}.15" if h > 0 else "x.-1"))
                GL.inc(q.dma_start(out=wbc_t, in_=wbar_d[h][c * 128:(c + 1) * 128, :]), 16)
                GL.inc(q.dma_start(out=vtc, in_=vtmB_d[h][c * 128:(c + 1) * 128, :]), 16)
                GL.inc(q.dma_start(out=epc, in_=ep_d[h][c * 128:(c + 1) * 128, :]), 16)
                GL.inc(q.dma_start(out=vfc.rearrange("p (a b) -> p a b", a=4),
                                   in_=vfm_d.rearrange("(c p) t -> p c t", p=128)[:, 4 * h: 4 * h + 4, c * 128:(c + 1) * 128]), 16)
                BCS.inc(q.dma_start(out=gb_t, in_=bass.AP(
                    grow_d.tensor, 32768 + h * 16384 + c * 512, [[0, 128], [1, 512]])), 16)
                mk(f"bs.ld.{h}.{c}", GL)

            @block.vector
            def _(v, h=h, c=c):
                w_(v, f"bs.ld.{h}.{c}")
                w_(v, f"bs.s16u.{h}.{c-1}" if c > 0 else (f"bs.s16u.{h-1}.15" if h > 0 else "x.-1"))
                for vc in range(4):
                    DV.inc(v.tensor_copy(S16(vc), SS[:, h * 2048 + vc * 512: h * 2048 + (vc + 1) * 512]))
                mk(f"bs.s16.{h}.{c}", DV)
                mk(f"bs.s16u.{h}.{c}", DV)

            @block.tensor
            def _(t, h=h, c=c):
                w_(t, f"bs.s16.{h}.{c}")
                w_(t, f"bs.am.{h}.{c-1}" if c > 0 else (f"bs.am.{h-1}.15" if h > 0 else "x.-1"))
                for kc in range(4):
                    mmi = t.matmul(PB[0][:, 0:128], vfc[:, kc * 128:(kc + 1) * 128],
                                   B16[:, (4 * h + kc) * T + c * 128: (4 * h + kc) * T + (c + 1) * 128],
                                   start=(kc == 0), stop=(kc == 3))
                PE.inc(mmi)
                mk(f"bs.amm.{h}.{c}", PE)

            @block.vector
            def _(v, h=h, c=c):
                w_(v, f"bs.amm.{h}.{c}")
                DV.inc(v.tensor_tensor(am16[:, 0:128], PB[0][:, 0:128], MKB[:], OP.mult))
                mk(f"bs.am.{h}.{c}", DV)

            @block.tensor
            def _(t, h=h, c=c):
                w_(t, f"bs.am.{h}.{c}")
                w_(t, f"bs.ye.{h}.{c-1}" if c > 0 else (f"bs.ye.{h-1}.15" if h > 0 else "x.-1"))
                t.matmul(PB[1][:], am16[:, 0:128], wbc_t, start=True, stop=False)
                for kc in range(4):
                    mmi = t.matmul(PB[1][:], B16[:, (4 * h + kc) * T + c * 128: (4 * h + kc) * T + (c + 1) * 128],
                                   S16(kc), start=False, stop=(kc == 3))
                PE.inc(mmi)
                mk(f"bs.ymm.{h}.{c}", PE)

            @block.vector
            def _(v, h=h, c=c):
                w_(v, f"bs.ymm.{h}.{c}")
                w_(v, f"bs.ysp.{h}.{c-1}" if c > 0 else (f"bs.ysp.{h-1}.15" if h > 0 else "x.-1"))
                DV.inc(v.tensor_tensor(ytc, PB[1][:], epc, OP.mult))
                mk(f"bs.ye.{h}.{c}", DV)

            @block.sync
            def _(q, h=h, c=c):
                w_(q, f"bs.ye.{h}.{c}")
                SPL.inc(q.dma_start(out=ytm_d[h][c * 128:(c + 1) * 128, :], in_=ytc), 16)
                mk(f"bs.ysp.{h}.{c}", SPL)

            for vc in range(4):
                @block.tensor
                def _(t, h=h, c=c, vc=vc):
                    w_(t, f"bs.su.{h}.{c}.{vc-1}" if vc > 0 else f"bs.ye.{h}.{c}")
                    PE.inc(t.matmul(PB[2][:], vtc[:, vc * 128:(vc + 1) * 128], wbc_t, start=True, stop=True))
                    mk(f"bs.smm.{h}.{c}.{vc}", PE)

                @block.vector
                def _(v, h=h, c=c, vc=vc):
                    w_(v, f"bs.smm.{h}.{c}.{vc}")
                    w_(v, "bs.ld." + f"{h}.{c}")
                    bld.fence(v, g("BCS"))
                    sl = SS[:, h * 2048 + vc * 512: h * 2048 + (vc + 1) * 512]
                    DV.inc(v.tensor_tensor(sl, sl, PB[2][:], OP.add))
                    fn(v, DV)
                    DV.inc(v.tensor_tensor(sl, sl, gb_t, OP.mult))
                    mk(f"bs.su.{h}.{c}.{vc}", DV)
            mk(f"bs.sdone.{h}.{c}", DV)

    @block.scalar
    def _(s):
        bar(s)
        for h in range(2):
            for kc in range(4):
                WS[0].inc(s.dma_start(out=fm(A16, 4 * h + kc),
                                      in_=ytm_d[h][0:T, kc * 128:(kc + 1) * 128], transpose=True), 16)
        load_whalfT(s, 3, 0, 1)
        mk("bo.w0", AC)

    for mo in range(DCH):
        if mo == 4:
            @block.scalar
            def _(s):
                bar(s)
                load_whalfT(s, 3, 1, 1)
                mk("bo.w1", AC)
        for tt in range(4):
            @block.tensor
            def _(t, mo=mo, tt=tt):
                if tt == 0:
                    bld.fence(t, WS[1])
                    bld.fence(t, WS[0])
                w_(t, f"bo.ev.{mo}.{tt-1}" if tt > 0 else f"bo.ev.{mo-2}.3")
                m4 = mo % 4
                for k in range(DCH):
                    mmi = t.matmul(PB[mo % 2][:], ws[1][:, k * 512 + m4 * 128: k * 512 + m4 * 128 + 128],
                                   fm(A16, k, tt * TT, (tt + 1) * TT),
                                   start=(k == 0), stop=(k == DCH - 1))
                PE.inc(mmi)
                mk(f"bo.mm.{mo}.{tt}", PE)

            @block.vector
            def _(v, mo=mo, tt=tt):
                w_(v, f"bo.mm.{mo}.{tt}")
                DV.inc(v.tensor_tensor(XR[:, mo * T + tt * TT: mo * T + (tt + 1) * TT],
                                       XR[:, mo * T + tt * TT: mo * T + (tt + 1) * TT],
                                       PB[mo % 2][:], OP.add))
                mk(f"bo.ev.{mo}.{tt}", DV)
    mk("attnb.done", DV)


def kernel(**inputs):
    in_maps = host_prep(inputs)
    nc = build(stage=STAGE, dumps=DUMPS)
    res = run_bass_kernel_spmd(nc, in_maps, core_ids=list(range(8)))
    return np.stack([np.asarray(r["out"], np.float32).T for r in res.results], axis=0)
